# version 08
# baseline (speedup 1.0000x reference)
import contextlib
import math

import numpy as np

import concourse.bass as bass
import concourse.mybir as mybir
from concourse.bass import IndirectOffsetOnAxis
from concourse.bass_utils import run_bass_kernel_spmd

F32 = mybir.dt.float32
BF16 = mybir.dt.bfloat16
I32 = mybir.dt.int32
U8 = mybir.dt.uint8
AF = mybir.ActivationFunctionType
ALU = mybir.AluOpType
AX = mybir.AxisListType

D = 1024
L = 16384
H = 16
DK = 64
U = 50
NG = 32
QTOK = 4096
NE = 4
DI = 4096
BISECT = 34
SCALE = 1.0 / math.sqrt(DK)


class Prog:
    ENGINES = ('pe', 'dve', 'act', 'pool', 'sp')

    def __init__(self, nc):
        self.nc = nc
        self.stack = contextlib.ExitStack()
        self.csem = {}
        self.ccount = {}
        for e in ('pe', 'dve', 'act', 'pool'):
            self.csem[e] = self.stack.enter_context(nc.semaphore(f"c_{e}"))
            self.ccount[e] = 0
        self.dsem = {}
        self.dcount = {}
        self.dqueue = {}
        self.ops = {e: [] for e in self.ENGINES}
        self.last_w = {}
        self.readers = {}
        self.know = {e: {} for e in self.ENGINES}
        self.read_misses = set()

    def chan(self, name, engine):
        if name not in self.dsem:
            self.dsem[name] = self.stack.enter_context(self.nc.semaphore(f"d_{name}"))
            self.dcount[name] = 0
            self.dqueue[name] = engine
        assert self.dqueue[name] == engine, (name, engine, self.dqueue[name])
        return name

    def _resolve(self, dep):
        if dep[0] == 'c':
            _, e, v = dep
            return (f"c_{e}", self.csem[e], v)
        _, ch = dep[:2]
        return (f"d_{ch}", self.dsem[ch], 16 * self.dcount[ch])

    def emit(self, engine, fn, reads=(), writes=(), dma=None, tag=None):
        deps = []
        for k in reads:
            w = self.last_w.get(k)
            if w is None:
                self.read_misses.add(k)
            else:
                deps.append(w)
        for k in writes:
            w = self.last_w.get(k)
            if w is not None:
                deps.append(w)
            deps.extend(self.readers.get(k, ()))
        need = {}
        for dep in deps:
            if dep[0] == 'c' and dep[1] == 'pe' and engine == 'pe' and dma is None:
                continue
            name, handle, v = self._resolve(dep)
            if self.know[engine].get(name, -1) >= v:
                continue
            if name in need and need[name][1] >= v:
                continue
            need[name] = (handle, v)
        waits = []
        for name, (handle, v) in need.items():
            waits.append((handle, v))
            self.know[engine][name] = v

        if dma is not None:
            ch = self.chan(dma, engine)
            self.dcount[ch] += 1
            inc = (self.dsem[ch], 16)
            marker = ('d', ch, self.dcount[ch])
        elif engine == 'sp':
            inc = None
            marker = None
        else:
            self.ccount[engine] += 1
            inc = (self.csem[engine], 1)
            marker = ('c', engine, self.ccount[engine])

        self.ops[engine].append((waits, fn, inc, tag))
        if marker is not None:
            for k in writes:
                self.last_w[k] = marker
                self.readers[k] = []
            for k in reads:
                self.readers.setdefault(k, []).append(marker)
        return marker

    def wait_all_dma(self, engine, channel):
        if channel not in self.dsem:
            return
        self.ops[engine].append(
            ([(self.dsem[channel], 16 * self.dcount[channel])], None, None, 'wd'))
        self.know[engine][f"d_{channel}"] = 16 * self.dcount[channel]

    def fence(self):
        targets = []
        for e in ('pe', 'dve', 'act', 'pool'):
            if self.ccount[e]:
                targets.append((f"c_{e}", self.csem[e], self.ccount[e]))
        for ch in self.dsem:
            if self.dcount[ch]:
                targets.append((f"d_{ch}", self.dsem[ch], 16 * self.dcount[ch]))
        for e in self.ENGINES:
            ws = []
            for name, handle, v in targets:
                if name == f"c_{e}":
                    continue
                if self.know[e].get(name, -1) >= v:
                    continue
                ws.append((handle, v))
                self.know[e][name] = v
            if ws:
                self.ops[e].append((ws, None, None, 'fence'))

    def build(self):
        nc = self.nc
        if self.read_misses:
            import sys
            print("PROG read-before-write keys:", sorted(map(str, self.read_misses)),
                  file=sys.stderr)
        with nc.Block() as blk:
            def replay(ename):
                def _go(eng):
                    for waits, fn, inc, tag in self.ops[ename]:
                        for handle, v in waits:
                            eng.wait_ge(handle, v)
                        if fn is not None:
                            inst = fn(eng)
                            if inc is not None:
                                inst.then_inc(inc[0], inc[1])
                return _go
            if self.ops['pe']: blk.tensor(replay('pe'))
            if self.ops['dve']: blk.vector(replay('dve'))
            if self.ops['act']: blk.scalar(replay('act'))
            if self.ops['pool']: blk.gpsimd(replay('pool'))
            if self.ops['sp']: blk.sync(replay('sp'))

    def close(self):
        self.stack.close()


def build_program(debug=False, maxphase=9, ngroups=NG, nbisect=BISECT, parts='all'):
    nc = bass.Bass("TRN2", target_bir_lowering=False)

    def din(name, shape, dt=F32):
        return nc.dram_tensor(name, shape, dt, kind="ExternalInput").ap()
    x_d = din("x", [L, D])
    xq_d = din("xq", [QTOK, D])
    wqT_d = din("wqT", [D, D]); wkT_d = din("wkT", [D, D]); wvT_d = din("wvT", [D, D])
    woT_d = din("woT", [D, D])
    bq_d = din("bq", [1, D]); bk_d = din("bk", [1, D]); bv_d = din("bv", [1, D])
    bo_d = din("bo", [1, D])
    idxk_d = din("idxk", [U], I32)
    wr_d = din("wr", [D, NE])
    w1r_d = din("w1r", [32, 8, NE, 128, 128])
    b1r_d = din("b1r", [NE, 32 * 128])
    wof_d = din("wof", [DI, D])
    bof_d = din("bof", [1, D]); gam_d = din("gam", [1, D]); bet_d = din("bet", [1, D])
    ident_d = din("ident", [128, 128])
    iotacol_d = din("iotacol", [128, 128])
    iotaM_d = din("iotaM", [128, H * 128])
    qoff_d = din("qoffc", [128, 1])
    out_d = nc.dram_tensor("y", [QTOK, D], F32, kind="ExternalOutput").ap()
    dbg = {}
    if debug:
        def dout(name, shape, dt=F32):
            dbg[name] = nc.dram_tensor(name, shape, dt, kind="ExternalOutput").ap()
        dout("dbg_M", [128, H * 128]); dout("dbg_idx", [64, H])
        dout("dbg_vmean", [1, D]); dout("dbg_base", [1, D])
        dout("dbg_delta", [128, D]); dout("dbg_y1", [128, D])
        dout("dbg_t1", [128, D]); dout("dbg_lo", [128, H])

    kfm_d = nc.dram_tensor("kfm", [8, 128, L], BF16, kind="Internal").ap()
    vscr_d = nc.dram_tensor("vscr", [8, 128, 128, 128], BF16, kind="Internal").ap()
    t1s_d = nc.dram_tensor("t1s", [QTOK, D], F32, kind="Internal").ap()
    intT_d = nc.dram_tensor("intT", [32, 128, 1024], BF16, kind="Internal").ap()
    y1s_d = nc.dram_tensor("y1s", [QTOK, D], F32, kind="Internal").ap()

    p = Prog(nc)
    _E = p.emit
    def mkE(label):
        def E(*a, **k):
            if parts == 'all' or label in parts:
                return _E(*a, **k)
        return E
    E = _E
    sbstack = contextlib.ExitStack()

    def sbuf(stk, n, s, d=F32):
        return stk.enter_context(nc.sbuf_tensor(n, s, d))

    def psum(stk, n, s, d=F32):
        return stk.enter_context(nc.psum_tensor(n, s, d))

    perm = sbstack
    id_bf = sbuf(perm, "id_bf_s", [128, 128], BF16)
    idf32 = sbuf(perm, "idf32_s", [128, 128], F32)
    iotacol = sbuf(perm, "iotacol_s", [128, 128], F32)
    qoffc = sbuf(perm, "qoffc_s", [128, 1], F32)
    ones_row = sbuf(perm, "ones_row_s", [1, 512], BF16)
    ones_col = sbuf(perm, "ones_col_s", [128, 1], BF16)
    ones_sq = sbuf(perm, "ones_sq_s", [128, 128], BF16)
    one1 = sbuf(perm, "one1_s", [1, 1], BF16)
    epsA = sbuf(perm, "epsA_s", [128, 1], F32)
    epsB = sbuf(perm, "epsB_s", [128, 1], F32)
    negb = sbuf(perm, "negb_s", [128, 1], F32)
    ksampT = sbuf(perm, "ksampT_s", [128, 8, 104], BF16)
    xbar_bf = sbuf(perm, "xbar_bf_s", [1, D], BF16)
    xbarT = sbuf(perm, "xbarT_s", [128, 8], BF16)
    vmean = sbuf(perm, "vmean_s", [1, D], F32)
    vmean_bf = sbuf(perm, "vmean_bf_s", [1, D], BF16)
    vmeanT = sbuf(perm, "vmeanT_s", [128, 8], F32)
    vmeanT_bf = sbuf(perm, "vmeanT_bf_s", [128, 8], BF16)
    base_bf = sbuf(perm, "base_bf_s", [1, D], BF16)
    idxT_f = sbuf(perm, "idxT_f_s", [64, H], F32)
    idxT_i = sbuf(perm, "idxT_i_s", [64, H], I32)
    delta = [sbuf(perm, f"delta{c}", [128, D], BF16) for c in range(8)]
    idxv = [sbuf(perm, f"idxv{c}", [128, 1], F32) for c in range(8)]
    junk1 = sbuf(perm, "junk1_s", [128, 1], F32)

    E('pool', lambda e: e.dma_start(id_bf[:, :], ident_d), writes=['id_bf'], dma='c_idbf')
    E('sp', lambda e: e.dma_start(idf32[:, :], ident_d), writes=['idf32'], dma='c_idf')
    E('sp', lambda e: e.dma_start(iotacol[:, :], iotacol_d), writes=['iotacol'], dma='c_ioc')
    E('sp', lambda e: e.dma_start(qoffc[:, :], qoff_d), writes=['qoffc'], dma='c_qof')
    E('dve', lambda e: e.memset(ones_row[:, :], 1.0), writes=['ones_row'])
    E('dve', lambda e: e.memset(ones_col[:, :], 1.0), writes=['ones_col'])
    E('dve', lambda e: e.memset(ones_sq[:, :], 1.0), writes=['ones_sq'])
    E('dve', lambda e: e.memset(one1[:, :], 1.0), writes=['one1'])
    E('dve', lambda e: e.memset(epsA[:, :], 1e-5), writes=['epsA'])
    E('dve', lambda e: e.memset(epsB[:, :], 1e-12), writes=['epsB'])
    E('dve', lambda e: e.memset(negb[:, :], -12.0), writes=['negb'])

    def layernorm(src_ap, src_keys, dst_ap, dst_key, rows, eps_tile, eps_key,
                  stats, skey, lnexp=False):
        rd = list(src_keys)
        E('dve', lambda e: e.reduce_sum(stats[0:rows, 0:1], src_ap, axis=AX.X),
          reads=rd, writes=[(skey, 0)])
        E('act', lambda e: e.activation(dst_ap, src_ap, AF.Square,
                                        accum_out=stats[0:rows, 1:2]),
          reads=rd, writes=[dst_key, (skey, 1)])
        E('dve', lambda e: e.tensor_scalar(stats[0:rows, 0:1], stats[0:rows, 0:1],
                                           1.0 / D, None, ALU.mult),
          reads=[(skey, 0)], writes=[(skey, 0)])
        E('dve', lambda e: e.tensor_scalar(stats[0:rows, 1:2], stats[0:rows, 1:2],
                                           1.0 / D, None, ALU.mult),
          reads=[(skey, 1)], writes=[(skey, 1)])
        E('dve', lambda e: e.tensor_tensor(stats[0:rows, 2:3], stats[0:rows, 0:1],
                                           stats[0:rows, 0:1], ALU.mult),
          reads=[(skey, 0)], writes=[(skey, 2)])
        E('dve', lambda e: e.tensor_tensor(stats[0:rows, 2:3], stats[0:rows, 1:2],
                                           stats[0:rows, 2:3], ALU.subtract),
          reads=[(skey, 1), (skey, 2)], writes=[(skey, 2)])
        if lnexp:
            E('act', lambda e: e.activation(stats[0:rows, 3:4], stats[0:rows, 2:3],
                                            AF.Ln, bias=eps_tile[0:rows, 0:1]),
              reads=[(skey, 2), eps_key], writes=[(skey, 3)])
            E('dve', lambda e: e.tensor_scalar(stats[0:rows, 3:4], stats[0:rows, 3:4],
                                               -0.5, None, ALU.mult),
              reads=[(skey, 3)], writes=[(skey, 3)])
            E('act', lambda e: e.activation(stats[0:rows, 3:4], stats[0:rows, 3:4],
                                            AF.Exp),
              reads=[(skey, 3)], writes=[(skey, 3)])
        else:
            E('act', lambda e: e.activation(stats[0:rows, 3:4], stats[0:rows, 2:3],
                                            AF.Sqrt, bias=eps_tile[0:rows, 0:1]),
              reads=[(skey, 2), eps_key], writes=[(skey, 3)])
            E('dve', lambda e: e.reciprocal(stats[0:rows, 3:4], stats[0:rows, 3:4]),
              reads=[(skey, 3)], writes=[(skey, 3)])
        E('dve', lambda e: e.tensor_scalar(dst_ap, src_ap, stats[0:rows, 0:1],
                                           stats[0:rows, 3:4], ALU.subtract, ALU.mult),
          reads=rd + [(skey, 0), (skey, 3)], writes=[dst_key])

    pha = contextlib.ExitStack()
    st_w = contextlib.ExitStack()
    st_m = contextlib.ExitStack()
    bq_sb = sbuf(perm, "bq_sb_s", [1, D], BF16)
    bo_sb = sbuf(perm, "bo_sb_s", [1, D], BF16)
    wqT = sbuf(st_w, "wqT_s", [128, 8, D], BF16)
    woT_sb = sbuf(st_w, "woT_sb_s", [128, 8, D], BF16)
    M_tm = sbuf(st_m, "M_tm_s", [128, H * 128], F32)
    lo_t = sbuf(st_m, "lo_t_s", [128, H], F32)
    wkT = sbuf(pha, "wkT_s", [128, 8, D], BF16)
    wvT = sbuf(pha, "wvT_s", [128, 8, D], BF16)
    bk_sb = sbuf(pha, "bk_sb_s", [1, D], BF16)
    bv_sb = sbuf(pha, "bv_sb_s", [1, D], BF16)
    E('pool', lambda e: e.dma_start(wqT.ap(), wqT_d.rearrange("(kc p) f -> p kc f", p=128)),
      writes=['wqT'], dma='c_wq')
    E('pool', lambda e: e.dma_start(wkT.ap(), wkT_d.rearrange("(kc p) f -> p kc f", p=128)),
      writes=['wkT'], dma='c_wk')
    E('pool', lambda e: e.dma_start(wvT.ap(), wvT_d.rearrange("(kc p) f -> p kc f", p=128)),
      writes=['wvT'], dma='c_wv')
    E('pool', lambda e: e.dma_start(woT_sb.ap(), woT_d.rearrange("(kc p) f -> p kc f", p=128)),
      writes=['woT_sb'], dma='c_wo')
    E('pool', lambda e: e.dma_start(bq_sb[:, :], bq_d), writes=['bq_sb'], dma='c_bq')
    E('pool', lambda e: e.dma_start(bk_sb[:, :], bk_d), writes=['bk_sb'], dma='c_bk')
    E('pool', lambda e: e.dma_start(bv_sb[:, :], bv_d), writes=['bv_sb'], dma='c_bv')
    E('pool', lambda e: e.dma_start(bo_sb[:, :], bo_d), writes=['bo_sb'], dma='c_bo')

    ps_tr = [psum(pha, f"ps_tr{i}", [128, 512], F32) for i in range(2)]
    ps_qk = [psum(pha, f"ps_qk{i}", [128, 512], F32) for i in range(3)]
    ps_sc = psum(pha, "ps_sc_s", [128, 512], F32)
    ps_xb0 = psum(pha, "ps_xb0_s", [1, 512], F32)
    ps_xb1 = psum(pha, "ps_xb1_s", [1, 512], F32)
    pp_pool = [ps_qk[0], ps_qk[1], ps_qk[2], ps_sc]
    pp_ctr = [0]
    def next_pp():
        i = pp_ctr[0] % 4
        pp_ctr[0] += 1
        return pp_pool[i], ('pp', i)

    E = mkE('p0b')
    idxk_sb = sbuf(pha, "idxk_sb_s", [U, 1], I32)
    xs_f = sbuf(pha, "xs_f_s", [U, D], F32)
    xs_bf = sbuf(pha, "xs_bf_s", [128, D], BF16)
    xs_T = sbuf(pha, "xs_T_s", [128, 8, 64], BF16)
    kbar = sbuf(pha, "kbar_s", [128, 1], F32)
    st0 = sbuf(pha, "st0_s", [128, 4], F32)

    E('sp', lambda e: e.dma_start(idxk_sb[:, :], idxk_d.rearrange("(a b) -> a b", b=1)),
      writes=['idxk_sb'], dma='c_idk')
    E('dve', lambda e: e.memset(ksampT.ap(), 0.0), writes=['ks_init'])
    E('pool', lambda e: e.indirect_dma_start(
        xs_f[:, :], None, x_d, IndirectOffsetOnAxis(idxk_sb[:, :], 0)),
      reads=['idxk_sb'], writes=['xs_f'], dma='g0')
    E('dve', lambda e: e.memset(xs_bf[:, :], 0.0), writes=['xs_bf'])
    layernorm(xs_f[:, :], ['xs_f'], xs_bf[0:U, :], 'xs_bf', U, epsA, 'epsA', st0, 'st0')
    for c in range(8):
        bank = ps_tr[c % 2]
        bkk = ('ps_tr', c % 2)
        E('pe', lambda e, c=c, bank=bank: e.matmul(
            bank[:, 0:128], xs_bf[:, c*128:(c+1)*128], id_bf[:, :],
            start=True, stop=True),
          reads=['xs_bf', 'id_bf'], writes=[bkk])
        E('dve', lambda e, c=c, bank=bank: e.tensor_copy(xs_T[:, c, :], bank[:, 0:64]),
          reads=[bkk], writes=[('xs_T', c)])
    xsT_keys = [('xs_T', c) for c in range(8)]
    for h in range(H):
        pr, hi = h // 2, h % 2
        bank, bkk = next_pp()
        def ksamp_mm(e, h=h, bank=bank):
            for k in range(8):
                e.matmul(bank[0:64, 0:64], wkT[:, k, h*64:(h+1)*64], xs_T[:, k, :],
                         start=(k == 0), stop=False)
            return e.matmul(bank[0:64, 0:64], bk_sb[:, h*64:(h+1)*64],
                            ones_row[:, 0:64], start=False, stop=True)
        E('pe', ksamp_mm, reads=xsT_keys + ['wkT', 'bk_sb', 'ones_row'],
          writes=[bkk])
        E('dve', lambda e, pr=pr, hi=hi, bank=bank: e.tensor_copy(
            ksampT[hi*64:(hi+1)*64, pr, hi*52:hi*52+50], bank[0:64, 0:50]),
          reads=[bkk, 'ks_init'], writes=[('ksampT', pr, hi)])
        E('dve', lambda e, bank=bank: e.reduce_sum(kbar[0:64, :], bank[0:64, 0:50], axis=AX.X),
          reads=[bkk], writes=['kbar'])
        E('dve', lambda e, pr=pr, hi=hi: e.tensor_scalar(
            ksampT[hi*64:(hi+1)*64, pr, hi*52+50:hi*52+51], kbar[0:64, :],
            1.0 / U, None, ALU.mult),
          reads=['kbar', 'ks_init'], writes=[('ksampT', pr, hi, 'bar')])
    ks_keys = [('ksampT', pr, hi) for pr in range(8) for hi in range(2)] + \
              [('ksampT', pr, hi, 'bar') for pr in range(8) for hi in range(2)] + ['ks_init']

    x_in = [sbuf(pha, f"x_in{i}", [128, 4, D], F32) for i in range(2)]
    xln2 = [sbuf(pha, f"xln{i}_s", [128, 4, D], BF16) for i in range(2)]
    xT = sbuf(pha, "xT_s", [128, 8, 512], BF16)
    qt = sbuf(pha, "qt_s", [128, 8, 512], BF16)
    ktile = [sbuf(pha, f"ktile{i}", [128, 512], BF16) for i in range(2)]
    vt = sbuf(pha, "vt_s", [128, 4, D], BF16)
    st1 = [[sbuf(pha, f"st1_{i}_{s}", [128, 4], F32) for s in range(4)] for i in range(2)]
    mtmp = sbuf(pha, "mtmp_s", [128, 8], F32)

    E = _E

    def emit_x_ln(g):
        gp = g % 2
        xln = xln2[gp]
        xi = x_in[gp]
        xik = ('x_in', gp)
        _E('sp', lambda e, g=g, xi=xi: e.dma_start(
            xi.ap(), x_d[g*512:(g+1)*512, :].rearrange("(s p) f -> p s f", p=128)),
          writes=[xik], dma=f'x{gp}')
        for s in range(4):
            layernorm(xi[:, s, :], [xik], xln[:, s, :], ('xln', gp, s), 128,
                      epsA, 'epsA', st1[gp][s], ('st1', gp, s))

    emit_x_ln(0)
    for g in range(ngroups):
        gp = g % 2
        xln = xln2[gp]
        E = mkE('tr')
        for s in range(4):
            for cb in range(2):
                bank = ps_tr[cb]
                bkk = ('ps_tr', cb)
                for j in range(4):
                    c = cb * 4 + j
                    E('pe', lambda e, s=s, c=c, j=j, bank=bank, xln=xln: e.matmul(
                        bank[:, j*128:(j+1)*128], xln[:, s, c*128:(c+1)*128],
                        id_bf[:, :], start=True, stop=True),
                      reads=[('xln', gp, s), 'id_bf'], writes=[bkk])
                E('dve', lambda e, s=s, cb=cb, bank=bank: e.tensor_copy(
                    xT.ap()[:, cb*4:(cb+1)*4, s*128:(s+1)*128],
                    bank.ap().rearrange("p (c f) -> p c f", c=4)),
                  reads=[bkk], writes=[('xT', s, cb)])
        xt_keys = [('xT', s, cb) for s in range(4) for cb in range(2)]
        E = mkE('q')
        for f in range(8):
            bank, bkk = next_pp()
            def qmm(e, f=f, bank=bank):
                for k in range(8):
                    e.matmul(bank[:, :], wqT[:, k, f*128:(f+1)*128], xT[:, k, :],
                             start=(k == 0), stop=False)
                return e.matmul(bank[:, :], bq_sb[:, f*128:(f+1)*128],
                                ones_row[:, :], start=False, stop=True)
            E('pe', qmm, reads=xt_keys + ['wqT', 'bq_sb', 'ones_row'], writes=[bkk])
            E('dve', lambda e, f=f, bank=bank: e.tensor_copy(qt[:, f, :], bank[:, :]),
              reads=[bkk], writes=[('qt', f)])
        E = mkE('k')
        for f in range(8):
            kt = ktile[f % 2]
            ktk = ('ktile', f % 2)
            bank, bkk = next_pp()
            def kmm(e, f=f, bank=bank):
                for k in range(8):
                    e.matmul(bank[:, :], wkT[:, k, f*128:(f+1)*128], xT[:, k, :],
                             start=(k == 0), stop=False)
                return e.matmul(bank[:, :], bk_sb[:, f*128:(f+1)*128],
                                ones_row[:, :], start=False, stop=True)
            E('pe', kmm, reads=xt_keys + ['wkT', 'bk_sb', 'ones_row'], writes=[bkk])
            E('dve', lambda e, kt=kt, bank=bank: e.tensor_copy(kt[:, :], bank[:, :]),
              reads=[bkk], writes=[ktk])
            E('sp', lambda e, g=g, f=f, kt=kt: e.dma_start(
                kfm_d[f, :, g*512:(g+1)*512], kt[:, :]),
              reads=[ktk], writes=['kfm_w'], dma='kw')
        E = mkE('v')
        for s in range(4):
            for hf in range(2):
                bank, bkk = next_pp()
                def vmm(e, s=s, hf=hf, bank=bank):
                    for k in range(8):
                        e.matmul(bank[:, :], xT[:, k, s*128:(s+1)*128],
                                 wvT[:, k, hf*512:(hf+1)*512], start=(k == 0), stop=False)
                    return e.matmul(bank[:, :], ones_row[0:1, 0:128],
                                    bv_sb[:, hf*512:(hf+1)*512], start=False, stop=True)
                E('pe', vmm, reads=xt_keys + ['wvT', 'bv_sb', 'ones_row'], writes=[bkk])
                E('dve', lambda e, s=s, hf=hf, bank=bank: e.tensor_copy(
                    vt[:, s, hf*512:(hf+1)*512], bank[:, :]),
                  reads=[bkk], writes=[('vt', s, hf)])
        vt_keys = [('vt', s, hf) for s in range(4) for hf in range(2)]
        for pr in range(8):
            E('sp', lambda e, g=g, pr=pr: e.dma_start(
                vscr_d[pr, g*4:(g+1)*4, :, :].rearrange("s p d -> p s d"),
                vt[:, :, pr*128:(pr+1)*128]),
              reads=vt_keys, writes=['vscr_w'], dma='vw')
        if g + 1 < ngroups:
            emit_x_ln(g + 1)
        E = mkE('xb')
        for s in range(4):
            E('pe', lambda e, g=g, s=s, xln=xln: e.matmul(
                ps_xb0[:, :], ones_col[:, :], xln[:, s, 0:512],
                start=(g == 0 and s == 0), stop=(g == ngroups - 1 and s == 3)),
              reads=[('xln', gp, s), 'ones_col'], writes=[('ps_xb', 0)])
            E('pe', lambda e, g=g, s=s, xln=xln: e.matmul(
                ps_xb1[:, :], ones_col[:, :], xln[:, s, 512:1024],
                start=(g == 0 and s == 0), stop=(g == ngroups - 1 and s == 3)),
              reads=[('xln', gp, s), 'ones_col'], writes=[('ps_xb', 1)])
        E = mkE('sc')
        for s in range(4):
            for bk_i in range(2):
                bank, bkk = next_pp()
                def smm(e, s=s, bk_i=bk_i, bank=bank):
                    last = None
                    for j in range(4):
                        pr = bk_i * 4 + j
                        last = e.matmul(
                            bank[:, j*104:(j+1)*104],
                            qt[:, pr, s*128:(s+1)*128],
                            ksampT[:, pr, :],
                            start=True, stop=True)
                    return last
                E('pe', smm, reads=[('qt', f) for f in range(8)] + ks_keys,
                  writes=[bkk])
                E('dve', lambda e, bank=bank: e.reduce_max(
                    mtmp.ap().rearrange("p (j hi o) -> p j hi o", j=4, hi=2),
                    bank[:, 0:416].rearrange("p (j hi c) -> p j hi c", j=4, hi=2)[:, :, :, 0:50],
                    axis=AX.X),
                  reads=[bkk], writes=['mtmp'])
                E('dve', lambda e, g=g, s=s, bk_i=bk_i, bank=bank: e.tensor_tensor(
                    M_tm.ap().rearrange("p (h j) -> p h j", h=H)[:, bk_i*8:(bk_i+1)*8, g*4+s],
                    mtmp[:, :],
                    bank[:, 0:416].rearrange("p (j hi c) -> p j hi c", j=4, hi=2)[:, :, :, 50],
                    ALU.subtract),
                  reads=['mtmp', bkk], writes=[('M', bk_i, g, s)])

    E = mkE('p15')
    Mkeys = [('M', b, g, s) for b in range(2) for g in range(NG) for s in range(4)]

    E('dve', lambda e: e.tensor_scalar(xbar_bf[:, 0:512], ps_xb0[:, :], 1.0 / L,
                                       None, ALU.mult),
      reads=[('ps_xb', 0)], writes=['xbar_bf'])
    E('dve', lambda e: e.tensor_scalar(xbar_bf[:, 512:1024], ps_xb1[:, :], 1.0 / L,
                                       None, ALU.mult),
      reads=[('ps_xb', 1)], writes=['xbar_bf'])
    for c in range(8):
        bank, bkk = next_pp()
        E('pe', lambda e, c=c, bank=bank: e.matmul(bank[:, 0:1], xbar_bf[:, c*128:(c+1)*128],
                                        one1[:, :], start=True, stop=True),
          reads=['xbar_bf', 'one1'], writes=[bkk])
        E('dve', lambda e, c=c, bank=bank: e.tensor_copy(xbarT[:, c:c+1], bank[:, 0:1]),
          reads=[bkk], writes=[('xbarT', c)])
    xbT_keys = [('xbarT', c) for c in range(8)]
    def vmean_mm0(e):
        for c in range(8):
            e.matmul(ps_qk[0][0:1, :], xbarT[:, c:c+1], wvT[:, c, 0:512],
                     start=(c == 0), stop=False)
        return e.matmul(ps_qk[0][0:1, :], one1[:, :], bv_sb[:, 0:512],
                        start=False, stop=True)
    def vmean_mm1(e):
        for c in range(8):
            e.matmul(ps_qk[1][0:1, :], xbarT[:, c:c+1], wvT[:, c, 512:1024],
                     start=(c == 0), stop=False)
        return e.matmul(ps_qk[1][0:1, :], one1[:, :], bv_sb[:, 512:1024],
                        start=False, stop=True)
    E('pe', vmean_mm0, reads=xbT_keys + ['wvT', 'one1', 'bv_sb'], writes=[('pp', 0)])
    E('pe', vmean_mm1, reads=xbT_keys + ['wvT', 'one1', 'bv_sb'], writes=[('pp', 1)])
    E('dve', lambda e: e.tensor_copy(vmean[:, 0:512], ps_qk[0][0:1, :]),
      reads=[('pp', 0)], writes=['vmean'])
    E('dve', lambda e: e.tensor_copy(vmean[:, 512:1024], ps_qk[1][0:1, :]),
      reads=[('pp', 1)], writes=['vmean'])
    E('dve', lambda e: e.tensor_copy(vmean_bf[:, :], vmean[:, :]),
      reads=['vmean'], writes=['vmean_bf'])
    for c in range(8):
        bank, bkk = next_pp()
        E('pe', lambda e, c=c, bank=bank: e.matmul(bank[:, 0:1], vmean_bf[:, c*128:(c+1)*128],
                                        one1[:, :], start=True, stop=True),
          reads=['vmean_bf', 'one1'], writes=[bkk])
        E('dve', lambda e, c=c, bank=bank: e.tensor_copy(vmeanT[:, c:c+1], bank[:, 0:1]),
          reads=[bkk], writes=[('vmeanT', c)])
    E('dve', lambda e: e.tensor_copy(vmeanT_bf[:, :], vmeanT[:, :]),
      reads=[('vmeanT', c) for c in range(8)], writes=['vmeanT_bf'])
    def base_mm0(e):
        for c in range(8):
            e.matmul(ps_qk[0][0:1, :], vmeanT_bf[:, c:c+1], woT_sb[:, c, 0:512],
                     start=(c == 0), stop=False)
        return e.matmul(ps_qk[0][0:1, :], one1[:, :], bo_sb[:, 0:512],
                        start=False, stop=True)
    def base_mm1(e):
        for c in range(8):
            e.matmul(ps_qk[1][0:1, :], vmeanT_bf[:, c:c+1], woT_sb[:, c, 512:1024],
                     start=(c == 0), stop=False)
        return e.matmul(ps_qk[1][0:1, :], one1[:, :], bo_sb[:, 512:1024],
                        start=False, stop=True)
    E('pe', base_mm0, reads=['vmeanT_bf', 'woT_sb', 'one1', 'bo_sb'], writes=[('pp', 0)])
    E('pe', base_mm1, reads=['vmeanT_bf', 'woT_sb', 'one1', 'bo_sb'], writes=[('pp', 1)])
    E('dve', lambda e: e.tensor_copy(base_bf[:, 0:512], ps_qk[0][0:1, :]),
      reads=[('pp', 0)], writes=['base_bf'])
    E('dve', lambda e: e.tensor_copy(base_bf[:, 512:1024], ps_qk[1][0:1, :]),
      reads=[('pp', 1)], writes=['base_bf'])
    if debug:
        E('sp', lambda e: e.dma_start(dbg["dbg_vmean"], vmean[:, :]), reads=['vmean'], dma='dbg')
        basef = sbuf(pha, "basef_s", [1, D], F32)
        E('dve', lambda e: e.tensor_copy(basef[:, :], base_bf[:, :]),
          reads=['base_bf'], writes=['basef'])
        E('sp', lambda e: e.dma_start(dbg["dbg_base"], basef[:, :]), reads=['basef'], dma='dbg')
        E('sp', lambda e: e.dma_start(dbg["dbg_M"], M_tm[:, :]), reads=Mkeys, dma='dbg')

    E = _E
    p.fence()
    pha.close()
    if maxphase < 2:
        if debug:
            p.wait_all_dma('pool', 'dbg')
        p.build(); p.close()
        st_m.close(); st_w.close(); sbstack.close()
        return nc
    ph2 = contextlib.ExitStack()
    hi_t = sbuf(ph2, "hi_t_s", [128, H], F32)
    tt_t = sbuf(ph2, "tt_t_s", [128, H], F32)
    cnt_f = sbuf(ph2, "cnt_f_s", [128, H], F32)
    cnt_bf = sbuf(ph2, "cnt_bf_s", [128, H], BF16)
    cntr = sbuf(ph2, "cntr_s", [128, H], F32)
    geu = sbuf(ph2, "geu_s", [128, H], U8)
    gle = sbuf(ph2, "gle_s", [128, H], U8)
    junkc = sbuf(ph2, "junkc_s", [128, 128], F32)
    mask_u8 = sbuf(ph2, "mask_u8_s", [128, H * 128], U8)
    neg1 = sbuf(ph2, "neg1_s", [128, 1], F32)
    iotaM = sbuf(ph2, "iotaM_s", [128, H * 128], F32)
    miota = M_tm
    cand = sbuf(ph2, "cand_s", [128, H * 16], F32)
    candcat = sbuf(ph2, "candcat_s", [16, 2048], F32)
    cmax8 = sbuf(ph2, "cmax8_s", [16, 8], F32)
    sort56 = sbuf(ph2, "sort56_s", [16, 56], F32)
    sortpad = sbuf(ph2, "sortpad_s", [16, 64], F32)
    selneg = sbuf(ph2, "selneg_s", [16, 64], U8)
    big9 = sbuf(ph2, "big9_s", [16, 64], F32)
    candT = sbuf(ph2, "candT_s", [128, 128], F32)
    ps2_a = psum(ph2, "ps2_a_s", [128, 512], F32)
    ps2_b = psum(ph2, "ps2_b_s", [128, 512], F32)

    E('sp', lambda e: e.dma_start(iotaM[:, :], iotaM_d), writes=['iotaM'], dma='c_iom')
    E('dve', lambda e: e.memset(lo_t[:, :], 0.0), writes=['lo'])
    E('dve', lambda e: e.memset(hi_t[:, :], 256.0), writes=['hi'])
    for it in range(nbisect):
        E('dve', lambda e: e.tensor_tensor(tt_t[:, :], lo_t[:, :], hi_t[:, :], ALU.add),
          reads=['lo', 'hi'], writes=['tt'])
        E('dve', lambda e: e.tensor_scalar(tt_t[:, :], tt_t[:, :], 0.5, None, ALU.mult),
          reads=['tt'], writes=['tt'])
        for h in range(H):
            E('dve', lambda e, h=h: e.tensor_scalar(
                junkc[:, :], M_tm[:, h*128:(h+1)*128], tt_t[:, h:h+1], 0.0,
                ALU.is_gt, ALU.add, accum_out=cnt_f[:, h:h+1]),
              reads=Mkeys + ['tt'], writes=['junkc', ('cnt', h)])
        E('dve', lambda e: e.tensor_copy(cnt_bf[:, :], cnt_f[:, :]),
          reads=[('cnt', h) for h in range(H)], writes=['cntbf'])
        E('pe', lambda e: e.matmul(ps2_a[:, 0:16], ones_sq[:, :], cnt_bf[:, :],
                                   start=True, stop=True),
          reads=['cntbf', 'ones_sq'], writes=[('ps2_a', 0)])
        E('dve', lambda e: e.tensor_copy(cntr[:, :], ps2_a[:, 0:16]),
          reads=[('ps2_a', 0)], writes=['cntr'])
        E('dve', lambda e: e.tensor_scalar(geu[:, :], cntr[:, :], 49.5, None, ALU.is_gt),
          reads=['cntr'], writes=['geu'])
        E('dve', lambda e: e.tensor_scalar(gle[:, :], cntr[:, :], 49.5, None, ALU.is_le),
          reads=['cntr'], writes=['gle'])
        E('dve', lambda e: e.select(lo_t[:, :], geu[:, :], tt_t[:, :], lo_t[:, :],
                                    add_drain=True),
          reads=['geu', 'tt', 'lo'], writes=['lo'])
        E('dve', lambda e: e.select(hi_t[:, :], gle[:, :], tt_t[:, :], hi_t[:, :],
                                    add_drain=True),
          reads=['gle', 'tt', 'hi'], writes=['hi'])
    if debug:
        E('sp', lambda e: e.dma_start(dbg["dbg_lo"], lo_t[:, :]), reads=['lo'], dma='dbg')
    E('dve', lambda e: e.memset(neg1[:, :], -1.0), writes=['neg1'])
    for h in range(H):
        E('dve', lambda e, h=h: e.tensor_scalar(
            mask_u8[:, h*128:(h+1)*128], M_tm[:, h*128:(h+1)*128],
            lo_t[:, h:h+1], None, ALU.is_gt),
          reads=Mkeys + ['lo'], writes=[('msk', h)])
    E('dve', lambda e: e.select(miota[:, :], mask_u8[:, :], iotaM[:, :],
                                neg1[:, 0:1].to_broadcast([128, H * 128]),
                                add_drain=True),
      reads=[('msk', h) for h in range(H)] + ['iotaM', 'neg1'],
      writes=Mkeys + ['miota'])
    for rnd in range(2):
        for h in range(H):
            E('dve', lambda e, rnd=rnd, h=h: e.max(
                out=cand[:, h*16+rnd*8:h*16+rnd*8+8], in_=miota[:, h*128:(h+1)*128]),
              reads=['miota'], writes=[('cand', h, rnd)])
            E('dve', lambda e, rnd=rnd, h=h: e.match_replace(
                out=miota[:, h*128:(h+1)*128],
                in_to_replace=cand[:, h*16+rnd*8:h*16+rnd*8+8],
                in_values=miota[:, h*128:(h+1)*128], imm_value=-1.0),
              reads=['miota', ('cand', h, rnd)], writes=['miota'])
    candkeys = [('cand', h, r) for h in range(H) for r in range(2)]
    for half in range(2):
        E('pe', lambda e, half=half: e.matmul(
            ps2_a[:, 0:128],
            cand[:, half*128:(half+1)*128], idf32[:, :], start=True, stop=True),
          reads=candkeys + ['idf32'], writes=[('ps2_a', 0)])
        E('dve', lambda e: e.tensor_copy(candT[:, :], ps2_a[:, 0:128]),
          reads=[('ps2_a', 0)], writes=['candT'])
        E('pool', lambda e, half=half: e.dma_start(
            candcat[half*8:(half+1)*8, :].rearrange("h (c s) -> h c s", c=16),
            candT[:, :]),
          reads=['candT'], writes=['candcat'], dma='cc')
    for rnd in range(7):
        E('dve', lambda e: e.max(out=cmax8[:, :], in_=candcat[:, :]),
          reads=['candcat'], writes=['cmax8'])
        E('dve', lambda e, rnd=rnd: e.tensor_copy(sort56[:, rnd*8:(rnd+1)*8],
                                                  cmax8[:, :]),
          reads=['cmax8'], writes=[('s56', rnd)])
        E('dve', lambda e: e.match_replace(
            out=candcat[:, :], in_to_replace=cmax8[:, :], in_values=candcat[:, :],
            imm_value=-1.0),
          reads=['candcat', 'cmax8'], writes=['candcat'])
    s56keys = [('s56', r) for r in range(7)]
    E('dve', lambda e: e.memset(sortpad[:, :], 1.0e9), writes=['sortpad'])
    E('dve', lambda e: e.tensor_copy(sortpad[:, 0:50], sort56[:, 0:50]),
      reads=s56keys, writes=['sortpad'])
    E('dve', lambda e: e.tensor_scalar(selneg[:, :], sortpad[:, :], -0.5, None, ALU.is_lt),
      reads=['sortpad'], writes=['selneg'])
    E('dve', lambda e: e.memset(big9[:, :], 1.0e9), writes=['big9'])
    E('dve', lambda e: e.select(sortpad[:, :], selneg[:, :], big9[:, :], sortpad[:, :],
                                add_drain=True),
      reads=['selneg', 'big9', 'sortpad'], writes=['sortpad'])
    E('pe', lambda e: e.matmul(ps2_b[0:64, 0:16], sortpad[:, :], idf32[0:16, 0:16],
                               start=True, stop=True),
      reads=['sortpad', 'idf32'], writes=[('ps2_b', 0)])
    E('dve', lambda e: e.tensor_copy(idxT_f[:, :], ps2_b[0:64, 0:16]),
      reads=[('ps2_b', 0)], writes=['idxT_f'])
    E('dve', lambda e: e.tensor_copy(idxT_i[:, :], idxT_f[:, :]),
      reads=['idxT_f'], writes=['idxT_i'])
    if debug:
        E('sp', lambda e: e.dma_start(dbg["dbg_idx"], idxT_f[:, :]), reads=['idxT_f'], dma='dbg')

    p.fence()
    ph2.close()
    st_m.close()

    if maxphase < 3:
        if debug:
            p.wait_all_dma('pool', 'dbg')
        p.build(); p.close()
        st_w.close(); sbstack.close()
        return nc

    phb = contextlib.ExitStack()
    kpair = [sbuf(phb, f"kpair{i}", [128, L], BF16) for i in range(2)]
    xg_f = sbuf(phb, "xg_f_s", [128, D], F32)
    xg_bf = sbuf(phb, "xg_bf_s", [128, D], BF16)
    xg_T = sbuf(phb, "xg_T_s", [128, 8, 128], BF16)
    qtopT = sbuf(phb, "qtopT_s", [128, 128], BF16)
    P_sb = sbuf(phb, "P_sb_s", [128, L], BF16)
    sumP = sbuf(phb, "sumP_s", [128, 32], F32)
    rsum = sbuf(phb, "rsum_s", [128, 1], F32)
    PT_sb = [sbuf(phb, f"PT_sb{i}", [128, 512], BF16) for i in range(2)]
    vsc = [sbuf(phb, f"vsc{i}", [128, 8, 128], BF16) for i in range(2)]
    dctx = sbuf(phb, "dctx_s", [128, 128], BF16)
    st3 = sbuf(phb, "st3_s", [128, 4], F32)
    ps_s = [psum(phb, f"ps_s{i}", [128, 512], F32) for i in range(2)]
    ps_pt = [psum(phb, f"ps_pt{i}", [128, 512], F32) for i in range(2)]
    ps_cx = psum(phb, "ps_cx_s", [128, 128], F32)
    ps_d = [psum(phb, f"ps_d{i}", [128, 512], F32) for i in range(2)]

    for c in range(8):
        E('dve', lambda e, c=c: e.memset(delta[c].ap(), 0.0), writes=[('dz', c)])
        E('dve', lambda e, c=c: e.memset(idxv[c].ap(), 2.0e9), writes=[('iz', c)])

    for pr in range(8):
        kp = kpair[pr % 2]
        kpk = ('kpair', pr % 2)
        E('sp', lambda e, pr=pr, kp=kp: e.dma_start(kp.ap(), kfm_d[pr, :, :]),
          reads=['kfm_w'], writes=[kpk], dma=f'kr{pr % 2}')
        E('dve', lambda e: e.memset(xg_f.ap(), 0.0), writes=['xg_f'])
        E('pool', lambda e, pr=pr: e.indirect_dma_start(
            xg_f[0:U, :], None, x_d, IndirectOffsetOnAxis(idxT_i[0:U, 2*pr:2*pr+1], 0),
            bounds_check=L - 1, oob_is_err=False),
          reads=['idxT_i', 'xg_f'], writes=[('xg_g', 0)], dma='g1')
        E('pool', lambda e, pr=pr: e.indirect_dma_start(
            xg_f[64:64+U, :], None, x_d, IndirectOffsetOnAxis(idxT_i[0:U, 2*pr+1:2*pr+2], 0),
            bounds_check=L - 1, oob_is_err=False),
          reads=['idxT_i', 'xg_f'], writes=[('xg_g', 1)], dma='g1')
        layernorm(xg_f[:, :], ['xg_f', ('xg_g', 0), ('xg_g', 1)], xg_bf[:, :], 'xg_bf',
                  128, epsA, 'epsA', st3, 'st3', lnexp=True)
        for c in range(8):
            bank = ps_pt[c % 2]
            bkk = ('ps_pt2', c % 2)
            E('pe', lambda e, c=c, bank=bank: e.matmul(
                bank[:, 0:128], xg_bf[:, c*128:(c+1)*128], id_bf[:, :],
                start=True, stop=True),
              reads=['xg_bf', 'id_bf'], writes=[bkk])
            E('dve', lambda e, c=c, bank=bank: e.tensor_copy(
                xg_T[:, c, :], bank[:, 0:128]),
              reads=[bkk], writes=[('xg_T', c)])
        def qtop_mm(e, pr=pr):
            for k in range(8):
                e.matmul(ps_d[0][:, 0:128], wqT[:, k, pr*128:(pr+1)*128], xg_T[:, k, :],
                         start=(k == 0), stop=False)
            return e.matmul(ps_d[0][:, 0:128], bq_sb[:, pr*128:(pr+1)*128],
                            ones_row[:, 0:128], start=False, stop=True)
        E('pe', qtop_mm, reads=[('xg_T', c) for c in range(8)] +
          ['wqT', 'bq_sb', 'ones_row'], writes=['ps_d0'])
        E('dve', lambda e: e.memset(qtopT.ap(), 0.0), writes=['qtop0'])
        E('dve', lambda e: e.tensor_copy(qtopT[0:64, 0:50], ps_d[0][0:64, 0:50]),
          reads=['ps_d0', 'qtop0'], writes=[('qtopT', 0)])
        E('dve', lambda e: e.tensor_copy(qtopT[64:128, 64:114], ps_d[0][64:128, 64:114]),
          reads=['ps_d0', 'qtop0'], writes=[('qtopT', 1)])
        for kc in range(32):
            bank = ps_s[kc % 2]
            bkk = ('ps_s', kc % 2)
            E('pe', lambda e, kc=kc, kp=kp, bank=bank: e.matmul(
                bank[:, :], qtopT[:, :], kp[:, kc*512:(kc+1)*512],
                start=True, stop=True),
              reads=[('qtopT', 0), ('qtopT', 1), 'qtop0', kpk], writes=[bkk])
            E('act', lambda e, kc=kc, bank=bank: e.activation(
                P_sb[:, kc*512:(kc+1)*512], bank[:, :], AF.Exp,
                bias=negb[:, :], scale=SCALE, accum_out=sumP[:, kc:kc+1]),
              reads=[bkk, 'negb'], writes=[('P', kc), ('sumP', kc)])
        E('dve', lambda e: e.reduce_sum(rsum[:, :], sumP[:, :], axis=AX.X),
          reads=[('sumP', kc) for kc in range(32)], writes=['rsum'])
        E('dve', lambda e: e.reciprocal(rsum[:, :], rsum[:, :]),
          reads=['rsum'], writes=['rsum'])
        E('dve', lambda e: e.tensor_scalar(P_sb[:, :], P_sb[:, :], rsum[:, :], None,
                                           ALU.mult),
          reads=[('P', kc) for kc in range(32)] + ['rsum'], writes=['Pn'])
        for scb in range(16):
            vb = vsc[scb % 2]
            vbk = ('vsc', scb % 2)
            E('sp', lambda e, pr=pr, scb=scb, vb=vb: e.dma_start(
                vb.ap(), vscr_d[pr, scb*8:(scb+1)*8, :, :].rearrange("s p d -> p s d")),
              reads=['vscr_w'], writes=[vbk], dma=f'vr{scb % 2}')
            for q4 in range(2):
                sc0 = scb * 8 + q4 * 4
                ptb = PT_sb[q4]
                ptk = ('PT', q4)
                psb = ps_pt[q4]
                psk = ('ps_pt2', q4)
                def ptmm(e, sc0=sc0, psb=psb):
                    last = None
                    for jj in range(4):
                        last = e.matmul(psb[:, jj*128:(jj+1)*128],
                                        P_sb[:, (sc0+jj)*128:(sc0+jj+1)*128],
                                        id_bf[:, :], start=True, stop=True)
                    return last
                E('pe', ptmm, reads=['Pn', 'id_bf'], writes=[psk])
                E('dve', lambda e, ptb=ptb, psb=psb: e.tensor_copy(ptb[:, :], psb[:, :]),
                  reads=[psk], writes=[ptk])
                def cxmm(e, sc0=sc0, q4=q4, vb=vb, ptb=ptb):
                    last = None
                    for jj in range(4):
                        sc = sc0 + jj
                        last = e.matmul(ps_cx[:, :], vb[:, q4*4+jj, :],
                                        ptb[:, jj*128:(jj+1)*128],
                                        start=(sc == 0), stop=(sc == 127))
                    return last
                E('pe', cxmm, reads=[vbk, ptk], writes=['ps_cx'])
        E('dve', lambda e: e.memset(dctx.ap(), 0.0), writes=['dctx0'])
        E('dve', lambda e, pr=pr: e.tensor_scalar(
            dctx[0:64, 0:50], ps_cx[0:64, 0:50], vmeanT[0:64, pr:pr+1], None,
            ALU.subtract),
          reads=['ps_cx', ('vmeanT', pr), 'dctx0'], writes=[('dctx', 0)])
        E('dve', lambda e, pr=pr: e.tensor_scalar(
            dctx[64:128, 64:114], ps_cx[64:128, 64:114], vmeanT[64:128, pr:pr+1], None,
            ALU.subtract),
          reads=['ps_cx', ('vmeanT', pr), 'dctx0'], writes=[('dctx', 1)])
        def dmm(e, pr=pr):
            e.matmul(ps_d[0][:, :], dctx[:, :], woT_sb[:, pr, 0:512],
                     start=True, stop=True)
            return e.matmul(ps_d[1][:, :], dctx[:, :], woT_sb[:, pr, 512:1024],
                            start=True, stop=True)
        E('pe', dmm, reads=[('dctx', 0), ('dctx', 1), 'dctx0', 'woT_sb'],
          writes=['ps_d0', 'ps_d1'])
        E('dve', lambda e, pr=pr: e.tensor_copy(delta[pr][:, 0:512], ps_d[0][:, :]),
          reads=['ps_d0', ('dz', pr)],
          writes=[('delta', pr, 0, 0), ('delta', pr, 64, 0)])
        E('dve', lambda e, pr=pr: e.tensor_copy(delta[pr][:, 512:1024], ps_d[1][:, :]),
          reads=['ps_d1', ('dz', pr)],
          writes=[('delta', pr, 0, 1), ('delta', pr, 64, 1)])
        E('dve', lambda e, pr=pr: e.tensor_copy(idxv[pr][0:50, :],
                                                idxT_f[0:50, 2*pr:2*pr+1]),
          reads=['idxT_f', ('iz', pr)], writes=[('idxv', pr, 0)])
        E('dve', lambda e, pr=pr: e.tensor_copy(idxv[pr][64:114, :],
                                                idxT_f[0:50, 2*pr+1:2*pr+2]),
          reads=['idxT_f', ('iz', pr)], writes=[('idxv', pr, 64)])
    if debug:
        dltf = sbuf(phb, "dltf_s", [128, D], F32)
        E('dve', lambda e: e.tensor_copy(dltf[:, :], delta[0].ap()),
          reads=[('delta', 0, 0, 0), ('delta', 0, 0, 1), ('delta', 0, 64, 0),
                 ('delta', 0, 64, 1), ('dz', 0)],
          writes=['dltf'])
        E('sp', lambda e: e.dma_start(dbg["dbg_delta"], dltf[:, :]), reads=['dltf'], dma='dbg')

    p.fence()
    phb.close()
    st_w.close()

    if maxphase < 5:
        if debug:
            p.wait_all_dma('pool', 'dbg')
        p.build(); p.close()
        sbstack.close()
        return nc

    phc = contextlib.ExitStack()
    wr_sb = sbuf(phc, "wr_sb_s", [128, 8, NE], BF16)
    b1_sb = sbuf(phc, "b1_sb_s", [NE, 32 * 128], BF16)
    gam_b = sbuf(phc, "gam_b_s", [128, D], F32)
    bet_b = sbuf(phc, "bet_b_s", [128, D], F32)
    bof_sb = sbuf(phc, "bof_sb_s", [1, D], BF16)
    gb_tmp = sbuf(phc, "gb_tmp_s", [1, D], BF16)
    xq_in = [sbuf(phc, f"xq_in{i}", [128, D], F32) for i in range(2)]
    Sel8 = sbuf(phc, "Sel8_s", [128, 8, 128], BF16)
    ish8 = sbuf(phc, "ish8_s", [128, 8], F32)
    idxv8 = sbuf(phc, "idxv8_s", [128, 8], F32)
    attn_f = sbuf(phc, "attn_f_s", [128, D], F32)
    y1_t = sbuf(phc, "y1_t_s", [128, D], F32)
    t1_t = sbuf(phc, "t1_t_s", [128, D], F32)
    t1bf = sbuf(phc, "t1bf_s", [128, D], BF16)
    t1T_tmp = sbuf(phc, "t1T_tmp_s", [128, D], BF16)
    sts = [sbuf(phc, f"sts{i}", [128, 4], F32) for i in range(4)]
    oh4 = sbuf(phc, "oh4_s", [128, NE], F32)
    ohbf = sbuf(phc, "ohbf_s", [128, NE], BF16)
    noty = sbuf(phc, "noty_s", [128, 1], F32)
    eq4 = sbuf(phc, "eq4_s", [128, NE], F32)
    m4 = sbuf(phc, "m4_s", [128, 1], F32)
    diag4 = sbuf(phc, "diag4_s", [128, 512], BF16)
    t1eT = sbuf(phc, "t1eT_s", [128, 8, NE, 1024], BF16)
    onehT = sbuf(phc, "onehT_s", [NE, 1024], BF16)
    inter_st = [sbuf(phc, f"inter_st{i}_x", [128, 512], BF16) for i in range(2)]
    intc = [sbuf(phc, f"intc{i}_x", [128, 512], BF16) for i in range(2)]
    w1sl = [sbuf(phc, f"w1sl{i}", [128, 8, NE, 128], BF16) for i in range(2)]
    wofc = [sbuf(phc, f"wofc{i}", [128, D], BF16) for i in range(2)]
    t2_t = sbuf(phc, "t2_t_s", [128, D], F32)
    t3_t = sbuf(phc, "t3_t_s", [128, D], F32)
    ps_ab = [psum(phc, f"ps_ab{i}", [128, 1024], F32) for i in range(2)]
    ps_z = [psum(phc, f"ps_z{i}", [128, 512], F32) for i in range(2)]
    ps_tT = [psum(phc, f"ps_tT{i}", [128, 512], F32) for i in range(2)]

    E = mkE('gb')
    E('pool', lambda e: e.dma_start(wr_sb.ap(), wr_d.rearrange("(kc p) f -> p kc f", p=128)),
      writes=['wr_sb'], dma='c_wr')
    E('pool', lambda e: e.dma_start(b1_sb.ap(), b1r_d), writes=['b1_sb'], dma='c_b1')
    E('pool', lambda e: e.dma_start(bof_sb[:, :], bof_d), writes=['bof_sb'], dma='c_bof')
    E('pool', lambda e: e.dma_start(gb_tmp[:, :], gam_d), writes=['gb_tmp'], dma='c_gb')
    for hf in range(2):
        E('pe', lambda e, hf=hf: e.matmul(ps_z[0][:, :], ones_row[0:1, 0:128],
                                          gb_tmp[:, hf*512:(hf+1)*512],
                                          start=True, stop=True),
          reads=['gb_tmp', 'ones_row'], writes=[('ps_z', 0)])
        E('dve', lambda e, hf=hf: e.tensor_copy(gam_b[:, hf*512:(hf+1)*512],
                                                ps_z[0][:, :]),
          reads=[('ps_z', 0)], writes=['gam_b'])
    E('pool', lambda e: e.dma_start(gb_tmp[:, :], bet_d), writes=['gb_tmp'], dma='c_gb')
    for hf in range(2):
        E('pe', lambda e, hf=hf: e.matmul(ps_z[0][:, :], ones_row[0:1, 0:128],
                                          gb_tmp[:, hf*512:(hf+1)*512],
                                          start=True, stop=True),
          reads=['gb_tmp', 'ones_row'], writes=[('ps_z', 0)])
        E('dve', lambda e, hf=hf: e.tensor_copy(bet_b[:, hf*512:(hf+1)*512],
                                                ps_z[0][:, :]),
          reads=[('ps_z', 0)], writes=['bet_b'])

    for ch in range(8):
        E('dve', lambda e, ch=ch: e.tensor_copy(idxv8[:, ch:ch+1], idxv[ch][:, :]),
          reads=[('idxv', ch, 0), ('idxv', ch, 64), ('iz', ch)], writes=['idxv8'])
    dkeys = [('delta', ch, bs, hf) for ch in range(8) for bs in (0, 64) for hf in range(2)] + \
            [('dz', c) for c in range(8)]
    ikeys = [('idxv', ch, bs) for ch in range(8) for bs in (0, 64)] + \
            [('iz', c) for c in range(8)]

    for sg in range(4):
        sgt0 = sg * 8
        E = mkE('as')
        for tl in range(8):
            t = sgt0 + tl
            xi = xq_in[tl % 2]
            xik = ('xq_in', tl % 2)
            E('sp', lambda e, t=t, xi=xi: e.dma_start(xi.ap(), xq_d[t*128:(t+1)*128, :]),
              writes=[xik], dma=f'xq{tl % 2}')
            pab = ps_ab[tl % 2]
            pa, pb = pab[:, 0:512], pab[:, 512:1024]
            pak, pbk = ('ps_ab', tl % 2, 0), ('ps_ab', tl % 2, 1)
            E('dve', lambda e, t=t: e.tensor_scalar(
                ish8[:, :], idxv8[:, :], qoffc[:, :], float(t * 128),
                ALU.subtract, ALU.subtract),
              reads=['idxv8', 'qoffc'], writes=['ish8'])
            E('dve', lambda e: e.tensor_tensor(
                Sel8.ap(),
                ish8.ap().unsqueeze(2).to_broadcast([128, 8, 128]),
                iotacol.ap().unsqueeze(1).to_broadcast([128, 8, 128]),
                ALU.is_equal),
              reads=['ish8', 'iotacol'], writes=['Sel8'])
            for ch in range(8):
                E('pe', lambda e, ch=ch, pa=pa: e.matmul(
                    pa, Sel8[:, ch, :], delta[ch][:, 0:512],
                    start=(ch == 0), stop=False),
                  reads=['Sel8'] + dkeys, writes=[pak])
                E('pe', lambda e, ch=ch, pb=pb: e.matmul(
                    pb, Sel8[:, ch, :], delta[ch][:, 512:1024],
                    start=(ch == 0), stop=False),
                  reads=['Sel8'] + dkeys, writes=[pbk])
            E('pe', lambda e, pa=pa: e.matmul(pa, ones_row[0:1, 0:128],
                                              base_bf[:, 0:512], start=False, stop=True),
              reads=['ones_row', 'base_bf'], writes=[pak])
            E('pe', lambda e, pb=pb: e.matmul(pb, ones_row[0:1, 0:128],
                                              base_bf[:, 512:1024], start=False, stop=True),
              reads=['ones_row', 'base_bf'], writes=[pbk])
            E('dve', lambda e, pa=pa: e.tensor_copy(attn_f[:, 0:512], pa),
              reads=[pak], writes=['attn_f'])
            E('dve', lambda e, pb=pb: e.tensor_copy(attn_f[:, 512:1024], pb),
              reads=[pbk], writes=['attn_f'])
            layernorm(attn_f[:, :], ['attn_f'], y1_t[:, :], 'y1', 128, epsA, 'epsA',
                      sts[0], 'sA')
            E('dve', lambda e, xi=xi: e.tensor_tensor(y1_t[:, :], y1_t[:, :], xi.ap(),
                                                      ALU.add),
              reads=['y1', xik], writes=['y1'])
            E('sp', lambda e, t=t: e.dma_start(y1s_d[t*128:(t+1)*128, :], y1_t[:, :]),
              reads=['y1'], writes=[('y1s', t)], dma='y1w')
            layernorm(y1_t[:, :], ['y1'], t1_t[:, :], 't1', 128, epsA, 'epsA',
                      sts[1], 'sB')
            E('sp', lambda e, t=t: e.dma_start(t1s_d[t*128:(t+1)*128, :], t1_t[:, :]),
              reads=['t1'], writes=[('t1s', t)], dma='t1w')
            E('dve', lambda e: e.tensor_copy(t1bf[:, :], t1_t[:, :]),
              reads=['t1'], writes=['t1bf'])
            if debug and t == 0:
                E('sp', lambda e: e.dma_start(dbg["dbg_y1"], y1_t[:, :]), reads=['y1'], dma='dbg')
                E('sp', lambda e: e.dma_start(dbg["dbg_t1"], t1_t[:, :]), reads=['t1'], dma='dbg')
            E = mkE('as2')
            for cb in range(2):
                bank = ps_tT[cb]
                bkk = ('ps_tT', cb)
                for j in range(4):
                    c = cb * 4 + j
                    E('pe', lambda e, c=c, j=j, bank=bank: e.matmul(
                        bank[:, j*128:(j+1)*128], t1bf[:, c*128:(c+1)*128], id_bf[:, :],
                        start=True, stop=True),
                      reads=['t1bf', 'id_bf'], writes=[bkk])
                E('dve', lambda e, cb=cb, bank=bank: e.tensor_copy(
                    t1T_tmp[:, cb*512:(cb+1)*512], bank[:, :]),
                  reads=[bkk], writes=[('t1T', cb)])
            def lg_mm(e):
                last = None
                for c in range(8):
                    last = e.matmul(ps_tT[0][:, 0:NE], t1T_tmp[:, c*128:(c+1)*128],
                                    wr_sb[:, c, :], start=(c == 0), stop=(c == 7))
                return last
            E('pe', lg_mm, reads=[('t1T', 0), ('t1T', 1), 'wr_sb'], writes=[('ps_tT', 0)])
            E('dve', lambda e: e.reduce_max(m4[:, :], ps_tT[0][:, 0:NE], axis=AX.X),
              reads=[('ps_tT', 0)], writes=['m4'])
            E('dve', lambda e: e.tensor_scalar(eq4[:, :], ps_tT[0][:, 0:NE], m4[:, :],
                                               None, ALU.is_ge),
              reads=[('ps_tT', 0), 'm4'], writes=['eq4'])
            E('dve', lambda e: e.tensor_copy(oh4[:, 0:1], eq4[:, 0:1]),
              reads=['eq4'], writes=['oh4'])
            E('dve', lambda e: e.tensor_scalar(noty[:, :], eq4[:, 0:1], -1.0, 1.0,
                                               ALU.mult, ALU.add),
              reads=['eq4'], writes=['noty'])
            for ee in range(1, NE):
                E('dve', lambda e, ee=ee: e.tensor_tensor(
                    oh4[:, ee:ee+1], eq4[:, ee:ee+1], noty[:, :], ALU.mult),
                  reads=['eq4', 'noty'], writes=['oh4'])
                if ee < NE - 1:
                    E('dve', lambda e, ee=ee: e.tensor_scalar(
                        junk1[:, :], oh4[:, ee:ee+1], -1.0, 1.0, ALU.mult, ALU.add),
                      reads=['oh4'], writes=['junk1'])
                    E('dve', lambda e: e.tensor_tensor(noty[:, :], noty[:, :],
                                                       junk1[:, :], ALU.mult),
                      reads=['noty', 'junk1'], writes=['noty'])
            E('dve', lambda e: e.tensor_copy(ohbf[:, :], oh4[:, :]),
              reads=['oh4'], writes=['ohbf'])
            E = mkE('as3')
            E('pe', lambda e: e.matmul(ps_tT[1][0:NE, 0:128], ohbf[:, :], id_bf[:, :],
                                       start=True, stop=True),
              reads=['ohbf', 'id_bf'], writes=[('ps_tT', 1)])
            E('dve', lambda e, tl=tl: e.tensor_copy(onehT[:, tl*128:(tl+1)*128],
                                                    ps_tT[1][0:NE, 0:128]),
              reads=[('ps_tT', 1)], writes=[('onehT', tl)])
            for ee in range(NE):
                E('dve', lambda e, ee=ee: e.tensor_scalar(
                    diag4[:, ee*128:(ee+1)*128], id_bf[:, :], oh4[:, ee:ee+1], None,
                    ALU.mult),
                  reads=['id_bf', 'oh4'], writes=[('diag4', ee)])
            for c in range(8):
                bank = ps_z[c % 2]
                bkk = ('ps_z', c % 2)
                E('pe', lambda e, c=c, bank=bank: e.matmul(
                    bank[:, :], t1bf[:, c*128:(c+1)*128], diag4[:, :],
                    start=True, stop=True),
                  reads=['t1bf'] + [('diag4', ee) for ee in range(NE)], writes=[bkk])
                E('dve', lambda e, c=c, tl=tl, bank=bank: e.tensor_copy(
                    t1eT[:, c, :, tl*128:(tl+1)*128],
                    bank.ap().rearrange("p (e f) -> p e f", e=NE)),
                  reads=[bkk], writes=[('t1eT', c, tl)])
        E = mkE('as')
        E = mkE('hp')
        t1e_keys = [('t1eT', c, tl) for c in range(8) for tl in range(8)]
        oh_keys = [('onehT', tl) for tl in range(8)]
        for ifc in range(32):
            wsl = w1sl[ifc % 2]
            wk = ('w1sl', ifc % 2)
            E('pool', lambda e, ifc=ifc, wsl=wsl: e.dma_start(
                wsl.ap(), w1r_d[ifc].rearrange("kc e p f -> p kc e f")),
              writes=[wk], dma=f'w1_{ifc % 2}')
            for hg in range(2):
                bank = ps_z[hg]
                bkk = ('ps_z', hg)
                def hmm(e, ifc=ifc, hg=hg, wsl=wsl, bank=bank):
                    first = True
                    for ee in range(NE):
                        for k in range(8):
                            e.matmul(bank[:, :], wsl[:, k, ee, :],
                                     t1eT[:, k, ee, hg*512:(hg+1)*512],
                                     start=first, stop=False)
                            first = False
                    return e.matmul(bank[:, :], b1_sb[:, ifc*128:(ifc+1)*128],
                                    onehT[:, hg*512:(hg+1)*512], start=False, stop=True)
                E('pe', hmm, reads=[wk, 'b1_sb'] + t1e_keys + oh_keys, writes=[bkk])
                ist = inter_st[(ifc * 2 + hg) % 2]
                istk = ('ist', (ifc * 2 + hg) % 2)
                E('act', lambda e, bank=bank, ist=ist: e.activation(
                    ist[:, :], bank[:, :], AF.Gelu),
                  reads=[bkk], writes=[istk])
                E('sp', lambda e, ifc=ifc, hg=hg, ist=ist: e.dma_start(
                    intT_d[ifc, :, hg*512:(hg+1)*512], ist[:, :]),
                  reads=[istk], writes=[('intd', ifc, hg)], dma='intw')
        E = mkE('wo')
        wo_banks = [ps_ab[0][:, 0:512], ps_ab[0][:, 512:1024],
                    ps_ab[1][:, 0:512], ps_ab[1][:, 512:1024],
                    ps_z[0][:, :], ps_z[1][:, :], ps_tT[0][:, :], ps_tT[1][:, :]]
        wo_keys = [('ps_ab', 0, 0), ('ps_ab', 0, 1), ('ps_ab', 1, 0), ('ps_ab', 1, 1),
                   ('ps_z', 0), ('ps_z', 1), ('ps_tT', 0), ('ps_tT', 1)]
        for hb in range(2):
            for ifc in range(32):
                wc = wofc[ifc % 2]
                wck = ('wofc', ifc % 2)
                E('pool', lambda e, ifc=ifc, wc=wc: e.dma_start(
                    wc.ap(), wof_d[ifc*128:(ifc+1)*128, :]),
                  writes=[wck], dma=f'wo{ifc % 2}')
                ic = intc[ifc % 2]
                ick = ('intc', ifc % 2)
                E('sp', lambda e, ifc=ifc, hb=hb, ic=ic: e.dma_start(
                    ic.ap(), intT_d[ifc, :, hb*512:(hb+1)*512]),
                  reads=[('intd', ifc, hb)], writes=[ick], dma=f'ir{ifc % 2}')
                for t4 in range(4):
                    tl = hb * 4 + t4
                    t = sgt0 + tl
                    pa, pb = wo_banks[2*t4], wo_banks[2*t4+1]
                    pak, pbk = wo_keys[2*t4], wo_keys[2*t4+1]
                    E('pe', lambda e, ifc=ifc, t4=t4, wc=wc, ic=ic, pa=pa: e.matmul(
                        pa, ic[:, t4*128:(t4+1)*128], wc[:, 0:512],
                        start=(ifc == 0), stop=False),
                      reads=[wck, ick], writes=[pak])
                    E('pe', lambda e, ifc=ifc, t4=t4, wc=wc, ic=ic, pb=pb: e.matmul(
                        pb, ic[:, t4*128:(t4+1)*128], wc[:, 512:1024],
                        start=(ifc == 0), stop=False),
                      reads=[wck, ick], writes=[pbk])
            for t4 in range(4):
                pa, pb = wo_banks[2*t4], wo_banks[2*t4+1]
                pak, pbk = wo_keys[2*t4], wo_keys[2*t4+1]
                E('pe', lambda e, pa=pa: e.matmul(pa, ones_row[0:1, 0:128],
                                                  bof_sb[:, 0:512], start=False, stop=True),
                  reads=['ones_row', 'bof_sb'], writes=[pak])
                E('pe', lambda e, pb=pb: e.matmul(pb, ones_row[0:1, 0:128],
                                                  bof_sb[:, 512:1024], start=False, stop=True),
                  reads=['ones_row', 'bof_sb'], writes=[pbk])
            for t4 in range(4):
                tl = hb * 4 + t4
                t = sgt0 + tl
                pa, pb = wo_banks[2*t4], wo_banks[2*t4+1]
                pak, pbk = wo_keys[2*t4], wo_keys[2*t4+1]
                xi = xq_in[t4 % 2]
                xik = ('xq_in', t4 % 2)
                E('sp', lambda e, t=t, xi=xi: e.dma_start(xi.ap(), t1s_d[t*128:(t+1)*128, :]),
                  reads=[('t1s', t)], writes=[xik], dma=f'xq{t4 % 2}')
                E('dve', lambda e, pa=pa, xi=xi: e.tensor_tensor(
                    t2_t[:, 0:512], pa, xi.ap()[:, 0:512], ALU.add),
                  reads=[pak, xik], writes=['T2'])
                E('dve', lambda e, pb=pb, xi=xi: e.tensor_tensor(
                    t2_t[:, 512:1024], pb, xi.ap()[:, 512:1024], ALU.add),
                  reads=[pbk, xik], writes=['T2'])
                layernorm(t2_t[:, :], ['T2'], t3_t[:, :], 'T3', 128, epsB, 'epsB',
                          sts[2], 'sC')
                E('dve', lambda e: e.tensor_tensor(t3_t[:, :], t3_t[:, :], gam_b[:, :],
                                                   ALU.mult),
                  reads=['T3', 'gam_b'], writes=['T3'])
                E('dve', lambda e: e.tensor_tensor(t3_t[:, :], t3_t[:, :], bet_b[:, :],
                                                   ALU.add),
                  reads=['T3', 'bet_b'], writes=['T3'])
                layernorm(t3_t[:, :], ['T3'], t2_t[:, :], 'T2', 128, epsA, 'epsA',
                          sts[3], 'sD')
                E('sp', lambda e, t=t, xi=xi: e.dma_start(xi.ap(), y1s_d[t*128:(t+1)*128, :]),
                  reads=[('y1s', t)], writes=[xik], dma=f'xq{t4 % 2}')
                E('dve', lambda e, xi=xi: e.tensor_tensor(t3_t[:, :], t2_t[:, :],
                                                          xi.ap(), ALU.add),
                  reads=['T2', xik], writes=['T3'])
                E('sp', lambda e, t=t: e.dma_start(out_d[t*128:(t+1)*128, :], t3_t[:, :]),
                  reads=['T3'], dma='out')

    E = _E
    if debug:
        p.wait_all_dma('pool', 'dbg')
    p.wait_all_dma('pool', 'out')
    p.build()
    p.close()
    phc.close()
    sbstack.close()
    return nc


_CACHE = {}


def _host_inputs(inputs):
    x = np.asarray(inputs['L0_attn_x'], np.float32)
    wqT = np.ascontiguousarray(np.asarray(inputs['L0_attn_Wq'], np.float32).T)
    wkT = np.ascontiguousarray(np.asarray(inputs['L0_attn_Wk'], np.float32).T)
    wvT = np.ascontiguousarray(np.asarray(inputs['L0_attn_Wv'], np.float32).T)
    woT = np.ascontiguousarray(np.asarray(inputs['L0_attn_Wo'], np.float32).T)
    bq = np.ascontiguousarray(np.asarray(inputs['L0_attn_bq'], np.float32)[None, :])
    bk = np.ascontiguousarray(np.asarray(inputs['L0_attn_bk'], np.float32)[None, :])
    bv = np.ascontiguousarray(np.asarray(inputs['L0_attn_bv'], np.float32)[None, :])
    bo = np.ascontiguousarray(np.asarray(inputs['L0_attn_bo'], np.float32)[None, :])
    idxk = np.ascontiguousarray(np.asarray(inputs['L0_attn_idx_k'], np.int32))
    Wr = np.ascontiguousarray(np.asarray(inputs['L0_ffn_W_router'], np.float32))
    W1 = np.asarray(inputs['L0_ffn_W1'], np.float32)
    w1r = np.ascontiguousarray(W1.reshape(NE, 8, 128, 32, 128).transpose(3, 1, 0, 2, 4))
    b1r = np.ascontiguousarray(np.asarray(inputs['L0_ffn_b1'], np.float32)
                               .reshape(NE, 32 * 128))
    Wof = np.ascontiguousarray(np.asarray(inputs['L0_ffn_Wo'], np.float32))
    bof = np.ascontiguousarray(np.asarray(inputs['L0_ffn_bo'], np.float32)[None, :])
    gam = np.ascontiguousarray(np.asarray(inputs['L0_ffn_ln_gamma'], np.float32)[None, :])
    bet = np.ascontiguousarray(np.asarray(inputs['L0_ffn_ln_beta'], np.float32)[None, :])
    ident = np.eye(128, dtype=np.float32)
    iotacol = np.tile(np.arange(128, dtype=np.float32)[None, :], (128, 1))
    iotaM = np.empty((128, H * 128), np.float32)
    for h in range(H):
        iotaM[:, h*128:(h+1)*128] = (np.arange(128)[None, :] * 128
                                     + np.arange(128)[:, None])
    xbs = [np.ascontiguousarray(x[0]), np.ascontiguousarray(x[1])]
    in_maps = []
    for c in range(8):
        b, q = c // 4, c % 4
        in_maps.append({
            "x": xbs[b],
            "xq": np.ascontiguousarray(xbs[b][q*QTOK:(q+1)*QTOK]),
            "wqT": wqT, "wkT": wkT, "wvT": wvT, "woT": woT,
            "bq": bq, "bk": bk, "bv": bv, "bo": bo,
            "idxk": idxk, "wr": Wr, "w1r": w1r, "b1r": b1r,
            "wof": Wof, "bof": bof, "gam": gam, "bet": bet,
            "ident": ident, "iotacol": iotacol, "iotaM": iotaM,
            "qoffc": np.full((128, 1), q * QTOK, np.float32),
        })
    return in_maps


def kernel(**inputs):
    if 'nc' not in _CACHE:
        _CACHE['nc'] = build_program(debug=False)
    nc = _CACHE['nc']
    in_maps = _host_inputs(inputs)
    res = run_bass_kernel_spmd(nc, in_maps, core_ids=list(range(8)))
    out = np.empty((2, L, D), np.float32)
    for c in range(8):
        b, q = c // 4, c % 4
        out[b, q*QTOK:(q+1)*QTOK] = res.results[c]["y"]
    return out


# version 09
# speedup vs baseline: 1.1372x; 1.1372x over previous
import contextlib
import math

import numpy as np

import concourse.bass as bass
import concourse.mybir as mybir
from concourse.bass import IndirectOffsetOnAxis
from concourse.bass_utils import run_bass_kernel_spmd

F32 = mybir.dt.float32
BF16 = mybir.dt.bfloat16
I32 = mybir.dt.int32
U8 = mybir.dt.uint8
AF = mybir.ActivationFunctionType
ALU = mybir.AluOpType
AX = mybir.AxisListType

D = 1024
L = 16384
H = 16
DK = 64
U = 50
NG = 32
QTOK = 4096
NE = 4
DI = 4096
BISECT = 34
SCALE = 1.0 / math.sqrt(DK)


class Prog:
    ENGINES = ('pe', 'dve', 'act', 'pool', 'sp')

    def __init__(self, nc):
        self.nc = nc
        self.stack = contextlib.ExitStack()
        self.csem = {}
        self.ccount = {}
        for e in ('pe', 'dve', 'act', 'pool'):
            self.csem[e] = self.stack.enter_context(nc.semaphore(f"c_{e}"))
            self.ccount[e] = 0
        self.dsem = {}
        self.dcount = {}
        self.dqueue = {}
        self.ops = {e: [] for e in self.ENGINES}
        self.last_w = {}
        self.readers = {}
        self.know = {e: {} for e in self.ENGINES}
        self.read_misses = set()

    def chan(self, name, engine):
        if name not in self.dsem:
            self.dsem[name] = self.stack.enter_context(self.nc.semaphore(f"d_{name}"))
            self.dcount[name] = 0
            self.dqueue[name] = engine
        assert self.dqueue[name] == engine, (name, engine, self.dqueue[name])
        return name

    def _resolve(self, dep):
        if dep[0] == 'c':
            _, e, v = dep
            return (f"c_{e}", self.csem[e], v)
        _, ch = dep[:2]
        return (f"d_{ch}", self.dsem[ch], 16 * self.dcount[ch])

    def emit(self, engine, fn, reads=(), writes=(), dma=None, tag=None):
        deps = []
        for k in reads:
            w = self.last_w.get(k)
            if w is None:
                self.read_misses.add(k)
            else:
                deps.append(w)
        for k in writes:
            w = self.last_w.get(k)
            if w is not None:
                deps.append(w)
            deps.extend(self.readers.get(k, ()))
        need = {}
        for dep in deps:
            if dep[0] == 'c' and dep[1] == 'pe' and engine == 'pe' and dma is None:
                continue
            name, handle, v = self._resolve(dep)
            if self.know[engine].get(name, -1) >= v:
                continue
            if name in need and need[name][1] >= v:
                continue
            need[name] = (handle, v)
        waits = []
        for name, (handle, v) in need.items():
            waits.append((handle, v))
            self.know[engine][name] = v

        if dma is not None:
            ch = self.chan(dma, engine)
            self.dcount[ch] += 1
            inc = (self.dsem[ch], 16)
            marker = ('d', ch, self.dcount[ch])
        elif engine == 'sp':
            inc = None
            marker = None
        else:
            self.ccount[engine] += 1
            inc = (self.csem[engine], 1)
            marker = ('c', engine, self.ccount[engine])

        self.ops[engine].append((waits, fn, inc, tag))
        if marker is not None:
            for k in writes:
                self.last_w[k] = marker
                self.readers[k] = []
            for k in reads:
                self.readers.setdefault(k, []).append(marker)
        return marker

    def wait_all_dma(self, engine, channel):
        if channel not in self.dsem:
            return
        self.ops[engine].append(
            ([(self.dsem[channel], 16 * self.dcount[channel])], None, None, 'wd'))
        self.know[engine][f"d_{channel}"] = 16 * self.dcount[channel]

    def fence(self):
        targets = []
        for e in ('pe', 'dve', 'act', 'pool'):
            if self.ccount[e]:
                targets.append((f"c_{e}", self.csem[e], self.ccount[e]))
        for ch in self.dsem:
            if self.dcount[ch]:
                targets.append((f"d_{ch}", self.dsem[ch], 16 * self.dcount[ch]))
        for e in self.ENGINES:
            ws = []
            for name, handle, v in targets:
                if name == f"c_{e}":
                    continue
                if self.know[e].get(name, -1) >= v:
                    continue
                ws.append((handle, v))
                self.know[e][name] = v
            if ws:
                self.ops[e].append((ws, None, None, 'fence'))

    def build(self):
        nc = self.nc
        if self.read_misses:
            import sys
            print("PROG read-before-write keys:", sorted(map(str, self.read_misses)),
                  file=sys.stderr)
        with nc.Block() as blk:
            def replay(ename):
                def _go(eng):
                    for waits, fn, inc, tag in self.ops[ename]:
                        for handle, v in waits:
                            eng.wait_ge(handle, v)
                        if fn is not None:
                            inst = fn(eng)
                            if inc is not None:
                                inst.then_inc(inc[0], inc[1])
                return _go
            if self.ops['pe']: blk.tensor(replay('pe'))
            if self.ops['dve']: blk.vector(replay('dve'))
            if self.ops['act']: blk.scalar(replay('act'))
            if self.ops['pool']: blk.gpsimd(replay('pool'))
            if self.ops['sp']: blk.sync(replay('sp'))

    def close(self):
        self.stack.close()


def build_program(debug=False, maxphase=9, ngroups=NG, nbisect=BISECT, parts='all'):
    nc = bass.Bass("TRN2", target_bir_lowering=False)

    def din(name, shape, dt=F32):
        return nc.dram_tensor(name, shape, dt, kind="ExternalInput").ap()
    x_d = din("x", [L, D])
    xq_d = din("xq", [QTOK, D])
    wqT_d = din("wqT", [D, D]); wkT_d = din("wkT", [D, D]); wvT_d = din("wvT", [D, D])
    woT_d = din("woT", [D, D])
    bq_d = din("bq", [1, D]); bk_d = din("bk", [1, D]); bv_d = din("bv", [1, D])
    bo_d = din("bo", [1, D])
    idxk_d = din("idxk", [U], I32)
    wr_d = din("wr", [D, NE])
    w1r_d = din("w1r", [32, 8, NE, 128, 128])
    b1r_d = din("b1r", [NE, 32 * 128])
    wof_d = din("wof", [DI, D])
    bof_d = din("bof", [1, D]); gam_d = din("gam", [1, D]); bet_d = din("bet", [1, D])
    ident_d = din("ident", [128, 128])
    iotacol_d = din("iotacol", [128, 128])
    iotaM_d = din("iotaM", [128, H * 128])
    qoff_d = din("qoffc", [128, 1])
    out_d = nc.dram_tensor("y", [QTOK, D], F32, kind="ExternalOutput").ap()
    dbg = {}
    if debug:
        def dout(name, shape, dt=F32):
            dbg[name] = nc.dram_tensor(name, shape, dt, kind="ExternalOutput").ap()
        dout("dbg_M", [128, H * 128]); dout("dbg_idx", [64, H])
        dout("dbg_vmean", [1, D]); dout("dbg_base", [1, D])
        dout("dbg_delta", [128, D]); dout("dbg_y1", [128, D])
        dout("dbg_t1", [128, D]); dout("dbg_lo", [128, H])

    kfm_d = nc.dram_tensor("kfm", [8, 128, L], BF16, kind="Internal").ap()
    vscr_d = nc.dram_tensor("vscr", [8, 128, 128, 128], BF16, kind="Internal").ap()
    t1s_d = nc.dram_tensor("t1s", [QTOK, D], F32, kind="Internal").ap()
    intT_d = nc.dram_tensor("intT", [32, 128, 1024], BF16, kind="Internal").ap()
    y1s_d = nc.dram_tensor("y1s", [QTOK, D], F32, kind="Internal").ap()

    p = Prog(nc)
    _E = p.emit
    def mkE(label):
        def E(*a, **k):
            if parts == 'all' or label in parts:
                return _E(*a, **k)
        return E
    E = _E
    sbstack = contextlib.ExitStack()

    def sbuf(stk, n, s, d=F32):
        return stk.enter_context(nc.sbuf_tensor(n, s, d))

    def psum(stk, n, s, d=F32):
        return stk.enter_context(nc.psum_tensor(n, s, d))

    perm = sbstack
    id_bf = sbuf(perm, "id_bf_s", [128, 128], BF16)
    idf32 = sbuf(perm, "idf32_s", [128, 128], F32)
    iotacol = sbuf(perm, "iotacol_s", [128, 128], F32)
    qoffc = sbuf(perm, "qoffc_s", [128, 1], F32)
    ones_row = sbuf(perm, "ones_row_s", [1, 512], BF16)
    ones_col = sbuf(perm, "ones_col_s", [128, 1], BF16)
    ones_sq = sbuf(perm, "ones_sq_s", [128, 128], BF16)
    one1 = sbuf(perm, "one1_s", [1, 1], BF16)
    epsA = sbuf(perm, "epsA_s", [128, 1], F32)
    epsB = sbuf(perm, "epsB_s", [128, 1], F32)
    negb = sbuf(perm, "negb_s", [128, 1], F32)
    ksampT = sbuf(perm, "ksampT_s", [128, 8, 104], BF16)
    xbar_bf = sbuf(perm, "xbar_bf_s", [1, D], BF16)
    xbarT = sbuf(perm, "xbarT_s", [128, 8], BF16)
    vmean = sbuf(perm, "vmean_s", [1, D], F32)
    vmean_bf = sbuf(perm, "vmean_bf_s", [1, D], BF16)
    vmeanT = sbuf(perm, "vmeanT_s", [128, 8], F32)
    vmeanT_bf = sbuf(perm, "vmeanT_bf_s", [128, 8], BF16)
    base_bf = sbuf(perm, "base_bf_s", [1, D], BF16)
    idxT_f = sbuf(perm, "idxT_f_s", [64, H], F32)
    idxT_i = sbuf(perm, "idxT_i_s", [64, H], I32)
    delta = [sbuf(perm, f"delta{c}", [128, D], BF16) for c in range(8)]
    idxv = [sbuf(perm, f"idxv{c}", [128, 1], F32) for c in range(8)]
    junk1 = sbuf(perm, "junk1_s", [128, 1], F32)

    E('pool', lambda e: e.dma_start(id_bf[:, :], ident_d), writes=['id_bf'], dma='c_idbf')
    E('sp', lambda e: e.dma_start(idf32[:, :], ident_d), writes=['idf32'], dma='c_idf')
    E('sp', lambda e: e.dma_start(iotacol[:, :], iotacol_d), writes=['iotacol'], dma='c_ioc')
    E('sp', lambda e: e.dma_start(qoffc[:, :], qoff_d), writes=['qoffc'], dma='c_qof')
    E('dve', lambda e: e.memset(ones_row[:, :], 1.0), writes=['ones_row'])
    E('dve', lambda e: e.memset(ones_col[:, :], 1.0), writes=['ones_col'])
    E('dve', lambda e: e.memset(ones_sq[:, :], 1.0), writes=['ones_sq'])
    E('dve', lambda e: e.memset(one1[:, :], 1.0), writes=['one1'])
    E('dve', lambda e: e.memset(epsA[:, :], 1e-5), writes=['epsA'])
    E('dve', lambda e: e.memset(epsB[:, :], 1e-12), writes=['epsB'])
    E('dve', lambda e: e.memset(negb[:, :], -12.0), writes=['negb'])

    def layernorm(src_ap, src_keys, dst_ap, dst_key, rows, eps_tile, eps_key,
                  stats, skey, lnexp=False):
        rd = list(src_keys)
        E('dve', lambda e: e.reduce_sum(stats[0:rows, 0:1], src_ap, axis=AX.X),
          reads=rd, writes=[(skey, 0)])
        E('act', lambda e: e.activation(dst_ap, src_ap, AF.Square,
                                        accum_out=stats[0:rows, 1:2]),
          reads=rd, writes=[dst_key, (skey, 1)])
        E('dve', lambda e: e.tensor_scalar(stats[0:rows, 0:1], stats[0:rows, 0:1],
                                           1.0 / D, None, ALU.mult),
          reads=[(skey, 0)], writes=[(skey, 0)])
        E('dve', lambda e: e.tensor_scalar(stats[0:rows, 1:2], stats[0:rows, 1:2],
                                           1.0 / D, None, ALU.mult),
          reads=[(skey, 1)], writes=[(skey, 1)])
        E('dve', lambda e: e.tensor_tensor(stats[0:rows, 2:3], stats[0:rows, 0:1],
                                           stats[0:rows, 0:1], ALU.mult),
          reads=[(skey, 0)], writes=[(skey, 2)])
        E('dve', lambda e: e.tensor_tensor(stats[0:rows, 2:3], stats[0:rows, 1:2],
                                           stats[0:rows, 2:3], ALU.subtract),
          reads=[(skey, 1), (skey, 2)], writes=[(skey, 2)])
        if lnexp:
            E('act', lambda e: e.activation(stats[0:rows, 3:4], stats[0:rows, 2:3],
                                            AF.Ln, bias=eps_tile[0:rows, 0:1]),
              reads=[(skey, 2), eps_key], writes=[(skey, 3)])
            E('dve', lambda e: e.tensor_scalar(stats[0:rows, 3:4], stats[0:rows, 3:4],
                                               -0.5, None, ALU.mult),
              reads=[(skey, 3)], writes=[(skey, 3)])
            E('act', lambda e: e.activation(stats[0:rows, 3:4], stats[0:rows, 3:4],
                                            AF.Exp),
              reads=[(skey, 3)], writes=[(skey, 3)])
        else:
            E('act', lambda e: e.activation(stats[0:rows, 3:4], stats[0:rows, 2:3],
                                            AF.Sqrt, bias=eps_tile[0:rows, 0:1]),
              reads=[(skey, 2), eps_key], writes=[(skey, 3)])
            E('dve', lambda e: e.reciprocal(stats[0:rows, 3:4], stats[0:rows, 3:4]),
              reads=[(skey, 3)], writes=[(skey, 3)])
        E('dve', lambda e: e.tensor_scalar(dst_ap, src_ap, stats[0:rows, 0:1],
                                           stats[0:rows, 3:4], ALU.subtract, ALU.mult),
          reads=rd + [(skey, 0), (skey, 3)], writes=[dst_key])

    pha = contextlib.ExitStack()
    st_w = contextlib.ExitStack()
    st_m = contextlib.ExitStack()
    bq_sb = sbuf(perm, "bq_sb_s", [1, D], BF16)
    bo_sb = sbuf(perm, "bo_sb_s", [1, D], BF16)
    wqT = sbuf(st_w, "wqT_s", [128, 8, D], BF16)
    woT_sb = sbuf(st_w, "woT_sb_s", [128, 8, D], BF16)
    M_tm = sbuf(st_m, "M_tm_s", [128, H * 128], F32)
    lo_t = sbuf(st_m, "lo_t_s", [128, H], F32)
    wkT = sbuf(pha, "wkT_s", [128, 8, D], BF16)
    wvT = sbuf(pha, "wvT_s", [128, 8, D], BF16)
    bk_sb = sbuf(pha, "bk_sb_s", [1, D], BF16)
    bv_sb = sbuf(pha, "bv_sb_s", [1, D], BF16)
    E('pool', lambda e: e.dma_start(wqT.ap(), wqT_d.rearrange("(kc p) f -> p kc f", p=128)),
      writes=['wqT'], dma='c_wq')
    E('pool', lambda e: e.dma_start(wkT.ap(), wkT_d.rearrange("(kc p) f -> p kc f", p=128)),
      writes=['wkT'], dma='c_wk')
    E('pool', lambda e: e.dma_start(wvT.ap(), wvT_d.rearrange("(kc p) f -> p kc f", p=128)),
      writes=['wvT'], dma='c_wv')
    E('pool', lambda e: e.dma_start(woT_sb.ap(), woT_d.rearrange("(kc p) f -> p kc f", p=128)),
      writes=['woT_sb'], dma='c_wo')
    E('pool', lambda e: e.dma_start(bq_sb[:, :], bq_d), writes=['bq_sb'], dma='c_bq')
    E('pool', lambda e: e.dma_start(bk_sb[:, :], bk_d), writes=['bk_sb'], dma='c_bk')
    E('pool', lambda e: e.dma_start(bv_sb[:, :], bv_d), writes=['bv_sb'], dma='c_bv')
    E('pool', lambda e: e.dma_start(bo_sb[:, :], bo_d), writes=['bo_sb'], dma='c_bo')

    ps_tr = [psum(pha, f"ps_tr{i}", [128, 512], F32) for i in range(2)]
    ps_qk = [psum(pha, f"ps_qk{i}", [128, 512], F32) for i in range(3)]
    ps_sc = psum(pha, "ps_sc_s", [128, 512], F32)
    ps_xb0 = psum(pha, "ps_xb0_s", [1, 512], F32)
    ps_xb1 = psum(pha, "ps_xb1_s", [1, 512], F32)
    pp_pool = [ps_qk[0], ps_qk[1], ps_qk[2], ps_sc]
    pp_ctr = [0]
    def next_pp():
        i = pp_ctr[0] % 4
        pp_ctr[0] += 1
        return pp_pool[i], ('pp', i)

    E = mkE('p0b')
    idxk_sb = sbuf(pha, "idxk_sb_s", [U, 1], I32)
    xs_f = sbuf(pha, "xs_f_s", [U, D], F32)
    xs_bf = sbuf(pha, "xs_bf_s", [128, D], BF16)
    xs_T = sbuf(pha, "xs_T_s", [128, 8, 64], BF16)
    kbar = sbuf(pha, "kbar_s", [128, 1], F32)
    st0 = sbuf(pha, "st0_s", [128, 4], F32)

    E('sp', lambda e: e.dma_start(idxk_sb[:, :], idxk_d.rearrange("(a b) -> a b", b=1)),
      writes=['idxk_sb'], dma='c_idk')
    E('dve', lambda e: e.memset(ksampT.ap(), 0.0), writes=['ks_init'])
    E('pool', lambda e: e.indirect_dma_start(
        xs_f[:, :], None, x_d, IndirectOffsetOnAxis(idxk_sb[:, :], 0)),
      reads=['idxk_sb'], writes=['xs_f'], dma='g0')
    E('dve', lambda e: e.memset(xs_bf[:, :], 0.0), writes=['xs_bf'])
    layernorm(xs_f[:, :], ['xs_f'], xs_bf[0:U, :], 'xs_bf', U, epsA, 'epsA', st0, 'st0')
    for c in range(8):
        bank = ps_tr[c % 2]
        bkk = ('ps_tr', c % 2)
        E('pe', lambda e, c=c, bank=bank: e.matmul(
            bank[:, 0:128], xs_bf[:, c*128:(c+1)*128], id_bf[:, :],
            start=True, stop=True),
          reads=['xs_bf', 'id_bf'], writes=[bkk])
        E('dve', lambda e, c=c, bank=bank: e.tensor_copy(xs_T[:, c, :], bank[:, 0:64]),
          reads=[bkk], writes=[('xs_T', c)])
    xsT_keys = [('xs_T', c) for c in range(8)]
    for h in range(H):
        pr, hi = h // 2, h % 2
        bank, bkk = next_pp()
        def ksamp_mm(e, h=h, bank=bank):
            for k in range(8):
                e.matmul(bank[0:64, 0:64], wkT[:, k, h*64:(h+1)*64], xs_T[:, k, :],
                         start=(k == 0), stop=False)
            return e.matmul(bank[0:64, 0:64], bk_sb[:, h*64:(h+1)*64],
                            ones_row[:, 0:64], start=False, stop=True)
        E('pe', ksamp_mm, reads=xsT_keys + ['wkT', 'bk_sb', 'ones_row'],
          writes=[bkk])
        E('dve', lambda e, pr=pr, hi=hi, bank=bank: e.tensor_copy(
            ksampT[hi*64:(hi+1)*64, pr, hi*52:hi*52+50], bank[0:64, 0:50]),
          reads=[bkk, 'ks_init'], writes=[('ksampT', pr, hi)])
        E('dve', lambda e, bank=bank: e.reduce_sum(kbar[0:64, :], bank[0:64, 0:50], axis=AX.X),
          reads=[bkk], writes=['kbar'])
        E('dve', lambda e, pr=pr, hi=hi: e.tensor_scalar(
            ksampT[hi*64:(hi+1)*64, pr, hi*52+50:hi*52+51], kbar[0:64, :],
            1.0 / U, None, ALU.mult),
          reads=['kbar', 'ks_init'], writes=[('ksampT', pr, hi, 'bar')])
    ks_keys = [('ksampT', pr, hi) for pr in range(8) for hi in range(2)] + \
              [('ksampT', pr, hi, 'bar') for pr in range(8) for hi in range(2)] + ['ks_init']

    x_in = [sbuf(pha, f"x_in{i}", [128, 4, D], F32) for i in range(2)]
    xln2 = [sbuf(pha, f"xln{i}_s", [128, 4, D], BF16) for i in range(2)]
    xT = sbuf(pha, "xT_s", [128, 8, 512], BF16)
    qt = sbuf(pha, "qt_s", [128, 8, 512], BF16)
    ktile = [sbuf(pha, f"ktile{i}", [128, 512], BF16) for i in range(2)]
    vt = sbuf(pha, "vt_s", [128, 4, D], BF16)
    st1 = [[sbuf(pha, f"st1_{i}_{s}", [128, 4], F32) for s in range(4)] for i in range(2)]
    mtmp = sbuf(pha, "mtmp_s", [128, 8], F32)

    E = _E

    def emit_x_dma(g):
        gp = g % 2
        xi = x_in[gp]
        _E('sp', lambda e, g=g, xi=xi: e.dma_start(
            xi.ap(), x_d[g*512:(g+1)*512, :].rearrange("(s p) f -> p s f", p=128)),
          writes=[('x_in', gp)], dma=f'x{gp}')

    def emit_ln_slice(g, s):
        gp = g % 2
        layernorm(x_in[gp][:, s, :], [('x_in', gp)], xln2[gp][:, s, :],
                  ('xln', gp, s), 128, epsA, 'epsA', st1[gp][s], ('st1', gp, s))

    emit_x_dma(0)
    for s in range(4):
        emit_ln_slice(0, s)
    for g in range(ngroups):
        gp = g % 2
        xln = xln2[gp]
        if g + 1 < ngroups:
            emit_x_dma(g + 1)
        E = mkE('tr')
        for s in range(4):
            for cb in range(2):
                bank = ps_tr[cb]
                bkk = ('ps_tr', cb)
                for j in range(4):
                    c = cb * 4 + j
                    E('pe', lambda e, s=s, c=c, j=j, bank=bank, xln=xln: e.matmul(
                        bank[:, j*128:(j+1)*128], xln[:, s, c*128:(c+1)*128],
                        id_bf[:, :], start=True, stop=True),
                      reads=[('xln', gp, s), 'id_bf'], writes=[bkk])
                E('dve', lambda e, s=s, cb=cb, bank=bank: e.tensor_copy(
                    xT.ap()[:, cb*4:(cb+1)*4, s*128:(s+1)*128],
                    bank.ap().rearrange("p (c f) -> p c f", c=4)),
                  reads=[bkk], writes=[('xT', s, cb)])
        xt_keys = [('xT', s, cb) for s in range(4) for cb in range(2)]
        E = mkE('q')
        for f in range(8):
            bank, bkk = next_pp()
            def qmm(e, f=f, bank=bank):
                for k in range(8):
                    e.matmul(bank[:, :], wqT[:, k, f*128:(f+1)*128], xT[:, k, :],
                             start=(k == 0), stop=False)
                return e.matmul(bank[:, :], bq_sb[:, f*128:(f+1)*128],
                                ones_row[:, :], start=False, stop=True)
            E('pe', qmm, reads=xt_keys + ['wqT', 'bq_sb', 'ones_row'], writes=[bkk])
            E('dve', lambda e, f=f, bank=bank: e.tensor_copy(qt[:, f, :], bank[:, :]),
              reads=[bkk], writes=[('qt', f)])
            if f in (2, 6) and g + 1 < ngroups:
                emit_ln_slice(g + 1, f // 4)
        E = mkE('k')
        for f in range(8):
            kt = ktile[f % 2]
            ktk = ('ktile', f % 2)
            bank, bkk = next_pp()
            def kmm(e, f=f, bank=bank):
                for k in range(8):
                    e.matmul(bank[:, :], wkT[:, k, f*128:(f+1)*128], xT[:, k, :],
                             start=(k == 0), stop=False)
                return e.matmul(bank[:, :], bk_sb[:, f*128:(f+1)*128],
                                ones_row[:, :], start=False, stop=True)
            E('pe', kmm, reads=xt_keys + ['wkT', 'bk_sb', 'ones_row'], writes=[bkk])
            E('dve', lambda e, kt=kt, bank=bank: e.tensor_copy(kt[:, :], bank[:, :]),
              reads=[bkk], writes=[ktk])
            E('sp', lambda e, g=g, f=f, kt=kt: e.dma_start(
                kfm_d[f, :, g*512:(g+1)*512], kt[:, :]),
              reads=[ktk], writes=['kfm_w'], dma='kw')
            if f in (2, 6) and g + 1 < ngroups:
                emit_ln_slice(g + 1, 2 + f // 4)
        E = mkE('v')
        for s in range(4):
            for hf in range(2):
                bank, bkk = next_pp()
                def vmm(e, s=s, hf=hf, bank=bank):
                    for k in range(8):
                        e.matmul(bank[:, :], xT[:, k, s*128:(s+1)*128],
                                 wvT[:, k, hf*512:(hf+1)*512], start=(k == 0), stop=False)
                    return e.matmul(bank[:, :], ones_row[0:1, 0:128],
                                    bv_sb[:, hf*512:(hf+1)*512], start=False, stop=True)
                E('pe', vmm, reads=xt_keys + ['wvT', 'bv_sb', 'ones_row'], writes=[bkk])
                E('dve', lambda e, s=s, hf=hf, bank=bank: e.tensor_copy(
                    vt[:, s, hf*512:(hf+1)*512], bank[:, :]),
                  reads=[bkk], writes=[('vt', s, hf)])
        vt_keys = [('vt', s, hf) for s in range(4) for hf in range(2)]
        for pr in range(8):
            E('sp', lambda e, g=g, pr=pr: e.dma_start(
                vscr_d[pr, g*4:(g+1)*4, :, :].rearrange("s p d -> p s d"),
                vt[:, :, pr*128:(pr+1)*128]),
              reads=vt_keys, writes=['vscr_w'], dma='vw')
        E = mkE('xb')
        for s in range(4):
            E('pe', lambda e, g=g, s=s, xln=xln: e.matmul(
                ps_xb0[:, :], ones_col[:, :], xln[:, s, 0:512],
                start=(g == 0 and s == 0), stop=(g == ngroups - 1 and s == 3)),
              reads=[('xln', gp, s), 'ones_col'], writes=[('ps_xb', 0)])
            E('pe', lambda e, g=g, s=s, xln=xln: e.matmul(
                ps_xb1[:, :], ones_col[:, :], xln[:, s, 512:1024],
                start=(g == 0 and s == 0), stop=(g == ngroups - 1 and s == 3)),
              reads=[('xln', gp, s), 'ones_col'], writes=[('ps_xb', 1)])
        E = mkE('sc')
        for s in range(4):
            for bk_i in range(2):
                bank, bkk = next_pp()
                def smm(e, s=s, bk_i=bk_i, bank=bank):
                    last = None
                    for j in range(4):
                        pr = bk_i * 4 + j
                        last = e.matmul(
                            bank[:, j*104:(j+1)*104],
                            qt[:, pr, s*128:(s+1)*128],
                            ksampT[:, pr, :],
                            start=True, stop=True)
                    return last
                E('pe', smm, reads=[('qt', f) for f in range(8)] + ks_keys,
                  writes=[bkk])
                E('dve', lambda e, bank=bank: e.reduce_max(
                    mtmp.ap().rearrange("p (j hi o) -> p j hi o", j=4, hi=2),
                    bank[:, 0:416].rearrange("p (j hi c) -> p j hi c", j=4, hi=2)[:, :, :, 0:50],
                    axis=AX.X),
                  reads=[bkk], writes=['mtmp'])
                E('dve', lambda e, g=g, s=s, bk_i=bk_i, bank=bank: e.tensor_tensor(
                    M_tm.ap().rearrange("p (h j) -> p h j", h=H)[:, bk_i*8:(bk_i+1)*8, g*4+s],
                    mtmp[:, :],
                    bank[:, 0:416].rearrange("p (j hi c) -> p j hi c", j=4, hi=2)[:, :, :, 50],
                    ALU.subtract),
                  reads=['mtmp', bkk], writes=[('M', bk_i, g, s)])

    E = mkE('p15')
    Mkeys = [('M', b, g, s) for b in range(2) for g in range(NG) for s in range(4)]

    E('dve', lambda e: e.tensor_scalar(xbar_bf[:, 0:512], ps_xb0[:, :], 1.0 / L,
                                       None, ALU.mult),
      reads=[('ps_xb', 0)], writes=['xbar_bf'])
    E('dve', lambda e: e.tensor_scalar(xbar_bf[:, 512:1024], ps_xb1[:, :], 1.0 / L,
                                       None, ALU.mult),
      reads=[('ps_xb', 1)], writes=['xbar_bf'])
    for c in range(8):
        bank, bkk = next_pp()
        E('pe', lambda e, c=c, bank=bank: e.matmul(bank[:, 0:1], xbar_bf[:, c*128:(c+1)*128],
                                        one1[:, :], start=True, stop=True),
          reads=['xbar_bf', 'one1'], writes=[bkk])
        E('dve', lambda e, c=c, bank=bank: e.tensor_copy(xbarT[:, c:c+1], bank[:, 0:1]),
          reads=[bkk], writes=[('xbarT', c)])
    xbT_keys = [('xbarT', c) for c in range(8)]
    def vmean_mm0(e):
        for c in range(8):
            e.matmul(ps_qk[0][0:1, :], xbarT[:, c:c+1], wvT[:, c, 0:512],
                     start=(c == 0), stop=False)
        return e.matmul(ps_qk[0][0:1, :], one1[:, :], bv_sb[:, 0:512],
                        start=False, stop=True)
    def vmean_mm1(e):
        for c in range(8):
            e.matmul(ps_qk[1][0:1, :], xbarT[:, c:c+1], wvT[:, c, 512:1024],
                     start=(c == 0), stop=False)
        return e.matmul(ps_qk[1][0:1, :], one1[:, :], bv_sb[:, 512:1024],
                        start=False, stop=True)
    E('pe', vmean_mm0, reads=xbT_keys + ['wvT', 'one1', 'bv_sb'], writes=[('pp', 0)])
    E('pe', vmean_mm1, reads=xbT_keys + ['wvT', 'one1', 'bv_sb'], writes=[('pp', 1)])
    E('dve', lambda e: e.tensor_copy(vmean[:, 0:512], ps_qk[0][0:1, :]),
      reads=[('pp', 0)], writes=['vmean'])
    E('dve', lambda e: e.tensor_copy(vmean[:, 512:1024], ps_qk[1][0:1, :]),
      reads=[('pp', 1)], writes=['vmean'])
    E('dve', lambda e: e.tensor_copy(vmean_bf[:, :], vmean[:, :]),
      reads=['vmean'], writes=['vmean_bf'])
    for c in range(8):
        bank, bkk = next_pp()
        E('pe', lambda e, c=c, bank=bank: e.matmul(bank[:, 0:1], vmean_bf[:, c*128:(c+1)*128],
                                        one1[:, :], start=True, stop=True),
          reads=['vmean_bf', 'one1'], writes=[bkk])
        E('dve', lambda e, c=c, bank=bank: e.tensor_copy(vmeanT[:, c:c+1], bank[:, 0:1]),
          reads=[bkk], writes=[('vmeanT', c)])
    E('dve', lambda e: e.tensor_copy(vmeanT_bf[:, :], vmeanT[:, :]),
      reads=[('vmeanT', c) for c in range(8)], writes=['vmeanT_bf'])
    def base_mm0(e):
        for c in range(8):
            e.matmul(ps_qk[0][0:1, :], vmeanT_bf[:, c:c+1], woT_sb[:, c, 0:512],
                     start=(c == 0), stop=False)
        return e.matmul(ps_qk[0][0:1, :], one1[:, :], bo_sb[:, 0:512],
                        start=False, stop=True)
    def base_mm1(e):
        for c in range(8):
            e.matmul(ps_qk[1][0:1, :], vmeanT_bf[:, c:c+1], woT_sb[:, c, 512:1024],
                     start=(c == 0), stop=False)
        return e.matmul(ps_qk[1][0:1, :], one1[:, :], bo_sb[:, 512:1024],
                        start=False, stop=True)
    E('pe', base_mm0, reads=['vmeanT_bf', 'woT_sb', 'one1', 'bo_sb'], writes=[('pp', 0)])
    E('pe', base_mm1, reads=['vmeanT_bf', 'woT_sb', 'one1', 'bo_sb'], writes=[('pp', 1)])
    E('dve', lambda e: e.tensor_copy(base_bf[:, 0:512], ps_qk[0][0:1, :]),
      reads=[('pp', 0)], writes=['base_bf'])
    E('dve', lambda e: e.tensor_copy(base_bf[:, 512:1024], ps_qk[1][0:1, :]),
      reads=[('pp', 1)], writes=['base_bf'])
    if debug:
        E('sp', lambda e: e.dma_start(dbg["dbg_vmean"], vmean[:, :]), reads=['vmean'], dma='dbg')
        basef = sbuf(pha, "basef_s", [1, D], F32)
        E('dve', lambda e: e.tensor_copy(basef[:, :], base_bf[:, :]),
          reads=['base_bf'], writes=['basef'])
        E('sp', lambda e: e.dma_start(dbg["dbg_base"], basef[:, :]), reads=['basef'], dma='dbg')
        E('sp', lambda e: e.dma_start(dbg["dbg_M"], M_tm[:, :]), reads=Mkeys, dma='dbg')

    E = _E
    p.fence()
    pha.close()
    if maxphase < 2:
        if debug:
            p.wait_all_dma('pool', 'dbg')
        p.build(); p.close()
        st_m.close(); st_w.close(); sbstack.close()
        return nc
    ph2 = contextlib.ExitStack()
    hi_t = sbuf(ph2, "hi_t_s", [128, H], F32)
    tt_t = sbuf(ph2, "tt_t_s", [128, H], F32)
    cnt_f = sbuf(ph2, "cnt_f_s", [128, H], F32)
    cnt_bf = sbuf(ph2, "cnt_bf_s", [128, H], BF16)
    cntr = sbuf(ph2, "cntr_s", [128, H], F32)
    geu = sbuf(ph2, "geu_s", [128, H], U8)
    gle = sbuf(ph2, "gle_s", [128, H], U8)
    junkc = sbuf(ph2, "junkc_s", [128, 128], F32)
    mask_u8 = sbuf(ph2, "mask_u8_s", [128, H * 128], U8)
    neg1 = sbuf(ph2, "neg1_s", [128, 1], F32)
    iotaM = sbuf(ph2, "iotaM_s", [128, H * 128], F32)
    miota = M_tm
    cand = sbuf(ph2, "cand_s", [128, H * 16], F32)
    candcat = sbuf(ph2, "candcat_s", [16, 2048], F32)
    cmax8 = sbuf(ph2, "cmax8_s", [16, 8], F32)
    sort56 = sbuf(ph2, "sort56_s", [16, 56], F32)
    sortpad = sbuf(ph2, "sortpad_s", [16, 64], F32)
    selneg = sbuf(ph2, "selneg_s", [16, 64], U8)
    big9 = sbuf(ph2, "big9_s", [16, 64], F32)
    candT = sbuf(ph2, "candT_s", [128, 128], F32)
    ps2_a = psum(ph2, "ps2_a_s", [128, 512], F32)
    ps2_b = psum(ph2, "ps2_b_s", [128, 512], F32)

    E('sp', lambda e: e.dma_start(iotaM[:, :], iotaM_d), writes=['iotaM'], dma='c_iom')
    E('dve', lambda e: e.memset(lo_t[:, :], 0.0), writes=['lo'])
    E('dve', lambda e: e.memset(hi_t[:, :], 256.0), writes=['hi'])
    for it in range(nbisect):
        E('dve', lambda e: e.tensor_tensor(tt_t[:, :], lo_t[:, :], hi_t[:, :], ALU.add),
          reads=['lo', 'hi'], writes=['tt'])
        E('dve', lambda e: e.tensor_scalar(tt_t[:, :], tt_t[:, :], 0.5, None, ALU.mult),
          reads=['tt'], writes=['tt'])
        for h in range(H):
            E('dve', lambda e, h=h: e.tensor_scalar(
                junkc[:, :], M_tm[:, h*128:(h+1)*128], tt_t[:, h:h+1], 0.0,
                ALU.is_gt, ALU.add, accum_out=cnt_f[:, h:h+1]),
              reads=Mkeys + ['tt'], writes=['junkc', ('cnt', h)])
        E('dve', lambda e: e.tensor_copy(cnt_bf[:, :], cnt_f[:, :]),
          reads=[('cnt', h) for h in range(H)], writes=['cntbf'])
        E('pe', lambda e: e.matmul(ps2_a[:, 0:16], ones_sq[:, :], cnt_bf[:, :],
                                   start=True, stop=True),
          reads=['cntbf', 'ones_sq'], writes=[('ps2_a', 0)])
        E('dve', lambda e: e.tensor_copy(cntr[:, :], ps2_a[:, 0:16]),
          reads=[('ps2_a', 0)], writes=['cntr'])
        E('dve', lambda e: e.tensor_scalar(geu[:, :], cntr[:, :], 49.5, None, ALU.is_gt),
          reads=['cntr'], writes=['geu'])
        E('dve', lambda e: e.tensor_scalar(gle[:, :], cntr[:, :], 49.5, None, ALU.is_le),
          reads=['cntr'], writes=['gle'])
        E('dve', lambda e: e.select(lo_t[:, :], geu[:, :], tt_t[:, :], lo_t[:, :],
                                    add_drain=True),
          reads=['geu', 'tt', 'lo'], writes=['lo'])
        E('dve', lambda e: e.select(hi_t[:, :], gle[:, :], tt_t[:, :], hi_t[:, :],
                                    add_drain=True),
          reads=['gle', 'tt', 'hi'], writes=['hi'])
    if debug:
        E('sp', lambda e: e.dma_start(dbg["dbg_lo"], lo_t[:, :]), reads=['lo'], dma='dbg')
    E('dve', lambda e: e.memset(neg1[:, :], -1.0), writes=['neg1'])
    for h in range(H):
        E('dve', lambda e, h=h: e.tensor_scalar(
            mask_u8[:, h*128:(h+1)*128], M_tm[:, h*128:(h+1)*128],
            lo_t[:, h:h+1], None, ALU.is_gt),
          reads=Mkeys + ['lo'], writes=[('msk', h)])
    E('dve', lambda e: e.select(miota[:, :], mask_u8[:, :], iotaM[:, :],
                                neg1[:, 0:1].to_broadcast([128, H * 128]),
                                add_drain=True),
      reads=[('msk', h) for h in range(H)] + ['iotaM', 'neg1'],
      writes=Mkeys + ['miota'])
    for rnd in range(2):
        for h in range(H):
            E('dve', lambda e, rnd=rnd, h=h: e.max(
                out=cand[:, h*16+rnd*8:h*16+rnd*8+8], in_=miota[:, h*128:(h+1)*128]),
              reads=['miota'], writes=[('cand', h, rnd)])
            E('dve', lambda e, rnd=rnd, h=h: e.match_replace(
                out=miota[:, h*128:(h+1)*128],
                in_to_replace=cand[:, h*16+rnd*8:h*16+rnd*8+8],
                in_values=miota[:, h*128:(h+1)*128], imm_value=-1.0),
              reads=['miota', ('cand', h, rnd)], writes=['miota'])
    candkeys = [('cand', h, r) for h in range(H) for r in range(2)]
    for half in range(2):
        E('pe', lambda e, half=half: e.matmul(
            ps2_a[:, 0:128],
            cand[:, half*128:(half+1)*128], idf32[:, :], start=True, stop=True),
          reads=candkeys + ['idf32'], writes=[('ps2_a', 0)])
        E('dve', lambda e: e.tensor_copy(candT[:, :], ps2_a[:, 0:128]),
          reads=[('ps2_a', 0)], writes=['candT'])
        E('pool', lambda e, half=half: e.dma_start(
            candcat[half*8:(half+1)*8, :].rearrange("h (c s) -> h c s", c=16),
            candT[:, :]),
          reads=['candT'], writes=['candcat'], dma='cc')
    for rnd in range(7):
        E('dve', lambda e: e.max(out=cmax8[:, :], in_=candcat[:, :]),
          reads=['candcat'], writes=['cmax8'])
        E('dve', lambda e, rnd=rnd: e.tensor_copy(sort56[:, rnd*8:(rnd+1)*8],
                                                  cmax8[:, :]),
          reads=['cmax8'], writes=[('s56', rnd)])
        E('dve', lambda e: e.match_replace(
            out=candcat[:, :], in_to_replace=cmax8[:, :], in_values=candcat[:, :],
            imm_value=-1.0),
          reads=['candcat', 'cmax8'], writes=['candcat'])
    s56keys = [('s56', r) for r in range(7)]
    E('dve', lambda e: e.memset(sortpad[:, :], 1.0e9), writes=['sortpad'])
    E('dve', lambda e: e.tensor_copy(sortpad[:, 0:50], sort56[:, 0:50]),
      reads=s56keys, writes=['sortpad'])
    E('dve', lambda e: e.tensor_scalar(selneg[:, :], sortpad[:, :], -0.5, None, ALU.is_lt),
      reads=['sortpad'], writes=['selneg'])
    E('dve', lambda e: e.memset(big9[:, :], 1.0e9), writes=['big9'])
    E('dve', lambda e: e.select(sortpad[:, :], selneg[:, :], big9[:, :], sortpad[:, :],
                                add_drain=True),
      reads=['selneg', 'big9', 'sortpad'], writes=['sortpad'])
    E('pe', lambda e: e.matmul(ps2_b[0:64, 0:16], sortpad[:, :], idf32[0:16, 0:16],
                               start=True, stop=True),
      reads=['sortpad', 'idf32'], writes=[('ps2_b', 0)])
    E('dve', lambda e: e.tensor_copy(idxT_f[:, :], ps2_b[0:64, 0:16]),
      reads=[('ps2_b', 0)], writes=['idxT_f'])
    E('dve', lambda e: e.tensor_copy(idxT_i[:, :], idxT_f[:, :]),
      reads=['idxT_f'], writes=['idxT_i'])
    if debug:
        E('sp', lambda e: e.dma_start(dbg["dbg_idx"], idxT_f[:, :]), reads=['idxT_f'], dma='dbg')

    p.fence()
    ph2.close()
    st_m.close()

    if maxphase < 3:
        if debug:
            p.wait_all_dma('pool', 'dbg')
        p.build(); p.close()
        st_w.close(); sbstack.close()
        return nc

    phb = contextlib.ExitStack()
    kpair = [sbuf(phb, f"kpair{i}", [128, L], BF16) for i in range(2)]
    xg_f = sbuf(phb, "xg_f_s", [128, D], F32)
    xg_bf = sbuf(phb, "xg_bf_s", [128, D], BF16)
    xg_T = sbuf(phb, "xg_T_s", [128, 8, 128], BF16)
    qtopT = sbuf(phb, "qtopT_s", [128, 128], BF16)
    P_sb = sbuf(phb, "P_sb_s", [128, L], BF16)
    sumP = sbuf(phb, "sumP_s", [128, 32], F32)
    rsum = sbuf(phb, "rsum_s", [128, 1], F32)
    PT_sb = [sbuf(phb, f"PT_sb{i}", [128, 512], BF16) for i in range(2)]
    vsc = [sbuf(phb, f"vsc{i}", [128, 8, 128], BF16) for i in range(2)]
    dctx = sbuf(phb, "dctx_s", [128, 128], BF16)
    st3 = sbuf(phb, "st3_s", [128, 4], F32)
    ps_s = [psum(phb, f"ps_s{i}", [128, 512], F32) for i in range(2)]
    ps_pt = [psum(phb, f"ps_pt{i}", [128, 512], F32) for i in range(2)]
    ps_cx = psum(phb, "ps_cx_s", [128, 128], F32)
    ps_d = [psum(phb, f"ps_d{i}", [128, 512], F32) for i in range(2)]

    for c in range(8):
        E('dve', lambda e, c=c: e.memset(delta[c].ap(), 0.0), writes=[('dz', c)])
        E('dve', lambda e, c=c: e.memset(idxv[c].ap(), 2.0e9), writes=[('iz', c)])

    for pr in range(8):
        kp = kpair[pr % 2]
        kpk = ('kpair', pr % 2)
        E('sp', lambda e, pr=pr, kp=kp: e.dma_start(kp.ap(), kfm_d[pr, :, :]),
          reads=['kfm_w'], writes=[kpk], dma=f'kr{pr % 2}')
        E('dve', lambda e: e.memset(xg_f.ap(), 0.0), writes=['xg_f'])
        E('pool', lambda e, pr=pr: e.indirect_dma_start(
            xg_f[0:U, :], None, x_d, IndirectOffsetOnAxis(idxT_i[0:U, 2*pr:2*pr+1], 0),
            bounds_check=L - 1, oob_is_err=False),
          reads=['idxT_i', 'xg_f'], writes=[('xg_g', 0)], dma='g1')
        E('pool', lambda e, pr=pr: e.indirect_dma_start(
            xg_f[64:64+U, :], None, x_d, IndirectOffsetOnAxis(idxT_i[0:U, 2*pr+1:2*pr+2], 0),
            bounds_check=L - 1, oob_is_err=False),
          reads=['idxT_i', 'xg_f'], writes=[('xg_g', 1)], dma='g1')
        layernorm(xg_f[:, :], ['xg_f', ('xg_g', 0), ('xg_g', 1)], xg_bf[:, :], 'xg_bf',
                  128, epsA, 'epsA', st3, 'st3', lnexp=True)
        for c in range(8):
            bank = ps_pt[c % 2]
            bkk = ('ps_pt2', c % 2)
            E('pe', lambda e, c=c, bank=bank: e.matmul(
                bank[:, 0:128], xg_bf[:, c*128:(c+1)*128], id_bf[:, :],
                start=True, stop=True),
              reads=['xg_bf', 'id_bf'], writes=[bkk])
            E('dve', lambda e, c=c, bank=bank: e.tensor_copy(
                xg_T[:, c, :], bank[:, 0:128]),
              reads=[bkk], writes=[('xg_T', c)])
        def qtop_mm(e, pr=pr):
            for k in range(8):
                e.matmul(ps_d[0][:, 0:128], wqT[:, k, pr*128:(pr+1)*128], xg_T[:, k, :],
                         start=(k == 0), stop=False)
            return e.matmul(ps_d[0][:, 0:128], bq_sb[:, pr*128:(pr+1)*128],
                            ones_row[:, 0:128], start=False, stop=True)
        E('pe', qtop_mm, reads=[('xg_T', c) for c in range(8)] +
          ['wqT', 'bq_sb', 'ones_row'], writes=['ps_d0'])
        E('dve', lambda e: e.memset(qtopT.ap(), 0.0), writes=['qtop0'])
        E('dve', lambda e: e.tensor_copy(qtopT[0:64, 0:50], ps_d[0][0:64, 0:50]),
          reads=['ps_d0', 'qtop0'], writes=[('qtopT', 0)])
        E('dve', lambda e: e.tensor_copy(qtopT[64:128, 64:114], ps_d[0][64:128, 64:114]),
          reads=['ps_d0', 'qtop0'], writes=[('qtopT', 1)])
        for kc in range(32):
            bank = ps_s[kc % 2]
            bkk = ('ps_s', kc % 2)
            E('pe', lambda e, kc=kc, kp=kp, bank=bank: e.matmul(
                bank[:, :], qtopT[:, :], kp[:, kc*512:(kc+1)*512],
                start=True, stop=True),
              reads=[('qtopT', 0), ('qtopT', 1), 'qtop0', kpk], writes=[bkk])
            E('act', lambda e, kc=kc, bank=bank: e.activation(
                P_sb[:, kc*512:(kc+1)*512], bank[:, :], AF.Exp,
                bias=negb[:, :], scale=SCALE, accum_out=sumP[:, kc:kc+1]),
              reads=[bkk, 'negb'], writes=[('P', kc), ('sumP', kc)])
        E('dve', lambda e: e.reduce_sum(rsum[:, :], sumP[:, :], axis=AX.X),
          reads=[('sumP', kc) for kc in range(32)], writes=['rsum'])
        E('dve', lambda e: e.reciprocal(rsum[:, :], rsum[:, :]),
          reads=['rsum'], writes=['rsum'])
        E('dve', lambda e: e.tensor_scalar(P_sb[:, :], P_sb[:, :], rsum[:, :], None,
                                           ALU.mult),
          reads=[('P', kc) for kc in range(32)] + ['rsum'], writes=['Pn'])
        for scb in range(16):
            vb = vsc[scb % 2]
            vbk = ('vsc', scb % 2)
            E('sp', lambda e, pr=pr, scb=scb, vb=vb: e.dma_start(
                vb.ap(), vscr_d[pr, scb*8:(scb+1)*8, :, :].rearrange("s p d -> p s d")),
              reads=['vscr_w'], writes=[vbk], dma=f'vr{scb % 2}')
            for q4 in range(2):
                sc0 = scb * 8 + q4 * 4
                ptb = PT_sb[q4]
                ptk = ('PT', q4)
                psb = ps_pt[q4]
                psk = ('ps_pt2', q4)
                def ptmm(e, sc0=sc0, psb=psb):
                    last = None
                    for jj in range(4):
                        last = e.matmul(psb[:, jj*128:(jj+1)*128],
                                        P_sb[:, (sc0+jj)*128:(sc0+jj+1)*128],
                                        id_bf[:, :], start=True, stop=True)
                    return last
                E('pe', ptmm, reads=['Pn', 'id_bf'], writes=[psk])
                E('dve', lambda e, ptb=ptb, psb=psb: e.tensor_copy(ptb[:, :], psb[:, :]),
                  reads=[psk], writes=[ptk])
                def cxmm(e, sc0=sc0, q4=q4, vb=vb, ptb=ptb):
                    last = None
                    for jj in range(4):
                        sc = sc0 + jj
                        last = e.matmul(ps_cx[:, :], vb[:, q4*4+jj, :],
                                        ptb[:, jj*128:(jj+1)*128],
                                        start=(sc == 0), stop=(sc == 127))
                    return last
                E('pe', cxmm, reads=[vbk, ptk], writes=['ps_cx'])
        E('dve', lambda e: e.memset(dctx.ap(), 0.0), writes=['dctx0'])
        E('dve', lambda e, pr=pr: e.tensor_scalar(
            dctx[0:64, 0:50], ps_cx[0:64, 0:50], vmeanT[0:64, pr:pr+1], None,
            ALU.subtract),
          reads=['ps_cx', ('vmeanT', pr), 'dctx0'], writes=[('dctx', 0)])
        E('dve', lambda e, pr=pr: e.tensor_scalar(
            dctx[64:128, 64:114], ps_cx[64:128, 64:114], vmeanT[64:128, pr:pr+1], None,
            ALU.subtract),
          reads=['ps_cx', ('vmeanT', pr), 'dctx0'], writes=[('dctx', 1)])
        def dmm(e, pr=pr):
            e.matmul(ps_d[0][:, :], dctx[:, :], woT_sb[:, pr, 0:512],
                     start=True, stop=True)
            return e.matmul(ps_d[1][:, :], dctx[:, :], woT_sb[:, pr, 512:1024],
                            start=True, stop=True)
        E('pe', dmm, reads=[('dctx', 0), ('dctx', 1), 'dctx0', 'woT_sb'],
          writes=['ps_d0', 'ps_d1'])
        E('dve', lambda e, pr=pr: e.tensor_copy(delta[pr][:, 0:512], ps_d[0][:, :]),
          reads=['ps_d0', ('dz', pr)],
          writes=[('delta', pr, 0, 0), ('delta', pr, 64, 0)])
        E('dve', lambda e, pr=pr: e.tensor_copy(delta[pr][:, 512:1024], ps_d[1][:, :]),
          reads=['ps_d1', ('dz', pr)],
          writes=[('delta', pr, 0, 1), ('delta', pr, 64, 1)])
        E('dve', lambda e, pr=pr: e.tensor_copy(idxv[pr][0:50, :],
                                                idxT_f[0:50, 2*pr:2*pr+1]),
          reads=['idxT_f', ('iz', pr)], writes=[('idxv', pr, 0)])
        E('dve', lambda e, pr=pr: e.tensor_copy(idxv[pr][64:114, :],
                                                idxT_f[0:50, 2*pr+1:2*pr+2]),
          reads=['idxT_f', ('iz', pr)], writes=[('idxv', pr, 64)])
    if debug:
        dltf = sbuf(phb, "dltf_s", [128, D], F32)
        E('dve', lambda e: e.tensor_copy(dltf[:, :], delta[0].ap()),
          reads=[('delta', 0, 0, 0), ('delta', 0, 0, 1), ('delta', 0, 64, 0),
                 ('delta', 0, 64, 1), ('dz', 0)],
          writes=['dltf'])
        E('sp', lambda e: e.dma_start(dbg["dbg_delta"], dltf[:, :]), reads=['dltf'], dma='dbg')

    p.fence()
    phb.close()
    st_w.close()

    if maxphase < 5:
        if debug:
            p.wait_all_dma('pool', 'dbg')
        p.build(); p.close()
        sbstack.close()
        return nc

    phc = contextlib.ExitStack()
    wr_sb = sbuf(phc, "wr_sb_s", [128, 8, NE], BF16)
    b1_sb = sbuf(phc, "b1_sb_s", [NE, 32 * 128], BF16)
    gam_b = sbuf(phc, "gam_b_s", [128, D], F32)
    bet_b = sbuf(phc, "bet_b_s", [128, D], F32)
    bof_sb = sbuf(phc, "bof_sb_s", [1, D], BF16)
    gb_tmp = sbuf(phc, "gb_tmp_s", [1, D], BF16)
    xq_in = [sbuf(phc, f"xq_in{i}", [128, D], F32) for i in range(2)]
    Sel8 = sbuf(phc, "Sel8_s", [128, 8, 128], BF16)
    ish8 = sbuf(phc, "ish8_s", [128, 8], F32)
    idxv8 = sbuf(phc, "idxv8_s", [128, 8], F32)
    attn_f = sbuf(phc, "attn_f_s", [128, D], F32)
    y1_t = sbuf(phc, "y1_t_s", [128, D], F32)
    t1_t = sbuf(phc, "t1_t_s", [128, D], F32)
    t1bf = sbuf(phc, "t1bf_s", [128, D], BF16)
    t1T_tmp = sbuf(phc, "t1T_tmp_s", [128, D], BF16)
    sts = [sbuf(phc, f"sts{i}", [128, 4], F32) for i in range(4)]
    oh4 = sbuf(phc, "oh4_s", [128, NE], F32)
    ohbf = sbuf(phc, "ohbf_s", [128, NE], BF16)
    noty = sbuf(phc, "noty_s", [128, 1], F32)
    eq4 = sbuf(phc, "eq4_s", [128, NE], F32)
    m4 = sbuf(phc, "m4_s", [128, 1], F32)
    diag4 = sbuf(phc, "diag4_s", [128, 512], BF16)
    t1eT = sbuf(phc, "t1eT_s", [128, 8, NE, 1024], BF16)
    onehT = sbuf(phc, "onehT_s", [NE, 1024], BF16)
    inter_st = [sbuf(phc, f"inter_st{i}_x", [128, 512], BF16) for i in range(2)]
    intc = [sbuf(phc, f"intc{i}_x", [128, 512], BF16) for i in range(2)]
    w1sl = [sbuf(phc, f"w1sl{i}", [128, 8, NE, 128], BF16) for i in range(2)]
    wofc = [sbuf(phc, f"wofc{i}", [128, D], BF16) for i in range(2)]
    t2_t = sbuf(phc, "t2_t_s", [128, D], F32)
    t3_t = sbuf(phc, "t3_t_s", [128, D], F32)
    ps_ab = [psum(phc, f"ps_ab{i}", [128, 1024], F32) for i in range(2)]
    ps_z = [psum(phc, f"ps_z{i}", [128, 512], F32) for i in range(2)]
    ps_tT = [psum(phc, f"ps_tT{i}", [128, 512], F32) for i in range(2)]

    E = mkE('gb')
    E('pool', lambda e: e.dma_start(wr_sb.ap(), wr_d.rearrange("(kc p) f -> p kc f", p=128)),
      writes=['wr_sb'], dma='c_wr')
    E('pool', lambda e: e.dma_start(b1_sb.ap(), b1r_d), writes=['b1_sb'], dma='c_b1')
    E('pool', lambda e: e.dma_start(bof_sb[:, :], bof_d), writes=['bof_sb'], dma='c_bof')
    E('pool', lambda e: e.dma_start(gb_tmp[:, :], gam_d), writes=['gb_tmp'], dma='c_gb')
    for hf in range(2):
        E('pe', lambda e, hf=hf: e.matmul(ps_z[0][:, :], ones_row[0:1, 0:128],
                                          gb_tmp[:, hf*512:(hf+1)*512],
                                          start=True, stop=True),
          reads=['gb_tmp', 'ones_row'], writes=[('ps_z', 0)])
        E('dve', lambda e, hf=hf: e.tensor_copy(gam_b[:, hf*512:(hf+1)*512],
                                                ps_z[0][:, :]),
          reads=[('ps_z', 0)], writes=['gam_b'])
    E('pool', lambda e: e.dma_start(gb_tmp[:, :], bet_d), writes=['gb_tmp'], dma='c_gb')
    for hf in range(2):
        E('pe', lambda e, hf=hf: e.matmul(ps_z[0][:, :], ones_row[0:1, 0:128],
                                          gb_tmp[:, hf*512:(hf+1)*512],
                                          start=True, stop=True),
          reads=['gb_tmp', 'ones_row'], writes=[('ps_z', 0)])
        E('dve', lambda e, hf=hf: e.tensor_copy(bet_b[:, hf*512:(hf+1)*512],
                                                ps_z[0][:, :]),
          reads=[('ps_z', 0)], writes=['bet_b'])

    for ch in range(8):
        E('dve', lambda e, ch=ch: e.tensor_copy(idxv8[:, ch:ch+1], idxv[ch][:, :]),
          reads=[('idxv', ch, 0), ('idxv', ch, 64), ('iz', ch)], writes=['idxv8'])
    dkeys = [('delta', ch, bs, hf) for ch in range(8) for bs in (0, 64) for hf in range(2)] + \
            [('dz', c) for c in range(8)]
    ikeys = [('idxv', ch, bs) for ch in range(8) for bs in (0, 64)] + \
            [('iz', c) for c in range(8)]

    for sg in range(4):
        sgt0 = sg * 8
        E = mkE('as')
        for tl in range(8):
            t = sgt0 + tl
            xi = xq_in[tl % 2]
            xik = ('xq_in', tl % 2)
            E('sp', lambda e, t=t, xi=xi: e.dma_start(xi.ap(), xq_d[t*128:(t+1)*128, :]),
              writes=[xik], dma=f'xq{tl % 2}')
            pab = ps_ab[tl % 2]
            pa, pb = pab[:, 0:512], pab[:, 512:1024]
            pak, pbk = ('ps_ab', tl % 2, 0), ('ps_ab', tl % 2, 1)
            E('dve', lambda e, t=t: e.tensor_scalar(
                ish8[:, :], idxv8[:, :], qoffc[:, :], float(t * 128),
                ALU.subtract, ALU.subtract),
              reads=['idxv8', 'qoffc'], writes=['ish8'])
            E('dve', lambda e: e.tensor_tensor(
                Sel8.ap(),
                ish8.ap().unsqueeze(2).to_broadcast([128, 8, 128]),
                iotacol.ap().unsqueeze(1).to_broadcast([128, 8, 128]),
                ALU.is_equal),
              reads=['ish8', 'iotacol'], writes=['Sel8'])
            for ch in range(8):
                E('pe', lambda e, ch=ch, pa=pa: e.matmul(
                    pa, Sel8[:, ch, :], delta[ch][:, 0:512],
                    start=(ch == 0), stop=False),
                  reads=['Sel8'] + dkeys, writes=[pak])
                E('pe', lambda e, ch=ch, pb=pb: e.matmul(
                    pb, Sel8[:, ch, :], delta[ch][:, 512:1024],
                    start=(ch == 0), stop=False),
                  reads=['Sel8'] + dkeys, writes=[pbk])
            E('pe', lambda e, pa=pa: e.matmul(pa, ones_row[0:1, 0:128],
                                              base_bf[:, 0:512], start=False, stop=True),
              reads=['ones_row', 'base_bf'], writes=[pak])
            E('pe', lambda e, pb=pb: e.matmul(pb, ones_row[0:1, 0:128],
                                              base_bf[:, 512:1024], start=False, stop=True),
              reads=['ones_row', 'base_bf'], writes=[pbk])
            E('dve', lambda e, pa=pa: e.tensor_copy(attn_f[:, 0:512], pa),
              reads=[pak], writes=['attn_f'])
            E('dve', lambda e, pb=pb: e.tensor_copy(attn_f[:, 512:1024], pb),
              reads=[pbk], writes=['attn_f'])
            layernorm(attn_f[:, :], ['attn_f'], y1_t[:, :], 'y1', 128, epsA, 'epsA',
                      sts[0], 'sA')
            E('dve', lambda e, xi=xi: e.tensor_tensor(y1_t[:, :], y1_t[:, :], xi.ap(),
                                                      ALU.add),
              reads=['y1', xik], writes=['y1'])
            E('sp', lambda e, t=t: e.dma_start(y1s_d[t*128:(t+1)*128, :], y1_t[:, :]),
              reads=['y1'], writes=[('y1s', t)], dma='y1w')
            layernorm(y1_t[:, :], ['y1'], t1_t[:, :], 't1', 128, epsA, 'epsA',
                      sts[1], 'sB')
            E('sp', lambda e, t=t: e.dma_start(t1s_d[t*128:(t+1)*128, :], t1_t[:, :]),
              reads=['t1'], writes=[('t1s', t)], dma='t1w')
            E('dve', lambda e: e.tensor_copy(t1bf[:, :], t1_t[:, :]),
              reads=['t1'], writes=['t1bf'])
            if debug and t == 0:
                E('sp', lambda e: e.dma_start(dbg["dbg_y1"], y1_t[:, :]), reads=['y1'], dma='dbg')
                E('sp', lambda e: e.dma_start(dbg["dbg_t1"], t1_t[:, :]), reads=['t1'], dma='dbg')
            E = mkE('as2')
            for cb in range(2):
                bank = ps_tT[cb]
                bkk = ('ps_tT', cb)
                for j in range(4):
                    c = cb * 4 + j
                    E('pe', lambda e, c=c, j=j, bank=bank: e.matmul(
                        bank[:, j*128:(j+1)*128], t1bf[:, c*128:(c+1)*128], id_bf[:, :],
                        start=True, stop=True),
                      reads=['t1bf', 'id_bf'], writes=[bkk])
                E('dve', lambda e, cb=cb, bank=bank: e.tensor_copy(
                    t1T_tmp[:, cb*512:(cb+1)*512], bank[:, :]),
                  reads=[bkk], writes=[('t1T', cb)])
            def lg_mm(e):
                last = None
                for c in range(8):
                    last = e.matmul(ps_tT[0][:, 0:NE], t1T_tmp[:, c*128:(c+1)*128],
                                    wr_sb[:, c, :], start=(c == 0), stop=(c == 7))
                return last
            E('pe', lg_mm, reads=[('t1T', 0), ('t1T', 1), 'wr_sb'], writes=[('ps_tT', 0)])
            E('dve', lambda e: e.reduce_max(m4[:, :], ps_tT[0][:, 0:NE], axis=AX.X),
              reads=[('ps_tT', 0)], writes=['m4'])
            E('dve', lambda e: e.tensor_scalar(eq4[:, :], ps_tT[0][:, 0:NE], m4[:, :],
                                               None, ALU.is_ge),
              reads=[('ps_tT', 0), 'm4'], writes=['eq4'])
            E('dve', lambda e: e.tensor_copy(oh4[:, 0:1], eq4[:, 0:1]),
              reads=['eq4'], writes=['oh4'])
            E('dve', lambda e: e.tensor_scalar(noty[:, :], eq4[:, 0:1], -1.0, 1.0,
                                               ALU.mult, ALU.add),
              reads=['eq4'], writes=['noty'])
            for ee in range(1, NE):
                E('dve', lambda e, ee=ee: e.tensor_tensor(
                    oh4[:, ee:ee+1], eq4[:, ee:ee+1], noty[:, :], ALU.mult),
                  reads=['eq4', 'noty'], writes=['oh4'])
                if ee < NE - 1:
                    E('dve', lambda e, ee=ee: e.tensor_scalar(
                        junk1[:, :], oh4[:, ee:ee+1], -1.0, 1.0, ALU.mult, ALU.add),
                      reads=['oh4'], writes=['junk1'])
                    E('dve', lambda e: e.tensor_tensor(noty[:, :], noty[:, :],
                                                       junk1[:, :], ALU.mult),
                      reads=['noty', 'junk1'], writes=['noty'])
            E('dve', lambda e: e.tensor_copy(ohbf[:, :], oh4[:, :]),
              reads=['oh4'], writes=['ohbf'])
            E = mkE('as3')
            E('pe', lambda e: e.matmul(ps_tT[1][0:NE, 0:128], ohbf[:, :], id_bf[:, :],
                                       start=True, stop=True),
              reads=['ohbf', 'id_bf'], writes=[('ps_tT', 1)])
            E('dve', lambda e, tl=tl: e.tensor_copy(onehT[:, tl*128:(tl+1)*128],
                                                    ps_tT[1][0:NE, 0:128]),
              reads=[('ps_tT', 1)], writes=[('onehT', tl)])
            for ee in range(NE):
                E('dve', lambda e, ee=ee: e.tensor_scalar(
                    diag4[:, ee*128:(ee+1)*128], id_bf[:, :], oh4[:, ee:ee+1], None,
                    ALU.mult),
                  reads=['id_bf', 'oh4'], writes=[('diag4', ee)])
            for c in range(8):
                bank = ps_z[c % 2]
                bkk = ('ps_z', c % 2)
                E('pe', lambda e, c=c, bank=bank: e.matmul(
                    bank[:, :], t1bf[:, c*128:(c+1)*128], diag4[:, :],
                    start=True, stop=True),
                  reads=['t1bf'] + [('diag4', ee) for ee in range(NE)], writes=[bkk])
                E('dve', lambda e, c=c, tl=tl, bank=bank: e.tensor_copy(
                    t1eT[:, c, :, tl*128:(tl+1)*128],
                    bank.ap().rearrange("p (e f) -> p e f", e=NE)),
                  reads=[bkk], writes=[('t1eT', c, tl)])
        E = mkE('as')
        E = mkE('hp')
        t1e_keys = [('t1eT', c, tl) for c in range(8) for tl in range(8)]
        oh_keys = [('onehT', tl) for tl in range(8)]
        for ifc in range(32):
            wsl = w1sl[ifc % 2]
            wk = ('w1sl', ifc % 2)
            E('pool', lambda e, ifc=ifc, wsl=wsl: e.dma_start(
                wsl.ap(), w1r_d[ifc].rearrange("kc e p f -> p kc e f")),
              writes=[wk], dma=f'w1_{ifc % 2}')
            for hg in range(2):
                bank = ps_z[hg]
                bkk = ('ps_z', hg)
                def hmm(e, ifc=ifc, hg=hg, wsl=wsl, bank=bank):
                    first = True
                    for ee in range(NE):
                        for k in range(8):
                            e.matmul(bank[:, :], wsl[:, k, ee, :],
                                     t1eT[:, k, ee, hg*512:(hg+1)*512],
                                     start=first, stop=False)
                            first = False
                    return e.matmul(bank[:, :], b1_sb[:, ifc*128:(ifc+1)*128],
                                    onehT[:, hg*512:(hg+1)*512], start=False, stop=True)
                E('pe', hmm, reads=[wk, 'b1_sb'] + t1e_keys + oh_keys, writes=[bkk])
                ist = inter_st[(ifc * 2 + hg) % 2]
                istk = ('ist', (ifc * 2 + hg) % 2)
                E('act', lambda e, bank=bank, ist=ist: e.activation(
                    ist[:, :], bank[:, :], AF.Gelu),
                  reads=[bkk], writes=[istk])
                E('sp', lambda e, ifc=ifc, hg=hg, ist=ist: e.dma_start(
                    intT_d[ifc, :, hg*512:(hg+1)*512], ist[:, :]),
                  reads=[istk], writes=[('intd', ifc, hg)], dma='intw')
        E = mkE('wo')
        wo_banks = [ps_ab[0][:, 0:512], ps_ab[0][:, 512:1024],
                    ps_ab[1][:, 0:512], ps_ab[1][:, 512:1024],
                    ps_z[0][:, :], ps_z[1][:, :], ps_tT[0][:, :], ps_tT[1][:, :]]
        wo_keys = [('ps_ab', 0, 0), ('ps_ab', 0, 1), ('ps_ab', 1, 0), ('ps_ab', 1, 1),
                   ('ps_z', 0), ('ps_z', 1), ('ps_tT', 0), ('ps_tT', 1)]
        for hb in range(2):
            for ifc in range(32):
                wc = wofc[ifc % 2]
                wck = ('wofc', ifc % 2)
                E('pool', lambda e, ifc=ifc, wc=wc: e.dma_start(
                    wc.ap(), wof_d[ifc*128:(ifc+1)*128, :]),
                  writes=[wck], dma=f'wo{ifc % 2}')
                ic = intc[ifc % 2]
                ick = ('intc', ifc % 2)
                E('sp', lambda e, ifc=ifc, hb=hb, ic=ic: e.dma_start(
                    ic.ap(), intT_d[ifc, :, hb*512:(hb+1)*512]),
                  reads=[('intd', ifc, hb)], writes=[ick], dma=f'ir{ifc % 2}')
                for t4 in range(4):
                    tl = hb * 4 + t4
                    t = sgt0 + tl
                    pa, pb = wo_banks[2*t4], wo_banks[2*t4+1]
                    pak, pbk = wo_keys[2*t4], wo_keys[2*t4+1]
                    E('pe', lambda e, ifc=ifc, t4=t4, wc=wc, ic=ic, pa=pa: e.matmul(
                        pa, ic[:, t4*128:(t4+1)*128], wc[:, 0:512],
                        start=(ifc == 0), stop=False),
                      reads=[wck, ick], writes=[pak])
                    E('pe', lambda e, ifc=ifc, t4=t4, wc=wc, ic=ic, pb=pb: e.matmul(
                        pb, ic[:, t4*128:(t4+1)*128], wc[:, 512:1024],
                        start=(ifc == 0), stop=False),
                      reads=[wck, ick], writes=[pbk])
            for t4 in range(4):
                pa, pb = wo_banks[2*t4], wo_banks[2*t4+1]
                pak, pbk = wo_keys[2*t4], wo_keys[2*t4+1]
                E('pe', lambda e, pa=pa: e.matmul(pa, ones_row[0:1, 0:128],
                                                  bof_sb[:, 0:512], start=False, stop=True),
                  reads=['ones_row', 'bof_sb'], writes=[pak])
                E('pe', lambda e, pb=pb: e.matmul(pb, ones_row[0:1, 0:128],
                                                  bof_sb[:, 512:1024], start=False, stop=True),
                  reads=['ones_row', 'bof_sb'], writes=[pbk])
            for t4 in range(4):
                tl = hb * 4 + t4
                t = sgt0 + tl
                pa, pb = wo_banks[2*t4], wo_banks[2*t4+1]
                pak, pbk = wo_keys[2*t4], wo_keys[2*t4+1]
                xi = xq_in[t4 % 2]
                xik = ('xq_in', t4 % 2)
                E('sp', lambda e, t=t, xi=xi: e.dma_start(xi.ap(), t1s_d[t*128:(t+1)*128, :]),
                  reads=[('t1s', t)], writes=[xik], dma=f'xq{t4 % 2}')
                E('dve', lambda e, pa=pa, xi=xi: e.tensor_tensor(
                    t2_t[:, 0:512], pa, xi.ap()[:, 0:512], ALU.add),
                  reads=[pak, xik], writes=['T2'])
                E('dve', lambda e, pb=pb, xi=xi: e.tensor_tensor(
                    t2_t[:, 512:1024], pb, xi.ap()[:, 512:1024], ALU.add),
                  reads=[pbk, xik], writes=['T2'])
                layernorm(t2_t[:, :], ['T2'], t3_t[:, :], 'T3', 128, epsB, 'epsB',
                          sts[2], 'sC')
                E('dve', lambda e: e.tensor_tensor(t3_t[:, :], t3_t[:, :], gam_b[:, :],
                                                   ALU.mult),
                  reads=['T3', 'gam_b'], writes=['T3'])
                E('dve', lambda e: e.tensor_tensor(t3_t[:, :], t3_t[:, :], bet_b[:, :],
                                                   ALU.add),
                  reads=['T3', 'bet_b'], writes=['T3'])
                layernorm(t3_t[:, :], ['T3'], t2_t[:, :], 'T2', 128, epsA, 'epsA',
                          sts[3], 'sD')
                E('sp', lambda e, t=t, xi=xi: e.dma_start(xi.ap(), y1s_d[t*128:(t+1)*128, :]),
                  reads=[('y1s', t)], writes=[xik], dma=f'xq{t4 % 2}')
                E('dve', lambda e, xi=xi: e.tensor_tensor(t3_t[:, :], t2_t[:, :],
                                                          xi.ap(), ALU.add),
                  reads=['T2', xik], writes=['T3'])
                E('sp', lambda e, t=t: e.dma_start(out_d[t*128:(t+1)*128, :], t3_t[:, :]),
                  reads=['T3'], dma='out')

    E = _E
    if debug:
        p.wait_all_dma('pool', 'dbg')
    p.wait_all_dma('pool', 'out')
    p.build()
    p.close()
    phc.close()
    sbstack.close()
    return nc


_CACHE = {}


def _host_inputs(inputs):
    x = np.asarray(inputs['L0_attn_x'], np.float32)
    wqT = np.ascontiguousarray(np.asarray(inputs['L0_attn_Wq'], np.float32).T)
    wkT = np.ascontiguousarray(np.asarray(inputs['L0_attn_Wk'], np.float32).T)
    wvT = np.ascontiguousarray(np.asarray(inputs['L0_attn_Wv'], np.float32).T)
    woT = np.ascontiguousarray(np.asarray(inputs['L0_attn_Wo'], np.float32).T)
    bq = np.ascontiguousarray(np.asarray(inputs['L0_attn_bq'], np.float32)[None, :])
    bk = np.ascontiguousarray(np.asarray(inputs['L0_attn_bk'], np.float32)[None, :])
    bv = np.ascontiguousarray(np.asarray(inputs['L0_attn_bv'], np.float32)[None, :])
    bo = np.ascontiguousarray(np.asarray(inputs['L0_attn_bo'], np.float32)[None, :])
    idxk = np.ascontiguousarray(np.asarray(inputs['L0_attn_idx_k'], np.int32))
    Wr = np.ascontiguousarray(np.asarray(inputs['L0_ffn_W_router'], np.float32))
    W1 = np.asarray(inputs['L0_ffn_W1'], np.float32)
    w1r = np.ascontiguousarray(W1.reshape(NE, 8, 128, 32, 128).transpose(3, 1, 0, 2, 4))
    b1r = np.ascontiguousarray(np.asarray(inputs['L0_ffn_b1'], np.float32)
                               .reshape(NE, 32 * 128))
    Wof = np.ascontiguousarray(np.asarray(inputs['L0_ffn_Wo'], np.float32))
    bof = np.ascontiguousarray(np.asarray(inputs['L0_ffn_bo'], np.float32)[None, :])
    gam = np.ascontiguousarray(np.asarray(inputs['L0_ffn_ln_gamma'], np.float32)[None, :])
    bet = np.ascontiguousarray(np.asarray(inputs['L0_ffn_ln_beta'], np.float32)[None, :])
    ident = np.eye(128, dtype=np.float32)
    iotacol = np.tile(np.arange(128, dtype=np.float32)[None, :], (128, 1))
    iotaM = np.empty((128, H * 128), np.float32)
    for h in range(H):
        iotaM[:, h*128:(h+1)*128] = (np.arange(128)[None, :] * 128
                                     + np.arange(128)[:, None])
    xbs = [np.ascontiguousarray(x[0]), np.ascontiguousarray(x[1])]
    in_maps = []
    for c in range(8):
        b, q = c // 4, c % 4
        in_maps.append({
            "x": xbs[b],
            "xq": np.ascontiguousarray(xbs[b][q*QTOK:(q+1)*QTOK]),
            "wqT": wqT, "wkT": wkT, "wvT": wvT, "woT": woT,
            "bq": bq, "bk": bk, "bv": bv, "bo": bo,
            "idxk": idxk, "wr": Wr, "w1r": w1r, "b1r": b1r,
            "wof": Wof, "bof": bof, "gam": gam, "bet": bet,
            "ident": ident, "iotacol": iotacol, "iotaM": iotaM,
            "qoffc": np.full((128, 1), q * QTOK, np.float32),
        })
    return in_maps


def kernel(**inputs):
    if 'nc' not in _CACHE:
        _CACHE['nc'] = build_program(debug=False)
    nc = _CACHE['nc']
    in_maps = _host_inputs(inputs)
    res = run_bass_kernel_spmd(nc, in_maps, core_ids=list(range(8)))
    out = np.empty((2, L, D), np.float32)
    for c in range(8):
        b, q = c // 4, c % 4
        out[b, q*QTOK:(q+1)*QTOK] = res.results[c]["y"]
    return out


# version 11
# speedup vs baseline: 1.1552x; 1.0159x over previous
import contextlib
import math

import numpy as np

import concourse.bass as bass
import concourse.mybir as mybir
from concourse.bass import IndirectOffsetOnAxis
from concourse.bass_utils import run_bass_kernel_spmd

F32 = mybir.dt.float32
BF16 = mybir.dt.bfloat16
I32 = mybir.dt.int32
U8 = mybir.dt.uint8
AF = mybir.ActivationFunctionType
ALU = mybir.AluOpType
AX = mybir.AxisListType

D = 1024
L = 16384
H = 16
DK = 64
U = 50
NG = 32
QTOK = 4096
NE = 4
DI = 4096
BISECT = 20
SCALE = 1.0 / math.sqrt(DK)


class Prog:
    ENGINES = ('pe', 'dve', 'act', 'pool', 'sp')

    def __init__(self, nc):
        self.nc = nc
        self.stack = contextlib.ExitStack()
        self.csem = {}
        self.ccount = {}
        for e in ('pe', 'dve', 'act', 'pool'):
            self.csem[e] = self.stack.enter_context(nc.semaphore(f"c_{e}"))
            self.ccount[e] = 0
        self.dsem = {}
        self.dcount = {}
        self.dqueue = {}
        self.ops = {e: [] for e in self.ENGINES}
        self.last_w = {}
        self.readers = {}
        self.know = {e: {} for e in self.ENGINES}
        self.read_misses = set()

    def chan(self, name, engine):
        if name not in self.dsem:
            self.dsem[name] = self.stack.enter_context(self.nc.semaphore(f"d_{name}"))
            self.dcount[name] = 0
            self.dqueue[name] = engine
        assert self.dqueue[name] == engine, (name, engine, self.dqueue[name])
        return name

    def _resolve(self, dep):
        if dep[0] == 'c':
            _, e, v = dep
            return (f"c_{e}", self.csem[e], v)
        _, ch = dep[:2]
        return (f"d_{ch}", self.dsem[ch], 16 * self.dcount[ch])

    def emit(self, engine, fn, reads=(), writes=(), dma=None, tag=None):
        deps = []
        for k in reads:
            w = self.last_w.get(k)
            if w is None:
                self.read_misses.add(k)
            else:
                deps.append(w)
        for k in writes:
            w = self.last_w.get(k)
            if w is not None:
                deps.append(w)
            deps.extend(self.readers.get(k, ()))
        need = {}
        for dep in deps:
            if dep[0] == 'c' and dep[1] == 'pe' and engine == 'pe' and dma is None:
                continue
            name, handle, v = self._resolve(dep)
            if self.know[engine].get(name, -1) >= v:
                continue
            if name in need and need[name][1] >= v:
                continue
            need[name] = (handle, v)
        waits = []
        for name, (handle, v) in need.items():
            waits.append((handle, v))
            self.know[engine][name] = v

        if dma is not None:
            ch = self.chan(dma, engine)
            self.dcount[ch] += 1
            inc = (self.dsem[ch], 16)
            marker = ('d', ch, self.dcount[ch])
        elif engine == 'sp':
            inc = None
            marker = None
        else:
            self.ccount[engine] += 1
            inc = (self.csem[engine], 1)
            marker = ('c', engine, self.ccount[engine])

        self.ops[engine].append((waits, fn, inc, tag))
        if marker is not None:
            for k in writes:
                self.last_w[k] = marker
                self.readers[k] = []
            for k in reads:
                self.readers.setdefault(k, []).append(marker)
        return marker

    def wait_all_dma(self, engine, channel):
        if channel not in self.dsem:
            return
        self.ops[engine].append(
            ([(self.dsem[channel], 16 * self.dcount[channel])], None, None, 'wd'))
        self.know[engine][f"d_{channel}"] = 16 * self.dcount[channel]

    def fence(self):
        targets = []
        for e in ('pe', 'dve', 'act', 'pool'):
            if self.ccount[e]:
                targets.append((f"c_{e}", self.csem[e], self.ccount[e]))
        for ch in self.dsem:
            if self.dcount[ch]:
                targets.append((f"d_{ch}", self.dsem[ch], 16 * self.dcount[ch]))
        for e in self.ENGINES:
            ws = []
            for name, handle, v in targets:
                if name == f"c_{e}":
                    continue
                if self.know[e].get(name, -1) >= v:
                    continue
                ws.append((handle, v))
                self.know[e][name] = v
            if ws:
                self.ops[e].append((ws, None, None, 'fence'))

    def build(self):
        nc = self.nc
        if self.read_misses:
            import sys
            print("PROG read-before-write keys:", sorted(map(str, self.read_misses)),
                  file=sys.stderr)
        with nc.Block() as blk:
            def replay(ename):
                def _go(eng):
                    for waits, fn, inc, tag in self.ops[ename]:
                        for handle, v in waits:
                            eng.wait_ge(handle, v)
                        if fn is not None:
                            inst = fn(eng)
                            if inc is not None:
                                inst.then_inc(inc[0], inc[1])
                return _go
            if self.ops['pe']: blk.tensor(replay('pe'))
            if self.ops['dve']: blk.vector(replay('dve'))
            if self.ops['act']: blk.scalar(replay('act'))
            if self.ops['pool']: blk.gpsimd(replay('pool'))
            if self.ops['sp']: blk.sync(replay('sp'))

    def close(self):
        self.stack.close()


def build_program(debug=False, maxphase=9, ngroups=NG, nbisect=BISECT, parts='all'):
    nc = bass.Bass("TRN2", target_bir_lowering=False)

    def din(name, shape, dt=F32):
        return nc.dram_tensor(name, shape, dt, kind="ExternalInput").ap()
    x_d = din("x", [L, D])
    xq_d = din("xq", [QTOK, D])
    wqT_d = din("wqT", [D, D]); wkT_d = din("wkT", [D, D]); wvT_d = din("wvT", [D, D])
    woT_d = din("woT", [D, D])
    bq_d = din("bq", [1, D]); bk_d = din("bk", [1, D]); bv_d = din("bv", [1, D])
    bo_d = din("bo", [1, D])
    idxk_d = din("idxk", [U], I32)
    wr_d = din("wr", [D, NE])
    w1r_d = din("w1r", [32, 8, NE, 128, 128])
    b1r_d = din("b1r", [NE, 32 * 128])
    wof_d = din("wof", [DI, D])
    bof_d = din("bof", [1, D]); gam_d = din("gam", [1, D]); bet_d = din("bet", [1, D])
    ident_d = din("ident", [128, 128])
    iotacol_d = din("iotacol", [128, 128])
    iotaM_d = din("iotaM", [128, H * 128])
    qoff_d = din("qoffc", [128, 1])
    out_d = nc.dram_tensor("y", [QTOK, D], F32, kind="ExternalOutput").ap()
    dbg = {}
    if debug:
        def dout(name, shape, dt=F32):
            dbg[name] = nc.dram_tensor(name, shape, dt, kind="ExternalOutput").ap()
        dout("dbg_M", [128, H * 128]); dout("dbg_idx", [64, H])
        dout("dbg_vmean", [1, D]); dout("dbg_base", [1, D])
        dout("dbg_delta", [128, D]); dout("dbg_y1", [128, D])
        dout("dbg_t1", [128, D]); dout("dbg_lo", [128, H])

    kfm_d = nc.dram_tensor("kfm", [8, 128, L], BF16, kind="Internal").ap()
    vscr_d = nc.dram_tensor("vscr", [8, 128, 128, 128], BF16, kind="Internal").ap()
    t1s_d = nc.dram_tensor("t1s", [QTOK, D], F32, kind="Internal").ap()
    intT_d = nc.dram_tensor("intT", [32, 128, 1024], BF16, kind="Internal").ap()
    y1s_d = nc.dram_tensor("y1s", [QTOK, D], F32, kind="Internal").ap()

    p = Prog(nc)
    _E = p.emit
    def mkE(label):
        def E(*a, **k):
            if parts == 'all' or label in parts:
                return _E(*a, **k)
        return E
    E = _E
    sbstack = contextlib.ExitStack()

    def sbuf(stk, n, s, d=F32):
        return stk.enter_context(nc.sbuf_tensor(n, s, d))

    def psum(stk, n, s, d=F32):
        return stk.enter_context(nc.psum_tensor(n, s, d))

    perm = sbstack
    id_bf = sbuf(perm, "id_bf_s", [128, 128], BF16)
    idf32 = sbuf(perm, "idf32_s", [128, 128], F32)
    iotacol = sbuf(perm, "iotacol_s", [128, 128], F32)
    qoffc = sbuf(perm, "qoffc_s", [128, 1], F32)
    ones_row = sbuf(perm, "ones_row_s", [1, 512], BF16)
    ones_col = sbuf(perm, "ones_col_s", [128, 1], BF16)
    ones_sq = sbuf(perm, "ones_sq_s", [128, 128], BF16)
    one1 = sbuf(perm, "one1_s", [1, 1], BF16)
    epsA = sbuf(perm, "epsA_s", [128, 1], F32)
    epsB = sbuf(perm, "epsB_s", [128, 1], F32)
    negb = sbuf(perm, "negb_s", [128, 1], F32)
    ksampT = sbuf(perm, "ksampT_s", [128, 8, 104], BF16)
    xbar_bf = sbuf(perm, "xbar_bf_s", [1, D], BF16)
    xbarT = sbuf(perm, "xbarT_s", [128, 8], BF16)
    vmean = sbuf(perm, "vmean_s", [1, D], F32)
    vmean_bf = sbuf(perm, "vmean_bf_s", [1, D], BF16)
    vmeanT = sbuf(perm, "vmeanT_s", [128, 8], F32)
    vmeanT_bf = sbuf(perm, "vmeanT_bf_s", [128, 8], BF16)
    base_bf = sbuf(perm, "base_bf_s", [1, D], BF16)
    idxT_f = sbuf(perm, "idxT_f_s", [64, H], F32)
    idxT_i = sbuf(perm, "idxT_i_s", [64, H], I32)
    delta = [sbuf(perm, f"delta{c}", [128, D], BF16) for c in range(8)]
    idxv = [sbuf(perm, f"idxv{c}", [128, 1], F32) for c in range(8)]
    junk1 = sbuf(perm, "junk1_s", [128, 1], F32)

    E('pool', lambda e: e.dma_start(id_bf[:, :], ident_d), writes=['id_bf'], dma='c_idbf')
    E('sp', lambda e: e.dma_start(idf32[:, :], ident_d), writes=['idf32'], dma='c_idf')
    E('sp', lambda e: e.dma_start(iotacol[:, :], iotacol_d), writes=['iotacol'], dma='c_ioc')
    E('sp', lambda e: e.dma_start(qoffc[:, :], qoff_d), writes=['qoffc'], dma='c_qof')
    E('dve', lambda e: e.memset(ones_row[:, :], 1.0), writes=['ones_row'])
    E('dve', lambda e: e.memset(ones_col[:, :], 1.0), writes=['ones_col'])
    E('dve', lambda e: e.memset(ones_sq[:, :], 1.0), writes=['ones_sq'])
    E('dve', lambda e: e.memset(one1[:, :], 1.0), writes=['one1'])
    E('dve', lambda e: e.memset(epsA[:, :], 1e-5), writes=['epsA'])
    E('dve', lambda e: e.memset(epsB[:, :], 1e-12), writes=['epsB'])
    E('dve', lambda e: e.memset(negb[:, :], -12.0), writes=['negb'])

    def layernorm(src_ap, src_keys, dst_ap, dst_key, rows, eps_tile, eps_key,
                  stats, skey, lnexp=False):
        rd = list(src_keys)
        E('dve', lambda e: e.reduce_sum(stats[0:rows, 0:1], src_ap, axis=AX.X),
          reads=rd, writes=[(skey, 0)])
        E('act', lambda e: e.activation(dst_ap, src_ap, AF.Square,
                                        accum_out=stats[0:rows, 1:2]),
          reads=rd, writes=[dst_key, (skey, 1)])
        E('dve', lambda e: e.tensor_scalar(stats[0:rows, 0:1], stats[0:rows, 0:1],
                                           1.0 / D, None, ALU.mult),
          reads=[(skey, 0)], writes=[(skey, 0)])
        E('dve', lambda e: e.tensor_scalar(stats[0:rows, 1:2], stats[0:rows, 1:2],
                                           1.0 / D, None, ALU.mult),
          reads=[(skey, 1)], writes=[(skey, 1)])
        E('dve', lambda e: e.tensor_tensor(stats[0:rows, 2:3], stats[0:rows, 0:1],
                                           stats[0:rows, 0:1], ALU.mult),
          reads=[(skey, 0)], writes=[(skey, 2)])
        E('dve', lambda e: e.tensor_tensor(stats[0:rows, 2:3], stats[0:rows, 1:2],
                                           stats[0:rows, 2:3], ALU.subtract),
          reads=[(skey, 1), (skey, 2)], writes=[(skey, 2)])
        if lnexp:
            E('act', lambda e: e.activation(stats[0:rows, 3:4], stats[0:rows, 2:3],
                                            AF.Ln, bias=eps_tile[0:rows, 0:1]),
              reads=[(skey, 2), eps_key], writes=[(skey, 3)])
            E('dve', lambda e: e.tensor_scalar(stats[0:rows, 3:4], stats[0:rows, 3:4],
                                               -0.5, None, ALU.mult),
              reads=[(skey, 3)], writes=[(skey, 3)])
            E('act', lambda e: e.activation(stats[0:rows, 3:4], stats[0:rows, 3:4],
                                            AF.Exp),
              reads=[(skey, 3)], writes=[(skey, 3)])
        else:
            E('act', lambda e: e.activation(stats[0:rows, 3:4], stats[0:rows, 2:3],
                                            AF.Sqrt, bias=eps_tile[0:rows, 0:1]),
              reads=[(skey, 2), eps_key], writes=[(skey, 3)])
            E('dve', lambda e: e.reciprocal(stats[0:rows, 3:4], stats[0:rows, 3:4]),
              reads=[(skey, 3)], writes=[(skey, 3)])
        E('dve', lambda e: e.tensor_scalar(dst_ap, src_ap, stats[0:rows, 0:1],
                                           stats[0:rows, 3:4], ALU.subtract, ALU.mult),
          reads=rd + [(skey, 0), (skey, 3)], writes=[dst_key])

    pha = contextlib.ExitStack()
    st_w = contextlib.ExitStack()
    st_m = contextlib.ExitStack()
    bq_sb = sbuf(perm, "bq_sb_s", [1, D], BF16)
    bo_sb = sbuf(perm, "bo_sb_s", [1, D], BF16)
    wqT = sbuf(st_w, "wqT_s", [128, 8, D], BF16)
    woT_sb = sbuf(st_w, "woT_sb_s", [128, 8, D], BF16)
    M_tm = sbuf(st_m, "M_tm_s", [128, H * 128], F32)
    lo_t = sbuf(st_m, "lo_t_s", [128, H], F32)
    wkT = sbuf(pha, "wkT_s", [128, 8, D], BF16)
    wvT = sbuf(pha, "wvT_s", [128, 8, D], BF16)
    bk_sb = sbuf(pha, "bk_sb_s", [1, D], BF16)
    bv_sb = sbuf(pha, "bv_sb_s", [1, D], BF16)
    E('pool', lambda e: e.dma_start(wqT.ap(), wqT_d.rearrange("(kc p) f -> p kc f", p=128)),
      writes=['wqT'], dma='c_wq')
    E('pool', lambda e: e.dma_start(wkT.ap(), wkT_d.rearrange("(kc p) f -> p kc f", p=128)),
      writes=['wkT'], dma='c_wk')
    E('pool', lambda e: e.dma_start(wvT.ap(), wvT_d.rearrange("(kc p) f -> p kc f", p=128)),
      writes=['wvT'], dma='c_wv')
    E('pool', lambda e: e.dma_start(woT_sb.ap(), woT_d.rearrange("(kc p) f -> p kc f", p=128)),
      writes=['woT_sb'], dma='c_wo')
    E('pool', lambda e: e.dma_start(bq_sb[:, :], bq_d), writes=['bq_sb'], dma='c_bq')
    E('pool', lambda e: e.dma_start(bk_sb[:, :], bk_d), writes=['bk_sb'], dma='c_bk')
    E('pool', lambda e: e.dma_start(bv_sb[:, :], bv_d), writes=['bv_sb'], dma='c_bv')
    E('pool', lambda e: e.dma_start(bo_sb[:, :], bo_d), writes=['bo_sb'], dma='c_bo')

    ps_tr = [psum(pha, f"ps_tr{i}", [128, 512], F32) for i in range(2)]
    ps_qk = [psum(pha, f"ps_qk{i}", [128, 512], F32) for i in range(3)]
    ps_sc = psum(pha, "ps_sc_s", [128, 512], F32)
    ps_xb0 = psum(pha, "ps_xb0_s", [1, 512], F32)
    ps_xb1 = psum(pha, "ps_xb1_s", [1, 512], F32)
    pp_pool = [ps_qk[0], ps_qk[1], ps_qk[2], ps_sc]
    pp_ctr = [0]
    def next_pp():
        i = pp_ctr[0] % 4
        pp_ctr[0] += 1
        return pp_pool[i], ('pp', i)

    E = mkE('p0b')
    idxk_sb = sbuf(pha, "idxk_sb_s", [U, 1], I32)
    xs_f = sbuf(pha, "xs_f_s", [U, D], F32)
    xs_bf = sbuf(pha, "xs_bf_s", [128, D], BF16)
    xs_T = sbuf(pha, "xs_T_s", [128, 8, 64], BF16)
    kbar = sbuf(pha, "kbar_s", [128, 1], F32)
    st0 = sbuf(pha, "st0_s", [128, 4], F32)

    E('sp', lambda e: e.dma_start(idxk_sb[:, :], idxk_d.rearrange("(a b) -> a b", b=1)),
      writes=['idxk_sb'], dma='c_idk')
    E('dve', lambda e: e.memset(ksampT.ap(), 0.0), writes=['ks_init'])
    E('pool', lambda e: e.indirect_dma_start(
        xs_f[:, :], None, x_d, IndirectOffsetOnAxis(idxk_sb[:, :], 0)),
      reads=['idxk_sb'], writes=['xs_f'], dma='g0')
    E('dve', lambda e: e.memset(xs_bf[:, :], 0.0), writes=['xs_bf'])
    layernorm(xs_f[:, :], ['xs_f'], xs_bf[0:U, :], 'xs_bf', U, epsA, 'epsA', st0, 'st0')
    for c in range(8):
        bank = ps_tr[c % 2]
        bkk = ('ps_tr', c % 2)
        E('pe', lambda e, c=c, bank=bank: e.matmul(
            bank[:, 0:128], xs_bf[:, c*128:(c+1)*128], id_bf[:, :],
            start=True, stop=True),
          reads=['xs_bf', 'id_bf'], writes=[bkk])
        E('dve', lambda e, c=c, bank=bank: e.tensor_copy(xs_T[:, c, :], bank[:, 0:64]),
          reads=[bkk], writes=[('xs_T', c)])
    xsT_keys = [('xs_T', c) for c in range(8)]
    for h in range(H):
        pr, hi = h // 2, h % 2
        bank, bkk = next_pp()
        def ksamp_mm(e, h=h, bank=bank):
            for k in range(8):
                e.matmul(bank[0:64, 0:64], wkT[:, k, h*64:(h+1)*64], xs_T[:, k, :],
                         start=(k == 0), stop=False)
            return e.matmul(bank[0:64, 0:64], bk_sb[:, h*64:(h+1)*64],
                            ones_row[:, 0:64], start=False, stop=True)
        E('pe', ksamp_mm, reads=xsT_keys + ['wkT', 'bk_sb', 'ones_row'],
          writes=[bkk])
        E('dve', lambda e, pr=pr, hi=hi, bank=bank: e.tensor_copy(
            ksampT[hi*64:(hi+1)*64, pr, hi*52:hi*52+50], bank[0:64, 0:50]),
          reads=[bkk, 'ks_init'], writes=[('ksampT', pr, hi)])
        E('dve', lambda e, bank=bank: e.reduce_sum(kbar[0:64, :], bank[0:64, 0:50], axis=AX.X),
          reads=[bkk], writes=['kbar'])
        E('dve', lambda e, pr=pr, hi=hi: e.tensor_scalar(
            ksampT[hi*64:(hi+1)*64, pr, hi*52+50:hi*52+51], kbar[0:64, :],
            1.0 / U, None, ALU.mult),
          reads=['kbar', 'ks_init'], writes=[('ksampT', pr, hi, 'bar')])
    ks_keys = [('ksampT', pr, hi) for pr in range(8) for hi in range(2)] + \
              [('ksampT', pr, hi, 'bar') for pr in range(8) for hi in range(2)] + ['ks_init']

    x_in = [sbuf(pha, f"x_in{i}", [128, 4, D], F32) for i in range(2)]
    xln2 = [sbuf(pha, f"xln{i}_s", [128, 4, D], BF16) for i in range(2)]
    xT = sbuf(pha, "xT_s", [128, 8, 512], BF16)
    qt = sbuf(pha, "qt_s", [128, 8, 512], BF16)
    ktile = [sbuf(pha, f"ktile{i}", [128, 512], BF16) for i in range(2)]
    vt = sbuf(pha, "vt_s", [128, 4, D], BF16)
    st1 = [[sbuf(pha, f"st1_{i}_{s}", [128, 4], F32) for s in range(4)] for i in range(2)]
    mtmp = sbuf(pha, "mtmp_s", [128, 8], F32)

    E = _E

    def emit_x_dma(g):
        gp = g % 2
        xi = x_in[gp]
        _E('sp', lambda e, g=g, xi=xi: e.dma_start(
            xi.ap(), x_d[g*512:(g+1)*512, :].rearrange("(s p) f -> p s f", p=128)),
          writes=[('x_in', gp)], dma=f'x{gp}')

    def emit_ln_slice(g, s):
        gp = g % 2
        layernorm(x_in[gp][:, s, :], [('x_in', gp)], xln2[gp][:, s, :],
                  ('xln', gp, s), 128, epsA, 'epsA', st1[gp][s], ('st1', gp, s))

    emit_x_dma(0)
    for s in range(4):
        emit_ln_slice(0, s)
    for g in range(ngroups):
        gp = g % 2
        xln = xln2[gp]
        if g + 1 < ngroups:
            emit_x_dma(g + 1)
        E = mkE('tr')
        for s in range(4):
            for cb in range(2):
                bank = ps_tr[cb]
                bkk = ('ps_tr', cb)
                for j in range(4):
                    c = cb * 4 + j
                    E('pe', lambda e, s=s, c=c, j=j, bank=bank, xln=xln: e.matmul(
                        bank[:, j*128:(j+1)*128], xln[:, s, c*128:(c+1)*128],
                        id_bf[:, :], start=True, stop=True),
                      reads=[('xln', gp, s), 'id_bf'], writes=[bkk])
                E('dve', lambda e, s=s, cb=cb, bank=bank: e.tensor_copy(
                    xT.ap()[:, cb*4:(cb+1)*4, s*128:(s+1)*128],
                    bank.ap().rearrange("p (c f) -> p c f", c=4)),
                  reads=[bkk], writes=[('xT', s, cb)])
        xt_keys = [('xT', s, cb) for s in range(4) for cb in range(2)]
        E = mkE('q')
        for f in range(8):
            bank, bkk = next_pp()
            def qmm(e, f=f, bank=bank):
                for k in range(8):
                    e.matmul(bank[:, :], wqT[:, k, f*128:(f+1)*128], xT[:, k, :],
                             start=(k == 0), stop=False)
                return e.matmul(bank[:, :], bq_sb[:, f*128:(f+1)*128],
                                ones_row[:, :], start=False, stop=True)
            E('pe', qmm, reads=xt_keys + ['wqT', 'bq_sb', 'ones_row'], writes=[bkk])
            E('dve', lambda e, f=f, bank=bank: e.tensor_copy(qt[:, f, :], bank[:, :]),
              reads=[bkk], writes=[('qt', f)])
            if f in (2, 6) and g + 1 < ngroups:
                emit_ln_slice(g + 1, f // 4)
        E = mkE('k')
        for f in range(8):
            kt = ktile[f % 2]
            ktk = ('ktile', f % 2)
            bank, bkk = next_pp()
            def kmm(e, f=f, bank=bank):
                for k in range(8):
                    e.matmul(bank[:, :], wkT[:, k, f*128:(f+1)*128], xT[:, k, :],
                             start=(k == 0), stop=False)
                return e.matmul(bank[:, :], bk_sb[:, f*128:(f+1)*128],
                                ones_row[:, :], start=False, stop=True)
            E('pe', kmm, reads=xt_keys + ['wkT', 'bk_sb', 'ones_row'], writes=[bkk])
            E('dve', lambda e, kt=kt, bank=bank: e.tensor_copy(kt[:, :], bank[:, :]),
              reads=[bkk], writes=[ktk])
            E('sp', lambda e, g=g, f=f, kt=kt: e.dma_start(
                kfm_d[f, :, g*512:(g+1)*512], kt[:, :]),
              reads=[ktk], writes=['kfm_w'], dma='kw')
            if f in (2, 6) and g + 1 < ngroups:
                emit_ln_slice(g + 1, 2 + f // 4)
        E = mkE('v')
        for s in range(4):
            for hf in range(2):
                bank, bkk = next_pp()
                def vmm(e, s=s, hf=hf, bank=bank):
                    for k in range(8):
                        e.matmul(bank[:, :], xT[:, k, s*128:(s+1)*128],
                                 wvT[:, k, hf*512:(hf+1)*512], start=(k == 0), stop=False)
                    return e.matmul(bank[:, :], ones_row[0:1, 0:128],
                                    bv_sb[:, hf*512:(hf+1)*512], start=False, stop=True)
                E('pe', vmm, reads=xt_keys + ['wvT', 'bv_sb', 'ones_row'], writes=[bkk])
                E('dve', lambda e, s=s, hf=hf, bank=bank: e.tensor_copy(
                    vt[:, s, hf*512:(hf+1)*512], bank[:, :]),
                  reads=[bkk], writes=[('vt', s, hf)])
        vt_keys = [('vt', s, hf) for s in range(4) for hf in range(2)]
        for pr in range(8):
            E('sp', lambda e, g=g, pr=pr: e.dma_start(
                vscr_d[pr, g*4:(g+1)*4, :, :].rearrange("s p d -> p s d"),
                vt[:, :, pr*128:(pr+1)*128]),
              reads=vt_keys, writes=['vscr_w'], dma='vw')
        E = mkE('xb')
        for s in range(4):
            E('pe', lambda e, g=g, s=s, xln=xln: e.matmul(
                ps_xb0[:, :], ones_col[:, :], xln[:, s, 0:512],
                start=(g == 0 and s == 0), stop=(g == ngroups - 1 and s == 3)),
              reads=[('xln', gp, s), 'ones_col'], writes=[('ps_xb', 0)])
            E('pe', lambda e, g=g, s=s, xln=xln: e.matmul(
                ps_xb1[:, :], ones_col[:, :], xln[:, s, 512:1024],
                start=(g == 0 and s == 0), stop=(g == ngroups - 1 and s == 3)),
              reads=[('xln', gp, s), 'ones_col'], writes=[('ps_xb', 1)])
        E = mkE('sc')
        for s in range(4):
            for bk_i in range(2):
                bank, bkk = next_pp()
                def smm(e, s=s, bk_i=bk_i, bank=bank):
                    last = None
                    for j in range(4):
                        pr = bk_i * 4 + j
                        last = e.matmul(
                            bank[:, j*104:(j+1)*104],
                            qt[:, pr, s*128:(s+1)*128],
                            ksampT[:, pr, :],
                            start=True, stop=True)
                    return last
                E('pe', smm, reads=[('qt', f) for f in range(8)] + ks_keys,
                  writes=[bkk])
                E('dve', lambda e, bank=bank: e.reduce_max(
                    mtmp.ap().rearrange("p (j hi o) -> p j hi o", j=4, hi=2),
                    bank[:, 0:416].rearrange("p (j hi c) -> p j hi c", j=4, hi=2)[:, :, :, 0:50],
                    axis=AX.X),
                  reads=[bkk], writes=['mtmp'])
                E('dve', lambda e, g=g, s=s, bk_i=bk_i, bank=bank: e.tensor_tensor(
                    M_tm.ap().rearrange("p (h j) -> p h j", h=H)[:, bk_i*8:(bk_i+1)*8, g*4+s],
                    mtmp[:, :],
                    bank[:, 0:416].rearrange("p (j hi c) -> p j hi c", j=4, hi=2)[:, :, :, 50],
                    ALU.subtract),
                  reads=['mtmp', bkk], writes=[('M', bk_i, g, s)])

    E = mkE('p15')
    Mkeys = [('M', b, g, s) for b in range(2) for g in range(NG) for s in range(4)]

    E('dve', lambda e: e.tensor_scalar(xbar_bf[:, 0:512], ps_xb0[:, :], 1.0 / L,
                                       None, ALU.mult),
      reads=[('ps_xb', 0)], writes=['xbar_bf'])
    E('dve', lambda e: e.tensor_scalar(xbar_bf[:, 512:1024], ps_xb1[:, :], 1.0 / L,
                                       None, ALU.mult),
      reads=[('ps_xb', 1)], writes=['xbar_bf'])
    for c in range(8):
        bank, bkk = next_pp()
        E('pe', lambda e, c=c, bank=bank: e.matmul(bank[:, 0:1], xbar_bf[:, c*128:(c+1)*128],
                                        one1[:, :], start=True, stop=True),
          reads=['xbar_bf', 'one1'], writes=[bkk])
        E('dve', lambda e, c=c, bank=bank: e.tensor_copy(xbarT[:, c:c+1], bank[:, 0:1]),
          reads=[bkk], writes=[('xbarT', c)])
    xbT_keys = [('xbarT', c) for c in range(8)]
    def vmean_mm0(e):
        for c in range(8):
            e.matmul(ps_qk[0][0:1, :], xbarT[:, c:c+1], wvT[:, c, 0:512],
                     start=(c == 0), stop=False)
        return e.matmul(ps_qk[0][0:1, :], one1[:, :], bv_sb[:, 0:512],
                        start=False, stop=True)
    def vmean_mm1(e):
        for c in range(8):
            e.matmul(ps_qk[1][0:1, :], xbarT[:, c:c+1], wvT[:, c, 512:1024],
                     start=(c == 0), stop=False)
        return e.matmul(ps_qk[1][0:1, :], one1[:, :], bv_sb[:, 512:1024],
                        start=False, stop=True)
    E('pe', vmean_mm0, reads=xbT_keys + ['wvT', 'one1', 'bv_sb'], writes=[('pp', 0)])
    E('pe', vmean_mm1, reads=xbT_keys + ['wvT', 'one1', 'bv_sb'], writes=[('pp', 1)])
    E('dve', lambda e: e.tensor_copy(vmean[:, 0:512], ps_qk[0][0:1, :]),
      reads=[('pp', 0)], writes=['vmean'])
    E('dve', lambda e: e.tensor_copy(vmean[:, 512:1024], ps_qk[1][0:1, :]),
      reads=[('pp', 1)], writes=['vmean'])
    E('dve', lambda e: e.tensor_copy(vmean_bf[:, :], vmean[:, :]),
      reads=['vmean'], writes=['vmean_bf'])
    for c in range(8):
        bank, bkk = next_pp()
        E('pe', lambda e, c=c, bank=bank: e.matmul(bank[:, 0:1], vmean_bf[:, c*128:(c+1)*128],
                                        one1[:, :], start=True, stop=True),
          reads=['vmean_bf', 'one1'], writes=[bkk])
        E('dve', lambda e, c=c, bank=bank: e.tensor_copy(vmeanT[:, c:c+1], bank[:, 0:1]),
          reads=[bkk], writes=[('vmeanT', c)])
    E('dve', lambda e: e.tensor_copy(vmeanT_bf[:, :], vmeanT[:, :]),
      reads=[('vmeanT', c) for c in range(8)], writes=['vmeanT_bf'])
    def base_mm0(e):
        for c in range(8):
            e.matmul(ps_qk[0][0:1, :], vmeanT_bf[:, c:c+1], woT_sb[:, c, 0:512],
                     start=(c == 0), stop=False)
        return e.matmul(ps_qk[0][0:1, :], one1[:, :], bo_sb[:, 0:512],
                        start=False, stop=True)
    def base_mm1(e):
        for c in range(8):
            e.matmul(ps_qk[1][0:1, :], vmeanT_bf[:, c:c+1], woT_sb[:, c, 512:1024],
                     start=(c == 0), stop=False)
        return e.matmul(ps_qk[1][0:1, :], one1[:, :], bo_sb[:, 512:1024],
                        start=False, stop=True)
    E('pe', base_mm0, reads=['vmeanT_bf', 'woT_sb', 'one1', 'bo_sb'], writes=[('pp', 0)])
    E('pe', base_mm1, reads=['vmeanT_bf', 'woT_sb', 'one1', 'bo_sb'], writes=[('pp', 1)])
    E('dve', lambda e: e.tensor_copy(base_bf[:, 0:512], ps_qk[0][0:1, :]),
      reads=[('pp', 0)], writes=['base_bf'])
    E('dve', lambda e: e.tensor_copy(base_bf[:, 512:1024], ps_qk[1][0:1, :]),
      reads=[('pp', 1)], writes=['base_bf'])
    if debug:
        E('sp', lambda e: e.dma_start(dbg["dbg_vmean"], vmean[:, :]), reads=['vmean'], dma='dbg')
        basef = sbuf(pha, "basef_s", [1, D], F32)
        E('dve', lambda e: e.tensor_copy(basef[:, :], base_bf[:, :]),
          reads=['base_bf'], writes=['basef'])
        E('sp', lambda e: e.dma_start(dbg["dbg_base"], basef[:, :]), reads=['basef'], dma='dbg')
        E('sp', lambda e: e.dma_start(dbg["dbg_M"], M_tm[:, :]), reads=Mkeys, dma='dbg')

    E = _E
    p.fence()
    pha.close()
    if maxphase < 2:
        if debug:
            p.wait_all_dma('pool', 'dbg')
        p.build(); p.close()
        st_m.close(); st_w.close(); sbstack.close()
        return nc
    ph2 = contextlib.ExitStack()
    hi_t = sbuf(ph2, "hi_t_s", [128, H], F32)
    tt_t = sbuf(ph2, "tt_t_s", [128, H], F32)
    cnt_f = sbuf(ph2, "cnt_f_s", [128, H], F32)
    cnt_bf = sbuf(ph2, "cnt_bf_s", [128, H], BF16)
    cntr = sbuf(ph2, "cntr_s", [128, H], F32)
    geu = sbuf(ph2, "geu_s", [128, H], U8)
    gle = sbuf(ph2, "gle_s", [128, H], U8)
    junkc = sbuf(ph2, "junkc_s", [128, 128], F32)
    mask_u8 = sbuf(ph2, "mask_u8_s", [128, H * 128], U8)
    neg1 = sbuf(ph2, "neg1_s", [128, 1], F32)
    iotaM = sbuf(ph2, "iotaM_s", [128, H * 128], F32)
    miota = M_tm
    cand = sbuf(ph2, "cand_s", [128, H * 16], F32)
    candcat = sbuf(ph2, "candcat_s", [16, 2048], F32)
    cmax8 = sbuf(ph2, "cmax8_s", [16, 8], F32)
    sort56 = sbuf(ph2, "sort56_s", [16, 56], F32)
    sortpad = sbuf(ph2, "sortpad_s", [16, 64], F32)
    selneg = sbuf(ph2, "selneg_s", [16, 64], U8)
    big9 = sbuf(ph2, "big9_s", [16, 64], F32)
    candT = sbuf(ph2, "candT_s", [128, 128], F32)
    ps2_a = psum(ph2, "ps2_a_s", [128, 512], F32)
    ps2_b = psum(ph2, "ps2_b_s", [128, 512], F32)

    E('sp', lambda e: e.dma_start(iotaM[:, :], iotaM_d), writes=['iotaM'], dma='c_iom')
    E('dve', lambda e: e.memset(lo_t[:, :], 0.0), writes=['lo'])
    E('dve', lambda e: e.memset(hi_t[:, :], 256.0), writes=['hi'])
    for it in range(nbisect):
        E('dve', lambda e: e.tensor_tensor(tt_t[:, :], lo_t[:, :], hi_t[:, :], ALU.add),
          reads=['lo', 'hi'], writes=['tt'])
        E('dve', lambda e: e.tensor_scalar(tt_t[:, :], tt_t[:, :], 0.5, None, ALU.mult),
          reads=['tt'], writes=['tt'])
        for h in range(H):
            E('dve', lambda e, h=h: e.tensor_scalar(
                junkc[:, :], M_tm[:, h*128:(h+1)*128], tt_t[:, h:h+1], 0.0,
                ALU.is_gt, ALU.add, accum_out=cnt_f[:, h:h+1]),
              reads=Mkeys + ['tt'], writes=['junkc', ('cnt', h)])
        E('dve', lambda e: e.tensor_copy(cnt_bf[:, :], cnt_f[:, :]),
          reads=[('cnt', h) for h in range(H)], writes=['cntbf'])
        E('pe', lambda e: e.matmul(ps2_a[:, 0:16], ones_sq[:, :], cnt_bf[:, :],
                                   start=True, stop=True),
          reads=['cntbf', 'ones_sq'], writes=[('ps2_a', 0)])
        E('dve', lambda e: e.tensor_copy(cntr[:, :], ps2_a[:, 0:16]),
          reads=[('ps2_a', 0)], writes=['cntr'])
        E('dve', lambda e: e.tensor_scalar(geu[:, :], cntr[:, :], 49.5, None, ALU.is_gt),
          reads=['cntr'], writes=['geu'])
        E('dve', lambda e: e.tensor_scalar(gle[:, :], cntr[:, :], 49.5, None, ALU.is_le),
          reads=['cntr'], writes=['gle'])
        E('dve', lambda e: e.select(lo_t[:, :], geu[:, :], tt_t[:, :], lo_t[:, :],
                                    add_drain=True),
          reads=['geu', 'tt', 'lo'], writes=['lo'])
        E('dve', lambda e: e.select(hi_t[:, :], gle[:, :], tt_t[:, :], hi_t[:, :],
                                    add_drain=True),
          reads=['gle', 'tt', 'hi'], writes=['hi'])
    if debug:
        E('sp', lambda e: e.dma_start(dbg["dbg_lo"], lo_t[:, :]), reads=['lo'], dma='dbg')
    E('dve', lambda e: e.memset(neg1[:, :], -1.0), writes=['neg1'])
    for h in range(H):
        E('dve', lambda e, h=h: e.tensor_scalar(
            mask_u8[:, h*128:(h+1)*128], M_tm[:, h*128:(h+1)*128],
            lo_t[:, h:h+1], None, ALU.is_gt),
          reads=Mkeys + ['lo'], writes=[('msk', h)])
    E('dve', lambda e: e.select(miota[:, :], mask_u8[:, :], iotaM[:, :],
                                neg1[:, 0:1].to_broadcast([128, H * 128]),
                                add_drain=True),
      reads=[('msk', h) for h in range(H)] + ['iotaM', 'neg1'],
      writes=Mkeys + ['miota'])
    for rnd in range(2):
        for h in range(H):
            E('dve', lambda e, rnd=rnd, h=h: e.max(
                out=cand[:, h*16+rnd*8:h*16+rnd*8+8], in_=miota[:, h*128:(h+1)*128]),
              reads=['miota'], writes=[('cand', h, rnd)])
            E('dve', lambda e, rnd=rnd, h=h: e.match_replace(
                out=miota[:, h*128:(h+1)*128],
                in_to_replace=cand[:, h*16+rnd*8:h*16+rnd*8+8],
                in_values=miota[:, h*128:(h+1)*128], imm_value=-1.0),
              reads=['miota', ('cand', h, rnd)], writes=['miota'])
    candkeys = [('cand', h, r) for h in range(H) for r in range(2)]
    for half in range(2):
        E('pe', lambda e, half=half: e.matmul(
            ps2_a[:, 0:128],
            cand[:, half*128:(half+1)*128], idf32[:, :], start=True, stop=True),
          reads=candkeys + ['idf32'], writes=[('ps2_a', 0)])
        E('dve', lambda e: e.tensor_copy(candT[:, :], ps2_a[:, 0:128]),
          reads=[('ps2_a', 0)], writes=['candT'])
        E('pool', lambda e, half=half: e.dma_start(
            candcat[half*8:(half+1)*8, :].rearrange("h (c s) -> h c s", c=16),
            candT[:, :]),
          reads=['candT'], writes=['candcat'], dma='cc')
    for rnd in range(7):
        E('dve', lambda e: e.max(out=cmax8[:, :], in_=candcat[:, :]),
          reads=['candcat'], writes=['cmax8'])
        E('dve', lambda e, rnd=rnd: e.tensor_copy(sort56[:, rnd*8:(rnd+1)*8],
                                                  cmax8[:, :]),
          reads=['cmax8'], writes=[('s56', rnd)])
        E('dve', lambda e: e.match_replace(
            out=candcat[:, :], in_to_replace=cmax8[:, :], in_values=candcat[:, :],
            imm_value=-1.0),
          reads=['candcat', 'cmax8'], writes=['candcat'])
    s56keys = [('s56', r) for r in range(7)]
    E('dve', lambda e: e.memset(sortpad[:, :], 1.0e9), writes=['sortpad'])
    E('dve', lambda e: e.tensor_copy(sortpad[:, 0:50], sort56[:, 0:50]),
      reads=s56keys, writes=['sortpad'])
    E('dve', lambda e: e.tensor_scalar(selneg[:, :], sortpad[:, :], -0.5, None, ALU.is_lt),
      reads=['sortpad'], writes=['selneg'])
    E('dve', lambda e: e.memset(big9[:, :], 1.0e9), writes=['big9'])
    E('dve', lambda e: e.select(sortpad[:, :], selneg[:, :], big9[:, :], sortpad[:, :],
                                add_drain=True),
      reads=['selneg', 'big9', 'sortpad'], writes=['sortpad'])
    E('pe', lambda e: e.matmul(ps2_b[0:64, 0:16], sortpad[:, :], idf32[0:16, 0:16],
                               start=True, stop=True),
      reads=['sortpad', 'idf32'], writes=[('ps2_b', 0)])
    E('dve', lambda e: e.tensor_copy(idxT_f[:, :], ps2_b[0:64, 0:16]),
      reads=[('ps2_b', 0)], writes=['idxT_f'])
    E('dve', lambda e: e.tensor_copy(idxT_i[:, :], idxT_f[:, :]),
      reads=['idxT_f'], writes=['idxT_i'])
    if debug:
        E('sp', lambda e: e.dma_start(dbg["dbg_idx"], idxT_f[:, :]), reads=['idxT_f'], dma='dbg')

    p.fence()
    ph2.close()
    st_m.close()

    if maxphase < 3:
        if debug:
            p.wait_all_dma('pool', 'dbg')
        p.build(); p.close()
        st_w.close(); sbstack.close()
        return nc

    phb = contextlib.ExitStack()
    kpair = [sbuf(phb, f"kpair{i}", [128, L], BF16) for i in range(2)]
    xg_f = sbuf(phb, "xg_f_s", [128, D], F32)
    xg_bf = sbuf(phb, "xg_bf_s", [128, D], BF16)
    xg_T = sbuf(phb, "xg_T_s", [128, 8, 128], BF16)
    qtopT = sbuf(phb, "qtopT_s", [128, 128], BF16)
    P_sb = sbuf(phb, "P_sb_s", [128, L], BF16)
    sumP = sbuf(phb, "sumP_s", [128, 32], F32)
    rsum = sbuf(phb, "rsum_s", [128, 1], F32)
    PT_sb = [sbuf(phb, f"PT_sb{i}", [128, 512], BF16) for i in range(2)]
    vsc = [sbuf(phb, f"vsc{i}", [128, 8, 128], BF16) for i in range(2)]
    dctx = sbuf(phb, "dctx_s", [128, 128], BF16)
    st3 = sbuf(phb, "st3_s", [128, 4], F32)
    ps_s = [psum(phb, f"ps_s{i}", [128, 512], F32) for i in range(2)]
    ps_pt = [psum(phb, f"ps_pt{i}", [128, 512], F32) for i in range(2)]
    ps_cx = psum(phb, "ps_cx_s", [128, 128], F32)
    ps_d = [psum(phb, f"ps_d{i}", [128, 512], F32) for i in range(2)]

    for c in range(8):
        E('dve', lambda e, c=c: e.memset(delta[c].ap(), 0.0), writes=[('dz', c)])
        E('dve', lambda e, c=c: e.memset(idxv[c].ap(), 2.0e9), writes=[('iz', c)])

    for pr in range(8):
        kp = kpair[pr % 2]
        kpk = ('kpair', pr % 2)
        E('sp', lambda e, pr=pr, kp=kp: e.dma_start(kp.ap(), kfm_d[pr, :, :]),
          reads=['kfm_w'], writes=[kpk], dma=f'kr{pr % 2}')
        E('dve', lambda e: e.memset(xg_f.ap(), 0.0), writes=['xg_f'])
        E('pool', lambda e, pr=pr: e.indirect_dma_start(
            xg_f[0:U, :], None, x_d, IndirectOffsetOnAxis(idxT_i[0:U, 2*pr:2*pr+1], 0),
            bounds_check=L - 1, oob_is_err=False),
          reads=['idxT_i', 'xg_f'], writes=[('xg_g', 0)], dma='g1')
        E('pool', lambda e, pr=pr: e.indirect_dma_start(
            xg_f[64:64+U, :], None, x_d, IndirectOffsetOnAxis(idxT_i[0:U, 2*pr+1:2*pr+2], 0),
            bounds_check=L - 1, oob_is_err=False),
          reads=['idxT_i', 'xg_f'], writes=[('xg_g', 1)], dma='g1')
        layernorm(xg_f[:, :], ['xg_f', ('xg_g', 0), ('xg_g', 1)], xg_bf[:, :], 'xg_bf',
                  128, epsA, 'epsA', st3, 'st3', lnexp=True)
        for c in range(8):
            bank = ps_pt[c % 2]
            bkk = ('ps_pt2', c % 2)
            E('pe', lambda e, c=c, bank=bank: e.matmul(
                bank[:, 0:128], xg_bf[:, c*128:(c+1)*128], id_bf[:, :],
                start=True, stop=True),
              reads=['xg_bf', 'id_bf'], writes=[bkk])
            E('dve', lambda e, c=c, bank=bank: e.tensor_copy(
                xg_T[:, c, :], bank[:, 0:128]),
              reads=[bkk], writes=[('xg_T', c)])
        def qtop_mm(e, pr=pr):
            for k in range(8):
                e.matmul(ps_d[0][:, 0:128], wqT[:, k, pr*128:(pr+1)*128], xg_T[:, k, :],
                         start=(k == 0), stop=False)
            return e.matmul(ps_d[0][:, 0:128], bq_sb[:, pr*128:(pr+1)*128],
                            ones_row[:, 0:128], start=False, stop=True)
        E('pe', qtop_mm, reads=[('xg_T', c) for c in range(8)] +
          ['wqT', 'bq_sb', 'ones_row'], writes=['ps_d0'])
        E('dve', lambda e: e.memset(qtopT.ap(), 0.0), writes=['qtop0'])
        E('dve', lambda e: e.tensor_copy(qtopT[0:64, 0:50], ps_d[0][0:64, 0:50]),
          reads=['ps_d0', 'qtop0'], writes=[('qtopT', 0)])
        E('dve', lambda e: e.tensor_copy(qtopT[64:128, 64:114], ps_d[0][64:128, 64:114]),
          reads=['ps_d0', 'qtop0'], writes=[('qtopT', 1)])
        for kc in range(32):
            bank = ps_s[kc % 2]
            bkk = ('ps_s', kc % 2)
            E('pe', lambda e, kc=kc, kp=kp, bank=bank: e.matmul(
                bank[:, :], qtopT[:, :], kp[:, kc*512:(kc+1)*512],
                start=True, stop=True),
              reads=[('qtopT', 0), ('qtopT', 1), 'qtop0', kpk], writes=[bkk])
            E('act', lambda e, kc=kc, bank=bank: e.activation(
                P_sb[:, kc*512:(kc+1)*512], bank[:, :], AF.Exp,
                bias=negb[:, :], scale=SCALE, accum_out=sumP[:, kc:kc+1]),
              reads=[bkk, 'negb'], writes=[('P', kc), ('sumP', kc)])
        E('dve', lambda e: e.reduce_sum(rsum[:, :], sumP[:, :], axis=AX.X),
          reads=[('sumP', kc) for kc in range(32)], writes=['rsum'])
        E('dve', lambda e: e.reciprocal(rsum[:, :], rsum[:, :]),
          reads=['rsum'], writes=['rsum'])
        E('dve', lambda e: e.tensor_scalar(P_sb[:, :], P_sb[:, :], rsum[:, :], None,
                                           ALU.mult),
          reads=[('P', kc) for kc in range(32)] + ['rsum'], writes=['Pn'])
        for scb in range(16):
            vb = vsc[scb % 2]
            vbk = ('vsc', scb % 2)
            E('sp', lambda e, pr=pr, scb=scb, vb=vb: e.dma_start(
                vb.ap(), vscr_d[pr, scb*8:(scb+1)*8, :, :].rearrange("s p d -> p s d")),
              reads=['vscr_w'], writes=[vbk], dma=f'vr{scb % 2}')
            for q4 in range(2):
                sc0 = scb * 8 + q4 * 4
                ptb = PT_sb[q4]
                ptk = ('PT', q4)
                psb = ps_pt[q4]
                psk = ('ps_pt2', q4)
                def ptmm(e, sc0=sc0, psb=psb):
                    last = None
                    for jj in range(4):
                        last = e.matmul(psb[:, jj*128:(jj+1)*128],
                                        P_sb[:, (sc0+jj)*128:(sc0+jj+1)*128],
                                        id_bf[:, :], start=True, stop=True)
                    return last
                E('pe', ptmm, reads=['Pn', 'id_bf'], writes=[psk])
                E('dve', lambda e, ptb=ptb, psb=psb: e.tensor_copy(ptb[:, :], psb[:, :]),
                  reads=[psk], writes=[ptk])
                def cxmm(e, sc0=sc0, q4=q4, vb=vb, ptb=ptb):
                    last = None
                    for jj in range(4):
                        sc = sc0 + jj
                        last = e.matmul(ps_cx[:, :], vb[:, q4*4+jj, :],
                                        ptb[:, jj*128:(jj+1)*128],
                                        start=(sc == 0), stop=(sc == 127))
                    return last
                E('pe', cxmm, reads=[vbk, ptk], writes=['ps_cx'])
        E('dve', lambda e: e.memset(dctx.ap(), 0.0), writes=['dctx0'])
        E('dve', lambda e, pr=pr: e.tensor_scalar(
            dctx[0:64, 0:50], ps_cx[0:64, 0:50], vmeanT[0:64, pr:pr+1], None,
            ALU.subtract),
          reads=['ps_cx', ('vmeanT', pr), 'dctx0'], writes=[('dctx', 0)])
        E('dve', lambda e, pr=pr: e.tensor_scalar(
            dctx[64:128, 64:114], ps_cx[64:128, 64:114], vmeanT[64:128, pr:pr+1], None,
            ALU.subtract),
          reads=['ps_cx', ('vmeanT', pr), 'dctx0'], writes=[('dctx', 1)])
        def dmm(e, pr=pr):
            e.matmul(ps_d[0][:, :], dctx[:, :], woT_sb[:, pr, 0:512],
                     start=True, stop=True)
            return e.matmul(ps_d[1][:, :], dctx[:, :], woT_sb[:, pr, 512:1024],
                            start=True, stop=True)
        E('pe', dmm, reads=[('dctx', 0), ('dctx', 1), 'dctx0', 'woT_sb'],
          writes=['ps_d0', 'ps_d1'])
        E('dve', lambda e, pr=pr: e.tensor_copy(delta[pr][:, 0:512], ps_d[0][:, :]),
          reads=['ps_d0', ('dz', pr)],
          writes=[('delta', pr, 0, 0), ('delta', pr, 64, 0)])
        E('dve', lambda e, pr=pr: e.tensor_copy(delta[pr][:, 512:1024], ps_d[1][:, :]),
          reads=['ps_d1', ('dz', pr)],
          writes=[('delta', pr, 0, 1), ('delta', pr, 64, 1)])
        E('dve', lambda e, pr=pr: e.tensor_copy(idxv[pr][0:50, :],
                                                idxT_f[0:50, 2*pr:2*pr+1]),
          reads=['idxT_f', ('iz', pr)], writes=[('idxv', pr, 0)])
        E('dve', lambda e, pr=pr: e.tensor_copy(idxv[pr][64:114, :],
                                                idxT_f[0:50, 2*pr+1:2*pr+2]),
          reads=['idxT_f', ('iz', pr)], writes=[('idxv', pr, 64)])
    if debug:
        dltf = sbuf(phb, "dltf_s", [128, D], F32)
        E('dve', lambda e: e.tensor_copy(dltf[:, :], delta[0].ap()),
          reads=[('delta', 0, 0, 0), ('delta', 0, 0, 1), ('delta', 0, 64, 0),
                 ('delta', 0, 64, 1), ('dz', 0)],
          writes=['dltf'])
        E('sp', lambda e: e.dma_start(dbg["dbg_delta"], dltf[:, :]), reads=['dltf'], dma='dbg')

    p.fence()
    phb.close()
    st_w.close()

    if maxphase < 5:
        if debug:
            p.wait_all_dma('pool', 'dbg')
        p.build(); p.close()
        sbstack.close()
        return nc

    phc = contextlib.ExitStack()
    wr_sb = sbuf(phc, "wr_sb_s", [128, 8, NE], BF16)
    b1_sb = sbuf(phc, "b1_sb_s", [NE, 32 * 128], BF16)
    gam_b = sbuf(phc, "gam_b_s", [128, D], F32)
    bet_b = sbuf(phc, "bet_b_s", [128, D], F32)
    bof_sb = sbuf(phc, "bof_sb_s", [1, D], BF16)
    gb_tmp = sbuf(phc, "gb_tmp_s", [1, D], BF16)
    xq_in = [sbuf(phc, f"xq_in{i}", [128, D], F32) for i in range(2)]
    Sel8 = sbuf(phc, "Sel8_s", [128, 8, 128], BF16)
    ish8 = sbuf(phc, "ish8_s", [128, 8], F32)
    idxv8 = sbuf(phc, "idxv8_s", [128, 8], F32)
    attn_f = sbuf(phc, "attn_f_s", [128, D], F32)
    y1_t = sbuf(phc, "y1_t_s", [128, D], F32)
    t1_t = sbuf(phc, "t1_t_s", [128, D], F32)
    t1b8 = sbuf(phc, "t1b8_s", [128, 8, D], BF16)
    t1T_tmp = sbuf(phc, "t1T_tmp_s", [128, D], BF16)
    sts = [sbuf(phc, f"sts{i}", [128, 4], F32) for i in range(4)]
    oh4 = sbuf(phc, "oh4_s", [128, NE], F32)
    ohbf = sbuf(phc, "ohbf_s", [128, NE], BF16)
    noty = sbuf(phc, "noty_s", [128, 1], F32)
    eq4 = sbuf(phc, "eq4_s", [128, NE], F32)
    m4 = sbuf(phc, "m4_s", [128, 1], F32)
    diag8 = sbuf(phc, "diag8_s", [128, 8, 512], BF16)
    t1eT = sbuf(phc, "t1eT_s", [128, 8, NE, 1024], BF16)
    onehT2 = [sbuf(phc, f"onehT{i}_s", [NE, 1024], BF16) for i in range(2)]
    inter_st = [sbuf(phc, f"inter_st{i}_x", [128, 512], BF16) for i in range(2)]
    intc = [sbuf(phc, f"intc{i}_x", [128, 512], BF16) for i in range(2)]
    w1sl = [sbuf(phc, f"w1sl{i}", [128, 8, NE, 128], BF16) for i in range(2)]
    wofc = [sbuf(phc, f"wofc{i}", [128, D], BF16) for i in range(2)]
    t2_t = sbuf(phc, "t2_t_s", [128, D], F32)
    t3_t = sbuf(phc, "t3_t_s", [128, D], F32)
    ps_ab = [psum(phc, f"ps_ab{i}", [128, 1024], F32) for i in range(2)]
    ps_z = [psum(phc, f"ps_z{i}", [128, 512], F32) for i in range(2)]
    ps_tT = [psum(phc, f"ps_tT{i}", [128, 512], F32) for i in range(2)]

    E = mkE('gb')
    E('pool', lambda e: e.dma_start(wr_sb.ap(), wr_d.rearrange("(kc p) f -> p kc f", p=128)),
      writes=['wr_sb'], dma='c_wr')
    E('pool', lambda e: e.dma_start(b1_sb.ap(), b1r_d), writes=['b1_sb'], dma='c_b1')
    E('pool', lambda e: e.dma_start(bof_sb[:, :], bof_d), writes=['bof_sb'], dma='c_bof')
    E('pool', lambda e: e.dma_start(gb_tmp[:, :], gam_d), writes=['gb_tmp'], dma='c_gb')
    for hf in range(2):
        E('pe', lambda e, hf=hf: e.matmul(ps_z[0][:, :], ones_row[0:1, 0:128],
                                          gb_tmp[:, hf*512:(hf+1)*512],
                                          start=True, stop=True),
          reads=['gb_tmp', 'ones_row'], writes=[('ps_z', 0)])
        E('dve', lambda e, hf=hf: e.tensor_copy(gam_b[:, hf*512:(hf+1)*512],
                                                ps_z[0][:, :]),
          reads=[('ps_z', 0)], writes=['gam_b'])
    E('pool', lambda e: e.dma_start(gb_tmp[:, :], bet_d), writes=['gb_tmp'], dma='c_gb')
    for hf in range(2):
        E('pe', lambda e, hf=hf: e.matmul(ps_z[0][:, :], ones_row[0:1, 0:128],
                                          gb_tmp[:, hf*512:(hf+1)*512],
                                          start=True, stop=True),
          reads=['gb_tmp', 'ones_row'], writes=[('ps_z', 0)])
        E('dve', lambda e, hf=hf: e.tensor_copy(bet_b[:, hf*512:(hf+1)*512],
                                                ps_z[0][:, :]),
          reads=[('ps_z', 0)], writes=['bet_b'])

    for ch in range(8):
        E('dve', lambda e, ch=ch: e.tensor_copy(idxv8[:, ch:ch+1], idxv[ch][:, :]),
          reads=[('idxv', ch, 0), ('idxv', ch, 64), ('iz', ch)], writes=['idxv8'])
    dkeys = [('delta', ch, bs, hf) for ch in range(8) for bs in (0, 64) for hf in range(2)] + \
            [('dz', c) for c in range(8)]
    ikeys = [('idxv', ch, bs) for ch in range(8) for bs in (0, 64)] + \
            [('iz', c) for c in range(8)]

    def emit_A_tile(sg, tl):
        sgt0 = sg * 8
        t = sgt0 + tl
        sgp = sg % 2
        E = mkE('as')
        xi = xq_in[tl % 2]
        xik = ('xq_in', tl % 2)
        E('sp', lambda e, t=t, xi=xi: e.dma_start(xi.ap(), xq_d[t*128:(t+1)*128, :]),
          writes=[xik], dma=f'xq{tl % 2}')
        pab = ps_ab[tl % 2]
        pa, pb = pab[:, 0:512], pab[:, 512:1024]
        pak, pbk = ('ps_ab', tl % 2, 0), ('ps_ab', tl % 2, 1)
        E('dve', lambda e, t=t: e.tensor_scalar(
            ish8[:, :], idxv8[:, :], qoffc[:, :], float(t * 128),
            ALU.subtract, ALU.subtract),
          reads=['idxv8', 'qoffc'], writes=['ish8'])
        E('dve', lambda e: e.tensor_tensor(
            Sel8.ap(),
            ish8.ap().unsqueeze(2).to_broadcast([128, 8, 128]),
            iotacol.ap().unsqueeze(1).to_broadcast([128, 8, 128]),
            ALU.is_equal),
          reads=['ish8', 'iotacol'], writes=['Sel8'])
        for ch in range(8):
            E('pe', lambda e, ch=ch, pa=pa: e.matmul(
                pa, Sel8[:, ch, :], delta[ch][:, 0:512],
                start=(ch == 0), stop=False),
              reads=['Sel8'] + dkeys, writes=[pak])
            E('pe', lambda e, ch=ch, pb=pb: e.matmul(
                pb, Sel8[:, ch, :], delta[ch][:, 512:1024],
                start=(ch == 0), stop=False),
              reads=['Sel8'] + dkeys, writes=[pbk])
        E('pe', lambda e, pa=pa: e.matmul(pa, ones_row[0:1, 0:128],
                                          base_bf[:, 0:512], start=False, stop=True),
          reads=['ones_row', 'base_bf'], writes=[pak])
        E('pe', lambda e, pb=pb: e.matmul(pb, ones_row[0:1, 0:128],
                                          base_bf[:, 512:1024], start=False, stop=True),
          reads=['ones_row', 'base_bf'], writes=[pbk])
        E('dve', lambda e, pa=pa: e.tensor_copy(attn_f[:, 0:512], pa),
          reads=[pak], writes=['attn_f'])
        E('dve', lambda e, pb=pb: e.tensor_copy(attn_f[:, 512:1024], pb),
          reads=[pbk], writes=['attn_f'])
        layernorm(attn_f[:, :], ['attn_f'], y1_t[:, :], 'y1', 128, epsA, 'epsA',
                  sts[0], 'sA')
        E('dve', lambda e, xi=xi: e.tensor_tensor(y1_t[:, :], y1_t[:, :], xi.ap(),
                                                  ALU.add),
          reads=['y1', xik], writes=['y1'])
        E('sp', lambda e, t=t: e.dma_start(y1s_d[t*128:(t+1)*128, :], y1_t[:, :]),
          reads=['y1'], writes=[('y1s', t)], dma='y1w')
        layernorm(y1_t[:, :], ['y1'], t1_t[:, :], 't1', 128, epsA, 'epsA',
                  sts[1], 'sB')
        E('sp', lambda e, t=t: e.dma_start(t1s_d[t*128:(t+1)*128, :], t1_t[:, :]),
          reads=['t1'], writes=[('t1s', t)], dma='t1w')
        E('dve', lambda e, tl=tl: e.tensor_copy(t1b8[:, tl, :], t1_t[:, :]),
          reads=['t1'], writes=[('t1b8', tl)])
        if debug and t == 0:
            E('sp', lambda e: e.dma_start(dbg["dbg_y1"], y1_t[:, :]), reads=['y1'], dma='dbg')
            E('sp', lambda e: e.dma_start(dbg["dbg_t1"], t1_t[:, :]), reads=['t1'], dma='dbg')
        E = mkE('as2')
        for cb in range(2):
            bank = ps_tT[cb]
            bkk = ('ps_tT', cb)
            for j in range(4):
                c = cb * 4 + j
                E('pe', lambda e, c=c, j=j, bank=bank, tl=tl: e.matmul(
                    bank[:, j*128:(j+1)*128], t1b8[:, tl, c*128:(c+1)*128], id_bf[:, :],
                    start=True, stop=True),
                  reads=[('t1b8', tl), 'id_bf'], writes=[bkk])
            E('dve', lambda e, cb=cb, bank=bank: e.tensor_copy(
                t1T_tmp[:, cb*512:(cb+1)*512], bank[:, :]),
              reads=[bkk], writes=[('t1T', cb)])
        def lg_mm(e):
            last = None
            for c in range(8):
                last = e.matmul(ps_tT[0][:, 0:NE], t1T_tmp[:, c*128:(c+1)*128],
                                wr_sb[:, c, :], start=(c == 0), stop=(c == 7))
            return last
        E('pe', lg_mm, reads=[('t1T', 0), ('t1T', 1), 'wr_sb'], writes=[('ps_tT', 0)])
        E('dve', lambda e: e.reduce_max(m4[:, :], ps_tT[0][:, 0:NE], axis=AX.X),
          reads=[('ps_tT', 0)], writes=['m4'])
        E('dve', lambda e: e.tensor_scalar(eq4[:, :], ps_tT[0][:, 0:NE], m4[:, :],
                                           None, ALU.is_ge),
          reads=[('ps_tT', 0), 'm4'], writes=['eq4'])
        E('dve', lambda e: e.tensor_copy(oh4[:, 0:1], eq4[:, 0:1]),
          reads=['eq4'], writes=['oh4'])
        E('dve', lambda e: e.tensor_scalar(noty[:, :], eq4[:, 0:1], -1.0, 1.0,
                                           ALU.mult, ALU.add),
          reads=['eq4'], writes=['noty'])
        for ee in range(1, NE):
            E('dve', lambda e, ee=ee: e.tensor_tensor(
                oh4[:, ee:ee+1], eq4[:, ee:ee+1], noty[:, :], ALU.mult),
              reads=['eq4', 'noty'], writes=['oh4'])
            if ee < NE - 1:
                E('dve', lambda e, ee=ee: e.tensor_scalar(
                    junk1[:, :], oh4[:, ee:ee+1], -1.0, 1.0, ALU.mult, ALU.add),
                  reads=['oh4'], writes=['junk1'])
                E('dve', lambda e: e.tensor_tensor(noty[:, :], noty[:, :],
                                                   junk1[:, :], ALU.mult),
                  reads=['noty', 'junk1'], writes=['noty'])
        E('dve', lambda e: e.tensor_copy(ohbf[:, :], oh4[:, :]),
          reads=['oh4'], writes=['ohbf'])
        E = mkE('as3')
        E('pe', lambda e: e.matmul(ps_tT[1][0:NE, 0:128], ohbf[:, :], id_bf[:, :],
                                   start=True, stop=True),
          reads=['ohbf', 'id_bf'], writes=[('ps_tT', 1)])
        E('dve', lambda e, tl=tl, sgp=sgp: e.tensor_copy(
            onehT2[sgp][:, tl*128:(tl+1)*128], ps_tT[1][0:NE, 0:128]),
          reads=[('ps_tT', 1)], writes=[('onehT', sgp, tl)])
        for ee in range(NE):
            E('dve', lambda e, ee=ee, tl=tl: e.tensor_scalar(
                diag8[:, tl, ee*128:(ee+1)*128], id_bf[:, :], oh4[:, ee:ee+1], None,
                ALU.mult),
              reads=['id_bf', 'oh4'], writes=[('diag8', tl, ee)])

    def emit_B_tile(sg, tl):
        E = mkE('as3')
        for c in range(8):
            bank = ps_z[c % 2]
            bkk = ('ps_z', c % 2)
            E('pe', lambda e, c=c, tl=tl, bank=bank: e.matmul(
                bank[:, :], t1b8[:, tl, c*128:(c+1)*128], diag8[:, tl, :],
                start=True, stop=True),
              reads=[('t1b8', tl)] + [('diag8', tl, ee) for ee in range(NE)],
              writes=[bkk])
            E('dve', lambda e, c=c, tl=tl, bank=bank: e.tensor_copy(
                t1eT[:, c, :, tl*128:(tl+1)*128],
                bank.ap().rearrange("p (e f) -> p e f", e=NE)),
              reads=[bkk], writes=[('t1eT', c, tl)])

    t1e_keys = [('t1eT', c, tl) for c in range(8) for tl in range(8)]

    def emit_h(sg, hook=None):
        sgp = sg % 2
        E = mkE('hp')
        oh_keys = [('onehT', sgp, tl) for tl in range(8)]
        for ifc in range(32):
            wsl = w1sl[ifc % 2]
            wk = ('w1sl', ifc % 2)
            E('pool', lambda e, ifc=ifc, wsl=wsl: e.dma_start(
                wsl.ap(), w1r_d[ifc].rearrange("kc e p f -> p kc e f")),
              writes=[wk], dma=f'w1_{ifc % 2}')
            for hg in range(2):
                bank = ps_z[hg]
                bkk = ('ps_z', hg)
                def hmm(e, ifc=ifc, hg=hg, wsl=wsl, bank=bank, sgp=sgp):
                    first = True
                    for ee in range(NE):
                        for k in range(8):
                            e.matmul(bank[:, :], wsl[:, k, ee, :],
                                     t1eT[:, k, ee, hg*512:(hg+1)*512],
                                     start=first, stop=False)
                            first = False
                    return e.matmul(bank[:, :], b1_sb[:, ifc*128:(ifc+1)*128],
                                    onehT2[sgp][:, hg*512:(hg+1)*512],
                                    start=False, stop=True)
                E('pe', hmm, reads=[wk, 'b1_sb'] + t1e_keys + oh_keys, writes=[bkk])
                ist = inter_st[(ifc * 2 + hg) % 2]
                istk = ('ist', (ifc * 2 + hg) % 2)
                E('act', lambda e, bank=bank, ist=ist: e.activation(
                    ist[:, :], bank[:, :], AF.Gelu),
                  reads=[bkk], writes=[istk])
                E('sp', lambda e, ifc=ifc, hg=hg, ist=ist: e.dma_start(
                    intT_d[ifc, :, hg*512:(hg+1)*512], ist[:, :]),
                  reads=[istk], writes=[('intd', ifc, hg)], dma='intw')
            if hook is not None:
                hook(ifc)

    def emit_wo(sg):
        sgt0 = sg * 8
        E = mkE('wo')
        wo_banks = [ps_ab[0][:, 0:512], ps_ab[0][:, 512:1024],
                    ps_ab[1][:, 0:512], ps_ab[1][:, 512:1024],
                    ps_z[0][:, :], ps_z[1][:, :], ps_tT[0][:, :], ps_tT[1][:, :]]
        wo_keys = [('ps_ab', 0, 0), ('ps_ab', 0, 1), ('ps_ab', 1, 0), ('ps_ab', 1, 1),
                   ('ps_z', 0), ('ps_z', 1), ('ps_tT', 0), ('ps_tT', 1)]
        for hb in range(2):
            for ifc in range(32):
                wc = wofc[ifc % 2]
                wck = ('wofc', ifc % 2)
                E('pool', lambda e, ifc=ifc, wc=wc: e.dma_start(
                    wc.ap(), wof_d[ifc*128:(ifc+1)*128, :]),
                  writes=[wck], dma=f'wo{ifc % 2}')
                ic = intc[ifc % 2]
                ick = ('intc', ifc % 2)
                E('sp', lambda e, ifc=ifc, hb=hb, ic=ic: e.dma_start(
                    ic.ap(), intT_d[ifc, :, hb*512:(hb+1)*512]),
                  reads=[('intd', ifc, hb)], writes=[ick], dma=f'ir{ifc % 2}')
                for t4 in range(4):
                    tl = hb * 4 + t4
                    t = sgt0 + tl
                    pa, pb = wo_banks[2*t4], wo_banks[2*t4+1]
                    pak, pbk = wo_keys[2*t4], wo_keys[2*t4+1]
                    E('pe', lambda e, ifc=ifc, t4=t4, wc=wc, ic=ic, pa=pa: e.matmul(
                        pa, ic[:, t4*128:(t4+1)*128], wc[:, 0:512],
                        start=(ifc == 0), stop=False),
                      reads=[wck, ick], writes=[pak])
                    E('pe', lambda e, ifc=ifc, t4=t4, wc=wc, ic=ic, pb=pb: e.matmul(
                        pb, ic[:, t4*128:(t4+1)*128], wc[:, 512:1024],
                        start=(ifc == 0), stop=False),
                      reads=[wck, ick], writes=[pbk])
            for t4 in range(4):
                pa, pb = wo_banks[2*t4], wo_banks[2*t4+1]
                pak, pbk = wo_keys[2*t4], wo_keys[2*t4+1]
                E('pe', lambda e, pa=pa: e.matmul(pa, ones_row[0:1, 0:128],
                                                  bof_sb[:, 0:512], start=False, stop=True),
                  reads=['ones_row', 'bof_sb'], writes=[pak])
                E('pe', lambda e, pb=pb: e.matmul(pb, ones_row[0:1, 0:128],
                                                  bof_sb[:, 512:1024], start=False, stop=True),
                  reads=['ones_row', 'bof_sb'], writes=[pbk])
            for t4 in range(4):
                tl = hb * 4 + t4
                t = sgt0 + tl
                pa, pb = wo_banks[2*t4], wo_banks[2*t4+1]
                pak, pbk = wo_keys[2*t4], wo_keys[2*t4+1]
                xi = xq_in[t4 % 2]
                xik = ('xq_in', t4 % 2)
                E('sp', lambda e, t=t, xi=xi: e.dma_start(xi.ap(), t1s_d[t*128:(t+1)*128, :]),
                  reads=[('t1s', t)], writes=[xik], dma=f'xq{t4 % 2}')
                E('dve', lambda e, pa=pa, xi=xi: e.tensor_tensor(
                    t2_t[:, 0:512], pa, xi.ap()[:, 0:512], ALU.add),
                  reads=[pak, xik], writes=['T2'])
                E('dve', lambda e, pb=pb, xi=xi: e.tensor_tensor(
                    t2_t[:, 512:1024], pb, xi.ap()[:, 512:1024], ALU.add),
                  reads=[pbk, xik], writes=['T2'])
                layernorm(t2_t[:, :], ['T2'], t3_t[:, :], 'T3', 128, epsB, 'epsB',
                          sts[2], 'sC')
                E('dve', lambda e: e.tensor_tensor(t3_t[:, :], t3_t[:, :], gam_b[:, :],
                                                   ALU.mult),
                  reads=['T3', 'gam_b'], writes=['T3'])
                E('dve', lambda e: e.tensor_tensor(t3_t[:, :], t3_t[:, :], bet_b[:, :],
                                                   ALU.add),
                  reads=['T3', 'bet_b'], writes=['T3'])
                layernorm(t3_t[:, :], ['T3'], t2_t[:, :], 'T2', 128, epsA, 'epsA',
                          sts[3], 'sD')
                E('sp', lambda e, t=t, xi=xi: e.dma_start(xi.ap(), y1s_d[t*128:(t+1)*128, :]),
                  reads=[('y1s', t)], writes=[xik], dma=f'xq{t4 % 2}')
                E('dve', lambda e, xi=xi: e.tensor_tensor(t3_t[:, :], t2_t[:, :],
                                                          xi.ap(), ALU.add),
                  reads=['T2', xik], writes=['T3'])
                E('sp', lambda e, t=t: e.dma_start(out_d[t*128:(t+1)*128, :], t3_t[:, :]),
                  reads=['T3'], dma='out')

    for tl in range(8):
        emit_A_tile(0, tl)
    for tl in range(8):
        emit_B_tile(0, tl)
    for sg in range(4):
        def hook(ifc, sg=sg):
            if sg < 3 and ifc >= 3 and ifc % 3 == 0 and ifc <= 24:
                emit_A_tile(sg + 1, ifc // 3 - 1)
        emit_h(sg, hook)
        if sg < 3:
            for tl in range(8):
                emit_B_tile(sg + 1, tl)
        emit_wo(sg)

    E = _E
    if debug:
        p.wait_all_dma('pool', 'dbg')
    p.wait_all_dma('pool', 'out')
    p.build()
    p.close()
    phc.close()
    sbstack.close()
    return nc


_CACHE = {}


def _host_inputs(inputs):
    x = np.asarray(inputs['L0_attn_x'], np.float32)
    wqT = np.ascontiguousarray(np.asarray(inputs['L0_attn_Wq'], np.float32).T)
    wkT = np.ascontiguousarray(np.asarray(inputs['L0_attn_Wk'], np.float32).T)
    wvT = np.ascontiguousarray(np.asarray(inputs['L0_attn_Wv'], np.float32).T)
    woT = np.ascontiguousarray(np.asarray(inputs['L0_attn_Wo'], np.float32).T)
    bq = np.ascontiguousarray(np.asarray(inputs['L0_attn_bq'], np.float32)[None, :])
    bk = np.ascontiguousarray(np.asarray(inputs['L0_attn_bk'], np.float32)[None, :])
    bv = np.ascontiguousarray(np.asarray(inputs['L0_attn_bv'], np.float32)[None, :])
    bo = np.ascontiguousarray(np.asarray(inputs['L0_attn_bo'], np.float32)[None, :])
    idxk = np.ascontiguousarray(np.asarray(inputs['L0_attn_idx_k'], np.int32))
    Wr = np.ascontiguousarray(np.asarray(inputs['L0_ffn_W_router'], np.float32))
    W1 = np.asarray(inputs['L0_ffn_W1'], np.float32)
    w1r = np.ascontiguousarray(W1.reshape(NE, 8, 128, 32, 128).transpose(3, 1, 0, 2, 4))
    b1r = np.ascontiguousarray(np.asarray(inputs['L0_ffn_b1'], np.float32)
                               .reshape(NE, 32 * 128))
    Wof = np.ascontiguousarray(np.asarray(inputs['L0_ffn_Wo'], np.float32))
    bof = np.ascontiguousarray(np.asarray(inputs['L0_ffn_bo'], np.float32)[None, :])
    gam = np.ascontiguousarray(np.asarray(inputs['L0_ffn_ln_gamma'], np.float32)[None, :])
    bet = np.ascontiguousarray(np.asarray(inputs['L0_ffn_ln_beta'], np.float32)[None, :])
    ident = np.eye(128, dtype=np.float32)
    iotacol = np.tile(np.arange(128, dtype=np.float32)[None, :], (128, 1))
    iotaM = np.empty((128, H * 128), np.float32)
    for h in range(H):
        iotaM[:, h*128:(h+1)*128] = (np.arange(128)[None, :] * 128
                                     + np.arange(128)[:, None])
    xbs = [np.ascontiguousarray(x[0]), np.ascontiguousarray(x[1])]
    in_maps = []
    for c in range(8):
        b, q = c // 4, c % 4
        in_maps.append({
            "x": xbs[b],
            "xq": np.ascontiguousarray(xbs[b][q*QTOK:(q+1)*QTOK]),
            "wqT": wqT, "wkT": wkT, "wvT": wvT, "woT": woT,
            "bq": bq, "bk": bk, "bv": bv, "bo": bo,
            "idxk": idxk, "wr": Wr, "w1r": w1r, "b1r": b1r,
            "wof": Wof, "bof": bof, "gam": gam, "bet": bet,
            "ident": ident, "iotacol": iotacol, "iotaM": iotaM,
            "qoffc": np.full((128, 1), q * QTOK, np.float32),
        })
    return in_maps


def kernel(**inputs):
    if 'nc' not in _CACHE:
        _CACHE['nc'] = build_program(debug=False)
    nc = _CACHE['nc']
    in_maps = _host_inputs(inputs)
    res = run_bass_kernel_spmd(nc, in_maps, core_ids=list(range(8)))
    out = np.empty((2, L, D), np.float32)
    for c in range(8):
        b, q = c // 4, c % 4
        out[b, q*QTOK:(q+1)*QTOK] = res.results[c]["y"]
    return out


# version 12
# speedup vs baseline: 1.2400x; 1.0734x over previous
import contextlib
import math

import numpy as np

import concourse.bass as bass
import concourse.mybir as mybir
from concourse.bass import IndirectOffsetOnAxis
from concourse.bass_utils import run_bass_kernel_spmd

F32 = mybir.dt.float32
BF16 = mybir.dt.bfloat16
I32 = mybir.dt.int32
U8 = mybir.dt.uint8
AF = mybir.ActivationFunctionType
ALU = mybir.AluOpType
AX = mybir.AxisListType

D = 1024
L = 16384
H = 16
DK = 64
U = 50
NG = 32
QTOK = 4096
NE = 4
DI = 4096
BISECT = 20
SCALE = 1.0 / math.sqrt(DK)


class Prog:
    ENGINES = ('pe', 'dve', 'act', 'pool', 'sp')

    def __init__(self, nc):
        self.nc = nc
        self.stack = contextlib.ExitStack()
        self.csem = {}
        self.ccount = {}
        for e in ('pe', 'dve', 'act', 'pool'):
            self.csem[e] = self.stack.enter_context(nc.semaphore(f"c_{e}"))
            self.ccount[e] = 0
        self.dsem = {}
        self.dcount = {}
        self.dqueue = {}
        self.ops = {e: [] for e in self.ENGINES}
        self.last_w = {}
        self.readers = {}
        self.know = {e: {} for e in self.ENGINES}
        self.read_misses = set()

    def chan(self, name, engine):
        if name not in self.dsem:
            self.dsem[name] = self.stack.enter_context(self.nc.semaphore(f"d_{name}"))
            self.dcount[name] = 0
            self.dqueue[name] = engine
        assert self.dqueue[name] == engine, (name, engine, self.dqueue[name])
        return name

    def _resolve(self, dep):
        if dep[0] == 'c':
            _, e, v = dep
            return (f"c_{e}", self.csem[e], v)
        _, ch = dep[:2]
        return (f"d_{ch}", self.dsem[ch], 16 * self.dcount[ch])

    def emit(self, engine, fn, reads=(), writes=(), dma=None, tag=None):
        deps = []
        for k in reads:
            w = self.last_w.get(k)
            if w is None:
                self.read_misses.add(k)
            else:
                deps.append(w)
        for k in writes:
            w = self.last_w.get(k)
            if w is not None:
                deps.append(w)
            deps.extend(self.readers.get(k, ()))
        need = {}
        for dep in deps:
            if dep[0] == 'c' and dep[1] == 'pe' and engine == 'pe' and dma is None:
                continue
            name, handle, v = self._resolve(dep)
            if self.know[engine].get(name, -1) >= v:
                continue
            if name in need and need[name][1] >= v:
                continue
            need[name] = (handle, v)
        waits = []
        for name, (handle, v) in need.items():
            waits.append((handle, v))
            self.know[engine][name] = v

        if dma is not None:
            ch = self.chan(dma, engine)
            self.dcount[ch] += 1
            inc = (self.dsem[ch], 16)
            marker = ('d', ch, self.dcount[ch])
        elif engine == 'sp':
            inc = None
            marker = None
        else:
            self.ccount[engine] += 1
            inc = (self.csem[engine], 1)
            marker = ('c', engine, self.ccount[engine])

        self.ops[engine].append((waits, fn, inc, tag))
        if marker is not None:
            for k in writes:
                self.last_w[k] = marker
                self.readers[k] = []
            for k in reads:
                self.readers.setdefault(k, []).append(marker)
        return marker

    def wait_all_dma(self, engine, channel):
        if channel not in self.dsem:
            return
        self.ops[engine].append(
            ([(self.dsem[channel], 16 * self.dcount[channel])], None, None, 'wd'))
        self.know[engine][f"d_{channel}"] = 16 * self.dcount[channel]

    def fence(self):
        targets = []
        for e in ('pe', 'dve', 'act', 'pool'):
            if self.ccount[e]:
                targets.append((f"c_{e}", self.csem[e], self.ccount[e]))
        for ch in self.dsem:
            if self.dcount[ch]:
                targets.append((f"d_{ch}", self.dsem[ch], 16 * self.dcount[ch]))
        for e in self.ENGINES:
            ws = []
            for name, handle, v in targets:
                if name == f"c_{e}":
                    continue
                if self.know[e].get(name, -1) >= v:
                    continue
                ws.append((handle, v))
                self.know[e][name] = v
            if ws:
                self.ops[e].append((ws, None, None, 'fence'))

    def build(self):
        nc = self.nc
        if self.read_misses:
            import sys
            print("PROG read-before-write keys:", sorted(map(str, self.read_misses)),
                  file=sys.stderr)
        with nc.Block() as blk:
            def replay(ename):
                def _go(eng):
                    for waits, fn, inc, tag in self.ops[ename]:
                        for handle, v in waits:
                            eng.wait_ge(handle, v)
                        if fn is not None:
                            inst = fn(eng)
                            if inc is not None:
                                inst.then_inc(inc[0], inc[1])
                return _go
            if self.ops['pe']: blk.tensor(replay('pe'))
            if self.ops['dve']: blk.vector(replay('dve'))
            if self.ops['act']: blk.scalar(replay('act'))
            if self.ops['pool']: blk.gpsimd(replay('pool'))
            if self.ops['sp']: blk.sync(replay('sp'))

    def close(self):
        self.stack.close()


def build_program(debug=False, maxphase=9, ngroups=NG, nbisect=BISECT, parts='all'):
    nc = bass.Bass("TRN2", target_bir_lowering=False)

    def din(name, shape, dt=F32):
        return nc.dram_tensor(name, shape, dt, kind="ExternalInput").ap()
    x_d = din("x", [L, D])
    xq_d = din("xq", [QTOK, D])
    wqT_d = din("wqT", [D, D]); wkT_d = din("wkT", [D, D]); wvT_d = din("wvT", [D, D])
    woT_d = din("woT", [D, D])
    bq_d = din("bq", [1, D]); bk_d = din("bk", [1, D]); bv_d = din("bv", [1, D])
    bo_d = din("bo", [1, D])
    idxk_d = din("idxk", [U], I32)
    wr_d = din("wr", [D, NE])
    w1r_d = din("w1r", [32, 8, NE, 128, 128])
    b1r_d = din("b1r", [NE, 32 * 128])
    wof_d = din("wof", [DI, D])
    bof_d = din("bof", [1, D]); gam_d = din("gam", [1, D]); bet_d = din("bet", [1, D])
    ident_d = din("ident", [128, 128])
    iotacol_d = din("iotacol", [128, 128])
    iotaM_d = din("iotaM", [128, H * 128])
    qoff_d = din("qoffc", [128, 1])
    out_d = nc.dram_tensor("y", [QTOK, D], F32, kind="ExternalOutput").ap()
    dbg = {}
    if debug:
        def dout(name, shape, dt=F32):
            dbg[name] = nc.dram_tensor(name, shape, dt, kind="ExternalOutput").ap()
        dout("dbg_M", [128, H * 128]); dout("dbg_idx", [64, H])
        dout("dbg_vmean", [1, D]); dout("dbg_base", [1, D])
        dout("dbg_delta", [128, D]); dout("dbg_y1", [128, D])
        dout("dbg_t1", [128, D]); dout("dbg_lo", [128, H])

    kfm_d = nc.dram_tensor("kfm", [8, 128, L], BF16, kind="Internal").ap()
    vscr_d = nc.dram_tensor("vscr", [8, 128, 128, 128], BF16, kind="Internal").ap()
    t1s_d = nc.dram_tensor("t1s", [QTOK, D], F32, kind="Internal").ap()
    intT_d = nc.dram_tensor("intT", [32, 128, 1024], BF16, kind="Internal").ap()
    y1s_d = nc.dram_tensor("y1s", [QTOK, D], F32, kind="Internal").ap()

    p = Prog(nc)
    _E = p.emit
    def mkE(label):
        def E(*a, **k):
            if parts == 'all' or label in parts:
                return _E(*a, **k)
        return E
    E = _E
    sbstack = contextlib.ExitStack()

    def sbuf(stk, n, s, d=F32):
        return stk.enter_context(nc.sbuf_tensor(n, s, d))

    def psum(stk, n, s, d=F32):
        return stk.enter_context(nc.psum_tensor(n, s, d))

    perm = sbstack
    id_bf = sbuf(perm, "id_bf_s", [128, 128], BF16)
    idf32 = sbuf(perm, "idf32_s", [128, 128], F32)
    iotacol = sbuf(perm, "iotacol_s", [128, 128], F32)
    qoffc = sbuf(perm, "qoffc_s", [128, 1], F32)
    ones_row = sbuf(perm, "ones_row_s", [1, 512], BF16)
    ones_col = sbuf(perm, "ones_col_s", [128, 1], BF16)
    ones_sq = sbuf(perm, "ones_sq_s", [128, 128], BF16)
    one1 = sbuf(perm, "one1_s", [1, 1], BF16)
    epsA = sbuf(perm, "epsA_s", [128, 1], F32)
    epsB = sbuf(perm, "epsB_s", [128, 1], F32)
    negb = sbuf(perm, "negb_s", [128, 1], F32)
    ksampT = sbuf(perm, "ksampT_s", [128, 8, 104], BF16)
    xbar_bf = sbuf(perm, "xbar_bf_s", [1, D], BF16)
    xbarT = sbuf(perm, "xbarT_s", [128, 8], BF16)
    vmean = sbuf(perm, "vmean_s", [1, D], F32)
    vmean_bf = sbuf(perm, "vmean_bf_s", [1, D], BF16)
    vmeanT = sbuf(perm, "vmeanT_s", [128, 8], F32)
    vmeanT_bf = sbuf(perm, "vmeanT_bf_s", [128, 8], BF16)
    base_bf = sbuf(perm, "base_bf_s", [1, D], BF16)
    idxT_f = sbuf(perm, "idxT_f_s", [64, H], F32)
    idxT_i = sbuf(perm, "idxT_i_s", [64, H], I32)
    delta = [sbuf(perm, f"delta{c}", [128, D], BF16) for c in range(8)]
    idxv = [sbuf(perm, f"idxv{c}", [128, 1], F32) for c in range(8)]
    junk1 = sbuf(perm, "junk1_s", [128, 1], F32)

    E('pool', lambda e: e.dma_start(id_bf[:, :], ident_d), writes=['id_bf'], dma='c_idbf')
    E('sp', lambda e: e.dma_start(idf32[:, :], ident_d), writes=['idf32'], dma='c_idf')
    E('sp', lambda e: e.dma_start(iotacol[:, :], iotacol_d), writes=['iotacol'], dma='c_ioc')
    E('sp', lambda e: e.dma_start(qoffc[:, :], qoff_d), writes=['qoffc'], dma='c_qof')
    E('dve', lambda e: e.memset(ones_row[:, :], 1.0), writes=['ones_row'])
    E('dve', lambda e: e.memset(ones_col[:, :], 1.0), writes=['ones_col'])
    E('dve', lambda e: e.memset(ones_sq[:, :], 1.0), writes=['ones_sq'])
    E('dve', lambda e: e.memset(one1[:, :], 1.0), writes=['one1'])
    E('dve', lambda e: e.memset(epsA[:, :], 1e-5), writes=['epsA'])
    E('dve', lambda e: e.memset(epsB[:, :], 1e-12), writes=['epsB'])
    E('dve', lambda e: e.memset(negb[:, :], -12.0), writes=['negb'])

    def layernorm(src_ap, src_keys, dst_ap, dst_key, rows, eps_tile, eps_key,
                  stats, skey, lnexp=False):
        rd = list(src_keys)
        E('dve', lambda e: e.reduce_sum(stats[0:rows, 0:1], src_ap, axis=AX.X),
          reads=rd, writes=[(skey, 0)])
        E('act', lambda e: e.activation(dst_ap, src_ap, AF.Square,
                                        accum_out=stats[0:rows, 1:2]),
          reads=rd, writes=[dst_key, (skey, 1)])
        E('dve', lambda e: e.tensor_scalar(stats[0:rows, 0:1], stats[0:rows, 0:1],
                                           1.0 / D, None, ALU.mult),
          reads=[(skey, 0)], writes=[(skey, 0)])
        E('dve', lambda e: e.tensor_scalar(stats[0:rows, 1:2], stats[0:rows, 1:2],
                                           1.0 / D, None, ALU.mult),
          reads=[(skey, 1)], writes=[(skey, 1)])
        E('dve', lambda e: e.tensor_tensor(stats[0:rows, 2:3], stats[0:rows, 0:1],
                                           stats[0:rows, 0:1], ALU.mult),
          reads=[(skey, 0)], writes=[(skey, 2)])
        E('dve', lambda e: e.tensor_tensor(stats[0:rows, 2:3], stats[0:rows, 1:2],
                                           stats[0:rows, 2:3], ALU.subtract),
          reads=[(skey, 1), (skey, 2)], writes=[(skey, 2)])
        if lnexp:
            E('act', lambda e: e.activation(stats[0:rows, 3:4], stats[0:rows, 2:3],
                                            AF.Ln, bias=eps_tile[0:rows, 0:1]),
              reads=[(skey, 2), eps_key], writes=[(skey, 3)])
            E('dve', lambda e: e.tensor_scalar(stats[0:rows, 3:4], stats[0:rows, 3:4],
                                               -0.5, None, ALU.mult),
              reads=[(skey, 3)], writes=[(skey, 3)])
            E('act', lambda e: e.activation(stats[0:rows, 3:4], stats[0:rows, 3:4],
                                            AF.Exp),
              reads=[(skey, 3)], writes=[(skey, 3)])
        else:
            E('act', lambda e: e.activation(stats[0:rows, 3:4], stats[0:rows, 2:3],
                                            AF.Sqrt, bias=eps_tile[0:rows, 0:1]),
              reads=[(skey, 2), eps_key], writes=[(skey, 3)])
            E('dve', lambda e: e.reciprocal(stats[0:rows, 3:4], stats[0:rows, 3:4]),
              reads=[(skey, 3)], writes=[(skey, 3)])
        E('dve', lambda e: e.tensor_scalar(dst_ap, src_ap, stats[0:rows, 0:1],
                                           stats[0:rows, 3:4], ALU.subtract, ALU.mult),
          reads=rd + [(skey, 0), (skey, 3)], writes=[dst_key])

    pha = contextlib.ExitStack()
    st_w = contextlib.ExitStack()
    st_m = contextlib.ExitStack()
    bq_sb = sbuf(perm, "bq_sb_s", [1, D], BF16)
    bo_sb = sbuf(perm, "bo_sb_s", [1, D], BF16)
    wqT = sbuf(st_w, "wqT_s", [128, 8, D], BF16)
    woT_sb = sbuf(st_w, "woT_sb_s", [128, 8, D], BF16)
    M_tm = sbuf(st_m, "M_tm_s", [128, H * 128], F32)
    lo_t = sbuf(st_m, "lo_t_s", [128, H], F32)
    wkT = sbuf(pha, "wkT_s", [128, 8, D], BF16)
    wvT = sbuf(pha, "wvT_s", [128, 8, D], BF16)
    bk_sb = sbuf(pha, "bk_sb_s", [1, D], BF16)
    bv_sb = sbuf(pha, "bv_sb_s", [1, D], BF16)
    E('pool', lambda e: e.dma_start(wqT.ap(), wqT_d.rearrange("(kc p) f -> p kc f", p=128)),
      writes=['wqT'], dma='c_wq')
    E('pool', lambda e: e.dma_start(wkT.ap(), wkT_d.rearrange("(kc p) f -> p kc f", p=128)),
      writes=['wkT'], dma='c_wk')
    E('pool', lambda e: e.dma_start(wvT.ap(), wvT_d.rearrange("(kc p) f -> p kc f", p=128)),
      writes=['wvT'], dma='c_wv')
    E('pool', lambda e: e.dma_start(woT_sb.ap(), woT_d.rearrange("(kc p) f -> p kc f", p=128)),
      writes=['woT_sb'], dma='c_wo')
    E('pool', lambda e: e.dma_start(bq_sb[:, :], bq_d), writes=['bq_sb'], dma='c_bq')
    E('pool', lambda e: e.dma_start(bk_sb[:, :], bk_d), writes=['bk_sb'], dma='c_bk')
    E('pool', lambda e: e.dma_start(bv_sb[:, :], bv_d), writes=['bv_sb'], dma='c_bv')
    E('pool', lambda e: e.dma_start(bo_sb[:, :], bo_d), writes=['bo_sb'], dma='c_bo')

    ps_tr = [psum(pha, f"ps_tr{i}", [128, 512], F32) for i in range(2)]
    ps_qk = [psum(pha, f"ps_qk{i}", [128, 512], F32) for i in range(3)]
    ps_sc = psum(pha, "ps_sc_s", [128, 512], F32)
    ps_xb0 = psum(pha, "ps_xb0_s", [1, 512], F32)
    ps_xb1 = psum(pha, "ps_xb1_s", [1, 512], F32)
    pp_pool = [ps_qk[0], ps_qk[1], ps_qk[2], ps_sc]
    pp_ctr = [0]
    def next_pp():
        i = pp_ctr[0] % 4
        pp_ctr[0] += 1
        return pp_pool[i], ('pp', i)

    E = mkE('p0b')
    idxk_sb = sbuf(pha, "idxk_sb_s", [U, 1], I32)
    xs_f = sbuf(pha, "xs_f_s", [U, D], F32)
    xs_bf = sbuf(pha, "xs_bf_s", [128, D], BF16)
    xs_T = sbuf(pha, "xs_T_s", [128, 8, 64], BF16)
    kbar = sbuf(pha, "kbar_s", [128, 1], F32)
    st0 = sbuf(pha, "st0_s", [128, 4], F32)

    E('sp', lambda e: e.dma_start(idxk_sb[:, :], idxk_d.rearrange("(a b) -> a b", b=1)),
      writes=['idxk_sb'], dma='c_idk')
    E('dve', lambda e: e.memset(ksampT.ap(), 0.0), writes=['ks_init'])
    E('pool', lambda e: e.indirect_dma_start(
        xs_f[:, :], None, x_d, IndirectOffsetOnAxis(idxk_sb[:, :], 0)),
      reads=['idxk_sb'], writes=['xs_f'], dma='g0')
    E('dve', lambda e: e.memset(xs_bf[:, :], 0.0), writes=['xs_bf'])
    layernorm(xs_f[:, :], ['xs_f'], xs_bf[0:U, :], 'xs_bf', U, epsA, 'epsA', st0, 'st0')
    for c in range(8):
        bank = ps_tr[c % 2]
        bkk = ('ps_tr', c % 2)
        E('pe', lambda e, c=c, bank=bank: e.matmul(
            bank[:, 0:128], xs_bf[:, c*128:(c+1)*128], id_bf[:, :],
            start=True, stop=True),
          reads=['xs_bf', 'id_bf'], writes=[bkk])
        E('dve', lambda e, c=c, bank=bank: e.tensor_copy(xs_T[:, c, :], bank[:, 0:64]),
          reads=[bkk], writes=[('xs_T', c)])
    xsT_keys = [('xs_T', c) for c in range(8)]
    for h in range(H):
        pr, hi = h // 2, h % 2
        bank, bkk = next_pp()
        def ksamp_mm(e, h=h, bank=bank):
            for k in range(8):
                e.matmul(bank[0:64, 0:64], wkT[:, k, h*64:(h+1)*64], xs_T[:, k, :],
                         start=(k == 0), stop=False)
            return e.matmul(bank[0:64, 0:64], bk_sb[:, h*64:(h+1)*64],
                            ones_row[:, 0:64], start=False, stop=True)
        E('pe', ksamp_mm, reads=xsT_keys + ['wkT', 'bk_sb', 'ones_row'],
          writes=[bkk])
        E('dve', lambda e, pr=pr, hi=hi, bank=bank: e.tensor_copy(
            ksampT[hi*64:(hi+1)*64, pr, hi*52:hi*52+50], bank[0:64, 0:50]),
          reads=[bkk, 'ks_init'], writes=[('ksampT', pr, hi)])
        E('dve', lambda e, bank=bank: e.reduce_sum(kbar[0:64, :], bank[0:64, 0:50], axis=AX.X),
          reads=[bkk], writes=['kbar'])
        E('dve', lambda e, pr=pr, hi=hi: e.tensor_scalar(
            ksampT[hi*64:(hi+1)*64, pr, hi*52+50:hi*52+51], kbar[0:64, :],
            1.0 / U, None, ALU.mult),
          reads=['kbar', 'ks_init'], writes=[('ksampT', pr, hi, 'bar')])
    ks_keys = [('ksampT', pr, hi) for pr in range(8) for hi in range(2)] + \
              [('ksampT', pr, hi, 'bar') for pr in range(8) for hi in range(2)] + ['ks_init']

    x_in = [sbuf(pha, f"x_in{i}", [128, 4, D], F32) for i in range(2)]
    xln2 = [sbuf(pha, f"xln{i}_s", [128, 4, D], BF16) for i in range(2)]
    xT = sbuf(pha, "xT_s", [128, 8, 512], BF16)
    qt = sbuf(pha, "qt_s", [128, 8, 512], BF16)
    ktile = [sbuf(pha, f"ktile{i}", [128, 512], BF16) for i in range(2)]
    vt = sbuf(pha, "vt_s", [128, 4, D], BF16)
    st1 = [[sbuf(pha, f"st1_{i}_{s}", [128, 4], F32) for s in range(4)] for i in range(2)]
    mtmp = sbuf(pha, "mtmp_s", [128, 8], F32)

    E = _E

    def emit_x_dma(g):
        gp = g % 2
        xi = x_in[gp]
        _E('sp', lambda e, g=g, xi=xi: e.dma_start(
            xi.ap(), x_d[g*512:(g+1)*512, :].rearrange("(s p) f -> p s f", p=128)),
          writes=[('x_in', gp)], dma=f'x{gp}')

    def emit_ln_slice(g, s):
        gp = g % 2
        layernorm(x_in[gp][:, s, :], [('x_in', gp)], xln2[gp][:, s, :],
                  ('xln', gp, s), 128, epsA, 'epsA', st1[gp][s], ('st1', gp, s))

    emit_x_dma(0)
    for s in range(4):
        emit_ln_slice(0, s)
    for g in range(ngroups):
        gp = g % 2
        xln = xln2[gp]
        if g + 1 < ngroups:
            emit_x_dma(g + 1)
        E = mkE('tr')
        for s in range(4):
            for cb in range(2):
                bank = ps_tr[cb]
                bkk = ('ps_tr', cb)
                for j in range(4):
                    c = cb * 4 + j
                    E('pe', lambda e, s=s, c=c, j=j, bank=bank, xln=xln: e.matmul(
                        bank[:, j*128:(j+1)*128], xln[:, s, c*128:(c+1)*128],
                        id_bf[:, :], start=True, stop=True),
                      reads=[('xln', gp, s), 'id_bf'], writes=[bkk])
                E('dve', lambda e, s=s, cb=cb, bank=bank: e.tensor_copy(
                    xT.ap()[:, cb*4:(cb+1)*4, s*128:(s+1)*128],
                    bank.ap().rearrange("p (c f) -> p c f", c=4)),
                  reads=[bkk], writes=[('xT', s, cb)])
        xt_keys = [('xT', s, cb) for s in range(4) for cb in range(2)]
        E = mkE('q')
        for f in range(8):
            bank, bkk = next_pp()
            def qmm(e, f=f, bank=bank):
                for k in range(8):
                    e.matmul(bank[:, :], wqT[:, k, f*128:(f+1)*128], xT[:, k, :],
                             start=(k == 0), stop=False)
                return e.matmul(bank[:, :], bq_sb[:, f*128:(f+1)*128],
                                ones_row[:, :], start=False, stop=True)
            E('pe', qmm, reads=xt_keys + ['wqT', 'bq_sb', 'ones_row'], writes=[bkk])
            E('dve', lambda e, f=f, bank=bank: e.tensor_copy(qt[:, f, :], bank[:, :]),
              reads=[bkk], writes=[('qt', f)])
            if f in (2, 6) and g + 1 < ngroups:
                emit_ln_slice(g + 1, f // 4)
        E = mkE('k')
        for f in range(8):
            kt = ktile[f % 2]
            ktk = ('ktile', f % 2)
            bank, bkk = next_pp()
            def kmm(e, f=f, bank=bank):
                for k in range(8):
                    e.matmul(bank[:, :], wkT[:, k, f*128:(f+1)*128], xT[:, k, :],
                             start=(k == 0), stop=False)
                return e.matmul(bank[:, :], bk_sb[:, f*128:(f+1)*128],
                                ones_row[:, :], start=False, stop=True)
            E('pe', kmm, reads=xt_keys + ['wkT', 'bk_sb', 'ones_row'], writes=[bkk])
            E('dve', lambda e, kt=kt, bank=bank: e.tensor_copy(kt[:, :], bank[:, :]),
              reads=[bkk], writes=[ktk])
            E('sp', lambda e, g=g, f=f, kt=kt: e.dma_start(
                kfm_d[f, :, g*512:(g+1)*512], kt[:, :]),
              reads=[ktk], writes=['kfm_w'], dma='kw')
            if f in (2, 6) and g + 1 < ngroups:
                emit_ln_slice(g + 1, 2 + f // 4)
        E = mkE('v')
        for s in range(4):
            for hf in range(2):
                bank, bkk = next_pp()
                def vmm(e, s=s, hf=hf, bank=bank):
                    for k in range(8):
                        e.matmul(bank[:, :], xT[:, k, s*128:(s+1)*128],
                                 wvT[:, k, hf*512:(hf+1)*512], start=(k == 0), stop=False)
                    return e.matmul(bank[:, :], ones_row[0:1, 0:128],
                                    bv_sb[:, hf*512:(hf+1)*512], start=False, stop=True)
                E('pe', vmm, reads=xt_keys + ['wvT', 'bv_sb', 'ones_row'], writes=[bkk])
                E('dve', lambda e, s=s, hf=hf, bank=bank: e.tensor_copy(
                    vt[:, s, hf*512:(hf+1)*512], bank[:, :]),
                  reads=[bkk], writes=[('vt', s, hf)])
        vt_keys = [('vt', s, hf) for s in range(4) for hf in range(2)]
        for pr in range(8):
            E('sp', lambda e, g=g, pr=pr: e.dma_start(
                vscr_d[pr, g*4:(g+1)*4, :, :].rearrange("s p d -> p s d"),
                vt[:, :, pr*128:(pr+1)*128]),
              reads=vt_keys, writes=['vscr_w'], dma='vw')
        E = mkE('xb')
        for s in range(4):
            E('pe', lambda e, g=g, s=s, xln=xln: e.matmul(
                ps_xb0[:, :], ones_col[:, :], xln[:, s, 0:512],
                start=(g == 0 and s == 0), stop=(g == ngroups - 1 and s == 3)),
              reads=[('xln', gp, s), 'ones_col'], writes=[('ps_xb', 0)])
            E('pe', lambda e, g=g, s=s, xln=xln: e.matmul(
                ps_xb1[:, :], ones_col[:, :], xln[:, s, 512:1024],
                start=(g == 0 and s == 0), stop=(g == ngroups - 1 and s == 3)),
              reads=[('xln', gp, s), 'ones_col'], writes=[('ps_xb', 1)])
        E = mkE('sc')
        for s in range(4):
            for bk_i in range(2):
                bank, bkk = next_pp()
                def smm(e, s=s, bk_i=bk_i, bank=bank):
                    last = None
                    for j in range(4):
                        pr = bk_i * 4 + j
                        last = e.matmul(
                            bank[:, j*104:(j+1)*104],
                            qt[:, pr, s*128:(s+1)*128],
                            ksampT[:, pr, :],
                            start=True, stop=True)
                    return last
                E('pe', smm, reads=[('qt', f) for f in range(8)] + ks_keys,
                  writes=[bkk])
                E('dve', lambda e, bank=bank: e.reduce_max(
                    mtmp.ap().rearrange("p (j hi o) -> p j hi o", j=4, hi=2),
                    bank[:, 0:416].rearrange("p (j hi c) -> p j hi c", j=4, hi=2)[:, :, :, 0:50],
                    axis=AX.X),
                  reads=[bkk], writes=['mtmp'])
                E('dve', lambda e, g=g, s=s, bk_i=bk_i, bank=bank: e.tensor_tensor(
                    M_tm.ap().rearrange("p (h j) -> p h j", h=H)[:, bk_i*8:(bk_i+1)*8, g*4+s],
                    mtmp[:, :],
                    bank[:, 0:416].rearrange("p (j hi c) -> p j hi c", j=4, hi=2)[:, :, :, 50],
                    ALU.subtract),
                  reads=['mtmp', bkk], writes=[('M', bk_i, g, s)])

    E = mkE('p15')
    Mkeys = [('M', b, g, s) for b in range(2) for g in range(NG) for s in range(4)]

    E('dve', lambda e: e.tensor_scalar(xbar_bf[:, 0:512], ps_xb0[:, :], 1.0 / L,
                                       None, ALU.mult),
      reads=[('ps_xb', 0)], writes=['xbar_bf'])
    E('dve', lambda e: e.tensor_scalar(xbar_bf[:, 512:1024], ps_xb1[:, :], 1.0 / L,
                                       None, ALU.mult),
      reads=[('ps_xb', 1)], writes=['xbar_bf'])
    for c in range(8):
        bank, bkk = next_pp()
        E('pe', lambda e, c=c, bank=bank: e.matmul(bank[:, 0:1], xbar_bf[:, c*128:(c+1)*128],
                                        one1[:, :], start=True, stop=True),
          reads=['xbar_bf', 'one1'], writes=[bkk])
        E('dve', lambda e, c=c, bank=bank: e.tensor_copy(xbarT[:, c:c+1], bank[:, 0:1]),
          reads=[bkk], writes=[('xbarT', c)])
    xbT_keys = [('xbarT', c) for c in range(8)]
    def vmean_mm0(e):
        for c in range(8):
            e.matmul(ps_qk[0][0:1, :], xbarT[:, c:c+1], wvT[:, c, 0:512],
                     start=(c == 0), stop=False)
        return e.matmul(ps_qk[0][0:1, :], one1[:, :], bv_sb[:, 0:512],
                        start=False, stop=True)
    def vmean_mm1(e):
        for c in range(8):
            e.matmul(ps_qk[1][0:1, :], xbarT[:, c:c+1], wvT[:, c, 512:1024],
                     start=(c == 0), stop=False)
        return e.matmul(ps_qk[1][0:1, :], one1[:, :], bv_sb[:, 512:1024],
                        start=False, stop=True)
    E('pe', vmean_mm0, reads=xbT_keys + ['wvT', 'one1', 'bv_sb'], writes=[('pp', 0)])
    E('pe', vmean_mm1, reads=xbT_keys + ['wvT', 'one1', 'bv_sb'], writes=[('pp', 1)])
    E('dve', lambda e: e.tensor_copy(vmean[:, 0:512], ps_qk[0][0:1, :]),
      reads=[('pp', 0)], writes=['vmean'])
    E('dve', lambda e: e.tensor_copy(vmean[:, 512:1024], ps_qk[1][0:1, :]),
      reads=[('pp', 1)], writes=['vmean'])
    E('dve', lambda e: e.tensor_copy(vmean_bf[:, :], vmean[:, :]),
      reads=['vmean'], writes=['vmean_bf'])
    for c in range(8):
        bank, bkk = next_pp()
        E('pe', lambda e, c=c, bank=bank: e.matmul(bank[:, 0:1], vmean_bf[:, c*128:(c+1)*128],
                                        one1[:, :], start=True, stop=True),
          reads=['vmean_bf', 'one1'], writes=[bkk])
        E('dve', lambda e, c=c, bank=bank: e.tensor_copy(vmeanT[:, c:c+1], bank[:, 0:1]),
          reads=[bkk], writes=[('vmeanT', c)])
    E('dve', lambda e: e.tensor_copy(vmeanT_bf[:, :], vmeanT[:, :]),
      reads=[('vmeanT', c) for c in range(8)], writes=['vmeanT_bf'])
    def base_mm0(e):
        for c in range(8):
            e.matmul(ps_qk[0][0:1, :], vmeanT_bf[:, c:c+1], woT_sb[:, c, 0:512],
                     start=(c == 0), stop=False)
        return e.matmul(ps_qk[0][0:1, :], one1[:, :], bo_sb[:, 0:512],
                        start=False, stop=True)
    def base_mm1(e):
        for c in range(8):
            e.matmul(ps_qk[1][0:1, :], vmeanT_bf[:, c:c+1], woT_sb[:, c, 512:1024],
                     start=(c == 0), stop=False)
        return e.matmul(ps_qk[1][0:1, :], one1[:, :], bo_sb[:, 512:1024],
                        start=False, stop=True)
    E('pe', base_mm0, reads=['vmeanT_bf', 'woT_sb', 'one1', 'bo_sb'], writes=[('pp', 0)])
    E('pe', base_mm1, reads=['vmeanT_bf', 'woT_sb', 'one1', 'bo_sb'], writes=[('pp', 1)])
    E('dve', lambda e: e.tensor_copy(base_bf[:, 0:512], ps_qk[0][0:1, :]),
      reads=[('pp', 0)], writes=['base_bf'])
    E('dve', lambda e: e.tensor_copy(base_bf[:, 512:1024], ps_qk[1][0:1, :]),
      reads=[('pp', 1)], writes=['base_bf'])
    if debug:
        E('sp', lambda e: e.dma_start(dbg["dbg_vmean"], vmean[:, :]), reads=['vmean'], dma='dbg')
        basef = sbuf(pha, "basef_s", [1, D], F32)
        E('dve', lambda e: e.tensor_copy(basef[:, :], base_bf[:, :]),
          reads=['base_bf'], writes=['basef'])
        E('sp', lambda e: e.dma_start(dbg["dbg_base"], basef[:, :]), reads=['basef'], dma='dbg')
        E('sp', lambda e: e.dma_start(dbg["dbg_M"], M_tm[:, :]), reads=Mkeys, dma='dbg')

    E = _E
    p.fence()
    pha.close()
    if maxphase < 2:
        if debug:
            p.wait_all_dma('pool', 'dbg')
        p.build(); p.close()
        st_m.close(); st_w.close(); sbstack.close()
        return nc
    ph2 = contextlib.ExitStack()
    hi_t = sbuf(ph2, "hi_t_s", [128, H], F32)
    tt_t = sbuf(ph2, "tt_t_s", [128, H], F32)
    cnt_f = sbuf(ph2, "cnt_f_s", [128, H], F32)
    cnt_bf = sbuf(ph2, "cnt_bf_s", [128, H], BF16)
    cntr = sbuf(ph2, "cntr_s", [128, H], F32)
    geu = sbuf(ph2, "geu_s", [128, H], U8)
    gle = sbuf(ph2, "gle_s", [128, H], U8)
    junkc = sbuf(ph2, "junkc_s", [128, 128], F32)
    mask_u8 = sbuf(ph2, "mask_u8_s", [128, H * 128], U8)
    neg1 = sbuf(ph2, "neg1_s", [128, 1], F32)
    iotaM = sbuf(ph2, "iotaM_s", [128, H * 128], F32)
    miota = M_tm
    cand = sbuf(ph2, "cand_s", [128, H * 16], F32)
    candcat = sbuf(ph2, "candcat_s", [16, 2048], F32)
    cmax8 = sbuf(ph2, "cmax8_s", [16, 8], F32)
    sort56 = sbuf(ph2, "sort56_s", [16, 56], F32)
    sortpad = sbuf(ph2, "sortpad_s", [16, 64], F32)
    selneg = sbuf(ph2, "selneg_s", [16, 64], U8)
    big9 = sbuf(ph2, "big9_s", [16, 64], F32)
    candT = sbuf(ph2, "candT_s", [128, 128], F32)
    ps2_a = psum(ph2, "ps2_a_s", [128, 512], F32)
    ps2_b = psum(ph2, "ps2_b_s", [128, 512], F32)

    E('sp', lambda e: e.dma_start(iotaM[:, :], iotaM_d), writes=['iotaM'], dma='c_iom')
    E('dve', lambda e: e.memset(lo_t[:, :], 0.0), writes=['lo'])
    E('dve', lambda e: e.memset(hi_t[:, :], 256.0), writes=['hi'])
    for it in range(nbisect):
        E('dve', lambda e: e.tensor_tensor(tt_t[:, :], lo_t[:, :], hi_t[:, :], ALU.add),
          reads=['lo', 'hi'], writes=['tt'])
        E('dve', lambda e: e.tensor_scalar(tt_t[:, :], tt_t[:, :], 0.5, None, ALU.mult),
          reads=['tt'], writes=['tt'])
        for h in range(H):
            E('dve', lambda e, h=h: e.tensor_scalar(
                junkc[:, :], M_tm[:, h*128:(h+1)*128], tt_t[:, h:h+1], 0.0,
                ALU.is_gt, ALU.add, accum_out=cnt_f[:, h:h+1]),
              reads=Mkeys + ['tt'], writes=['junkc', ('cnt', h)])
        E('dve', lambda e: e.tensor_copy(cnt_bf[:, :], cnt_f[:, :]),
          reads=[('cnt', h) for h in range(H)], writes=['cntbf'])
        E('pe', lambda e: e.matmul(ps2_a[:, 0:16], ones_sq[:, :], cnt_bf[:, :],
                                   start=True, stop=True),
          reads=['cntbf', 'ones_sq'], writes=[('ps2_a', 0)])
        E('dve', lambda e: e.tensor_copy(cntr[:, :], ps2_a[:, 0:16]),
          reads=[('ps2_a', 0)], writes=['cntr'])
        E('dve', lambda e: e.tensor_scalar(geu[:, :], cntr[:, :], 49.5, None, ALU.is_gt),
          reads=['cntr'], writes=['geu'])
        E('dve', lambda e: e.tensor_scalar(gle[:, :], cntr[:, :], 49.5, None, ALU.is_le),
          reads=['cntr'], writes=['gle'])
        E('dve', lambda e: e.select(lo_t[:, :], geu[:, :], tt_t[:, :], lo_t[:, :],
                                    add_drain=True),
          reads=['geu', 'tt', 'lo'], writes=['lo'])
        E('dve', lambda e: e.select(hi_t[:, :], gle[:, :], tt_t[:, :], hi_t[:, :],
                                    add_drain=True),
          reads=['gle', 'tt', 'hi'], writes=['hi'])
    if debug:
        E('sp', lambda e: e.dma_start(dbg["dbg_lo"], lo_t[:, :]), reads=['lo'], dma='dbg')
    E('dve', lambda e: e.memset(neg1[:, :], -1.0), writes=['neg1'])
    for h in range(H):
        E('dve', lambda e, h=h: e.tensor_scalar(
            mask_u8[:, h*128:(h+1)*128], M_tm[:, h*128:(h+1)*128],
            lo_t[:, h:h+1], None, ALU.is_gt),
          reads=Mkeys + ['lo'], writes=[('msk', h)])
    E('dve', lambda e: e.select(miota[:, :], mask_u8[:, :], iotaM[:, :],
                                neg1[:, 0:1].to_broadcast([128, H * 128]),
                                add_drain=True),
      reads=[('msk', h) for h in range(H)] + ['iotaM', 'neg1'],
      writes=Mkeys + ['miota'])
    for rnd in range(2):
        for h in range(H):
            E('dve', lambda e, rnd=rnd, h=h: e.max(
                out=cand[:, h*16+rnd*8:h*16+rnd*8+8], in_=miota[:, h*128:(h+1)*128]),
              reads=['miota'], writes=[('cand', h, rnd)])
            E('dve', lambda e, rnd=rnd, h=h: e.match_replace(
                out=miota[:, h*128:(h+1)*128],
                in_to_replace=cand[:, h*16+rnd*8:h*16+rnd*8+8],
                in_values=miota[:, h*128:(h+1)*128], imm_value=-1.0),
              reads=['miota', ('cand', h, rnd)], writes=['miota'])
    candkeys = [('cand', h, r) for h in range(H) for r in range(2)]
    for half in range(2):
        E('pe', lambda e, half=half: e.matmul(
            ps2_a[:, 0:128],
            cand[:, half*128:(half+1)*128], idf32[:, :], start=True, stop=True),
          reads=candkeys + ['idf32'], writes=[('ps2_a', 0)])
        E('dve', lambda e: e.tensor_copy(candT[:, :], ps2_a[:, 0:128]),
          reads=[('ps2_a', 0)], writes=['candT'])
        E('pool', lambda e, half=half: e.dma_start(
            candcat[half*8:(half+1)*8, :].rearrange("h (c s) -> h c s", c=16),
            candT[:, :]),
          reads=['candT'], writes=['candcat'], dma='cc')
    for rnd in range(7):
        E('dve', lambda e: e.max(out=cmax8[:, :], in_=candcat[:, :]),
          reads=['candcat'], writes=['cmax8'])
        E('dve', lambda e, rnd=rnd: e.tensor_copy(sort56[:, rnd*8:(rnd+1)*8],
                                                  cmax8[:, :]),
          reads=['cmax8'], writes=[('s56', rnd)])
        E('dve', lambda e: e.match_replace(
            out=candcat[:, :], in_to_replace=cmax8[:, :], in_values=candcat[:, :],
            imm_value=-1.0),
          reads=['candcat', 'cmax8'], writes=['candcat'])
    s56keys = [('s56', r) for r in range(7)]
    E('dve', lambda e: e.memset(sortpad[:, :], 1.0e9), writes=['sortpad'])
    E('dve', lambda e: e.tensor_copy(sortpad[:, 0:50], sort56[:, 0:50]),
      reads=s56keys, writes=['sortpad'])
    E('dve', lambda e: e.tensor_scalar(selneg[:, :], sortpad[:, :], -0.5, None, ALU.is_lt),
      reads=['sortpad'], writes=['selneg'])
    E('dve', lambda e: e.memset(big9[:, :], 1.0e9), writes=['big9'])
    E('dve', lambda e: e.select(sortpad[:, :], selneg[:, :], big9[:, :], sortpad[:, :],
                                add_drain=True),
      reads=['selneg', 'big9', 'sortpad'], writes=['sortpad'])
    E('pe', lambda e: e.matmul(ps2_b[0:64, 0:16], sortpad[:, :], idf32[0:16, 0:16],
                               start=True, stop=True),
      reads=['sortpad', 'idf32'], writes=[('ps2_b', 0)])
    E('dve', lambda e: e.tensor_copy(idxT_f[:, :], ps2_b[0:64, 0:16]),
      reads=[('ps2_b', 0)], writes=['idxT_f'])
    E('dve', lambda e: e.tensor_copy(idxT_i[:, :], idxT_f[:, :]),
      reads=['idxT_f'], writes=['idxT_i'])
    if debug:
        E('sp', lambda e: e.dma_start(dbg["dbg_idx"], idxT_f[:, :]), reads=['idxT_f'], dma='dbg')

    p.fence()
    ph2.close()
    st_m.close()

    if maxphase < 3:
        if debug:
            p.wait_all_dma('pool', 'dbg')
        p.build(); p.close()
        st_w.close(); sbstack.close()
        return nc

    phb = contextlib.ExitStack()
    kpair = [sbuf(phb, f"kpair{i}", [128, L], BF16) for i in range(2)]
    xg_f = sbuf(phb, "xg_f_s", [128, D], F32)
    xg_bf = sbuf(phb, "xg_bf_s", [128, D], BF16)
    xg_T = sbuf(phb, "xg_T_s", [128, 8, 128], BF16)
    qtopT = sbuf(phb, "qtopT_s", [128, 128], BF16)
    P_sb = sbuf(phb, "P_sb_s", [128, L], BF16)
    sumP = sbuf(phb, "sumP_s", [128, 32], F32)
    rsum = sbuf(phb, "rsum_s", [128, 1], F32)
    PT_sb = [sbuf(phb, f"PT_sb{i}", [128, 512], BF16) for i in range(2)]
    vsc = [sbuf(phb, f"vsc{i}", [128, 8, 128], BF16) for i in range(2)]
    dctx = sbuf(phb, "dctx_s", [128, 128], BF16)
    st3 = sbuf(phb, "st3_s", [128, 4], F32)
    ps_s = [psum(phb, f"ps_s{i}", [128, 512], F32) for i in range(2)]
    ps_pt = [psum(phb, f"ps_pt{i}", [128, 512], F32) for i in range(2)]
    ps_cx = psum(phb, "ps_cx_s", [128, 128], F32)
    ps_d = [psum(phb, f"ps_d{i}", [128, 512], F32) for i in range(2)]

    for c in range(8):
        E('dve', lambda e, c=c: e.memset(delta[c].ap(), 0.0), writes=[('dz', c)])
        E('dve', lambda e, c=c: e.memset(idxv[c].ap(), 2.0e9), writes=[('iz', c)])

    for pr in range(8):
        kp = kpair[pr % 2]
        kpk = ('kpair', pr % 2)
        E('sp', lambda e, pr=pr, kp=kp: e.dma_start(kp.ap(), kfm_d[pr, :, :]),
          reads=['kfm_w'], writes=[kpk], dma=f'kr{pr % 2}')
        E('dve', lambda e: e.memset(xg_f.ap(), 0.0), writes=['xg_f'])
        E('pool', lambda e, pr=pr: e.indirect_dma_start(
            xg_f[0:U, :], None, x_d, IndirectOffsetOnAxis(idxT_i[0:U, 2*pr:2*pr+1], 0),
            bounds_check=L - 1, oob_is_err=False),
          reads=['idxT_i', 'xg_f'], writes=[('xg_g', 0)], dma='g1')
        E('pool', lambda e, pr=pr: e.indirect_dma_start(
            xg_f[64:64+U, :], None, x_d, IndirectOffsetOnAxis(idxT_i[0:U, 2*pr+1:2*pr+2], 0),
            bounds_check=L - 1, oob_is_err=False),
          reads=['idxT_i', 'xg_f'], writes=[('xg_g', 1)], dma='g1')
        layernorm(xg_f[:, :], ['xg_f', ('xg_g', 0), ('xg_g', 1)], xg_bf[:, :], 'xg_bf',
                  128, epsA, 'epsA', st3, 'st3', lnexp=True)
        for c in range(8):
            bank = ps_pt[c % 2]
            bkk = ('ps_pt2', c % 2)
            E('pe', lambda e, c=c, bank=bank: e.matmul(
                bank[:, 0:128], xg_bf[:, c*128:(c+1)*128], id_bf[:, :],
                start=True, stop=True),
              reads=['xg_bf', 'id_bf'], writes=[bkk])
            E('dve', lambda e, c=c, bank=bank: e.tensor_copy(
                xg_T[:, c, :], bank[:, 0:128]),
              reads=[bkk], writes=[('xg_T', c)])
        def qtop_mm(e, pr=pr):
            for k in range(8):
                e.matmul(ps_d[0][:, 0:128], wqT[:, k, pr*128:(pr+1)*128], xg_T[:, k, :],
                         start=(k == 0), stop=False)
            return e.matmul(ps_d[0][:, 0:128], bq_sb[:, pr*128:(pr+1)*128],
                            ones_row[:, 0:128], start=False, stop=True)
        E('pe', qtop_mm, reads=[('xg_T', c) for c in range(8)] +
          ['wqT', 'bq_sb', 'ones_row'], writes=['ps_d0'])
        E('dve', lambda e: e.memset(qtopT.ap(), 0.0), writes=['qtop0'])
        E('dve', lambda e: e.tensor_copy(qtopT[0:64, 0:50], ps_d[0][0:64, 0:50]),
          reads=['ps_d0', 'qtop0'], writes=[('qtopT', 0)])
        E('dve', lambda e: e.tensor_copy(qtopT[64:128, 64:114], ps_d[0][64:128, 64:114]),
          reads=['ps_d0', 'qtop0'], writes=[('qtopT', 1)])
        for kc in range(32):
            bank = ps_s[kc % 2]
            bkk = ('ps_s', kc % 2)
            E('pe', lambda e, kc=kc, kp=kp, bank=bank: e.matmul(
                bank[:, :], qtopT[:, :], kp[:, kc*512:(kc+1)*512],
                start=True, stop=True),
              reads=[('qtopT', 0), ('qtopT', 1), 'qtop0', kpk], writes=[bkk])
            E('act', lambda e, kc=kc, bank=bank: e.activation(
                P_sb[:, kc*512:(kc+1)*512], bank[:, :], AF.Exp,
                bias=negb[:, :], scale=SCALE, accum_out=sumP[:, kc:kc+1]),
              reads=[bkk, 'negb'], writes=[('P', kc), ('sumP', kc)])
        E('dve', lambda e: e.reduce_sum(rsum[:, :], sumP[:, :], axis=AX.X),
          reads=[('sumP', kc) for kc in range(32)], writes=['rsum'])
        E('dve', lambda e: e.reciprocal(rsum[:, :], rsum[:, :]),
          reads=['rsum'], writes=['rsum'])
        E('dve', lambda e: e.tensor_scalar(P_sb[:, :], P_sb[:, :], rsum[:, :], None,
                                           ALU.mult),
          reads=[('P', kc) for kc in range(32)] + ['rsum'], writes=['Pn'])
        for scb in range(16):
            vb = vsc[scb % 2]
            vbk = ('vsc', scb % 2)
            E('sp', lambda e, pr=pr, scb=scb, vb=vb: e.dma_start(
                vb.ap(), vscr_d[pr, scb*8:(scb+1)*8, :, :].rearrange("s p d -> p s d")),
              reads=['vscr_w'], writes=[vbk], dma=f'vr{scb % 2}')
            for q4 in range(2):
                sc0 = scb * 8 + q4 * 4
                ptb = PT_sb[q4]
                ptk = ('PT', q4)
                psb = ps_pt[q4]
                psk = ('ps_pt2', q4)
                def ptmm(e, sc0=sc0, psb=psb):
                    last = None
                    for jj in range(4):
                        last = e.matmul(psb[:, jj*128:(jj+1)*128],
                                        P_sb[:, (sc0+jj)*128:(sc0+jj+1)*128],
                                        id_bf[:, :], start=True, stop=True)
                    return last
                E('pe', ptmm, reads=['Pn', 'id_bf'], writes=[psk])
                E('dve', lambda e, ptb=ptb, psb=psb: e.tensor_copy(ptb[:, :], psb[:, :]),
                  reads=[psk], writes=[ptk])
                def cxmm(e, sc0=sc0, q4=q4, vb=vb, ptb=ptb):
                    last = None
                    for jj in range(4):
                        sc = sc0 + jj
                        last = e.matmul(ps_cx[:, :], vb[:, q4*4+jj, :],
                                        ptb[:, jj*128:(jj+1)*128],
                                        start=(sc == 0), stop=(sc == 127))
                    return last
                E('pe', cxmm, reads=[vbk, ptk], writes=['ps_cx'])
        E('dve', lambda e: e.memset(dctx.ap(), 0.0), writes=['dctx0'])
        E('dve', lambda e, pr=pr: e.tensor_scalar(
            dctx[0:64, 0:50], ps_cx[0:64, 0:50], vmeanT[0:64, pr:pr+1], None,
            ALU.subtract),
          reads=['ps_cx', ('vmeanT', pr), 'dctx0'], writes=[('dctx', 0)])
        E('dve', lambda e, pr=pr: e.tensor_scalar(
            dctx[64:128, 64:114], ps_cx[64:128, 64:114], vmeanT[64:128, pr:pr+1], None,
            ALU.subtract),
          reads=['ps_cx', ('vmeanT', pr), 'dctx0'], writes=[('dctx', 1)])
        def dmm(e, pr=pr):
            e.matmul(ps_d[0][:, :], dctx[:, :], woT_sb[:, pr, 0:512],
                     start=True, stop=True)
            return e.matmul(ps_d[1][:, :], dctx[:, :], woT_sb[:, pr, 512:1024],
                            start=True, stop=True)
        E('pe', dmm, reads=[('dctx', 0), ('dctx', 1), 'dctx0', 'woT_sb'],
          writes=['ps_d0', 'ps_d1'])
        E('dve', lambda e, pr=pr: e.tensor_copy(delta[pr][:, 0:512], ps_d[0][:, :]),
          reads=['ps_d0', ('dz', pr)],
          writes=[('delta', pr, 0, 0), ('delta', pr, 64, 0)])
        E('dve', lambda e, pr=pr: e.tensor_copy(delta[pr][:, 512:1024], ps_d[1][:, :]),
          reads=['ps_d1', ('dz', pr)],
          writes=[('delta', pr, 0, 1), ('delta', pr, 64, 1)])
        E('dve', lambda e, pr=pr: e.tensor_copy(idxv[pr][0:50, :],
                                                idxT_f[0:50, 2*pr:2*pr+1]),
          reads=['idxT_f', ('iz', pr)], writes=[('idxv', pr, 0)])
        E('dve', lambda e, pr=pr: e.tensor_copy(idxv[pr][64:114, :],
                                                idxT_f[0:50, 2*pr+1:2*pr+2]),
          reads=['idxT_f', ('iz', pr)], writes=[('idxv', pr, 64)])
    if debug:
        dltf = sbuf(phb, "dltf_s", [128, D], F32)
        E('dve', lambda e: e.tensor_copy(dltf[:, :], delta[0].ap()),
          reads=[('delta', 0, 0, 0), ('delta', 0, 0, 1), ('delta', 0, 64, 0),
                 ('delta', 0, 64, 1), ('dz', 0)],
          writes=['dltf'])
        E('sp', lambda e: e.dma_start(dbg["dbg_delta"], dltf[:, :]), reads=['dltf'], dma='dbg')

    p.fence()
    phb.close()
    st_w.close()

    if maxphase < 5:
        if debug:
            p.wait_all_dma('pool', 'dbg')
        p.build(); p.close()
        sbstack.close()
        return nc

    phc = contextlib.ExitStack()
    wr_sb = sbuf(phc, "wr_sb_s", [128, 8, NE], BF16)
    b1_sb = sbuf(phc, "b1_sb_s", [NE, 32 * 128], BF16)
    gam_b = sbuf(phc, "gam_b_s", [128, D], F32)
    bet_b = sbuf(phc, "bet_b_s", [128, D], F32)
    bof_sb = sbuf(phc, "bof_sb_s", [1, D], BF16)
    gb_tmp = sbuf(phc, "gb_tmp_s", [1, D], BF16)
    xq_in = [sbuf(phc, f"xq_in{i}", [128, D], F32) for i in range(2)]
    Sel8 = sbuf(phc, "Sel8_s", [128, 8, 128], BF16)
    ish8 = sbuf(phc, "ish8_s", [128, 8], F32)
    idxv8 = sbuf(phc, "idxv8_s", [128, 8], F32)
    attn_f = sbuf(phc, "attn_f_s", [128, D], F32)
    y1_t = sbuf(phc, "y1_t_s", [128, D], F32)
    t1_t = sbuf(phc, "t1_t_s", [128, D], F32)
    t1b8 = sbuf(phc, "t1b8_s", [128, 8, D], BF16)
    t1T_tmp = sbuf(phc, "t1T_tmp_s", [128, D], BF16)
    sts = [sbuf(phc, f"sts{i}", [128, 4], F32) for i in range(4)]
    oh4 = sbuf(phc, "oh4_s", [128, NE], F32)
    ohbf = sbuf(phc, "ohbf_s", [128, NE], BF16)
    noty = sbuf(phc, "noty_s", [128, 1], F32)
    eq4 = sbuf(phc, "eq4_s", [128, NE], F32)
    m4 = sbuf(phc, "m4_s", [128, 1], F32)
    diag8 = sbuf(phc, "diag8_s", [128, 8, 512], BF16)
    t1eT = sbuf(phc, "t1eT_s", [128, 8, NE, 1024], BF16)
    onehT2 = [sbuf(phc, f"onehT{i}_s", [NE, 1024], BF16) for i in range(2)]
    inter_st = [sbuf(phc, f"inter_st{i}_x", [128, 512], BF16) for i in range(2)]
    intc = [sbuf(phc, f"intc{i}_x", [128, 512], BF16) for i in range(2)]
    w1sl = [sbuf(phc, f"w1sl{i}", [128, 8, NE, 128], BF16) for i in range(2)]
    wofc = [sbuf(phc, f"wofc{i}", [128, D], BF16) for i in range(2)]
    t2_t = sbuf(phc, "t2_t_s", [128, D], F32)
    t3_t = sbuf(phc, "t3_t_s", [128, D], F32)
    ps_ab = [psum(phc, f"ps_ab{i}", [128, 1024], F32) for i in range(2)]
    ps_z = [psum(phc, f"ps_z{i}", [128, 512], F32) for i in range(2)]
    ps_tT = [psum(phc, f"ps_tT{i}", [128, 512], F32) for i in range(2)]

    E = mkE('gb')
    E('pool', lambda e: e.dma_start(wr_sb.ap(), wr_d.rearrange("(kc p) f -> p kc f", p=128)),
      writes=['wr_sb'], dma='c_wr')
    E('pool', lambda e: e.dma_start(b1_sb.ap(), b1r_d), writes=['b1_sb'], dma='c_b1')
    E('pool', lambda e: e.dma_start(bof_sb[:, :], bof_d), writes=['bof_sb'], dma='c_bof')
    E('pool', lambda e: e.dma_start(gb_tmp[:, :], gam_d), writes=['gb_tmp'], dma='c_gb')
    for hf in range(2):
        E('pe', lambda e, hf=hf: e.matmul(ps_z[0][:, :], ones_row[0:1, 0:128],
                                          gb_tmp[:, hf*512:(hf+1)*512],
                                          start=True, stop=True),
          reads=['gb_tmp', 'ones_row'], writes=[('ps_z', 0)])
        E('dve', lambda e, hf=hf: e.tensor_copy(gam_b[:, hf*512:(hf+1)*512],
                                                ps_z[0][:, :]),
          reads=[('ps_z', 0)], writes=['gam_b'])
    E('pool', lambda e: e.dma_start(gb_tmp[:, :], bet_d), writes=['gb_tmp'], dma='c_gb')
    for hf in range(2):
        E('pe', lambda e, hf=hf: e.matmul(ps_z[0][:, :], ones_row[0:1, 0:128],
                                          gb_tmp[:, hf*512:(hf+1)*512],
                                          start=True, stop=True),
          reads=['gb_tmp', 'ones_row'], writes=[('ps_z', 0)])
        E('dve', lambda e, hf=hf: e.tensor_copy(bet_b[:, hf*512:(hf+1)*512],
                                                ps_z[0][:, :]),
          reads=[('ps_z', 0)], writes=['bet_b'])

    for ch in range(8):
        E('dve', lambda e, ch=ch: e.tensor_copy(idxv8[:, ch:ch+1], idxv[ch][:, :]),
          reads=[('idxv', ch, 0), ('idxv', ch, 64), ('iz', ch)], writes=['idxv8'])
    dkeys = [('delta', ch, bs, hf) for ch in range(8) for bs in (0, 64) for hf in range(2)] + \
            [('dz', c) for c in range(8)]
    ikeys = [('idxv', ch, bs) for ch in range(8) for bs in (0, 64)] + \
            [('iz', c) for c in range(8)]

    def emit_A1_tile(sg, tl):
        sgt0 = sg * 8
        t = sgt0 + tl
        E = mkE('as')
        xi = xq_in[tl % 2]
        xik = ('xq_in', tl % 2)
        E('sp', lambda e, t=t, xi=xi: e.dma_start(xi.ap(), xq_d[t*128:(t+1)*128, :]),
          writes=[xik], dma=f'xq{tl % 2}')
        pab = ps_ab[tl % 2]
        pa, pb = pab[:, 0:512], pab[:, 512:1024]
        pak, pbk = ('ps_ab', tl % 2, 0), ('ps_ab', tl % 2, 1)
        E('dve', lambda e, t=t: e.tensor_scalar(
            ish8[:, :], idxv8[:, :], qoffc[:, :], float(t * 128),
            ALU.subtract, ALU.subtract),
          reads=['idxv8', 'qoffc'], writes=['ish8'])
        E('dve', lambda e: e.tensor_tensor(
            Sel8.ap(),
            ish8.ap().unsqueeze(2).to_broadcast([128, 8, 128]),
            iotacol.ap().unsqueeze(1).to_broadcast([128, 8, 128]),
            ALU.is_equal),
          reads=['ish8', 'iotacol'], writes=['Sel8'])
        for ch in range(8):
            E('pe', lambda e, ch=ch, pa=pa: e.matmul(
                pa, Sel8[:, ch, :], delta[ch][:, 0:512],
                start=(ch == 0), stop=False),
              reads=['Sel8'] + dkeys, writes=[pak])
            E('pe', lambda e, ch=ch, pb=pb: e.matmul(
                pb, Sel8[:, ch, :], delta[ch][:, 512:1024],
                start=(ch == 0), stop=False),
              reads=['Sel8'] + dkeys, writes=[pbk])
        E('pe', lambda e, pa=pa: e.matmul(pa, ones_row[0:1, 0:128],
                                          base_bf[:, 0:512], start=False, stop=True),
          reads=['ones_row', 'base_bf'], writes=[pak])
        E('pe', lambda e, pb=pb: e.matmul(pb, ones_row[0:1, 0:128],
                                          base_bf[:, 512:1024], start=False, stop=True),
          reads=['ones_row', 'base_bf'], writes=[pbk])
        E('dve', lambda e, pa=pa: e.tensor_copy(attn_f[:, 0:512], pa),
          reads=[pak], writes=['attn_f'])
        E('dve', lambda e, pb=pb: e.tensor_copy(attn_f[:, 512:1024], pb),
          reads=[pbk], writes=['attn_f'])
        layernorm(attn_f[:, :], ['attn_f'], y1_t[:, :], 'y1', 128, epsA, 'epsA',
                  sts[0], 'sA')
        E('dve', lambda e, xi=xi: e.tensor_tensor(y1_t[:, :], y1_t[:, :], xi.ap(),
                                                  ALU.add),
          reads=['y1', xik], writes=['y1'])
        E('sp', lambda e, t=t: e.dma_start(y1s_d[t*128:(t+1)*128, :], y1_t[:, :]),
          reads=['y1'], writes=[('y1s', t)], dma='y1w')
        layernorm(y1_t[:, :], ['y1'], t1_t[:, :], 't1', 128, epsA, 'epsA',
                  sts[1], 'sB')
        E('sp', lambda e, t=t: e.dma_start(t1s_d[t*128:(t+1)*128, :], t1_t[:, :]),
          reads=['t1'], writes=[('t1s', t)], dma='t1w')
        E('dve', lambda e, tl=tl: e.tensor_copy(t1b8[:, tl, :], t1_t[:, :]),
          reads=['t1'], writes=[('t1b8', tl)])
        if debug and t == 0:
            E('sp', lambda e: e.dma_start(dbg["dbg_y1"], y1_t[:, :]), reads=['y1'], dma='dbg')
            E('sp', lambda e: e.dma_start(dbg["dbg_t1"], t1_t[:, :]), reads=['t1'], dma='dbg')

    def emit_A2_tile(sg, tl):
        E = mkE('as2')
        for cb in range(2):
            bank = ps_tT[cb]
            bkk = ('ps_tT', cb)
            for j in range(4):
                c = cb * 4 + j
                E('pe', lambda e, c=c, j=j, bank=bank, tl=tl: e.matmul(
                    bank[:, j*128:(j+1)*128], t1b8[:, tl, c*128:(c+1)*128], id_bf[:, :],
                    start=True, stop=True),
                  reads=[('t1b8', tl), 'id_bf'], writes=[bkk])
            E('dve', lambda e, cb=cb, bank=bank: e.tensor_copy(
                t1T_tmp[:, cb*512:(cb+1)*512], bank[:, :]),
              reads=[bkk], writes=[('t1T', cb)])
        def lg_mm(e):
            last = None
            for c in range(8):
                last = e.matmul(ps_tT[0][:, 0:NE], t1T_tmp[:, c*128:(c+1)*128],
                                wr_sb[:, c, :], start=(c == 0), stop=(c == 7))
            return last
        E('pe', lg_mm, reads=[('t1T', 0), ('t1T', 1), 'wr_sb'], writes=[('ps_tT', 0)])
        E('dve', lambda e: e.reduce_max(m4[:, :], ps_tT[0][:, 0:NE], axis=AX.X),
          reads=[('ps_tT', 0)], writes=['m4'])
        E('dve', lambda e: e.tensor_scalar(eq4[:, :], ps_tT[0][:, 0:NE], m4[:, :],
                                           None, ALU.is_ge),
          reads=[('ps_tT', 0), 'm4'], writes=['eq4'])
        E('dve', lambda e: e.tensor_copy(oh4[:, 0:1], eq4[:, 0:1]),
          reads=['eq4'], writes=['oh4'])
        E('dve', lambda e: e.tensor_scalar(noty[:, :], eq4[:, 0:1], -1.0, 1.0,
                                           ALU.mult, ALU.add),
          reads=['eq4'], writes=['noty'])
        for ee in range(1, NE):
            E('dve', lambda e, ee=ee: e.tensor_tensor(
                oh4[:, ee:ee+1], eq4[:, ee:ee+1], noty[:, :], ALU.mult),
              reads=['eq4', 'noty'], writes=['oh4'])
            if ee < NE - 1:
                E('dve', lambda e, ee=ee: e.tensor_scalar(
                    junk1[:, :], oh4[:, ee:ee+1], -1.0, 1.0, ALU.mult, ALU.add),
                  reads=['oh4'], writes=['junk1'])
                E('dve', lambda e: e.tensor_tensor(noty[:, :], noty[:, :],
                                                   junk1[:, :], ALU.mult),
                  reads=['noty', 'junk1'], writes=['noty'])
        E('dve', lambda e: e.tensor_copy(ohbf[:, :], oh4[:, :]),
          reads=['oh4'], writes=['ohbf'])

    def emit_A3_tile(sg, tl):
        sgp = sg % 2
        E = mkE('as3')
        E('pe', lambda e: e.matmul(ps_tT[1][0:NE, 0:128], ohbf[:, :], id_bf[:, :],
                                   start=True, stop=True),
          reads=['ohbf', 'id_bf'], writes=[('ps_tT', 1)])
        E('dve', lambda e, tl=tl, sgp=sgp: e.tensor_copy(
            onehT2[sgp][:, tl*128:(tl+1)*128], ps_tT[1][0:NE, 0:128]),
          reads=[('ps_tT', 1)], writes=[('onehT', sgp, tl)])
        for ee in range(NE):
            E('dve', lambda e, ee=ee, tl=tl: e.tensor_scalar(
                diag8[:, tl, ee*128:(ee+1)*128], id_bf[:, :], oh4[:, ee:ee+1], None,
                ALU.mult),
              reads=['id_bf', 'oh4'], writes=[('diag8', tl, ee)])

    def emit_B_tile(sg, tl):
        E = mkE('as3')
        for c in range(8):
            bank = ps_z[c % 2]
            bkk = ('ps_z', c % 2)
            E('pe', lambda e, c=c, tl=tl, bank=bank: e.matmul(
                bank[:, :], t1b8[:, tl, c*128:(c+1)*128], diag8[:, tl, :],
                start=True, stop=True),
              reads=[('t1b8', tl)] + [('diag8', tl, ee) for ee in range(NE)],
              writes=[bkk])
            E('dve', lambda e, c=c, tl=tl, bank=bank: e.tensor_copy(
                t1eT[:, c, :, tl*128:(tl+1)*128],
                bank.ap().rearrange("p (e f) -> p e f", e=NE)),
              reads=[bkk], writes=[('t1eT', c, tl)])

    t1e_keys = [('t1eT', c, tl) for c in range(8) for tl in range(8)]

    def emit_h(sg, hook=None):
        sgp = sg % 2
        E = mkE('hp')
        oh_keys = [('onehT', sgp, tl) for tl in range(8)]
        for ifc in range(32):
            wsl = w1sl[ifc % 2]
            wk = ('w1sl', ifc % 2)
            E('pool', lambda e, ifc=ifc, wsl=wsl: e.dma_start(
                wsl.ap(), w1r_d[ifc].rearrange("kc e p f -> p kc e f")),
              writes=[wk], dma=f'w1_{ifc % 2}')
            for hg in range(2):
                bank = ps_z[hg]
                bkk = ('ps_z', hg)
                def hmm(e, ifc=ifc, hg=hg, wsl=wsl, bank=bank, sgp=sgp):
                    first = True
                    for ee in range(NE):
                        for k in range(8):
                            e.matmul(bank[:, :], wsl[:, k, ee, :],
                                     t1eT[:, k, ee, hg*512:(hg+1)*512],
                                     start=first, stop=False)
                            first = False
                    return e.matmul(bank[:, :], b1_sb[:, ifc*128:(ifc+1)*128],
                                    onehT2[sgp][:, hg*512:(hg+1)*512],
                                    start=False, stop=True)
                E('pe', hmm, reads=[wk, 'b1_sb'] + t1e_keys + oh_keys, writes=[bkk])
                ist = inter_st[(ifc * 2 + hg) % 2]
                istk = ('ist', (ifc * 2 + hg) % 2)
                E('act', lambda e, bank=bank, ist=ist: e.activation(
                    ist[:, :], bank[:, :], AF.Gelu),
                  reads=[bkk], writes=[istk])
                E('sp', lambda e, ifc=ifc, hg=hg, ist=ist: e.dma_start(
                    intT_d[ifc, :, hg*512:(hg+1)*512], ist[:, :]),
                  reads=[istk], writes=[('intd', ifc, hg)], dma='intw')
            if hook is not None:
                hook(ifc)

    def emit_wo(sg):
        sgt0 = sg * 8
        E = mkE('wo')
        wo_banks = [ps_ab[0][:, 0:512], ps_ab[0][:, 512:1024],
                    ps_ab[1][:, 0:512], ps_ab[1][:, 512:1024],
                    ps_z[0][:, :], ps_z[1][:, :], ps_tT[0][:, :], ps_tT[1][:, :]]
        wo_keys = [('ps_ab', 0, 0), ('ps_ab', 0, 1), ('ps_ab', 1, 0), ('ps_ab', 1, 1),
                   ('ps_z', 0), ('ps_z', 1), ('ps_tT', 0), ('ps_tT', 1)]
        for hb in range(2):
            for ifc in range(32):
                wc = wofc[ifc % 2]
                wck = ('wofc', ifc % 2)
                E('pool', lambda e, ifc=ifc, wc=wc: e.dma_start(
                    wc.ap(), wof_d[ifc*128:(ifc+1)*128, :]),
                  writes=[wck], dma=f'wo{ifc % 2}')
                ic = intc[ifc % 2]
                ick = ('intc', ifc % 2)
                E('sp', lambda e, ifc=ifc, hb=hb, ic=ic: e.dma_start(
                    ic.ap(), intT_d[ifc, :, hb*512:(hb+1)*512]),
                  reads=[('intd', ifc, hb)], writes=[ick], dma=f'ir{ifc % 2}')
                for t4 in range(4):
                    tl = hb * 4 + t4
                    t = sgt0 + tl
                    pa, pb = wo_banks[2*t4], wo_banks[2*t4+1]
                    pak, pbk = wo_keys[2*t4], wo_keys[2*t4+1]
                    E('pe', lambda e, ifc=ifc, t4=t4, wc=wc, ic=ic, pa=pa: e.matmul(
                        pa, ic[:, t4*128:(t4+1)*128], wc[:, 0:512],
                        start=(ifc == 0), stop=False),
                      reads=[wck, ick], writes=[pak])
                    E('pe', lambda e, ifc=ifc, t4=t4, wc=wc, ic=ic, pb=pb: e.matmul(
                        pb, ic[:, t4*128:(t4+1)*128], wc[:, 512:1024],
                        start=(ifc == 0), stop=False),
                      reads=[wck, ick], writes=[pbk])
            for t4 in range(4):
                pa, pb = wo_banks[2*t4], wo_banks[2*t4+1]
                pak, pbk = wo_keys[2*t4], wo_keys[2*t4+1]
                E('pe', lambda e, pa=pa: e.matmul(pa, ones_row[0:1, 0:128],
                                                  bof_sb[:, 0:512], start=False, stop=True),
                  reads=['ones_row', 'bof_sb'], writes=[pak])
                E('pe', lambda e, pb=pb: e.matmul(pb, ones_row[0:1, 0:128],
                                                  bof_sb[:, 512:1024], start=False, stop=True),
                  reads=['ones_row', 'bof_sb'], writes=[pbk])
            order = [0, 1, 2, 3] if hb == 0 else [2, 3, 0, 1]
            tbufs = [(xq_in[0], ('xq_in', 0)), (xq_in[1], ('xq_in', 1))]
            ybufs = [(attn_f, 'attn_f'), (y1_t, 'y1')]

            def rl_t1(i):
                t = sgt0 + hb * 4 + order[i]
                buf, key = tbufs[i % 2]
                E('sp', lambda e, t=t, buf=buf: e.dma_start(
                    buf.ap(), t1s_d[t*128:(t+1)*128, :]),
                  reads=[('t1s', t)], writes=[key], dma=f'rl{i % 2}')

            def rl_y1(i):
                t = sgt0 + hb * 4 + order[i]
                buf, key = ybufs[i % 2]
                E('sp', lambda e, t=t, buf=buf: e.dma_start(
                    buf.ap(), y1s_d[t*128:(t+1)*128, :]),
                  reads=[('y1s', t)], writes=[key], dma=f'rl{2 + i % 2}')

            rl_t1(0); rl_y1(0); rl_t1(1); rl_y1(1)
            for i, t4 in enumerate(order):
                t = sgt0 + hb * 4 + t4
                pa, pb = wo_banks[2*t4], wo_banks[2*t4+1]
                pak, pbk = wo_keys[2*t4], wo_keys[2*t4+1]
                tb, tbk = tbufs[i % 2]
                yb, ybk = ybufs[i % 2]
                E('dve', lambda e, pa=pa, tb=tb: e.tensor_tensor(
                    t2_t[:, 0:512], pa, tb[:, 0:512], ALU.add),
                  reads=[pak, tbk], writes=['T2'])
                E('dve', lambda e, pb=pb, tb=tb: e.tensor_tensor(
                    t2_t[:, 512:1024], pb, tb[:, 512:1024], ALU.add),
                  reads=[pbk, tbk], writes=['T2'])
                if i + 2 < 4:
                    rl_t1(i + 2)
                layernorm(t2_t[:, :], ['T2'], t3_t[:, :], 'T3', 128, epsB, 'epsB',
                          sts[2], 'sC')
                E('dve', lambda e: e.tensor_tensor(t3_t[:, :], t3_t[:, :], gam_b[:, :],
                                                   ALU.mult),
                  reads=['T3', 'gam_b'], writes=['T3'])
                E('dve', lambda e: e.tensor_tensor(t3_t[:, :], t3_t[:, :], bet_b[:, :],
                                                   ALU.add),
                  reads=['T3', 'bet_b'], writes=['T3'])
                layernorm(t3_t[:, :], ['T3'], t2_t[:, :], 'T2', 128, epsA, 'epsA',
                          sts[3], 'sD')
                E('dve', lambda e, yb=yb: e.tensor_tensor(t3_t[:, :], t2_t[:, :],
                                                          yb[:, :], ALU.add),
                  reads=['T2', ybk], writes=['T3'])
                if i + 2 < 4:
                    rl_y1(i + 2)
                E('sp', lambda e, t=t: e.dma_start(out_d[t*128:(t+1)*128, :], t3_t[:, :]),
                  reads=['T3'], dma='out')

    for tl in range(8):
        emit_A1_tile(0, tl)
        emit_A2_tile(0, tl)
        emit_A3_tile(0, tl)
    for tl in range(8):
        emit_B_tile(0, tl)
    for sg in range(4):
        def hook(ifc, sg=sg):
            if sg == 3:
                return
            if ifc >= 2 and (ifc - 2) % 3 == 0 and (ifc - 2) // 3 < 8:
                emit_A1_tile(sg + 1, (ifc - 2) // 3)
            if ifc >= 4 and (ifc - 4) % 3 == 0 and (ifc - 4) // 3 < 8:
                emit_A2_tile(sg + 1, (ifc - 4) // 3)
            if ifc >= 5 and (ifc - 5) % 3 == 0 and (ifc - 5) // 3 < 8:
                emit_A3_tile(sg + 1, (ifc - 5) // 3)
        emit_h(sg, hook)
        if sg < 3:
            for tl in range(8):
                emit_B_tile(sg + 1, tl)
        emit_wo(sg)

    E = _E
    if debug:
        p.wait_all_dma('pool', 'dbg')
    p.wait_all_dma('pool', 'out')
    p.build()
    p.close()
    phc.close()
    sbstack.close()
    return nc


_CACHE = {}


def _host_inputs(inputs):
    x = np.asarray(inputs['L0_attn_x'], np.float32)
    wqT = np.ascontiguousarray(np.asarray(inputs['L0_attn_Wq'], np.float32).T)
    wkT = np.ascontiguousarray(np.asarray(inputs['L0_attn_Wk'], np.float32).T)
    wvT = np.ascontiguousarray(np.asarray(inputs['L0_attn_Wv'], np.float32).T)
    woT = np.ascontiguousarray(np.asarray(inputs['L0_attn_Wo'], np.float32).T)
    bq = np.ascontiguousarray(np.asarray(inputs['L0_attn_bq'], np.float32)[None, :])
    bk = np.ascontiguousarray(np.asarray(inputs['L0_attn_bk'], np.float32)[None, :])
    bv = np.ascontiguousarray(np.asarray(inputs['L0_attn_bv'], np.float32)[None, :])
    bo = np.ascontiguousarray(np.asarray(inputs['L0_attn_bo'], np.float32)[None, :])
    idxk = np.ascontiguousarray(np.asarray(inputs['L0_attn_idx_k'], np.int32))
    Wr = np.ascontiguousarray(np.asarray(inputs['L0_ffn_W_router'], np.float32))
    W1 = np.asarray(inputs['L0_ffn_W1'], np.float32)
    w1r = np.ascontiguousarray(W1.reshape(NE, 8, 128, 32, 128).transpose(3, 1, 0, 2, 4))
    b1r = np.ascontiguousarray(np.asarray(inputs['L0_ffn_b1'], np.float32)
                               .reshape(NE, 32 * 128))
    Wof = np.ascontiguousarray(np.asarray(inputs['L0_ffn_Wo'], np.float32))
    bof = np.ascontiguousarray(np.asarray(inputs['L0_ffn_bo'], np.float32)[None, :])
    gam = np.ascontiguousarray(np.asarray(inputs['L0_ffn_ln_gamma'], np.float32)[None, :])
    bet = np.ascontiguousarray(np.asarray(inputs['L0_ffn_ln_beta'], np.float32)[None, :])
    ident = np.eye(128, dtype=np.float32)
    iotacol = np.tile(np.arange(128, dtype=np.float32)[None, :], (128, 1))
    iotaM = np.empty((128, H * 128), np.float32)
    for h in range(H):
        iotaM[:, h*128:(h+1)*128] = (np.arange(128)[None, :] * 128
                                     + np.arange(128)[:, None])
    xbs = [np.ascontiguousarray(x[0]), np.ascontiguousarray(x[1])]
    in_maps = []
    for c in range(8):
        b, q = c // 4, c % 4
        in_maps.append({
            "x": xbs[b],
            "xq": np.ascontiguousarray(xbs[b][q*QTOK:(q+1)*QTOK]),
            "wqT": wqT, "wkT": wkT, "wvT": wvT, "woT": woT,
            "bq": bq, "bk": bk, "bv": bv, "bo": bo,
            "idxk": idxk, "wr": Wr, "w1r": w1r, "b1r": b1r,
            "wof": Wof, "bof": bof, "gam": gam, "bet": bet,
            "ident": ident, "iotacol": iotacol, "iotaM": iotaM,
            "qoffc": np.full((128, 1), q * QTOK, np.float32),
        })
    return in_maps


def kernel(**inputs):
    if 'nc' not in _CACHE:
        _CACHE['nc'] = build_program(debug=False)
    nc = _CACHE['nc']
    in_maps = _host_inputs(inputs)
    res = run_bass_kernel_spmd(nc, in_maps, core_ids=list(range(8)))
    out = np.empty((2, L, D), np.float32)
    for c in range(8):
        b, q = c // 4, c % 4
        out[b, q*QTOK:(q+1)*QTOK] = res.results[c]["y"]
    return out


# version 13
# speedup vs baseline: 1.2478x; 1.0063x over previous
import contextlib
import math

import numpy as np

import concourse.bass as bass
import concourse.mybir as mybir
from concourse.bass import IndirectOffsetOnAxis
from concourse.bass_utils import run_bass_kernel_spmd

F32 = mybir.dt.float32
BF16 = mybir.dt.bfloat16
I32 = mybir.dt.int32
U8 = mybir.dt.uint8
AF = mybir.ActivationFunctionType
ALU = mybir.AluOpType
AX = mybir.AxisListType

D = 1024
L = 16384
H = 16
DK = 64
U = 50
NG = 32
QTOK = 4096
NE = 4
DI = 4096
BISECT = 20
SCALE = 1.0 / math.sqrt(DK)


class Prog:
    ENGINES = ('pe', 'dve', 'act', 'pool', 'sp')

    def __init__(self, nc):
        self.nc = nc
        self.stack = contextlib.ExitStack()
        self.csem = {}
        self.ccount = {}
        for e in ('pe', 'dve', 'act', 'pool'):
            self.csem[e] = self.stack.enter_context(nc.semaphore(f"c_{e}"))
            self.ccount[e] = 0
        self.dsem = {}
        self.dcount = {}
        self.dqueue = {}
        self.ops = {e: [] for e in self.ENGINES}
        self.last_w = {}
        self.readers = {}
        self.know = {e: {} for e in self.ENGINES}
        self.read_misses = set()

    def chan(self, name, engine):
        if name not in self.dsem:
            self.dsem[name] = self.stack.enter_context(self.nc.semaphore(f"d_{name}"))
            self.dcount[name] = 0
            self.dqueue[name] = engine
        assert self.dqueue[name] == engine, (name, engine, self.dqueue[name])
        return name

    def _resolve(self, dep):
        if dep[0] == 'c':
            _, e, v = dep
            return (f"c_{e}", self.csem[e], v)
        _, ch = dep[:2]
        return (f"d_{ch}", self.dsem[ch], 16 * self.dcount[ch])

    def emit(self, engine, fn, reads=(), writes=(), dma=None, tag=None):
        deps = []
        for k in reads:
            w = self.last_w.get(k)
            if w is None:
                self.read_misses.add(k)
            else:
                deps.append(w)
        for k in writes:
            w = self.last_w.get(k)
            if w is not None:
                deps.append(w)
            deps.extend(self.readers.get(k, ()))
        need = {}
        for dep in deps:
            if dep[0] == 'c' and dep[1] == 'pe' and engine == 'pe' and dma is None:
                continue
            name, handle, v = self._resolve(dep)
            if self.know[engine].get(name, -1) >= v:
                continue
            if name in need and need[name][1] >= v:
                continue
            need[name] = (handle, v)
        waits = []
        for name, (handle, v) in need.items():
            waits.append((handle, v))
            self.know[engine][name] = v

        if dma is not None:
            ch = self.chan(dma, engine)
            self.dcount[ch] += 1
            inc = (self.dsem[ch], 16)
            marker = ('d', ch, self.dcount[ch])
        elif engine == 'sp':
            inc = None
            marker = None
        else:
            self.ccount[engine] += 1
            inc = (self.csem[engine], 1)
            marker = ('c', engine, self.ccount[engine])

        self.ops[engine].append((waits, fn, inc, tag))
        if marker is not None:
            for k in writes:
                self.last_w[k] = marker
                self.readers[k] = []
            for k in reads:
                self.readers.setdefault(k, []).append(marker)
        return marker

    def wait_all_dma(self, engine, channel):
        if channel not in self.dsem:
            return
        self.ops[engine].append(
            ([(self.dsem[channel], 16 * self.dcount[channel])], None, None, 'wd'))
        self.know[engine][f"d_{channel}"] = 16 * self.dcount[channel]

    def fence(self):
        targets = []
        for e in ('pe', 'dve', 'act', 'pool'):
            if self.ccount[e]:
                targets.append((f"c_{e}", self.csem[e], self.ccount[e]))
        for ch in self.dsem:
            if self.dcount[ch]:
                targets.append((f"d_{ch}", self.dsem[ch], 16 * self.dcount[ch]))
        for e in self.ENGINES:
            ws = []
            for name, handle, v in targets:
                if name == f"c_{e}":
                    continue
                if self.know[e].get(name, -1) >= v:
                    continue
                ws.append((handle, v))
                self.know[e][name] = v
            if ws:
                self.ops[e].append((ws, None, None, 'fence'))

    def build(self):
        nc = self.nc
        if self.read_misses:
            import sys
            print("PROG read-before-write keys:", sorted(map(str, self.read_misses)),
                  file=sys.stderr)
        with nc.Block() as blk:
            def replay(ename):
                def _go(eng):
                    for waits, fn, inc, tag in self.ops[ename]:
                        for handle, v in waits:
                            eng.wait_ge(handle, v)
                        if fn is not None:
                            inst = fn(eng)
                            if inc is not None:
                                inst.then_inc(inc[0], inc[1])
                return _go
            if self.ops['pe']: blk.tensor(replay('pe'))
            if self.ops['dve']: blk.vector(replay('dve'))
            if self.ops['act']: blk.scalar(replay('act'))
            if self.ops['pool']: blk.gpsimd(replay('pool'))
            if self.ops['sp']: blk.sync(replay('sp'))

    def close(self):
        self.stack.close()


def build_program(debug=False, maxphase=9, ngroups=NG, nbisect=BISECT, parts='all'):
    nc = bass.Bass("TRN2", target_bir_lowering=False)

    def din(name, shape, dt=F32):
        return nc.dram_tensor(name, shape, dt, kind="ExternalInput").ap()
    x_d = din("x", [L, D])
    xq_d = din("xq", [QTOK, D])
    wqT_d = din("wqT", [D, D]); wkT_d = din("wkT", [D, D]); wvT_d = din("wvT", [D, D])
    woT_d = din("woT", [D, D])
    bq_d = din("bq", [1, D]); bk_d = din("bk", [1, D]); bv_d = din("bv", [1, D])
    bo_d = din("bo", [1, D])
    idxk_d = din("idxk", [U], I32)
    wr_d = din("wr", [D, NE])
    w1r_d = din("w1r", [32, 8, NE, 128, 128])
    b1r_d = din("b1r", [NE, 32 * 128])
    wof_d = din("wof", [DI, D])
    bof_d = din("bof", [1, D]); gam_d = din("gam", [1, D]); bet_d = din("bet", [1, D])
    ident_d = din("ident", [128, 128])
    iotacol_d = din("iotacol", [128, 128])
    iotaM_d = din("iotaM", [128, H * 128])
    qoff_d = din("qoffc", [128, 1])
    out_d = nc.dram_tensor("y", [QTOK, D], F32, kind="ExternalOutput").ap()
    dbg = {}
    if debug:
        def dout(name, shape, dt=F32):
            dbg[name] = nc.dram_tensor(name, shape, dt, kind="ExternalOutput").ap()
        dout("dbg_M", [128, H * 128]); dout("dbg_idx", [64, H])
        dout("dbg_vmean", [1, D]); dout("dbg_base", [1, D])
        dout("dbg_delta", [128, D]); dout("dbg_y1", [128, D])
        dout("dbg_t1", [128, D]); dout("dbg_lo", [128, H])

    kfm_d = nc.dram_tensor("kfm", [8, 128, L], BF16, kind="Internal").ap()
    vscr_d = nc.dram_tensor("vscr", [8, 128, 128, 128], BF16, kind="Internal").ap()
    t1s_d = nc.dram_tensor("t1s", [QTOK, D], F32, kind="Internal").ap()
    intT_d = nc.dram_tensor("intT", [32, 128, 1024], BF16, kind="Internal").ap()
    y1s_d = nc.dram_tensor("y1s", [QTOK, D], F32, kind="Internal").ap()

    p = Prog(nc)
    _E = p.emit
    def mkE(label):
        def E(*a, **k):
            if parts == 'all' or label in parts:
                return _E(*a, **k)
        return E
    E = _E
    sbstack = contextlib.ExitStack()

    def sbuf(stk, n, s, d=F32):
        return stk.enter_context(nc.sbuf_tensor(n, s, d))

    def psum(stk, n, s, d=F32):
        return stk.enter_context(nc.psum_tensor(n, s, d))

    perm = sbstack
    id_bf = sbuf(perm, "id_bf_s", [128, 128], BF16)
    idf32 = sbuf(perm, "idf32_s", [128, 128], F32)
    iotacol = sbuf(perm, "iotacol_s", [128, 128], F32)
    qoffc = sbuf(perm, "qoffc_s", [128, 1], F32)
    ones_row = sbuf(perm, "ones_row_s", [1, 512], BF16)
    ones_col = sbuf(perm, "ones_col_s", [128, 1], BF16)
    ones_sq = sbuf(perm, "ones_sq_s", [128, 128], BF16)
    one1 = sbuf(perm, "one1_s", [1, 1], BF16)
    epsA = sbuf(perm, "epsA_s", [128, 1], F32)
    epsB = sbuf(perm, "epsB_s", [128, 1], F32)
    negb = sbuf(perm, "negb_s", [128, 1], F32)
    ksampT = sbuf(perm, "ksampT_s", [128, 8, 104], BF16)
    xbar_bf = sbuf(perm, "xbar_bf_s", [1, D], BF16)
    xbarT = sbuf(perm, "xbarT_s", [128, 8], BF16)
    vmean = sbuf(perm, "vmean_s", [1, D], F32)
    vmean_bf = sbuf(perm, "vmean_bf_s", [1, D], BF16)
    vmeanT = sbuf(perm, "vmeanT_s", [128, 8], F32)
    vmeanT_bf = sbuf(perm, "vmeanT_bf_s", [128, 8], BF16)
    base_bf = sbuf(perm, "base_bf_s", [1, D], BF16)
    idxT_f = sbuf(perm, "idxT_f_s", [64, H], F32)
    idxT_i = sbuf(perm, "idxT_i_s", [64, H], I32)
    delta = [sbuf(perm, f"delta{c}", [128, D], BF16) for c in range(8)]
    idxv = [sbuf(perm, f"idxv{c}", [128, 1], F32) for c in range(8)]
    junk1 = sbuf(perm, "junk1_s", [128, 1], F32)

    E('pool', lambda e: e.dma_start(id_bf[:, :], ident_d), writes=['id_bf'], dma='c_idbf')
    E('sp', lambda e: e.dma_start(idf32[:, :], ident_d), writes=['idf32'], dma='c_idf')
    E('sp', lambda e: e.dma_start(iotacol[:, :], iotacol_d), writes=['iotacol'], dma='c_ioc')
    E('sp', lambda e: e.dma_start(qoffc[:, :], qoff_d), writes=['qoffc'], dma='c_qof')
    E('dve', lambda e: e.memset(ones_row[:, :], 1.0), writes=['ones_row'])
    E('dve', lambda e: e.memset(ones_col[:, :], 1.0), writes=['ones_col'])
    E('dve', lambda e: e.memset(ones_sq[:, :], 1.0), writes=['ones_sq'])
    E('dve', lambda e: e.memset(one1[:, :], 1.0), writes=['one1'])
    E('dve', lambda e: e.memset(epsA[:, :], 1e-5), writes=['epsA'])
    E('dve', lambda e: e.memset(epsB[:, :], 1e-12), writes=['epsB'])
    E('dve', lambda e: e.memset(negb[:, :], -12.0), writes=['negb'])

    def layernorm(src_ap, src_keys, dst_ap, dst_key, rows, eps_tile, eps_key,
                  stats, skey, lnexp=False):
        rd = list(src_keys)
        E('dve', lambda e: e.reduce_sum(stats[0:rows, 0:1], src_ap, axis=AX.X),
          reads=rd, writes=[(skey, 0)])
        E('act', lambda e: e.activation(dst_ap, src_ap, AF.Square,
                                        accum_out=stats[0:rows, 1:2]),
          reads=rd, writes=[dst_key, (skey, 1)])
        E('dve', lambda e: e.tensor_scalar(stats[0:rows, 0:1], stats[0:rows, 0:1],
                                           1.0 / D, None, ALU.mult),
          reads=[(skey, 0)], writes=[(skey, 0)])
        E('dve', lambda e: e.tensor_scalar(stats[0:rows, 1:2], stats[0:rows, 1:2],
                                           1.0 / D, None, ALU.mult),
          reads=[(skey, 1)], writes=[(skey, 1)])
        E('dve', lambda e: e.tensor_tensor(stats[0:rows, 2:3], stats[0:rows, 0:1],
                                           stats[0:rows, 0:1], ALU.mult),
          reads=[(skey, 0)], writes=[(skey, 2)])
        E('dve', lambda e: e.tensor_tensor(stats[0:rows, 2:3], stats[0:rows, 1:2],
                                           stats[0:rows, 2:3], ALU.subtract),
          reads=[(skey, 1), (skey, 2)], writes=[(skey, 2)])
        if lnexp:
            E('act', lambda e: e.activation(stats[0:rows, 3:4], stats[0:rows, 2:3],
                                            AF.Ln, bias=eps_tile[0:rows, 0:1]),
              reads=[(skey, 2), eps_key], writes=[(skey, 3)])
            E('dve', lambda e: e.tensor_scalar(stats[0:rows, 3:4], stats[0:rows, 3:4],
                                               -0.5, None, ALU.mult),
              reads=[(skey, 3)], writes=[(skey, 3)])
            E('act', lambda e: e.activation(stats[0:rows, 3:4], stats[0:rows, 3:4],
                                            AF.Exp),
              reads=[(skey, 3)], writes=[(skey, 3)])
        else:
            E('act', lambda e: e.activation(stats[0:rows, 3:4], stats[0:rows, 2:3],
                                            AF.Sqrt, bias=eps_tile[0:rows, 0:1]),
              reads=[(skey, 2), eps_key], writes=[(skey, 3)])
            E('dve', lambda e: e.reciprocal(stats[0:rows, 3:4], stats[0:rows, 3:4]),
              reads=[(skey, 3)], writes=[(skey, 3)])
        E('dve', lambda e: e.tensor_scalar(dst_ap, src_ap, stats[0:rows, 0:1],
                                           stats[0:rows, 3:4], ALU.subtract, ALU.mult),
          reads=rd + [(skey, 0), (skey, 3)], writes=[dst_key])

    pha = contextlib.ExitStack()
    st_w = contextlib.ExitStack()
    st_m = contextlib.ExitStack()
    bq_sb = sbuf(perm, "bq_sb_s", [1, D], BF16)
    bo_sb = sbuf(perm, "bo_sb_s", [1, D], BF16)
    wqT = sbuf(st_w, "wqT_s", [128, 8, D], BF16)
    woT_sb = sbuf(st_w, "woT_sb_s", [128, 8, D], BF16)
    M_tm = sbuf(st_m, "M_tm_s", [128, H * 128], F32)
    lo_t = sbuf(st_m, "lo_t_s", [128, H], F32)
    wkT = sbuf(pha, "wkT_s", [128, 8, D], BF16)
    wvT = sbuf(pha, "wvT_s", [128, 8, D], BF16)
    bk_sb = sbuf(pha, "bk_sb_s", [1, D], BF16)
    bv_sb = sbuf(pha, "bv_sb_s", [1, D], BF16)
    E('pool', lambda e: e.dma_start(wqT.ap(), wqT_d.rearrange("(kc p) f -> p kc f", p=128)),
      writes=['wqT'], dma='c_wq')
    E('pool', lambda e: e.dma_start(wkT.ap(), wkT_d.rearrange("(kc p) f -> p kc f", p=128)),
      writes=['wkT'], dma='c_wk')
    E('pool', lambda e: e.dma_start(wvT.ap(), wvT_d.rearrange("(kc p) f -> p kc f", p=128)),
      writes=['wvT'], dma='c_wv')
    E('pool', lambda e: e.dma_start(woT_sb.ap(), woT_d.rearrange("(kc p) f -> p kc f", p=128)),
      writes=['woT_sb'], dma='c_wo')
    E('pool', lambda e: e.dma_start(bq_sb[:, :], bq_d), writes=['bq_sb'], dma='c_bq')
    E('pool', lambda e: e.dma_start(bk_sb[:, :], bk_d), writes=['bk_sb'], dma='c_bk')
    E('pool', lambda e: e.dma_start(bv_sb[:, :], bv_d), writes=['bv_sb'], dma='c_bv')
    E('pool', lambda e: e.dma_start(bo_sb[:, :], bo_d), writes=['bo_sb'], dma='c_bo')

    ps_tr = [psum(pha, f"ps_tr{i}", [128, 512], F32) for i in range(2)]
    ps_qk = [psum(pha, f"ps_qk{i}", [128, 512], F32) for i in range(3)]
    ps_sc = psum(pha, "ps_sc_s", [128, 512], F32)
    ps_xb0 = psum(pha, "ps_xb0_s", [1, 512], F32)
    ps_xb1 = psum(pha, "ps_xb1_s", [1, 512], F32)
    pp_pool = [ps_qk[0], ps_qk[1], ps_qk[2], ps_sc]
    pp_ctr = [0]
    def next_pp():
        i = pp_ctr[0] % 4
        pp_ctr[0] += 1
        return pp_pool[i], ('pp', i)

    E = mkE('p0b')
    idxk_sb = sbuf(pha, "idxk_sb_s", [U, 1], I32)
    xs_f = sbuf(pha, "xs_f_s", [U, D], F32)
    xs_bf = sbuf(pha, "xs_bf_s", [128, D], BF16)
    xs_T = sbuf(pha, "xs_T_s", [128, 8, 64], BF16)
    kbar = sbuf(pha, "kbar_s", [128, 1], F32)
    st0 = sbuf(pha, "st0_s", [128, 4], F32)

    E('sp', lambda e: e.dma_start(idxk_sb[:, :], idxk_d.rearrange("(a b) -> a b", b=1)),
      writes=['idxk_sb'], dma='c_idk')
    E('dve', lambda e: e.memset(ksampT.ap(), 0.0), writes=['ks_init'])
    E('pool', lambda e: e.indirect_dma_start(
        xs_f[:, :], None, x_d, IndirectOffsetOnAxis(idxk_sb[:, :], 0)),
      reads=['idxk_sb'], writes=['xs_f'], dma='g0')
    E('dve', lambda e: e.memset(xs_bf[:, :], 0.0), writes=['xs_bf'])
    layernorm(xs_f[:, :], ['xs_f'], xs_bf[0:U, :], 'xs_bf', U, epsA, 'epsA', st0, 'st0')
    for c in range(8):
        bank = ps_tr[c % 2]
        bkk = ('ps_tr', c % 2)
        E('pe', lambda e, c=c, bank=bank: e.matmul(
            bank[:, 0:128], xs_bf[:, c*128:(c+1)*128], id_bf[:, :],
            start=True, stop=True),
          reads=['xs_bf', 'id_bf'], writes=[bkk])
        E('dve', lambda e, c=c, bank=bank: e.tensor_copy(xs_T[:, c, :], bank[:, 0:64]),
          reads=[bkk], writes=[('xs_T', c)])
    xsT_keys = [('xs_T', c) for c in range(8)]
    for h in range(H):
        pr, hi = h // 2, h % 2
        bank, bkk = next_pp()
        def ksamp_mm(e, h=h, bank=bank):
            for k in range(8):
                e.matmul(bank[0:64, 0:64], wkT[:, k, h*64:(h+1)*64], xs_T[:, k, :],
                         start=(k == 0), stop=False)
            return e.matmul(bank[0:64, 0:64], bk_sb[:, h*64:(h+1)*64],
                            ones_row[:, 0:64], start=False, stop=True)
        E('pe', ksamp_mm, reads=xsT_keys + ['wkT', 'bk_sb', 'ones_row'],
          writes=[bkk])
        E('dve', lambda e, pr=pr, hi=hi, bank=bank: e.tensor_copy(
            ksampT[hi*64:(hi+1)*64, pr, hi*52:hi*52+50], bank[0:64, 0:50]),
          reads=[bkk, 'ks_init'], writes=[('ksampT', pr, hi)])
        E('dve', lambda e, bank=bank: e.reduce_sum(kbar[0:64, :], bank[0:64, 0:50], axis=AX.X),
          reads=[bkk], writes=['kbar'])
        E('dve', lambda e, pr=pr, hi=hi: e.tensor_scalar(
            ksampT[hi*64:(hi+1)*64, pr, hi*52+50:hi*52+51], kbar[0:64, :],
            1.0 / U, None, ALU.mult),
          reads=['kbar', 'ks_init'], writes=[('ksampT', pr, hi, 'bar')])
    ks_keys = [('ksampT', pr, hi) for pr in range(8) for hi in range(2)] + \
              [('ksampT', pr, hi, 'bar') for pr in range(8) for hi in range(2)] + ['ks_init']

    x_in = [sbuf(pha, f"x_in{i}", [128, 4, D], F32) for i in range(2)]
    xln2 = [sbuf(pha, f"xln{i}_s", [128, 4, D], BF16) for i in range(2)]
    xT = sbuf(pha, "xT_s", [128, 8, 512], BF16)
    qt = sbuf(pha, "qt_s", [128, 8, 512], BF16)
    ktile = [sbuf(pha, f"ktile{i}", [128, 512], BF16) for i in range(2)]
    vt = sbuf(pha, "vt_s", [128, 4, D], BF16)
    st1 = [[sbuf(pha, f"st1_{i}_{s}", [128, 4], F32) for s in range(4)] for i in range(2)]
    mtmp = sbuf(pha, "mtmp_s", [128, 8], F32)

    E = _E

    def emit_x_dma(g):
        gp = g % 2
        xi = x_in[gp]
        _E('sp', lambda e, g=g, xi=xi: e.dma_start(
            xi.ap(), x_d[g*512:(g+1)*512, :].rearrange("(s p) f -> p s f", p=128)),
          writes=[('x_in', gp)], dma=f'x{gp}')

    def emit_ln_slice(g, s):
        gp = g % 2
        layernorm(x_in[gp][:, s, :], [('x_in', gp)], xln2[gp][:, s, :],
                  ('xln', gp, s), 128, epsA, 'epsA', st1[gp][s], ('st1', gp, s))

    emit_x_dma(0)
    for s in range(4):
        emit_ln_slice(0, s)
    for g in range(ngroups):
        gp = g % 2
        xln = xln2[gp]
        if g + 1 < ngroups:
            emit_x_dma(g + 1)
        E = mkE('tr')
        for s in range(4):
            for cb in range(2):
                bank = ps_tr[cb]
                bkk = ('ps_tr', cb)
                for j in range(4):
                    c = cb * 4 + j
                    E('pe', lambda e, s=s, c=c, j=j, bank=bank, xln=xln: e.matmul(
                        bank[:, j*128:(j+1)*128], xln[:, s, c*128:(c+1)*128],
                        id_bf[:, :], start=True, stop=True),
                      reads=[('xln', gp, s), 'id_bf'], writes=[bkk])
                E('dve', lambda e, s=s, cb=cb, bank=bank: e.tensor_copy(
                    xT.ap()[:, cb*4:(cb+1)*4, s*128:(s+1)*128],
                    bank.ap().rearrange("p (c f) -> p c f", c=4)),
                  reads=[bkk], writes=[('xT', s, cb)])
        xt_keys = [('xT', s, cb) for s in range(4) for cb in range(2)]
        E = mkE('q')
        for f in range(8):
            bank, bkk = next_pp()
            def qmm(e, f=f, bank=bank):
                for k in range(8):
                    e.matmul(bank[:, :], wqT[:, k, f*128:(f+1)*128], xT[:, k, :],
                             start=(k == 0), stop=False)
                return e.matmul(bank[:, :], bq_sb[:, f*128:(f+1)*128],
                                ones_row[:, :], start=False, stop=True)
            E('pe', qmm, reads=xt_keys + ['wqT', 'bq_sb', 'ones_row'], writes=[bkk])
            E('dve', lambda e, f=f, bank=bank: e.tensor_copy(qt[:, f, :], bank[:, :]),
              reads=[bkk], writes=[('qt', f)])
            if f in (2, 6) and g + 1 < ngroups:
                emit_ln_slice(g + 1, f // 4)
        E = mkE('k')
        for f in range(8):
            kt = ktile[f % 2]
            ktk = ('ktile', f % 2)
            bank, bkk = next_pp()
            def kmm(e, f=f, bank=bank):
                for k in range(8):
                    e.matmul(bank[:, :], wkT[:, k, f*128:(f+1)*128], xT[:, k, :],
                             start=(k == 0), stop=False)
                return e.matmul(bank[:, :], bk_sb[:, f*128:(f+1)*128],
                                ones_row[:, :], start=False, stop=True)
            E('pe', kmm, reads=xt_keys + ['wkT', 'bk_sb', 'ones_row'], writes=[bkk])
            E('dve', lambda e, kt=kt, bank=bank: e.tensor_copy(kt[:, :], bank[:, :]),
              reads=[bkk], writes=[ktk])
            E('sp', lambda e, g=g, f=f, kt=kt: e.dma_start(
                kfm_d[f, :, g*512:(g+1)*512], kt[:, :]),
              reads=[ktk], writes=['kfm_w'], dma='kw')
            if f in (2, 6) and g + 1 < ngroups:
                emit_ln_slice(g + 1, 2 + f // 4)
        E = mkE('v')
        for s in range(4):
            for hf in range(2):
                bank, bkk = next_pp()
                def vmm(e, s=s, hf=hf, bank=bank):
                    for k in range(8):
                        e.matmul(bank[:, :], xT[:, k, s*128:(s+1)*128],
                                 wvT[:, k, hf*512:(hf+1)*512], start=(k == 0), stop=False)
                    return e.matmul(bank[:, :], ones_row[0:1, 0:128],
                                    bv_sb[:, hf*512:(hf+1)*512], start=False, stop=True)
                E('pe', vmm, reads=xt_keys + ['wvT', 'bv_sb', 'ones_row'], writes=[bkk])
                E('dve', lambda e, s=s, hf=hf, bank=bank: e.tensor_copy(
                    vt[:, s, hf*512:(hf+1)*512], bank[:, :]),
                  reads=[bkk], writes=[('vt', s, hf)])
        vt_keys = [('vt', s, hf) for s in range(4) for hf in range(2)]
        for pr in range(8):
            E('sp', lambda e, g=g, pr=pr: e.dma_start(
                vscr_d[pr, g*4:(g+1)*4, :, :].rearrange("s p d -> p s d"),
                vt[:, :, pr*128:(pr+1)*128]),
              reads=vt_keys, writes=['vscr_w'], dma='vw')
        E = mkE('xb')
        for s in range(4):
            E('pe', lambda e, g=g, s=s, xln=xln: e.matmul(
                ps_xb0[:, :], ones_col[:, :], xln[:, s, 0:512],
                start=(g == 0 and s == 0), stop=(g == ngroups - 1 and s == 3)),
              reads=[('xln', gp, s), 'ones_col'], writes=[('ps_xb', 0)])
            E('pe', lambda e, g=g, s=s, xln=xln: e.matmul(
                ps_xb1[:, :], ones_col[:, :], xln[:, s, 512:1024],
                start=(g == 0 and s == 0), stop=(g == ngroups - 1 and s == 3)),
              reads=[('xln', gp, s), 'ones_col'], writes=[('ps_xb', 1)])
        E = mkE('sc')
        for s in range(4):
            for bk_i in range(2):
                bank, bkk = next_pp()
                def smm(e, s=s, bk_i=bk_i, bank=bank):
                    last = None
                    for j in range(4):
                        pr = bk_i * 4 + j
                        last = e.matmul(
                            bank[:, j*104:(j+1)*104],
                            qt[:, pr, s*128:(s+1)*128],
                            ksampT[:, pr, :],
                            start=True, stop=True)
                    return last
                E('pe', smm, reads=[('qt', f) for f in range(8)] + ks_keys,
                  writes=[bkk])
                E('dve', lambda e, bank=bank: e.reduce_max(
                    mtmp.ap().rearrange("p (j hi o) -> p j hi o", j=4, hi=2),
                    bank[:, 0:416].rearrange("p (j hi c) -> p j hi c", j=4, hi=2)[:, :, :, 0:50],
                    axis=AX.X),
                  reads=[bkk], writes=['mtmp'])
                E('dve', lambda e, g=g, s=s, bk_i=bk_i, bank=bank: e.tensor_tensor(
                    M_tm.ap().rearrange("p (h j) -> p h j", h=H)[:, bk_i*8:(bk_i+1)*8, g*4+s],
                    mtmp[:, :],
                    bank[:, 0:416].rearrange("p (j hi c) -> p j hi c", j=4, hi=2)[:, :, :, 50],
                    ALU.subtract),
                  reads=['mtmp', bkk], writes=[('M', bk_i, g, s)])

    E = mkE('p15')
    Mkeys = [('M', b, g, s) for b in range(2) for g in range(NG) for s in range(4)]

    E('dve', lambda e: e.tensor_scalar(xbar_bf[:, 0:512], ps_xb0[:, :], 1.0 / L,
                                       None, ALU.mult),
      reads=[('ps_xb', 0)], writes=['xbar_bf'])
    E('dve', lambda e: e.tensor_scalar(xbar_bf[:, 512:1024], ps_xb1[:, :], 1.0 / L,
                                       None, ALU.mult),
      reads=[('ps_xb', 1)], writes=['xbar_bf'])
    for c in range(8):
        bank, bkk = next_pp()
        E('pe', lambda e, c=c, bank=bank: e.matmul(bank[:, 0:1], xbar_bf[:, c*128:(c+1)*128],
                                        one1[:, :], start=True, stop=True),
          reads=['xbar_bf', 'one1'], writes=[bkk])
        E('dve', lambda e, c=c, bank=bank: e.tensor_copy(xbarT[:, c:c+1], bank[:, 0:1]),
          reads=[bkk], writes=[('xbarT', c)])
    xbT_keys = [('xbarT', c) for c in range(8)]
    def vmean_mm0(e):
        for c in range(8):
            e.matmul(ps_qk[0][0:1, :], xbarT[:, c:c+1], wvT[:, c, 0:512],
                     start=(c == 0), stop=False)
        return e.matmul(ps_qk[0][0:1, :], one1[:, :], bv_sb[:, 0:512],
                        start=False, stop=True)
    def vmean_mm1(e):
        for c in range(8):
            e.matmul(ps_qk[1][0:1, :], xbarT[:, c:c+1], wvT[:, c, 512:1024],
                     start=(c == 0), stop=False)
        return e.matmul(ps_qk[1][0:1, :], one1[:, :], bv_sb[:, 512:1024],
                        start=False, stop=True)
    E('pe', vmean_mm0, reads=xbT_keys + ['wvT', 'one1', 'bv_sb'], writes=[('pp', 0)])
    E('pe', vmean_mm1, reads=xbT_keys + ['wvT', 'one1', 'bv_sb'], writes=[('pp', 1)])
    E('dve', lambda e: e.tensor_copy(vmean[:, 0:512], ps_qk[0][0:1, :]),
      reads=[('pp', 0)], writes=['vmean'])
    E('dve', lambda e: e.tensor_copy(vmean[:, 512:1024], ps_qk[1][0:1, :]),
      reads=[('pp', 1)], writes=['vmean'])
    E('dve', lambda e: e.tensor_copy(vmean_bf[:, :], vmean[:, :]),
      reads=['vmean'], writes=['vmean_bf'])
    for c in range(8):
        bank, bkk = next_pp()
        E('pe', lambda e, c=c, bank=bank: e.matmul(bank[:, 0:1], vmean_bf[:, c*128:(c+1)*128],
                                        one1[:, :], start=True, stop=True),
          reads=['vmean_bf', 'one1'], writes=[bkk])
        E('dve', lambda e, c=c, bank=bank: e.tensor_copy(vmeanT[:, c:c+1], bank[:, 0:1]),
          reads=[bkk], writes=[('vmeanT', c)])
    E('dve', lambda e: e.tensor_copy(vmeanT_bf[:, :], vmeanT[:, :]),
      reads=[('vmeanT', c) for c in range(8)], writes=['vmeanT_bf'])
    def base_mm0(e):
        for c in range(8):
            e.matmul(ps_qk[0][0:1, :], vmeanT_bf[:, c:c+1], woT_sb[:, c, 0:512],
                     start=(c == 0), stop=False)
        return e.matmul(ps_qk[0][0:1, :], one1[:, :], bo_sb[:, 0:512],
                        start=False, stop=True)
    def base_mm1(e):
        for c in range(8):
            e.matmul(ps_qk[1][0:1, :], vmeanT_bf[:, c:c+1], woT_sb[:, c, 512:1024],
                     start=(c == 0), stop=False)
        return e.matmul(ps_qk[1][0:1, :], one1[:, :], bo_sb[:, 512:1024],
                        start=False, stop=True)
    E('pe', base_mm0, reads=['vmeanT_bf', 'woT_sb', 'one1', 'bo_sb'], writes=[('pp', 0)])
    E('pe', base_mm1, reads=['vmeanT_bf', 'woT_sb', 'one1', 'bo_sb'], writes=[('pp', 1)])
    E('dve', lambda e: e.tensor_copy(base_bf[:, 0:512], ps_qk[0][0:1, :]),
      reads=[('pp', 0)], writes=['base_bf'])
    E('dve', lambda e: e.tensor_copy(base_bf[:, 512:1024], ps_qk[1][0:1, :]),
      reads=[('pp', 1)], writes=['base_bf'])
    if debug:
        E('sp', lambda e: e.dma_start(dbg["dbg_vmean"], vmean[:, :]), reads=['vmean'], dma='dbg')
        basef = sbuf(pha, "basef_s", [1, D], F32)
        E('dve', lambda e: e.tensor_copy(basef[:, :], base_bf[:, :]),
          reads=['base_bf'], writes=['basef'])
        E('sp', lambda e: e.dma_start(dbg["dbg_base"], basef[:, :]), reads=['basef'], dma='dbg')
        E('sp', lambda e: e.dma_start(dbg["dbg_M"], M_tm[:, :]), reads=Mkeys, dma='dbg')

    E = _E
    p.fence()
    pha.close()
    if maxphase < 2:
        if debug:
            p.wait_all_dma('pool', 'dbg')
        p.build(); p.close()
        st_m.close(); st_w.close(); sbstack.close()
        return nc
    ph2 = contextlib.ExitStack()
    hi_t = sbuf(ph2, "hi_t_s", [128, H], F32)
    tt_t = sbuf(ph2, "tt_t_s", [128, H], F32)
    cnt_f = sbuf(ph2, "cnt_f_s", [128, H], F32)
    cnt_bf = sbuf(ph2, "cnt_bf_s", [128, H], BF16)
    cntr = sbuf(ph2, "cntr_s", [128, H], F32)
    geu = sbuf(ph2, "geu_s", [128, H], U8)
    gle = sbuf(ph2, "gle_s", [128, H], U8)
    junkc = sbuf(ph2, "junkc_s", [128, 128], F32)
    mask_u8 = sbuf(ph2, "mask_u8_s", [128, H * 128], U8)
    neg1 = sbuf(ph2, "neg1_s", [128, 1], F32)
    iotaM = sbuf(ph2, "iotaM_s", [128, H * 128], F32)
    miota = M_tm
    cand = sbuf(ph2, "cand_s", [128, H * 16], F32)
    candcat = sbuf(ph2, "candcat_s", [16, 2048], F32)
    cmax8 = sbuf(ph2, "cmax8_s", [16, 8], F32)
    sort56 = sbuf(ph2, "sort56_s", [16, 56], F32)
    sortpad = sbuf(ph2, "sortpad_s", [16, 64], F32)
    selneg = sbuf(ph2, "selneg_s", [16, 64], U8)
    big9 = sbuf(ph2, "big9_s", [16, 64], F32)
    candT = sbuf(ph2, "candT_s", [128, 128], F32)
    ps2_a = psum(ph2, "ps2_a_s", [128, 512], F32)
    ps2_b = psum(ph2, "ps2_b_s", [128, 512], F32)

    E('sp', lambda e: e.dma_start(iotaM[:, :], iotaM_d), writes=['iotaM'], dma='c_iom')
    E('dve', lambda e: e.memset(lo_t[:, :], 0.0), writes=['lo'])
    E('dve', lambda e: e.memset(hi_t[:, :], 256.0), writes=['hi'])
    for it in range(nbisect):
        E('dve', lambda e: e.tensor_tensor(tt_t[:, :], lo_t[:, :], hi_t[:, :], ALU.add),
          reads=['lo', 'hi'], writes=['tt'])
        E('dve', lambda e: e.tensor_scalar(tt_t[:, :], tt_t[:, :], 0.5, None, ALU.mult),
          reads=['tt'], writes=['tt'])
        for h in range(H):
            E('dve', lambda e, h=h: e.tensor_scalar(
                junkc[:, :], M_tm[:, h*128:(h+1)*128], tt_t[:, h:h+1], 0.0,
                ALU.is_gt, ALU.add, accum_out=cnt_f[:, h:h+1]),
              reads=Mkeys + ['tt'], writes=['junkc', ('cnt', h)])
        E('dve', lambda e: e.tensor_copy(cnt_bf[:, :], cnt_f[:, :]),
          reads=[('cnt', h) for h in range(H)], writes=['cntbf'])
        E('pe', lambda e: e.matmul(ps2_a[:, 0:16], ones_sq[:, :], cnt_bf[:, :],
                                   start=True, stop=True),
          reads=['cntbf', 'ones_sq'], writes=[('ps2_a', 0)])
        E('dve', lambda e: e.tensor_copy(cntr[:, :], ps2_a[:, 0:16]),
          reads=[('ps2_a', 0)], writes=['cntr'])
        E('dve', lambda e: e.tensor_scalar(geu[:, :], cntr[:, :], 49.5, None, ALU.is_gt),
          reads=['cntr'], writes=['geu'])
        E('dve', lambda e: e.tensor_scalar(gle[:, :], cntr[:, :], 49.5, None, ALU.is_le),
          reads=['cntr'], writes=['gle'])
        E('dve', lambda e: e.select(lo_t[:, :], geu[:, :], tt_t[:, :], lo_t[:, :],
                                    add_drain=True),
          reads=['geu', 'tt', 'lo'], writes=['lo'])
        E('dve', lambda e: e.select(hi_t[:, :], gle[:, :], tt_t[:, :], hi_t[:, :],
                                    add_drain=True),
          reads=['gle', 'tt', 'hi'], writes=['hi'])
    if debug:
        E('sp', lambda e: e.dma_start(dbg["dbg_lo"], lo_t[:, :]), reads=['lo'], dma='dbg')
    E('dve', lambda e: e.memset(neg1[:, :], -1.0), writes=['neg1'])
    for h in range(H):
        E('dve', lambda e, h=h: e.tensor_scalar(
            mask_u8[:, h*128:(h+1)*128], M_tm[:, h*128:(h+1)*128],
            lo_t[:, h:h+1], None, ALU.is_gt),
          reads=Mkeys + ['lo'], writes=[('msk', h)])
    E('dve', lambda e: e.select(miota[:, :], mask_u8[:, :], iotaM[:, :],
                                neg1[:, 0:1].to_broadcast([128, H * 128]),
                                add_drain=True),
      reads=[('msk', h) for h in range(H)] + ['iotaM', 'neg1'],
      writes=Mkeys + ['miota'])
    for rnd in range(2):
        for h in range(H):
            E('dve', lambda e, rnd=rnd, h=h: e.max(
                out=cand[:, h*16+rnd*8:h*16+rnd*8+8], in_=miota[:, h*128:(h+1)*128]),
              reads=['miota'], writes=[('cand', h, rnd)])
            E('dve', lambda e, rnd=rnd, h=h: e.match_replace(
                out=miota[:, h*128:(h+1)*128],
                in_to_replace=cand[:, h*16+rnd*8:h*16+rnd*8+8],
                in_values=miota[:, h*128:(h+1)*128], imm_value=-1.0),
              reads=['miota', ('cand', h, rnd)], writes=['miota'])
    candkeys = [('cand', h, r) for h in range(H) for r in range(2)]
    for half in range(2):
        E('pe', lambda e, half=half: e.matmul(
            ps2_a[:, 0:128],
            cand[:, half*128:(half+1)*128], idf32[:, :], start=True, stop=True),
          reads=candkeys + ['idf32'], writes=[('ps2_a', 0)])
        E('dve', lambda e: e.tensor_copy(candT[:, :], ps2_a[:, 0:128]),
          reads=[('ps2_a', 0)], writes=['candT'])
        E('pool', lambda e, half=half: e.dma_start(
            candcat[half*8:(half+1)*8, :].rearrange("h (c s) -> h c s", c=16),
            candT[:, :]),
          reads=['candT'], writes=['candcat'], dma='cc')
    for rnd in range(7):
        E('dve', lambda e: e.max(out=cmax8[:, :], in_=candcat[:, :]),
          reads=['candcat'], writes=['cmax8'])
        E('dve', lambda e, rnd=rnd: e.tensor_copy(sort56[:, rnd*8:(rnd+1)*8],
                                                  cmax8[:, :]),
          reads=['cmax8'], writes=[('s56', rnd)])
        E('dve', lambda e: e.match_replace(
            out=candcat[:, :], in_to_replace=cmax8[:, :], in_values=candcat[:, :],
            imm_value=-1.0),
          reads=['candcat', 'cmax8'], writes=['candcat'])
    s56keys = [('s56', r) for r in range(7)]
    E('dve', lambda e: e.memset(sortpad[:, :], 1.0e9), writes=['sortpad'])
    E('dve', lambda e: e.tensor_copy(sortpad[:, 0:50], sort56[:, 0:50]),
      reads=s56keys, writes=['sortpad'])
    E('dve', lambda e: e.tensor_scalar(selneg[:, :], sortpad[:, :], -0.5, None, ALU.is_lt),
      reads=['sortpad'], writes=['selneg'])
    E('dve', lambda e: e.memset(big9[:, :], 1.0e9), writes=['big9'])
    E('dve', lambda e: e.select(sortpad[:, :], selneg[:, :], big9[:, :], sortpad[:, :],
                                add_drain=True),
      reads=['selneg', 'big9', 'sortpad'], writes=['sortpad'])
    E('pe', lambda e: e.matmul(ps2_b[0:64, 0:16], sortpad[:, :], idf32[0:16, 0:16],
                               start=True, stop=True),
      reads=['sortpad', 'idf32'], writes=[('ps2_b', 0)])
    E('dve', lambda e: e.tensor_copy(idxT_f[:, :], ps2_b[0:64, 0:16]),
      reads=[('ps2_b', 0)], writes=['idxT_f'])
    E('dve', lambda e: e.tensor_copy(idxT_i[:, :], idxT_f[:, :]),
      reads=['idxT_f'], writes=['idxT_i'])
    if debug:
        E('sp', lambda e: e.dma_start(dbg["dbg_idx"], idxT_f[:, :]), reads=['idxT_f'], dma='dbg')

    p.fence()
    ph2.close()
    st_m.close()

    if maxphase < 3:
        if debug:
            p.wait_all_dma('pool', 'dbg')
        p.build(); p.close()
        st_w.close(); sbstack.close()
        return nc

    phb = contextlib.ExitStack()
    kpair = [sbuf(phb, f"kpair{i}", [128, L], BF16) for i in range(2)]
    xg_f = sbuf(phb, "xg_f_s", [128, D], F32)
    xg_bf = sbuf(phb, "xg_bf_s", [128, D], BF16)
    xg_T = sbuf(phb, "xg_T_s", [128, 8, 128], BF16)
    qtopT = sbuf(phb, "qtopT_s", [128, 128], BF16)
    P_sb = sbuf(phb, "P_sb_s", [128, L], BF16)
    sumP = sbuf(phb, "sumP_s", [128, 32], F32)
    rsum = sbuf(phb, "rsum_s", [128, 1], F32)
    PT_sb = [sbuf(phb, f"PT_sb{i}", [128, 512], BF16) for i in range(2)]
    vsc = [sbuf(phb, f"vsc{i}", [128, 8, 128], BF16) for i in range(2)]
    dctx = sbuf(phb, "dctx_s", [128, 128], BF16)
    st3 = sbuf(phb, "st3_s", [128, 4], F32)
    ps_s = [psum(phb, f"ps_s{i}", [128, 512], F32) for i in range(2)]
    ps_pt = [psum(phb, f"ps_pt{i}", [128, 512], F32) for i in range(2)]
    ps_cx = psum(phb, "ps_cx_s", [128, 128], F32)
    ps_d = [psum(phb, f"ps_d{i}", [128, 512], F32) for i in range(2)]

    for c in range(8):
        E('dve', lambda e, c=c: e.memset(delta[c].ap(), 0.0), writes=[('dz', c)])
        E('dve', lambda e, c=c: e.memset(idxv[c].ap(), 2.0e9), writes=[('iz', c)])

    for pr in range(8):
        kp = kpair[pr % 2]
        kpk = ('kpair', pr % 2)
        E('sp', lambda e, pr=pr, kp=kp: e.dma_start(kp.ap(), kfm_d[pr, :, :]),
          reads=['kfm_w'], writes=[kpk], dma=f'kr{pr % 2}')
        E('dve', lambda e: e.memset(xg_f.ap(), 0.0), writes=['xg_f'])
        E('pool', lambda e, pr=pr: e.indirect_dma_start(
            xg_f[0:U, :], None, x_d, IndirectOffsetOnAxis(idxT_i[0:U, 2*pr:2*pr+1], 0),
            bounds_check=L - 1, oob_is_err=False),
          reads=['idxT_i', 'xg_f'], writes=[('xg_g', 0)], dma='g1')
        E('pool', lambda e, pr=pr: e.indirect_dma_start(
            xg_f[64:64+U, :], None, x_d, IndirectOffsetOnAxis(idxT_i[0:U, 2*pr+1:2*pr+2], 0),
            bounds_check=L - 1, oob_is_err=False),
          reads=['idxT_i', 'xg_f'], writes=[('xg_g', 1)], dma='g1')
        layernorm(xg_f[:, :], ['xg_f', ('xg_g', 0), ('xg_g', 1)], xg_bf[:, :], 'xg_bf',
                  128, epsA, 'epsA', st3, 'st3', lnexp=True)
        for c in range(8):
            bank = ps_pt[c % 2]
            bkk = ('ps_pt2', c % 2)
            E('pe', lambda e, c=c, bank=bank: e.matmul(
                bank[:, 0:128], xg_bf[:, c*128:(c+1)*128], id_bf[:, :],
                start=True, stop=True),
              reads=['xg_bf', 'id_bf'], writes=[bkk])
            E('dve', lambda e, c=c, bank=bank: e.tensor_copy(
                xg_T[:, c, :], bank[:, 0:128]),
              reads=[bkk], writes=[('xg_T', c)])
        def qtop_mm(e, pr=pr):
            for k in range(8):
                e.matmul(ps_d[0][:, 0:128], wqT[:, k, pr*128:(pr+1)*128], xg_T[:, k, :],
                         start=(k == 0), stop=False)
            return e.matmul(ps_d[0][:, 0:128], bq_sb[:, pr*128:(pr+1)*128],
                            ones_row[:, 0:128], start=False, stop=True)
        E('pe', qtop_mm, reads=[('xg_T', c) for c in range(8)] +
          ['wqT', 'bq_sb', 'ones_row'], writes=['ps_d0'])
        E('dve', lambda e: e.memset(qtopT.ap(), 0.0), writes=['qtop0'])
        E('dve', lambda e: e.tensor_copy(qtopT[0:64, 0:50], ps_d[0][0:64, 0:50]),
          reads=['ps_d0', 'qtop0'], writes=[('qtopT', 0)])
        E('dve', lambda e: e.tensor_copy(qtopT[64:128, 64:114], ps_d[0][64:128, 64:114]),
          reads=['ps_d0', 'qtop0'], writes=[('qtopT', 1)])
        for kc in range(32):
            bank = ps_s[kc % 2]
            bkk = ('ps_s', kc % 2)
            E('pe', lambda e, kc=kc, kp=kp, bank=bank: e.matmul(
                bank[:, :], qtopT[:, :], kp[:, kc*512:(kc+1)*512],
                start=True, stop=True),
              reads=[('qtopT', 0), ('qtopT', 1), 'qtop0', kpk], writes=[bkk])
            E('act', lambda e, kc=kc, bank=bank: e.activation(
                P_sb[:, kc*512:(kc+1)*512], bank[:, :], AF.Exp,
                bias=negb[:, :], scale=SCALE, accum_out=sumP[:, kc:kc+1]),
              reads=[bkk, 'negb'], writes=[('P', kc), ('sumP', kc)])
        E('dve', lambda e: e.reduce_sum(rsum[:, :], sumP[:, :], axis=AX.X),
          reads=[('sumP', kc) for kc in range(32)], writes=['rsum'])
        E('dve', lambda e: e.reciprocal(rsum[:, :], rsum[:, :]),
          reads=['rsum'], writes=['rsum'])
        E('dve', lambda e: e.tensor_scalar(P_sb[:, :], P_sb[:, :], rsum[:, :], None,
                                           ALU.mult),
          reads=[('P', kc) for kc in range(32)] + ['rsum'], writes=['Pn'])
        for scb in range(16):
            vb = vsc[scb % 2]
            vbk = ('vsc', scb % 2)
            E('sp', lambda e, pr=pr, scb=scb, vb=vb: e.dma_start(
                vb.ap(), vscr_d[pr, scb*8:(scb+1)*8, :, :].rearrange("s p d -> p s d")),
              reads=['vscr_w'], writes=[vbk], dma=f'vr{scb % 2}')
            for q4 in range(2):
                sc0 = scb * 8 + q4 * 4
                ptb = PT_sb[q4]
                ptk = ('PT', q4)
                psb = ps_pt[q4]
                psk = ('ps_pt2', q4)
                def ptmm(e, sc0=sc0, psb=psb):
                    last = None
                    for jj in range(4):
                        last = e.matmul(psb[:, jj*128:(jj+1)*128],
                                        P_sb[:, (sc0+jj)*128:(sc0+jj+1)*128],
                                        id_bf[:, :], start=True, stop=True)
                    return last
                E('pe', ptmm, reads=['Pn', 'id_bf'], writes=[psk])
                E('dve', lambda e, ptb=ptb, psb=psb: e.tensor_copy(ptb[:, :], psb[:, :]),
                  reads=[psk], writes=[ptk])
                def cxmm(e, sc0=sc0, q4=q4, vb=vb, ptb=ptb):
                    last = None
                    for jj in range(4):
                        sc = sc0 + jj
                        last = e.matmul(ps_cx[:, :], vb[:, q4*4+jj, :],
                                        ptb[:, jj*128:(jj+1)*128],
                                        start=(sc == 0), stop=(sc == 127))
                    return last
                E('pe', cxmm, reads=[vbk, ptk], writes=['ps_cx'])
        E('dve', lambda e: e.memset(dctx.ap(), 0.0), writes=['dctx0'])
        E('dve', lambda e, pr=pr: e.tensor_scalar(
            dctx[0:64, 0:50], ps_cx[0:64, 0:50], vmeanT[0:64, pr:pr+1], None,
            ALU.subtract),
          reads=['ps_cx', ('vmeanT', pr), 'dctx0'], writes=[('dctx', 0)])
        E('dve', lambda e, pr=pr: e.tensor_scalar(
            dctx[64:128, 64:114], ps_cx[64:128, 64:114], vmeanT[64:128, pr:pr+1], None,
            ALU.subtract),
          reads=['ps_cx', ('vmeanT', pr), 'dctx0'], writes=[('dctx', 1)])
        def dmm(e, pr=pr):
            e.matmul(ps_d[0][:, :], dctx[:, :], woT_sb[:, pr, 0:512],
                     start=True, stop=True)
            return e.matmul(ps_d[1][:, :], dctx[:, :], woT_sb[:, pr, 512:1024],
                            start=True, stop=True)
        E('pe', dmm, reads=[('dctx', 0), ('dctx', 1), 'dctx0', 'woT_sb'],
          writes=['ps_d0', 'ps_d1'])
        E('dve', lambda e, pr=pr: e.tensor_copy(delta[pr][:, 0:512], ps_d[0][:, :]),
          reads=['ps_d0', ('dz', pr)],
          writes=[('delta', pr, 0, 0), ('delta', pr, 64, 0)])
        E('dve', lambda e, pr=pr: e.tensor_copy(delta[pr][:, 512:1024], ps_d[1][:, :]),
          reads=['ps_d1', ('dz', pr)],
          writes=[('delta', pr, 0, 1), ('delta', pr, 64, 1)])
        E('dve', lambda e, pr=pr: e.tensor_copy(idxv[pr][0:50, :],
                                                idxT_f[0:50, 2*pr:2*pr+1]),
          reads=['idxT_f', ('iz', pr)], writes=[('idxv', pr, 0)])
        E('dve', lambda e, pr=pr: e.tensor_copy(idxv[pr][64:114, :],
                                                idxT_f[0:50, 2*pr+1:2*pr+2]),
          reads=['idxT_f', ('iz', pr)], writes=[('idxv', pr, 64)])
    if debug:
        dltf = sbuf(phb, "dltf_s", [128, D], F32)
        E('dve', lambda e: e.tensor_copy(dltf[:, :], delta[0].ap()),
          reads=[('delta', 0, 0, 0), ('delta', 0, 0, 1), ('delta', 0, 64, 0),
                 ('delta', 0, 64, 1), ('dz', 0)],
          writes=['dltf'])
        E('sp', lambda e: e.dma_start(dbg["dbg_delta"], dltf[:, :]), reads=['dltf'], dma='dbg')

    p.fence()
    phb.close()
    st_w.close()

    if maxphase < 5:
        if debug:
            p.wait_all_dma('pool', 'dbg')
        p.build(); p.close()
        sbstack.close()
        return nc

    phc = contextlib.ExitStack()
    wr_sb = sbuf(phc, "wr_sb_s", [128, 8, NE], BF16)
    b1_sb = sbuf(phc, "b1_sb_s", [NE, 32 * 128], BF16)
    gam_b = sbuf(phc, "gam_b_s", [128, D], F32)
    bet_b = sbuf(phc, "bet_b_s", [128, D], F32)
    bof_sb = sbuf(phc, "bof_sb_s", [1, D], BF16)
    gb_tmp = sbuf(phc, "gb_tmp_s", [1, D], BF16)
    xq_in = [sbuf(phc, f"xq_in{i}", [128, D], F32) for i in range(2)]
    Sel8 = sbuf(phc, "Sel8_s", [128, 8, 128], BF16)
    ish8 = sbuf(phc, "ish8_s", [128, 8], F32)
    idxv8 = sbuf(phc, "idxv8_s", [128, 8], F32)
    attn_f = sbuf(phc, "attn_f_s", [128, D], F32)
    y1_t = sbuf(phc, "y1_t_s", [128, D], F32)
    t1_t = sbuf(phc, "t1_t_s", [128, D], F32)
    t1b8 = sbuf(phc, "t1b8_s", [128, 8, D], BF16)
    t1T_tmp = sbuf(phc, "t1T_tmp_s", [128, D], BF16)
    sts = [sbuf(phc, f"sts{i}", [128, 4], F32) for i in range(4)]
    oh4 = sbuf(phc, "oh4_s", [128, NE], F32)
    ohbf = sbuf(phc, "ohbf_s", [128, NE], BF16)
    noty = sbuf(phc, "noty_s", [128, 1], F32)
    eq4 = sbuf(phc, "eq4_s", [128, NE], F32)
    m4 = sbuf(phc, "m4_s", [128, 1], F32)
    diag8 = sbuf(phc, "diag8_s", [128, 8, 512], BF16)
    t1eT = sbuf(phc, "t1eT_s", [128, 8, NE, 1024], BF16)
    onehT2 = [sbuf(phc, f"onehT{i}_s", [NE, 1024], BF16) for i in range(2)]
    inter_st = [sbuf(phc, f"inter_st{i}_x", [128, 512], BF16) for i in range(2)]
    intc = [sbuf(phc, f"intc{i}_x", [128, 512], BF16) for i in range(2)]
    w1sl = [sbuf(phc, f"w1sl{i}", [128, 8, NE, 128], BF16) for i in range(2)]
    wofc = [sbuf(phc, f"wofc{i}", [128, D], BF16) for i in range(2)]
    t2_t = sbuf(phc, "t2_t_s", [128, D], F32)
    t3_t = sbuf(phc, "t3_t_s", [128, D], F32)
    ps_ab = [psum(phc, f"ps_ab{i}", [128, 1024], F32) for i in range(2)]
    ps_z = [psum(phc, f"ps_z{i}", [128, 512], F32) for i in range(2)]
    ps_tT = [psum(phc, f"ps_tT{i}", [128, 512], F32) for i in range(2)]

    E = mkE('gb')
    E('pool', lambda e: e.dma_start(wr_sb.ap(), wr_d.rearrange("(kc p) f -> p kc f", p=128)),
      writes=['wr_sb'], dma='c_wr')
    E('pool', lambda e: e.dma_start(b1_sb.ap(), b1r_d), writes=['b1_sb'], dma='c_b1')
    E('pool', lambda e: e.dma_start(bof_sb[:, :], bof_d), writes=['bof_sb'], dma='c_bof')
    E('pool', lambda e: e.dma_start(gb_tmp[:, :], gam_d), writes=['gb_tmp'], dma='c_gb')
    for hf in range(2):
        E('pe', lambda e, hf=hf: e.matmul(ps_z[0][:, :], ones_row[0:1, 0:128],
                                          gb_tmp[:, hf*512:(hf+1)*512],
                                          start=True, stop=True),
          reads=['gb_tmp', 'ones_row'], writes=[('ps_z', 0)])
        E('dve', lambda e, hf=hf: e.tensor_copy(gam_b[:, hf*512:(hf+1)*512],
                                                ps_z[0][:, :]),
          reads=[('ps_z', 0)], writes=['gam_b'])
    E('pool', lambda e: e.dma_start(gb_tmp[:, :], bet_d), writes=['gb_tmp'], dma='c_gb')
    for hf in range(2):
        E('pe', lambda e, hf=hf: e.matmul(ps_z[0][:, :], ones_row[0:1, 0:128],
                                          gb_tmp[:, hf*512:(hf+1)*512],
                                          start=True, stop=True),
          reads=['gb_tmp', 'ones_row'], writes=[('ps_z', 0)])
        E('dve', lambda e, hf=hf: e.tensor_copy(bet_b[:, hf*512:(hf+1)*512],
                                                ps_z[0][:, :]),
          reads=[('ps_z', 0)], writes=['bet_b'])

    for ch in range(8):
        E('dve', lambda e, ch=ch: e.tensor_copy(idxv8[:, ch:ch+1], idxv[ch][:, :]),
          reads=[('idxv', ch, 0), ('idxv', ch, 64), ('iz', ch)], writes=['idxv8'])
    dkeys = [('delta', ch, bs, hf) for ch in range(8) for bs in (0, 64) for hf in range(2)] + \
            [('dz', c) for c in range(8)]
    ikeys = [('idxv', ch, bs) for ch in range(8) for bs in (0, 64)] + \
            [('iz', c) for c in range(8)]

    def emit_A1_tile(sg, tl):
        sgt0 = sg * 8
        t = sgt0 + tl
        E = mkE('as')
        xi = xq_in[tl % 2]
        xik = ('xq_in', tl % 2)
        E('sp', lambda e, t=t, xi=xi: e.dma_start(xi.ap(), xq_d[t*128:(t+1)*128, :]),
          writes=[xik], dma=f'xq{tl % 2}')
        pab = ps_ab[tl % 2]
        pa, pb = pab[:, 0:512], pab[:, 512:1024]
        pak, pbk = ('ps_ab', tl % 2, 0), ('ps_ab', tl % 2, 1)
        E('dve', lambda e, t=t: e.tensor_scalar(
            ish8[:, :], idxv8[:, :], qoffc[:, :], float(t * 128),
            ALU.subtract, ALU.subtract),
          reads=['idxv8', 'qoffc'], writes=['ish8'])
        E('dve', lambda e: e.tensor_tensor(
            Sel8.ap(),
            ish8.ap().unsqueeze(2).to_broadcast([128, 8, 128]),
            iotacol.ap().unsqueeze(1).to_broadcast([128, 8, 128]),
            ALU.is_equal),
          reads=['ish8', 'iotacol'], writes=['Sel8'])
        for ch in range(8):
            E('pe', lambda e, ch=ch, pa=pa: e.matmul(
                pa, Sel8[:, ch, :], delta[ch][:, 0:512],
                start=(ch == 0), stop=False),
              reads=['Sel8'] + dkeys, writes=[pak])
            E('pe', lambda e, ch=ch, pb=pb: e.matmul(
                pb, Sel8[:, ch, :], delta[ch][:, 512:1024],
                start=(ch == 0), stop=False),
              reads=['Sel8'] + dkeys, writes=[pbk])
        E('pe', lambda e, pa=pa: e.matmul(pa, ones_row[0:1, 0:128],
                                          base_bf[:, 0:512], start=False, stop=True),
          reads=['ones_row', 'base_bf'], writes=[pak])
        E('pe', lambda e, pb=pb: e.matmul(pb, ones_row[0:1, 0:128],
                                          base_bf[:, 512:1024], start=False, stop=True),
          reads=['ones_row', 'base_bf'], writes=[pbk])
        E('dve', lambda e, pa=pa: e.tensor_copy(attn_f[:, 0:512], pa),
          reads=[pak], writes=['attn_f'])
        E('dve', lambda e, pb=pb: e.tensor_copy(attn_f[:, 512:1024], pb),
          reads=[pbk], writes=['attn_f'])
        layernorm(attn_f[:, :], ['attn_f'], y1_t[:, :], 'y1', 128, epsA, 'epsA',
                  sts[0], 'sA')
        E('dve', lambda e, xi=xi: e.tensor_tensor(y1_t[:, :], y1_t[:, :], xi.ap(),
                                                  ALU.add),
          reads=['y1', xik], writes=['y1'])
        E('sp', lambda e, t=t: e.dma_start(y1s_d[t*128:(t+1)*128, :], y1_t[:, :]),
          reads=['y1'], writes=[('y1s', t)], dma='y1w')
        layernorm(y1_t[:, :], ['y1'], t1_t[:, :], 't1', 128, epsA, 'epsA',
                  sts[1], 'sB')
        E('sp', lambda e, t=t: e.dma_start(t1s_d[t*128:(t+1)*128, :], t1_t[:, :]),
          reads=['t1'], writes=[('t1s', t)], dma='t1w')
        E('dve', lambda e, tl=tl: e.tensor_copy(t1b8[:, tl, :], t1_t[:, :]),
          reads=['t1'], writes=[('t1b8', tl)])
        if debug and t == 0:
            E('sp', lambda e: e.dma_start(dbg["dbg_y1"], y1_t[:, :]), reads=['y1'], dma='dbg')
            E('sp', lambda e: e.dma_start(dbg["dbg_t1"], t1_t[:, :]), reads=['t1'], dma='dbg')

    def emit_A2_tile(sg, tl):
        E = mkE('as2')
        for cb in range(2):
            bank = ps_tT[cb]
            bkk = ('ps_tT', cb)
            for j in range(4):
                c = cb * 4 + j
                E('pe', lambda e, c=c, j=j, bank=bank, tl=tl: e.matmul(
                    bank[:, j*128:(j+1)*128], t1b8[:, tl, c*128:(c+1)*128], id_bf[:, :],
                    start=True, stop=True),
                  reads=[('t1b8', tl), 'id_bf'], writes=[bkk])
            E('dve', lambda e, cb=cb, bank=bank: e.tensor_copy(
                t1T_tmp[:, cb*512:(cb+1)*512], bank[:, :]),
              reads=[bkk], writes=[('t1T', cb)])
        def lg_mm(e):
            last = None
            for c in range(8):
                last = e.matmul(ps_tT[0][:, 0:NE], t1T_tmp[:, c*128:(c+1)*128],
                                wr_sb[:, c, :], start=(c == 0), stop=(c == 7))
            return last
        E('pe', lg_mm, reads=[('t1T', 0), ('t1T', 1), 'wr_sb'], writes=[('ps_tT', 0)])
        E('dve', lambda e: e.reduce_max(m4[:, :], ps_tT[0][:, 0:NE], axis=AX.X),
          reads=[('ps_tT', 0)], writes=['m4'])
        E('dve', lambda e: e.tensor_scalar(eq4[:, :], ps_tT[0][:, 0:NE], m4[:, :],
                                           None, ALU.is_ge),
          reads=[('ps_tT', 0), 'm4'], writes=['eq4'])
        E('dve', lambda e: e.tensor_copy(oh4[:, 0:1], eq4[:, 0:1]),
          reads=['eq4'], writes=['oh4'])
        E('dve', lambda e: e.tensor_scalar(noty[:, :], eq4[:, 0:1], -1.0, 1.0,
                                           ALU.mult, ALU.add),
          reads=['eq4'], writes=['noty'])
        for ee in range(1, NE):
            E('dve', lambda e, ee=ee: e.tensor_tensor(
                oh4[:, ee:ee+1], eq4[:, ee:ee+1], noty[:, :], ALU.mult),
              reads=['eq4', 'noty'], writes=['oh4'])
            if ee < NE - 1:
                E('dve', lambda e, ee=ee: e.tensor_scalar(
                    junk1[:, :], oh4[:, ee:ee+1], -1.0, 1.0, ALU.mult, ALU.add),
                  reads=['oh4'], writes=['junk1'])
                E('dve', lambda e: e.tensor_tensor(noty[:, :], noty[:, :],
                                                   junk1[:, :], ALU.mult),
                  reads=['noty', 'junk1'], writes=['noty'])
        E('dve', lambda e: e.tensor_copy(ohbf[:, :], oh4[:, :]),
          reads=['oh4'], writes=['ohbf'])

    def emit_A3_tile(sg, tl):
        sgp = sg % 2
        E = mkE('as3')
        E('pe', lambda e: e.matmul(ps_tT[1][0:NE, 0:128], ohbf[:, :], id_bf[:, :],
                                   start=True, stop=True),
          reads=['ohbf', 'id_bf'], writes=[('ps_tT', 1)])
        E('dve', lambda e, tl=tl, sgp=sgp: e.tensor_copy(
            onehT2[sgp][:, tl*128:(tl+1)*128], ps_tT[1][0:NE, 0:128]),
          reads=[('ps_tT', 1)], writes=[('onehT', sgp, tl)])
        for ee in range(NE):
            E('dve', lambda e, ee=ee, tl=tl: e.tensor_scalar(
                diag8[:, tl, ee*128:(ee+1)*128], id_bf[:, :], oh4[:, ee:ee+1], None,
                ALU.mult),
              reads=['id_bf', 'oh4'], writes=[('diag8', tl, ee)])

    def emit_B_tile(sg, tl):
        E = mkE('as3')
        for c in range(8):
            bank = ps_z[c % 2]
            bkk = ('ps_z', c % 2)
            E('pe', lambda e, c=c, tl=tl, bank=bank: e.matmul(
                bank[:, :], t1b8[:, tl, c*128:(c+1)*128], diag8[:, tl, :],
                start=True, stop=True),
              reads=[('t1b8', tl)] + [('diag8', tl, ee) for ee in range(NE)],
              writes=[bkk])
            if c % 4 < 2:
                E('dve', lambda e, c=c, tl=tl, bank=bank: e.tensor_copy(
                    t1eT[:, c, :, tl*128:(tl+1)*128],
                    bank.ap().rearrange("p (e f) -> p e f", e=NE)),
                  reads=[bkk], writes=[('t1eT', c, tl)])
            else:
                E('act', lambda e, c=c, tl=tl, bank=bank: e.activation(
                    t1eT[:, c, :, tl*128:(tl+1)*128],
                    bank.ap().rearrange("p (e f) -> p e f", e=NE), AF.Copy),
                  reads=[bkk], writes=[('t1eT', c, tl)])

    t1e_keys = [('t1eT', c, tl) for c in range(8) for tl in range(8)]

    def emit_w1_prefetch():
        E = mkE('hp')
        for ifc in range(2):
            E('pool', lambda e, ifc=ifc, wsl=w1sl[ifc]: e.dma_start(
                wsl.ap(), w1r_d[ifc].rearrange("kc e p f -> p kc e f")),
              writes=[('w1sl', ifc)], dma=f'w1_{ifc}')

    def emit_h(sg, hook=None):
        sgp = sg % 2
        E = mkE('hp')
        oh_keys = [('onehT', sgp, tl) for tl in range(8)]
        for ifc in range(32):
            wsl = w1sl[ifc % 2]
            wk = ('w1sl', ifc % 2)
            if ifc >= 2:
                E('pool', lambda e, ifc=ifc, wsl=wsl: e.dma_start(
                    wsl.ap(), w1r_d[ifc].rearrange("kc e p f -> p kc e f")),
                  writes=[wk], dma=f'w1_{ifc % 2}')
            if ifc == 16:
                for pfc in range(2):
                    E('pool', lambda e, pfc=pfc, wc=wofc[pfc]: e.dma_start(
                        wc.ap(), wof_d[pfc*128:(pfc+1)*128, :]),
                      writes=[('wofc', pfc)], dma=f'wo{pfc}')
            for hg in range(2):
                bank = ps_z[hg]
                bkk = ('ps_z', hg)
                def hmm(e, ifc=ifc, hg=hg, wsl=wsl, bank=bank, sgp=sgp):
                    first = True
                    for ee in range(NE):
                        for k in range(8):
                            e.matmul(bank[:, :], wsl[:, k, ee, :],
                                     t1eT[:, k, ee, hg*512:(hg+1)*512],
                                     start=first, stop=False)
                            first = False
                    return e.matmul(bank[:, :], b1_sb[:, ifc*128:(ifc+1)*128],
                                    onehT2[sgp][:, hg*512:(hg+1)*512],
                                    start=False, stop=True)
                E('pe', hmm, reads=[wk, 'b1_sb'] + t1e_keys + oh_keys, writes=[bkk])
                ist = inter_st[(ifc * 2 + hg) % 2]
                istk = ('ist', (ifc * 2 + hg) % 2)
                E('act', lambda e, bank=bank, ist=ist: e.activation(
                    ist[:, :], bank[:, :], AF.Gelu),
                  reads=[bkk], writes=[istk])
                E('sp', lambda e, ifc=ifc, hg=hg, ist=ist: e.dma_start(
                    intT_d[ifc, :, hg*512:(hg+1)*512], ist[:, :]),
                  reads=[istk], writes=[('intd', ifc, hg)], dma='intw')
            if hook is not None:
                hook(ifc)

    def emit_wo(sg):
        sgt0 = sg * 8
        if sg < 3:
            emit_w1_prefetch()
        E = mkE('wo')
        wo_banks = [ps_ab[0][:, 0:512], ps_ab[0][:, 512:1024],
                    ps_ab[1][:, 0:512], ps_ab[1][:, 512:1024],
                    ps_z[0][:, :], ps_z[1][:, :], ps_tT[0][:, :], ps_tT[1][:, :]]
        wo_keys = [('ps_ab', 0, 0), ('ps_ab', 0, 1), ('ps_ab', 1, 0), ('ps_ab', 1, 1),
                   ('ps_z', 0), ('ps_z', 1), ('ps_tT', 0), ('ps_tT', 1)]
        for hb in range(2):
            for ifc in range(32):
                wc = wofc[ifc % 2]
                wck = ('wofc', ifc % 2)
                if hb == 1 or ifc >= 2:
                    E('pool', lambda e, ifc=ifc, wc=wc: e.dma_start(
                        wc.ap(), wof_d[ifc*128:(ifc+1)*128, :]),
                      writes=[wck], dma=f'wo{ifc % 2}')
                ic = intc[ifc % 2]
                ick = ('intc', ifc % 2)
                E('sp', lambda e, ifc=ifc, hb=hb, ic=ic: e.dma_start(
                    ic.ap(), intT_d[ifc, :, hb*512:(hb+1)*512]),
                  reads=[('intd', ifc, hb)], writes=[ick], dma=f'ir{ifc % 2}')
                for t4 in range(4):
                    tl = hb * 4 + t4
                    t = sgt0 + tl
                    pa, pb = wo_banks[2*t4], wo_banks[2*t4+1]
                    pak, pbk = wo_keys[2*t4], wo_keys[2*t4+1]
                    E('pe', lambda e, ifc=ifc, t4=t4, wc=wc, ic=ic, pa=pa: e.matmul(
                        pa, ic[:, t4*128:(t4+1)*128], wc[:, 0:512],
                        start=(ifc == 0), stop=False),
                      reads=[wck, ick], writes=[pak])
                    E('pe', lambda e, ifc=ifc, t4=t4, wc=wc, ic=ic, pb=pb: e.matmul(
                        pb, ic[:, t4*128:(t4+1)*128], wc[:, 512:1024],
                        start=(ifc == 0), stop=False),
                      reads=[wck, ick], writes=[pbk])
            for t4 in range(4):
                pa, pb = wo_banks[2*t4], wo_banks[2*t4+1]
                pak, pbk = wo_keys[2*t4], wo_keys[2*t4+1]
                E('pe', lambda e, pa=pa: e.matmul(pa, ones_row[0:1, 0:128],
                                                  bof_sb[:, 0:512], start=False, stop=True),
                  reads=['ones_row', 'bof_sb'], writes=[pak])
                E('pe', lambda e, pb=pb: e.matmul(pb, ones_row[0:1, 0:128],
                                                  bof_sb[:, 512:1024], start=False, stop=True),
                  reads=['ones_row', 'bof_sb'], writes=[pbk])
            order = [0, 1, 2, 3] if hb == 0 else [2, 3, 0, 1]
            tbufs = [(xq_in[0], ('xq_in', 0)), (xq_in[1], ('xq_in', 1))]
            ybufs = [(attn_f, 'attn_f'), (y1_t, 'y1')]

            def rl_t1(i):
                t = sgt0 + hb * 4 + order[i]
                buf, key = tbufs[i % 2]
                E('sp', lambda e, t=t, buf=buf: e.dma_start(
                    buf.ap(), t1s_d[t*128:(t+1)*128, :]),
                  reads=[('t1s', t)], writes=[key], dma=f'rl{i % 2}')

            def rl_y1(i):
                t = sgt0 + hb * 4 + order[i]
                buf, key = ybufs[i % 2]
                E('sp', lambda e, t=t, buf=buf: e.dma_start(
                    buf.ap(), y1s_d[t*128:(t+1)*128, :]),
                  reads=[('y1s', t)], writes=[key], dma=f'rl{2 + i % 2}')

            rl_t1(0); rl_y1(0); rl_t1(1); rl_y1(1)
            for i, t4 in enumerate(order):
                t = sgt0 + hb * 4 + t4
                pa, pb = wo_banks[2*t4], wo_banks[2*t4+1]
                pak, pbk = wo_keys[2*t4], wo_keys[2*t4+1]
                tb, tbk = tbufs[i % 2]
                yb, ybk = ybufs[i % 2]
                E('dve', lambda e, pa=pa, tb=tb: e.tensor_tensor(
                    t2_t[:, 0:512], pa, tb[:, 0:512], ALU.add),
                  reads=[pak, tbk], writes=['T2'])
                E('dve', lambda e, pb=pb, tb=tb: e.tensor_tensor(
                    t2_t[:, 512:1024], pb, tb[:, 512:1024], ALU.add),
                  reads=[pbk, tbk], writes=['T2'])
                if i + 2 < 4:
                    rl_t1(i + 2)
                layernorm(t2_t[:, :], ['T2'], t3_t[:, :], 'T3', 128, epsB, 'epsB',
                          sts[2], 'sC')
                E('dve', lambda e: e.tensor_tensor(t3_t[:, :], t3_t[:, :], gam_b[:, :],
                                                   ALU.mult),
                  reads=['T3', 'gam_b'], writes=['T3'])
                E('dve', lambda e: e.tensor_tensor(t3_t[:, :], t3_t[:, :], bet_b[:, :],
                                                   ALU.add),
                  reads=['T3', 'bet_b'], writes=['T3'])
                layernorm(t3_t[:, :], ['T3'], t2_t[:, :], 'T2', 128, epsA, 'epsA',
                          sts[3], 'sD')
                E('dve', lambda e, yb=yb: e.tensor_tensor(t3_t[:, :], t2_t[:, :],
                                                          yb[:, :], ALU.add),
                  reads=['T2', ybk], writes=['T3'])
                if i + 2 < 4:
                    rl_y1(i + 2)
                E('sp', lambda e, t=t: e.dma_start(out_d[t*128:(t+1)*128, :], t3_t[:, :]),
                  reads=['T3'], dma='out')

    emit_w1_prefetch()
    for tl in range(8):
        emit_A1_tile(0, tl)
        emit_A2_tile(0, tl)
        emit_A3_tile(0, tl)
    for tl in range(8):
        emit_B_tile(0, tl)
    for sg in range(4):
        def hook(ifc, sg=sg):
            if sg == 3:
                return
            if ifc >= 2 and (ifc - 2) % 3 == 0 and (ifc - 2) // 3 < 8:
                emit_A1_tile(sg + 1, (ifc - 2) // 3)
            if ifc >= 4 and (ifc - 4) % 3 == 0 and (ifc - 4) // 3 < 8:
                emit_A2_tile(sg + 1, (ifc - 4) // 3)
            if ifc >= 5 and (ifc - 5) % 3 == 0 and (ifc - 5) // 3 < 8:
                emit_A3_tile(sg + 1, (ifc - 5) // 3)
        emit_h(sg, hook)
        if sg < 3:
            for tl in range(8):
                emit_B_tile(sg + 1, tl)
        emit_wo(sg)

    E = _E
    if debug:
        p.wait_all_dma('pool', 'dbg')
    p.wait_all_dma('pool', 'out')
    p.build()
    p.close()
    phc.close()
    sbstack.close()
    return nc


_CACHE = {}


def _host_inputs(inputs):
    x = np.asarray(inputs['L0_attn_x'], np.float32)
    wqT = np.ascontiguousarray(np.asarray(inputs['L0_attn_Wq'], np.float32).T)
    wkT = np.ascontiguousarray(np.asarray(inputs['L0_attn_Wk'], np.float32).T)
    wvT = np.ascontiguousarray(np.asarray(inputs['L0_attn_Wv'], np.float32).T)
    woT = np.ascontiguousarray(np.asarray(inputs['L0_attn_Wo'], np.float32).T)
    bq = np.ascontiguousarray(np.asarray(inputs['L0_attn_bq'], np.float32)[None, :])
    bk = np.ascontiguousarray(np.asarray(inputs['L0_attn_bk'], np.float32)[None, :])
    bv = np.ascontiguousarray(np.asarray(inputs['L0_attn_bv'], np.float32)[None, :])
    bo = np.ascontiguousarray(np.asarray(inputs['L0_attn_bo'], np.float32)[None, :])
    idxk = np.ascontiguousarray(np.asarray(inputs['L0_attn_idx_k'], np.int32))
    Wr = np.ascontiguousarray(np.asarray(inputs['L0_ffn_W_router'], np.float32))
    W1 = np.asarray(inputs['L0_ffn_W1'], np.float32)
    w1r = np.ascontiguousarray(W1.reshape(NE, 8, 128, 32, 128).transpose(3, 1, 0, 2, 4))
    b1r = np.ascontiguousarray(np.asarray(inputs['L0_ffn_b1'], np.float32)
                               .reshape(NE, 32 * 128))
    Wof = np.ascontiguousarray(np.asarray(inputs['L0_ffn_Wo'], np.float32))
    bof = np.ascontiguousarray(np.asarray(inputs['L0_ffn_bo'], np.float32)[None, :])
    gam = np.ascontiguousarray(np.asarray(inputs['L0_ffn_ln_gamma'], np.float32)[None, :])
    bet = np.ascontiguousarray(np.asarray(inputs['L0_ffn_ln_beta'], np.float32)[None, :])
    ident = np.eye(128, dtype=np.float32)
    iotacol = np.tile(np.arange(128, dtype=np.float32)[None, :], (128, 1))
    iotaM = np.empty((128, H * 128), np.float32)
    for h in range(H):
        iotaM[:, h*128:(h+1)*128] = (np.arange(128)[None, :] * 128
                                     + np.arange(128)[:, None])
    xbs = [np.ascontiguousarray(x[0]), np.ascontiguousarray(x[1])]
    in_maps = []
    for c in range(8):
        b, q = c // 4, c % 4
        in_maps.append({
            "x": xbs[b],
            "xq": np.ascontiguousarray(xbs[b][q*QTOK:(q+1)*QTOK]),
            "wqT": wqT, "wkT": wkT, "wvT": wvT, "woT": woT,
            "bq": bq, "bk": bk, "bv": bv, "bo": bo,
            "idxk": idxk, "wr": Wr, "w1r": w1r, "b1r": b1r,
            "wof": Wof, "bof": bof, "gam": gam, "bet": bet,
            "ident": ident, "iotacol": iotacol, "iotaM": iotaM,
            "qoffc": np.full((128, 1), q * QTOK, np.float32),
        })
    return in_maps


def kernel(**inputs):
    if 'nc' not in _CACHE:
        _CACHE['nc'] = build_program(debug=False)
    nc = _CACHE['nc']
    in_maps = _host_inputs(inputs)
    res = run_bass_kernel_spmd(nc, in_maps, core_ids=list(range(8)))
    out = np.empty((2, L, D), np.float32)
    for c in range(8):
        b, q = c // 4, c % 4
        out[b, q*QTOK:(q+1)*QTOK] = res.results[c]["y"]
    return out


# version 14
# speedup vs baseline: 1.2513x; 1.0028x over previous
import contextlib
import math

import numpy as np

import concourse.bass as bass
import concourse.mybir as mybir
from concourse.bass import IndirectOffsetOnAxis
from concourse.bass_utils import run_bass_kernel_spmd

F32 = mybir.dt.float32
BF16 = mybir.dt.bfloat16
I32 = mybir.dt.int32
U8 = mybir.dt.uint8
AF = mybir.ActivationFunctionType
ALU = mybir.AluOpType
AX = mybir.AxisListType

D = 1024
L = 16384
H = 16
DK = 64
U = 50
NG = 32
QTOK = 4096
NE = 4
DI = 4096
BISECT = 20
SCALE = 1.0 / math.sqrt(DK)


class Prog:
    ENGINES = ('pe', 'dve', 'act', 'pool', 'sp')

    def __init__(self, nc):
        self.nc = nc
        self.stack = contextlib.ExitStack()
        self.csem = {}
        self.ccount = {}
        for e in ('pe', 'dve', 'act', 'pool'):
            self.csem[e] = self.stack.enter_context(nc.semaphore(f"c_{e}"))
            self.ccount[e] = 0
        self.dsem = {}
        self.dcount = {}
        self.dqueue = {}
        self.ops = {e: [] for e in self.ENGINES}
        self.last_w = {}
        self.readers = {}
        self.know = {e: {} for e in self.ENGINES}
        self.read_misses = set()

    def chan(self, name, engine):
        if name not in self.dsem:
            self.dsem[name] = self.stack.enter_context(self.nc.semaphore(f"d_{name}"))
            self.dcount[name] = 0
            self.dqueue[name] = engine
        assert self.dqueue[name] == engine, (name, engine, self.dqueue[name])
        return name

    def _resolve(self, dep):
        if dep[0] == 'c':
            _, e, v = dep
            return (f"c_{e}", self.csem[e], v)
        _, ch = dep[:2]
        return (f"d_{ch}", self.dsem[ch], 16 * self.dcount[ch])

    def emit(self, engine, fn, reads=(), writes=(), dma=None, tag=None):
        deps = []
        for k in reads:
            w = self.last_w.get(k)
            if w is None:
                self.read_misses.add(k)
            else:
                deps.append(w)
        for k in writes:
            w = self.last_w.get(k)
            if w is not None:
                deps.append(w)
            deps.extend(self.readers.get(k, ()))
        need = {}
        for dep in deps:
            if dep[0] == 'c' and dep[1] == 'pe' and engine == 'pe' and dma is None:
                continue
            name, handle, v = self._resolve(dep)
            if self.know[engine].get(name, -1) >= v:
                continue
            if name in need and need[name][1] >= v:
                continue
            need[name] = (handle, v)
        waits = []
        for name, (handle, v) in need.items():
            waits.append((handle, v))
            self.know[engine][name] = v

        if dma is not None:
            ch = self.chan(dma, engine)
            self.dcount[ch] += 1
            inc = (self.dsem[ch], 16)
            marker = ('d', ch, self.dcount[ch])
        elif engine == 'sp':
            inc = None
            marker = None
        else:
            self.ccount[engine] += 1
            inc = (self.csem[engine], 1)
            marker = ('c', engine, self.ccount[engine])

        self.ops[engine].append((waits, fn, inc, tag))
        if marker is not None:
            for k in writes:
                self.last_w[k] = marker
                self.readers[k] = []
            for k in reads:
                self.readers.setdefault(k, []).append(marker)
        return marker

    def wait_all_dma(self, engine, channel):
        if channel not in self.dsem:
            return
        self.ops[engine].append(
            ([(self.dsem[channel], 16 * self.dcount[channel])], None, None, 'wd'))
        self.know[engine][f"d_{channel}"] = 16 * self.dcount[channel]

    def fence(self):
        targets = []
        for e in ('pe', 'dve', 'act', 'pool'):
            if self.ccount[e]:
                targets.append((f"c_{e}", self.csem[e], self.ccount[e]))
        for ch in self.dsem:
            if self.dcount[ch]:
                targets.append((f"d_{ch}", self.dsem[ch], 16 * self.dcount[ch]))
        for e in self.ENGINES:
            ws = []
            for name, handle, v in targets:
                if name == f"c_{e}":
                    continue
                if self.know[e].get(name, -1) >= v:
                    continue
                ws.append((handle, v))
                self.know[e][name] = v
            if ws:
                self.ops[e].append((ws, None, None, 'fence'))

    def build(self):
        nc = self.nc
        if self.read_misses:
            import sys
            print("PROG read-before-write keys:", sorted(map(str, self.read_misses)),
                  file=sys.stderr)
        with nc.Block() as blk:
            def replay(ename):
                def _go(eng):
                    for waits, fn, inc, tag in self.ops[ename]:
                        for handle, v in waits:
                            eng.wait_ge(handle, v)
                        if fn is not None:
                            inst = fn(eng)
                            if inc is not None:
                                inst.then_inc(inc[0], inc[1])
                return _go
            if self.ops['pe']: blk.tensor(replay('pe'))
            if self.ops['dve']: blk.vector(replay('dve'))
            if self.ops['act']: blk.scalar(replay('act'))
            if self.ops['pool']: blk.gpsimd(replay('pool'))
            if self.ops['sp']: blk.sync(replay('sp'))

    def close(self):
        self.stack.close()


def build_program(debug=False, maxphase=9, ngroups=NG, nbisect=BISECT, parts='all'):
    nc = bass.Bass("TRN2", target_bir_lowering=False)

    def din(name, shape, dt=F32):
        return nc.dram_tensor(name, shape, dt, kind="ExternalInput").ap()
    x_d = din("x", [L, D])
    xq_d = din("xq", [QTOK, D])
    wqT_d = din("wqT", [D, D]); wkT_d = din("wkT", [D, D]); wvT_d = din("wvT", [D, D])
    woT_d = din("woT", [D, D])
    bq_d = din("bq", [1, D]); bk_d = din("bk", [1, D]); bv_d = din("bv", [1, D])
    bo_d = din("bo", [1, D])
    idxk_d = din("idxk", [U], I32)
    wr_d = din("wr", [D, NE])
    w1r_d = din("w1r", [32, 8, NE, 128, 128])
    b1r_d = din("b1r", [NE, 32 * 128])
    wof_d = din("wof", [DI, D])
    bof_d = din("bof", [1, D]); gam_d = din("gam", [1, D]); bet_d = din("bet", [1, D])
    ident_d = din("ident", [128, 128])
    iotacol_d = din("iotacol", [128, 128])
    iotaM_d = din("iotaM", [128, H * 128])
    qoff_d = din("qoffc", [128, 1])
    out_d = nc.dram_tensor("y", [QTOK, D], F32, kind="ExternalOutput").ap()
    dbg = {}
    if debug:
        def dout(name, shape, dt=F32):
            dbg[name] = nc.dram_tensor(name, shape, dt, kind="ExternalOutput").ap()
        dout("dbg_M", [128, H * 128]); dout("dbg_idx", [64, H])
        dout("dbg_vmean", [1, D]); dout("dbg_base", [1, D])
        dout("dbg_delta", [128, D]); dout("dbg_y1", [128, D])
        dout("dbg_t1", [128, D]); dout("dbg_lo", [128, H])

    kfm_d = nc.dram_tensor("kfm", [8, 128, L], BF16, kind="Internal").ap()
    vscr_d = nc.dram_tensor("vscr", [8, 128, 128, 128], BF16, kind="Internal").ap()
    t1s_d = nc.dram_tensor("t1s", [QTOK, D], F32, kind="Internal").ap()
    intT_d = nc.dram_tensor("intT", [32, 128, 1024], BF16, kind="Internal").ap()
    y1s_d = nc.dram_tensor("y1s", [QTOK, D], F32, kind="Internal").ap()

    p = Prog(nc)
    _E = p.emit
    def mkE(label):
        def E(*a, **k):
            if parts == 'all' or label in parts:
                return _E(*a, **k)
        return E
    E = _E
    sbstack = contextlib.ExitStack()

    def sbuf(stk, n, s, d=F32):
        return stk.enter_context(nc.sbuf_tensor(n, s, d))

    def psum(stk, n, s, d=F32):
        return stk.enter_context(nc.psum_tensor(n, s, d))

    perm = sbstack
    id_bf = sbuf(perm, "id_bf_s", [128, 128], BF16)
    idf32 = sbuf(perm, "idf32_s", [128, 128], F32)
    iotacol = sbuf(perm, "iotacol_s", [128, 128], F32)
    qoffc = sbuf(perm, "qoffc_s", [128, 1], F32)
    ones_row = sbuf(perm, "ones_row_s", [1, 512], BF16)
    ones_col = sbuf(perm, "ones_col_s", [128, 1], BF16)
    ones_sq = sbuf(perm, "ones_sq_s", [128, 128], BF16)
    one1 = sbuf(perm, "one1_s", [1, 1], BF16)
    epsA = sbuf(perm, "epsA_s", [128, 1], F32)
    epsB = sbuf(perm, "epsB_s", [128, 1], F32)
    negb = sbuf(perm, "negb_s", [128, 1], F32)
    ksampT = sbuf(perm, "ksampT_s", [128, 8, 104], BF16)
    xbar_bf = sbuf(perm, "xbar_bf_s", [1, D], BF16)
    xbarT = sbuf(perm, "xbarT_s", [128, 8], BF16)
    vmean = sbuf(perm, "vmean_s", [1, D], F32)
    vmean_bf = sbuf(perm, "vmean_bf_s", [1, D], BF16)
    vmeanT = sbuf(perm, "vmeanT_s", [128, 8], F32)
    vmeanT_bf = sbuf(perm, "vmeanT_bf_s", [128, 8], BF16)
    base_bf = sbuf(perm, "base_bf_s", [1, D], BF16)
    idxT_f = sbuf(perm, "idxT_f_s", [64, H], F32)
    idxT_i = sbuf(perm, "idxT_i_s", [64, H], I32)
    delta = [sbuf(perm, f"delta{c}", [128, D], BF16) for c in range(8)]
    idxv = [sbuf(perm, f"idxv{c}", [128, 1], F32) for c in range(8)]
    junk1 = sbuf(perm, "junk1_s", [128, 1], F32)

    E('pool', lambda e: e.dma_start(id_bf[:, :], ident_d), writes=['id_bf'], dma='c_idbf')
    E('sp', lambda e: e.dma_start(idf32[:, :], ident_d), writes=['idf32'], dma='c_idf')
    E('sp', lambda e: e.dma_start(iotacol[:, :], iotacol_d), writes=['iotacol'], dma='c_ioc')
    E('sp', lambda e: e.dma_start(qoffc[:, :], qoff_d), writes=['qoffc'], dma='c_qof')
    E('dve', lambda e: e.memset(ones_row[:, :], 1.0), writes=['ones_row'])
    E('dve', lambda e: e.memset(ones_col[:, :], 1.0), writes=['ones_col'])
    E('dve', lambda e: e.memset(ones_sq[:, :], 1.0), writes=['ones_sq'])
    E('dve', lambda e: e.memset(one1[:, :], 1.0), writes=['one1'])
    E('dve', lambda e: e.memset(epsA[:, :], 1e-5), writes=['epsA'])
    E('dve', lambda e: e.memset(epsB[:, :], 1e-12), writes=['epsB'])
    E('dve', lambda e: e.memset(negb[:, :], -12.0), writes=['negb'])

    def layernorm(src_ap, src_keys, dst_ap, dst_key, rows, eps_tile, eps_key,
                  stats, skey, lnexp=False):
        rd = list(src_keys)
        E('dve', lambda e: e.reduce_sum(stats[0:rows, 0:1], src_ap, axis=AX.X),
          reads=rd, writes=[(skey, 0)])
        E('act', lambda e: e.activation(dst_ap, src_ap, AF.Square,
                                        accum_out=stats[0:rows, 1:2]),
          reads=rd, writes=[dst_key, (skey, 1)])
        E('dve', lambda e: e.tensor_scalar(stats[0:rows, 0:1], stats[0:rows, 0:1],
                                           1.0 / D, None, ALU.mult),
          reads=[(skey, 0)], writes=[(skey, 0)])
        E('dve', lambda e: e.tensor_scalar(stats[0:rows, 1:2], stats[0:rows, 1:2],
                                           1.0 / D, None, ALU.mult),
          reads=[(skey, 1)], writes=[(skey, 1)])
        E('dve', lambda e: e.tensor_tensor(stats[0:rows, 2:3], stats[0:rows, 0:1],
                                           stats[0:rows, 0:1], ALU.mult),
          reads=[(skey, 0)], writes=[(skey, 2)])
        E('dve', lambda e: e.tensor_tensor(stats[0:rows, 2:3], stats[0:rows, 1:2],
                                           stats[0:rows, 2:3], ALU.subtract),
          reads=[(skey, 1), (skey, 2)], writes=[(skey, 2)])
        if lnexp:
            E('act', lambda e: e.activation(stats[0:rows, 3:4], stats[0:rows, 2:3],
                                            AF.Ln, bias=eps_tile[0:rows, 0:1]),
              reads=[(skey, 2), eps_key], writes=[(skey, 3)])
            E('dve', lambda e: e.tensor_scalar(stats[0:rows, 3:4], stats[0:rows, 3:4],
                                               -0.5, None, ALU.mult),
              reads=[(skey, 3)], writes=[(skey, 3)])
            E('act', lambda e: e.activation(stats[0:rows, 3:4], stats[0:rows, 3:4],
                                            AF.Exp),
              reads=[(skey, 3)], writes=[(skey, 3)])
        else:
            E('act', lambda e: e.activation(stats[0:rows, 3:4], stats[0:rows, 2:3],
                                            AF.Sqrt, bias=eps_tile[0:rows, 0:1]),
              reads=[(skey, 2), eps_key], writes=[(skey, 3)])
            E('dve', lambda e: e.reciprocal(stats[0:rows, 3:4], stats[0:rows, 3:4]),
              reads=[(skey, 3)], writes=[(skey, 3)])
        E('dve', lambda e: e.tensor_scalar(dst_ap, src_ap, stats[0:rows, 0:1],
                                           stats[0:rows, 3:4], ALU.subtract, ALU.mult),
          reads=rd + [(skey, 0), (skey, 3)], writes=[dst_key])

    pha = contextlib.ExitStack()
    st_w = contextlib.ExitStack()
    st_m = contextlib.ExitStack()
    bq_sb = sbuf(perm, "bq_sb_s", [1, D], BF16)
    bo_sb = sbuf(perm, "bo_sb_s", [1, D], BF16)
    wqT = sbuf(st_w, "wqT_s", [128, 8, D], BF16)
    woT_sb = sbuf(st_w, "woT_sb_s", [128, 8, D], BF16)
    M_tm = sbuf(st_m, "M_tm_s", [128, H * 128], F32)
    lo_t = sbuf(st_m, "lo_t_s", [128, H], F32)
    wkT = sbuf(pha, "wkT_s", [128, 8, D], BF16)
    wvT = sbuf(pha, "wvT_s", [128, 8, D], BF16)
    bk_sb = sbuf(pha, "bk_sb_s", [1, D], BF16)
    bv_sb = sbuf(pha, "bv_sb_s", [1, D], BF16)
    E('pool', lambda e: e.dma_start(wqT.ap(), wqT_d.rearrange("(kc p) f -> p kc f", p=128)),
      writes=['wqT'], dma='c_wq')
    E('pool', lambda e: e.dma_start(wkT.ap(), wkT_d.rearrange("(kc p) f -> p kc f", p=128)),
      writes=['wkT'], dma='c_wk')
    E('pool', lambda e: e.dma_start(wvT.ap(), wvT_d.rearrange("(kc p) f -> p kc f", p=128)),
      writes=['wvT'], dma='c_wv')
    E('pool', lambda e: e.dma_start(woT_sb.ap(), woT_d.rearrange("(kc p) f -> p kc f", p=128)),
      writes=['woT_sb'], dma='c_wo')
    E('pool', lambda e: e.dma_start(bq_sb[:, :], bq_d), writes=['bq_sb'], dma='c_bq')
    E('pool', lambda e: e.dma_start(bk_sb[:, :], bk_d), writes=['bk_sb'], dma='c_bk')
    E('pool', lambda e: e.dma_start(bv_sb[:, :], bv_d), writes=['bv_sb'], dma='c_bv')
    E('pool', lambda e: e.dma_start(bo_sb[:, :], bo_d), writes=['bo_sb'], dma='c_bo')

    ps_tr = [psum(pha, f"ps_tr{i}", [128, 512], F32) for i in range(2)]
    ps_qk = [psum(pha, f"ps_qk{i}", [128, 512], F32) for i in range(3)]
    ps_sc = psum(pha, "ps_sc_s", [128, 512], F32)
    ps_xb0 = psum(pha, "ps_xb0_s", [1, 512], F32)
    ps_xb1 = psum(pha, "ps_xb1_s", [1, 512], F32)
    pp_pool = [ps_qk[0], ps_qk[1], ps_qk[2], ps_sc]
    pp_ctr = [0]
    def next_pp():
        i = pp_ctr[0] % 4
        pp_ctr[0] += 1
        return pp_pool[i], ('pp', i)

    E = mkE('p0b')
    idxk_sb = sbuf(pha, "idxk_sb_s", [U, 1], I32)
    xs_f = sbuf(pha, "xs_f_s", [U, D], F32)
    xs_bf = sbuf(pha, "xs_bf_s", [128, D], BF16)
    xs_T = sbuf(pha, "xs_T_s", [128, 8, 64], BF16)
    kbar = sbuf(pha, "kbar_s", [128, 1], F32)
    st0 = sbuf(pha, "st0_s", [128, 4], F32)

    E('sp', lambda e: e.dma_start(idxk_sb[:, :], idxk_d.rearrange("(a b) -> a b", b=1)),
      writes=['idxk_sb'], dma='c_idk')
    E('dve', lambda e: e.memset(ksampT.ap(), 0.0), writes=['ks_init'])
    E('pool', lambda e: e.indirect_dma_start(
        xs_f[:, :], None, x_d, IndirectOffsetOnAxis(idxk_sb[:, :], 0)),
      reads=['idxk_sb'], writes=['xs_f'], dma='g0')
    E('dve', lambda e: e.memset(xs_bf[:, :], 0.0), writes=['xs_bf'])
    layernorm(xs_f[:, :], ['xs_f'], xs_bf[0:U, :], 'xs_bf', U, epsA, 'epsA', st0, 'st0')
    for c in range(8):
        bank = ps_tr[c % 2]
        bkk = ('ps_tr', c % 2)
        E('pe', lambda e, c=c, bank=bank: e.matmul(
            bank[:, 0:128], xs_bf[:, c*128:(c+1)*128], id_bf[:, :],
            start=True, stop=True),
          reads=['xs_bf', 'id_bf'], writes=[bkk])
        E('dve', lambda e, c=c, bank=bank: e.tensor_copy(xs_T[:, c, :], bank[:, 0:64]),
          reads=[bkk], writes=[('xs_T', c)])
    xsT_keys = [('xs_T', c) for c in range(8)]
    for h in range(H):
        pr, hi = h // 2, h % 2
        bank, bkk = next_pp()
        def ksamp_mm(e, h=h, bank=bank):
            for k in range(8):
                e.matmul(bank[0:64, 0:64], wkT[:, k, h*64:(h+1)*64], xs_T[:, k, :],
                         start=(k == 0), stop=False)
            return e.matmul(bank[0:64, 0:64], bk_sb[:, h*64:(h+1)*64],
                            ones_row[:, 0:64], start=False, stop=True)
        E('pe', ksamp_mm, reads=xsT_keys + ['wkT', 'bk_sb', 'ones_row'],
          writes=[bkk])
        E('dve', lambda e, pr=pr, hi=hi, bank=bank: e.tensor_copy(
            ksampT[hi*64:(hi+1)*64, pr, hi*52:hi*52+50], bank[0:64, 0:50]),
          reads=[bkk, 'ks_init'], writes=[('ksampT', pr, hi)])
        E('dve', lambda e, bank=bank: e.reduce_sum(kbar[0:64, :], bank[0:64, 0:50], axis=AX.X),
          reads=[bkk], writes=['kbar'])
        E('dve', lambda e, pr=pr, hi=hi: e.tensor_scalar(
            ksampT[hi*64:(hi+1)*64, pr, hi*52+50:hi*52+51], kbar[0:64, :],
            1.0 / U, None, ALU.mult),
          reads=['kbar', 'ks_init'], writes=[('ksampT', pr, hi, 'bar')])
    ks_keys = [('ksampT', pr, hi) for pr in range(8) for hi in range(2)] + \
              [('ksampT', pr, hi, 'bar') for pr in range(8) for hi in range(2)] + ['ks_init']

    x_in = [sbuf(pha, f"x_in{i}", [128, 4, D], F32) for i in range(2)]
    xln2 = [sbuf(pha, f"xln{i}_s", [128, 4, D], BF16) for i in range(2)]
    xT = sbuf(pha, "xT_s", [128, 8, 512], BF16)
    qt = sbuf(pha, "qt_s", [128, 8, 512], BF16)
    ktile = [sbuf(pha, f"ktile{i}", [128, 512], BF16) for i in range(2)]
    vt = sbuf(pha, "vt_s", [128, 4, D], BF16)
    st1 = [[sbuf(pha, f"st1_{i}_{s}", [128, 4], F32) for s in range(4)] for i in range(2)]
    mtmp = sbuf(pha, "mtmp_s", [128, 8], F32)

    E = _E

    def emit_x_dma(g):
        gp = g % 2
        xi = x_in[gp]
        _E('sp', lambda e, g=g, xi=xi: e.dma_start(
            xi.ap(), x_d[g*512:(g+1)*512, :].rearrange("(s p) f -> p s f", p=128)),
          writes=[('x_in', gp)], dma=f'x{gp}')

    def emit_ln_slice(g, s):
        gp = g % 2
        layernorm(x_in[gp][:, s, :], [('x_in', gp)], xln2[gp][:, s, :],
                  ('xln', gp, s), 128, epsA, 'epsA', st1[gp][s], ('st1', gp, s))

    emit_x_dma(0)
    for s in range(4):
        emit_ln_slice(0, s)
    for g in range(ngroups):
        gp = g % 2
        xln = xln2[gp]
        if g + 1 < ngroups:
            emit_x_dma(g + 1)
        E = mkE('tr')
        for s in range(4):
            for cb in range(2):
                bank = ps_tr[cb]
                bkk = ('ps_tr', cb)
                for j in range(4):
                    c = cb * 4 + j
                    E('pe', lambda e, s=s, c=c, j=j, bank=bank, xln=xln: e.matmul(
                        bank[:, j*128:(j+1)*128], xln[:, s, c*128:(c+1)*128],
                        id_bf[:, :], start=True, stop=True),
                      reads=[('xln', gp, s), 'id_bf'], writes=[bkk])
                E('dve', lambda e, s=s, cb=cb, bank=bank: e.tensor_copy(
                    xT.ap()[:, cb*4:(cb+1)*4, s*128:(s+1)*128],
                    bank.ap().rearrange("p (c f) -> p c f", c=4)),
                  reads=[bkk], writes=[('xT', s, cb)])
        xt_keys = [('xT', s, cb) for s in range(4) for cb in range(2)]
        E = mkE('q')
        for f in range(8):
            bank, bkk = next_pp()
            def qmm(e, f=f, bank=bank):
                for k in range(8):
                    e.matmul(bank[:, :], wqT[:, k, f*128:(f+1)*128], xT[:, k, :],
                             start=(k == 0), stop=False)
                return e.matmul(bank[:, :], bq_sb[:, f*128:(f+1)*128],
                                ones_row[:, :], start=False, stop=True)
            E('pe', qmm, reads=xt_keys + ['wqT', 'bq_sb', 'ones_row'], writes=[bkk])
            E('dve', lambda e, f=f, bank=bank: e.tensor_copy(qt[:, f, :], bank[:, :]),
              reads=[bkk], writes=[('qt', f)])
            if f in (2, 6) and g + 1 < ngroups:
                emit_ln_slice(g + 1, f // 4)
        E = mkE('k')
        for f in range(8):
            kt = ktile[f % 2]
            ktk = ('ktile', f % 2)
            bank, bkk = next_pp()
            def kmm(e, f=f, bank=bank):
                for k in range(8):
                    e.matmul(bank[:, :], wkT[:, k, f*128:(f+1)*128], xT[:, k, :],
                             start=(k == 0), stop=False)
                return e.matmul(bank[:, :], bk_sb[:, f*128:(f+1)*128],
                                ones_row[:, :], start=False, stop=True)
            E('pe', kmm, reads=xt_keys + ['wkT', 'bk_sb', 'ones_row'], writes=[bkk])
            E('dve', lambda e, kt=kt, bank=bank: e.tensor_copy(kt[:, :], bank[:, :]),
              reads=[bkk], writes=[ktk])
            E('sp', lambda e, g=g, f=f, kt=kt: e.dma_start(
                kfm_d[f, :, g*512:(g+1)*512], kt[:, :]),
              reads=[ktk], writes=['kfm_w'], dma='kw')
            if f in (2, 6) and g + 1 < ngroups:
                emit_ln_slice(g + 1, 2 + f // 4)
        E = mkE('v')
        for s in range(4):
            for hf in range(2):
                bank, bkk = next_pp()
                def vmm(e, s=s, hf=hf, bank=bank):
                    for k in range(8):
                        e.matmul(bank[:, :], xT[:, k, s*128:(s+1)*128],
                                 wvT[:, k, hf*512:(hf+1)*512], start=(k == 0), stop=False)
                    return e.matmul(bank[:, :], ones_row[0:1, 0:128],
                                    bv_sb[:, hf*512:(hf+1)*512], start=False, stop=True)
                E('pe', vmm, reads=xt_keys + ['wvT', 'bv_sb', 'ones_row'], writes=[bkk])
                E('dve', lambda e, s=s, hf=hf, bank=bank: e.tensor_copy(
                    vt[:, s, hf*512:(hf+1)*512], bank[:, :]),
                  reads=[bkk], writes=[('vt', s, hf)])
        vt_keys = [('vt', s, hf) for s in range(4) for hf in range(2)]
        for pr in range(8):
            E('sp', lambda e, g=g, pr=pr: e.dma_start(
                vscr_d[pr, g*4:(g+1)*4, :, :].rearrange("s p d -> p s d"),
                vt[:, :, pr*128:(pr+1)*128]),
              reads=vt_keys, writes=['vscr_w'], dma='vw')
        E = mkE('xb')
        for s in range(4):
            E('pe', lambda e, g=g, s=s, xln=xln: e.matmul(
                ps_xb0[:, :], ones_col[:, :], xln[:, s, 0:512],
                start=(g == 0 and s == 0), stop=(g == ngroups - 1 and s == 3)),
              reads=[('xln', gp, s), 'ones_col'], writes=[('ps_xb', 0)])
            E('pe', lambda e, g=g, s=s, xln=xln: e.matmul(
                ps_xb1[:, :], ones_col[:, :], xln[:, s, 512:1024],
                start=(g == 0 and s == 0), stop=(g == ngroups - 1 and s == 3)),
              reads=[('xln', gp, s), 'ones_col'], writes=[('ps_xb', 1)])
        E = mkE('sc')
        for s in range(4):
            for bk_i in range(2):
                bank, bkk = next_pp()
                def smm(e, s=s, bk_i=bk_i, bank=bank):
                    last = None
                    for j in range(4):
                        pr = bk_i * 4 + j
                        last = e.matmul(
                            bank[:, j*104:(j+1)*104],
                            qt[:, pr, s*128:(s+1)*128],
                            ksampT[:, pr, :],
                            start=True, stop=True)
                    return last
                E('pe', smm, reads=[('qt', f) for f in range(8)] + ks_keys,
                  writes=[bkk])
                E('dve', lambda e, bank=bank: e.reduce_max(
                    mtmp.ap().rearrange("p (j hi o) -> p j hi o", j=4, hi=2),
                    bank[:, 0:416].rearrange("p (j hi c) -> p j hi c", j=4, hi=2)[:, :, :, 0:50],
                    axis=AX.X),
                  reads=[bkk], writes=['mtmp'])
                E('dve', lambda e, g=g, s=s, bk_i=bk_i, bank=bank: e.tensor_tensor(
                    M_tm.ap().rearrange("p (h j) -> p h j", h=H)[:, bk_i*8:(bk_i+1)*8, g*4+s],
                    mtmp[:, :],
                    bank[:, 0:416].rearrange("p (j hi c) -> p j hi c", j=4, hi=2)[:, :, :, 50],
                    ALU.subtract),
                  reads=['mtmp', bkk], writes=[('M', bk_i, g, s)])

    E = mkE('p15')
    Mkeys = [('M', b, g, s) for b in range(2) for g in range(NG) for s in range(4)]

    E('dve', lambda e: e.tensor_scalar(xbar_bf[:, 0:512], ps_xb0[:, :], 1.0 / L,
                                       None, ALU.mult),
      reads=[('ps_xb', 0)], writes=['xbar_bf'])
    E('dve', lambda e: e.tensor_scalar(xbar_bf[:, 512:1024], ps_xb1[:, :], 1.0 / L,
                                       None, ALU.mult),
      reads=[('ps_xb', 1)], writes=['xbar_bf'])
    for c in range(8):
        bank, bkk = next_pp()
        E('pe', lambda e, c=c, bank=bank: e.matmul(bank[:, 0:1], xbar_bf[:, c*128:(c+1)*128],
                                        one1[:, :], start=True, stop=True),
          reads=['xbar_bf', 'one1'], writes=[bkk])
        E('dve', lambda e, c=c, bank=bank: e.tensor_copy(xbarT[:, c:c+1], bank[:, 0:1]),
          reads=[bkk], writes=[('xbarT', c)])
    xbT_keys = [('xbarT', c) for c in range(8)]
    def vmean_mm0(e):
        for c in range(8):
            e.matmul(ps_qk[0][0:1, :], xbarT[:, c:c+1], wvT[:, c, 0:512],
                     start=(c == 0), stop=False)
        return e.matmul(ps_qk[0][0:1, :], one1[:, :], bv_sb[:, 0:512],
                        start=False, stop=True)
    def vmean_mm1(e):
        for c in range(8):
            e.matmul(ps_qk[1][0:1, :], xbarT[:, c:c+1], wvT[:, c, 512:1024],
                     start=(c == 0), stop=False)
        return e.matmul(ps_qk[1][0:1, :], one1[:, :], bv_sb[:, 512:1024],
                        start=False, stop=True)
    E('pe', vmean_mm0, reads=xbT_keys + ['wvT', 'one1', 'bv_sb'], writes=[('pp', 0)])
    E('pe', vmean_mm1, reads=xbT_keys + ['wvT', 'one1', 'bv_sb'], writes=[('pp', 1)])
    E('dve', lambda e: e.tensor_copy(vmean[:, 0:512], ps_qk[0][0:1, :]),
      reads=[('pp', 0)], writes=['vmean'])
    E('dve', lambda e: e.tensor_copy(vmean[:, 512:1024], ps_qk[1][0:1, :]),
      reads=[('pp', 1)], writes=['vmean'])
    E('dve', lambda e: e.tensor_copy(vmean_bf[:, :], vmean[:, :]),
      reads=['vmean'], writes=['vmean_bf'])
    for c in range(8):
        bank, bkk = next_pp()
        E('pe', lambda e, c=c, bank=bank: e.matmul(bank[:, 0:1], vmean_bf[:, c*128:(c+1)*128],
                                        one1[:, :], start=True, stop=True),
          reads=['vmean_bf', 'one1'], writes=[bkk])
        E('dve', lambda e, c=c, bank=bank: e.tensor_copy(vmeanT[:, c:c+1], bank[:, 0:1]),
          reads=[bkk], writes=[('vmeanT', c)])
    E('dve', lambda e: e.tensor_copy(vmeanT_bf[:, :], vmeanT[:, :]),
      reads=[('vmeanT', c) for c in range(8)], writes=['vmeanT_bf'])
    def base_mm0(e):
        for c in range(8):
            e.matmul(ps_qk[0][0:1, :], vmeanT_bf[:, c:c+1], woT_sb[:, c, 0:512],
                     start=(c == 0), stop=False)
        return e.matmul(ps_qk[0][0:1, :], one1[:, :], bo_sb[:, 0:512],
                        start=False, stop=True)
    def base_mm1(e):
        for c in range(8):
            e.matmul(ps_qk[1][0:1, :], vmeanT_bf[:, c:c+1], woT_sb[:, c, 512:1024],
                     start=(c == 0), stop=False)
        return e.matmul(ps_qk[1][0:1, :], one1[:, :], bo_sb[:, 512:1024],
                        start=False, stop=True)
    E('pe', base_mm0, reads=['vmeanT_bf', 'woT_sb', 'one1', 'bo_sb'], writes=[('pp', 0)])
    E('pe', base_mm1, reads=['vmeanT_bf', 'woT_sb', 'one1', 'bo_sb'], writes=[('pp', 1)])
    E('dve', lambda e: e.tensor_copy(base_bf[:, 0:512], ps_qk[0][0:1, :]),
      reads=[('pp', 0)], writes=['base_bf'])
    E('dve', lambda e: e.tensor_copy(base_bf[:, 512:1024], ps_qk[1][0:1, :]),
      reads=[('pp', 1)], writes=['base_bf'])
    if debug:
        E('sp', lambda e: e.dma_start(dbg["dbg_vmean"], vmean[:, :]), reads=['vmean'], dma='dbg')
        basef = sbuf(pha, "basef_s", [1, D], F32)
        E('dve', lambda e: e.tensor_copy(basef[:, :], base_bf[:, :]),
          reads=['base_bf'], writes=['basef'])
        E('sp', lambda e: e.dma_start(dbg["dbg_base"], basef[:, :]), reads=['basef'], dma='dbg')
        E('sp', lambda e: e.dma_start(dbg["dbg_M"], M_tm[:, :]), reads=Mkeys, dma='dbg')

    E = _E
    p.fence()
    pha.close()
    if maxphase < 2:
        if debug:
            p.wait_all_dma('pool', 'dbg')
        p.build(); p.close()
        st_m.close(); st_w.close(); sbstack.close()
        return nc
    ph2 = contextlib.ExitStack()
    hi_t = sbuf(ph2, "hi_t_s", [128, H], F32)
    tt_t = sbuf(ph2, "tt_t_s", [128, H], F32)
    cnt_f = sbuf(ph2, "cnt_f_s", [128, H], F32)
    cnt_bf = sbuf(ph2, "cnt_bf_s", [128, H], BF16)
    cntr = sbuf(ph2, "cntr_s", [128, H], F32)
    geu = sbuf(ph2, "geu_s", [128, H], U8)
    gle = sbuf(ph2, "gle_s", [128, H], U8)
    junkc = sbuf(ph2, "junkc_s", [128, 128], F32)
    mask_u8 = sbuf(ph2, "mask_u8_s", [128, H * 128], U8)
    neg1 = sbuf(ph2, "neg1_s", [128, 1], F32)
    iotaM = sbuf(ph2, "iotaM_s", [128, H * 128], F32)
    miota = M_tm
    cand = sbuf(ph2, "cand_s", [128, H * 16], F32)
    candcat = sbuf(ph2, "candcat_s", [16, 2048], F32)
    cmax8 = sbuf(ph2, "cmax8_s", [16, 8], F32)
    sort56 = sbuf(ph2, "sort56_s", [16, 56], F32)
    sortpad = sbuf(ph2, "sortpad_s", [16, 64], F32)
    selneg = sbuf(ph2, "selneg_s", [16, 64], U8)
    big9 = sbuf(ph2, "big9_s", [16, 64], F32)
    candT = sbuf(ph2, "candT_s", [128, 128], F32)
    ps2_a = psum(ph2, "ps2_a_s", [128, 512], F32)
    ps2_b = psum(ph2, "ps2_b_s", [128, 512], F32)

    E('sp', lambda e: e.dma_start(iotaM[:, :], iotaM_d), writes=['iotaM'], dma='c_iom')
    E('dve', lambda e: e.memset(lo_t[:, :], 0.0), writes=['lo'])
    E('dve', lambda e: e.memset(hi_t[:, :], 256.0), writes=['hi'])
    for it in range(nbisect):
        E('dve', lambda e: e.tensor_tensor(tt_t[:, :], lo_t[:, :], hi_t[:, :], ALU.add),
          reads=['lo', 'hi'], writes=['tt'])
        E('dve', lambda e: e.tensor_scalar(tt_t[:, :], tt_t[:, :], 0.5, None, ALU.mult),
          reads=['tt'], writes=['tt'])
        for h in range(H):
            E('dve', lambda e, h=h: e.tensor_scalar(
                junkc[:, :], M_tm[:, h*128:(h+1)*128], tt_t[:, h:h+1], 0.0,
                ALU.is_gt, ALU.add, accum_out=cnt_f[:, h:h+1]),
              reads=Mkeys + ['tt'], writes=['junkc', ('cnt', h)])
        E('dve', lambda e: e.tensor_copy(cnt_bf[:, :], cnt_f[:, :]),
          reads=[('cnt', h) for h in range(H)], writes=['cntbf'])
        E('pe', lambda e: e.matmul(ps2_a[:, 0:16], ones_sq[:, :], cnt_bf[:, :],
                                   start=True, stop=True),
          reads=['cntbf', 'ones_sq'], writes=[('ps2_a', 0)])
        E('dve', lambda e: e.tensor_copy(cntr[:, :], ps2_a[:, 0:16]),
          reads=[('ps2_a', 0)], writes=['cntr'])
        E('dve', lambda e: e.tensor_scalar(geu[:, :], cntr[:, :], 49.5, None, ALU.is_gt),
          reads=['cntr'], writes=['geu'])
        E('dve', lambda e: e.tensor_scalar(gle[:, :], cntr[:, :], 49.5, None, ALU.is_le),
          reads=['cntr'], writes=['gle'])
        E('dve', lambda e: e.select(lo_t[:, :], geu[:, :], tt_t[:, :], lo_t[:, :],
                                    add_drain=True),
          reads=['geu', 'tt', 'lo'], writes=['lo'])
        E('dve', lambda e: e.select(hi_t[:, :], gle[:, :], tt_t[:, :], hi_t[:, :],
                                    add_drain=True),
          reads=['gle', 'tt', 'hi'], writes=['hi'])
    if debug:
        E('sp', lambda e: e.dma_start(dbg["dbg_lo"], lo_t[:, :]), reads=['lo'], dma='dbg')
    E('dve', lambda e: e.memset(neg1[:, :], -1.0), writes=['neg1'])
    for h in range(H):
        E('dve', lambda e, h=h: e.tensor_scalar(
            mask_u8[:, h*128:(h+1)*128], M_tm[:, h*128:(h+1)*128],
            lo_t[:, h:h+1], None, ALU.is_gt),
          reads=Mkeys + ['lo'], writes=[('msk', h)])
    E('dve', lambda e: e.select(miota[:, :], mask_u8[:, :], iotaM[:, :],
                                neg1[:, 0:1].to_broadcast([128, H * 128]),
                                add_drain=True),
      reads=[('msk', h) for h in range(H)] + ['iotaM', 'neg1'],
      writes=Mkeys + ['miota'])
    for rnd in range(2):
        for h in range(H):
            E('dve', lambda e, rnd=rnd, h=h: e.max(
                out=cand[:, h*16+rnd*8:h*16+rnd*8+8], in_=miota[:, h*128:(h+1)*128]),
              reads=['miota'], writes=[('cand', h, rnd)])
            E('dve', lambda e, rnd=rnd, h=h: e.match_replace(
                out=miota[:, h*128:(h+1)*128],
                in_to_replace=cand[:, h*16+rnd*8:h*16+rnd*8+8],
                in_values=miota[:, h*128:(h+1)*128], imm_value=-1.0),
              reads=['miota', ('cand', h, rnd)], writes=['miota'])
    candkeys = [('cand', h, r) for h in range(H) for r in range(2)]
    for half in range(2):
        E('pe', lambda e, half=half: e.matmul(
            ps2_a[:, 0:128],
            cand[:, half*128:(half+1)*128], idf32[:, :], start=True, stop=True),
          reads=candkeys + ['idf32'], writes=[('ps2_a', 0)])
        E('dve', lambda e: e.tensor_copy(candT[:, :], ps2_a[:, 0:128]),
          reads=[('ps2_a', 0)], writes=['candT'])
        E('pool', lambda e, half=half: e.dma_start(
            candcat[half*8:(half+1)*8, :].rearrange("h (c s) -> h c s", c=16),
            candT[:, :]),
          reads=['candT'], writes=['candcat'], dma='cc')
    for rnd in range(7):
        E('dve', lambda e: e.max(out=cmax8[:, :], in_=candcat[:, :]),
          reads=['candcat'], writes=['cmax8'])
        E('dve', lambda e, rnd=rnd: e.tensor_copy(sort56[:, rnd*8:(rnd+1)*8],
                                                  cmax8[:, :]),
          reads=['cmax8'], writes=[('s56', rnd)])
        E('dve', lambda e: e.match_replace(
            out=candcat[:, :], in_to_replace=cmax8[:, :], in_values=candcat[:, :],
            imm_value=-1.0),
          reads=['candcat', 'cmax8'], writes=['candcat'])
    s56keys = [('s56', r) for r in range(7)]
    E('dve', lambda e: e.memset(sortpad[:, :], 1.0e9), writes=['sortpad'])
    E('dve', lambda e: e.tensor_copy(sortpad[:, 0:50], sort56[:, 0:50]),
      reads=s56keys, writes=['sortpad'])
    E('dve', lambda e: e.tensor_scalar(selneg[:, :], sortpad[:, :], -0.5, None, ALU.is_lt),
      reads=['sortpad'], writes=['selneg'])
    E('dve', lambda e: e.memset(big9[:, :], 1.0e9), writes=['big9'])
    E('dve', lambda e: e.select(sortpad[:, :], selneg[:, :], big9[:, :], sortpad[:, :],
                                add_drain=True),
      reads=['selneg', 'big9', 'sortpad'], writes=['sortpad'])
    E('pe', lambda e: e.matmul(ps2_b[0:64, 0:16], sortpad[:, :], idf32[0:16, 0:16],
                               start=True, stop=True),
      reads=['sortpad', 'idf32'], writes=[('ps2_b', 0)])
    E('dve', lambda e: e.tensor_copy(idxT_f[:, :], ps2_b[0:64, 0:16]),
      reads=[('ps2_b', 0)], writes=['idxT_f'])
    E('dve', lambda e: e.tensor_copy(idxT_i[:, :], idxT_f[:, :]),
      reads=['idxT_f'], writes=['idxT_i'])
    if debug:
        E('sp', lambda e: e.dma_start(dbg["dbg_idx"], idxT_f[:, :]), reads=['idxT_f'], dma='dbg')

    p.fence()
    ph2.close()
    st_m.close()

    if maxphase < 3:
        if debug:
            p.wait_all_dma('pool', 'dbg')
        p.build(); p.close()
        st_w.close(); sbstack.close()
        return nc

    phb = contextlib.ExitStack()
    kpair = [sbuf(phb, f"kpair{i}", [128, L], BF16) for i in range(2)]
    xg_f = sbuf(phb, "xg_f_s", [128, D], F32)
    xg_bf = sbuf(phb, "xg_bf_s", [128, D], BF16)
    xg_T = sbuf(phb, "xg_T_s", [128, 8, 128], BF16)
    qtopT = sbuf(phb, "qtopT_s", [128, 128], BF16)
    P_sb = sbuf(phb, "P_sb_s", [128, L], BF16)
    sumP = sbuf(phb, "sumP_s", [128, 32], F32)
    rsum = sbuf(phb, "rsum_s", [128, 1], F32)
    PT_sb = [sbuf(phb, f"PT_sb{i}", [128, 512], BF16) for i in range(2)]
    vsc = [sbuf(phb, f"vsc{i}", [128, 8, 128], BF16) for i in range(2)]
    dctx = sbuf(phb, "dctx_s", [128, 128], BF16)
    st3 = sbuf(phb, "st3_s", [128, 4], F32)
    ps_s = [psum(phb, f"ps_s{i}", [128, 512], F32) for i in range(2)]
    ps_pt = [psum(phb, f"ps_pt{i}", [128, 512], F32) for i in range(2)]
    ps_cx = psum(phb, "ps_cx_s", [128, 128], F32)
    ps_d = [psum(phb, f"ps_d{i}", [128, 512], F32) for i in range(2)]

    for c in range(8):
        E('dve', lambda e, c=c: e.memset(delta[c].ap(), 0.0), writes=[('dz', c)])
        E('dve', lambda e, c=c: e.memset(idxv[c].ap(), 2.0e9), writes=[('iz', c)])

    for pr in range(8):
        kp = kpair[pr % 2]
        kpk = ('kpair', pr % 2)
        E('sp', lambda e, pr=pr, kp=kp: e.dma_start(kp.ap(), kfm_d[pr, :, :]),
          reads=['kfm_w'], writes=[kpk], dma=f'kr{pr % 2}')
        E('dve', lambda e: e.memset(xg_f.ap(), 0.0), writes=['xg_f'])
        E('pool', lambda e, pr=pr: e.indirect_dma_start(
            xg_f[0:U, :], None, x_d, IndirectOffsetOnAxis(idxT_i[0:U, 2*pr:2*pr+1], 0),
            bounds_check=L - 1, oob_is_err=False),
          reads=['idxT_i', 'xg_f'], writes=[('xg_g', 0)], dma='g1')
        E('pool', lambda e, pr=pr: e.indirect_dma_start(
            xg_f[64:64+U, :], None, x_d, IndirectOffsetOnAxis(idxT_i[0:U, 2*pr+1:2*pr+2], 0),
            bounds_check=L - 1, oob_is_err=False),
          reads=['idxT_i', 'xg_f'], writes=[('xg_g', 1)], dma='g1')
        layernorm(xg_f[:, :], ['xg_f', ('xg_g', 0), ('xg_g', 1)], xg_bf[:, :], 'xg_bf',
                  128, epsA, 'epsA', st3, 'st3', lnexp=True)
        for c in range(8):
            bank = ps_pt[c % 2]
            bkk = ('ps_pt2', c % 2)
            E('pe', lambda e, c=c, bank=bank: e.matmul(
                bank[:, 0:128], xg_bf[:, c*128:(c+1)*128], id_bf[:, :],
                start=True, stop=True),
              reads=['xg_bf', 'id_bf'], writes=[bkk])
            E('dve', lambda e, c=c, bank=bank: e.tensor_copy(
                xg_T[:, c, :], bank[:, 0:128]),
              reads=[bkk], writes=[('xg_T', c)])
        def qtop_mm(e, pr=pr):
            for k in range(8):
                e.matmul(ps_d[0][:, 0:128], wqT[:, k, pr*128:(pr+1)*128], xg_T[:, k, :],
                         start=(k == 0), stop=False)
            return e.matmul(ps_d[0][:, 0:128], bq_sb[:, pr*128:(pr+1)*128],
                            ones_row[:, 0:128], start=False, stop=True)
        E('pe', qtop_mm, reads=[('xg_T', c) for c in range(8)] +
          ['wqT', 'bq_sb', 'ones_row'], writes=['ps_d0'])
        E('dve', lambda e: e.memset(qtopT.ap(), 0.0), writes=['qtop0'])
        E('dve', lambda e: e.tensor_copy(qtopT[0:64, 0:50], ps_d[0][0:64, 0:50]),
          reads=['ps_d0', 'qtop0'], writes=[('qtopT', 0)])
        E('dve', lambda e: e.tensor_copy(qtopT[64:128, 64:114], ps_d[0][64:128, 64:114]),
          reads=['ps_d0', 'qtop0'], writes=[('qtopT', 1)])
        for kc in range(32):
            bank = ps_s[kc % 2]
            bkk = ('ps_s', kc % 2)
            E('pe', lambda e, kc=kc, kp=kp, bank=bank: e.matmul(
                bank[:, :], qtopT[:, :], kp[:, kc*512:(kc+1)*512],
                start=True, stop=True),
              reads=[('qtopT', 0), ('qtopT', 1), 'qtop0', kpk], writes=[bkk])
            E('act', lambda e, kc=kc, bank=bank: e.activation(
                P_sb[:, kc*512:(kc+1)*512], bank[:, :], AF.Exp,
                bias=negb[:, :], scale=SCALE, accum_out=sumP[:, kc:kc+1]),
              reads=[bkk, 'negb'], writes=[('P', kc), ('sumP', kc)])
        E('dve', lambda e: e.reduce_sum(rsum[:, :], sumP[:, :], axis=AX.X),
          reads=[('sumP', kc) for kc in range(32)], writes=['rsum'])
        E('dve', lambda e: e.reciprocal(rsum[:, :], rsum[:, :]),
          reads=['rsum'], writes=['rsum'])
        E('dve', lambda e: e.tensor_scalar(P_sb[:, :], P_sb[:, :], rsum[:, :], None,
                                           ALU.mult),
          reads=[('P', kc) for kc in range(32)] + ['rsum'], writes=['Pn'])
        for scb in range(16):
            vb = vsc[scb % 2]
            vbk = ('vsc', scb % 2)
            E('sp', lambda e, pr=pr, scb=scb, vb=vb: e.dma_start(
                vb.ap(), vscr_d[pr, scb*8:(scb+1)*8, :, :].rearrange("s p d -> p s d")),
              reads=['vscr_w'], writes=[vbk], dma=f'vr{scb % 2}')
            for q4 in range(2):
                sc0 = scb * 8 + q4 * 4
                ptb = PT_sb[q4]
                ptk = ('PT', q4)
                psb = ps_pt[q4]
                psk = ('ps_pt2', q4)
                def ptmm(e, sc0=sc0, psb=psb):
                    last = None
                    for jj in range(4):
                        last = e.matmul(psb[:, jj*128:(jj+1)*128],
                                        P_sb[:, (sc0+jj)*128:(sc0+jj+1)*128],
                                        id_bf[:, :], start=True, stop=True)
                    return last
                E('pe', ptmm, reads=['Pn', 'id_bf'], writes=[psk])
                E('dve', lambda e, ptb=ptb, psb=psb: e.tensor_copy(ptb[:, :], psb[:, :]),
                  reads=[psk], writes=[ptk])
                def cxmm(e, sc0=sc0, q4=q4, vb=vb, ptb=ptb):
                    last = None
                    for jj in range(4):
                        sc = sc0 + jj
                        last = e.matmul(ps_cx[:, :], vb[:, q4*4+jj, :],
                                        ptb[:, jj*128:(jj+1)*128],
                                        start=(sc == 0), stop=(sc == 127))
                    return last
                E('pe', cxmm, reads=[vbk, ptk], writes=['ps_cx'])
        E('dve', lambda e: e.memset(dctx.ap(), 0.0), writes=['dctx0'])
        E('dve', lambda e, pr=pr: e.tensor_scalar(
            dctx[0:64, 0:50], ps_cx[0:64, 0:50], vmeanT[0:64, pr:pr+1], None,
            ALU.subtract),
          reads=['ps_cx', ('vmeanT', pr), 'dctx0'], writes=[('dctx', 0)])
        E('dve', lambda e, pr=pr: e.tensor_scalar(
            dctx[64:128, 64:114], ps_cx[64:128, 64:114], vmeanT[64:128, pr:pr+1], None,
            ALU.subtract),
          reads=['ps_cx', ('vmeanT', pr), 'dctx0'], writes=[('dctx', 1)])
        def dmm(e, pr=pr):
            e.matmul(ps_d[0][:, :], dctx[:, :], woT_sb[:, pr, 0:512],
                     start=True, stop=True)
            return e.matmul(ps_d[1][:, :], dctx[:, :], woT_sb[:, pr, 512:1024],
                            start=True, stop=True)
        E('pe', dmm, reads=[('dctx', 0), ('dctx', 1), 'dctx0', 'woT_sb'],
          writes=['ps_d0', 'ps_d1'])
        E('dve', lambda e, pr=pr: e.tensor_copy(delta[pr][:, 0:512], ps_d[0][:, :]),
          reads=['ps_d0', ('dz', pr)],
          writes=[('delta', pr, 0, 0), ('delta', pr, 64, 0)])
        E('dve', lambda e, pr=pr: e.tensor_copy(delta[pr][:, 512:1024], ps_d[1][:, :]),
          reads=['ps_d1', ('dz', pr)],
          writes=[('delta', pr, 0, 1), ('delta', pr, 64, 1)])
        E('dve', lambda e, pr=pr: e.tensor_copy(idxv[pr][0:50, :],
                                                idxT_f[0:50, 2*pr:2*pr+1]),
          reads=['idxT_f', ('iz', pr)], writes=[('idxv', pr, 0)])
        E('dve', lambda e, pr=pr: e.tensor_copy(idxv[pr][64:114, :],
                                                idxT_f[0:50, 2*pr+1:2*pr+2]),
          reads=['idxT_f', ('iz', pr)], writes=[('idxv', pr, 64)])
    if debug:
        dltf = sbuf(phb, "dltf_s", [128, D], F32)
        E('dve', lambda e: e.tensor_copy(dltf[:, :], delta[0].ap()),
          reads=[('delta', 0, 0, 0), ('delta', 0, 0, 1), ('delta', 0, 64, 0),
                 ('delta', 0, 64, 1), ('dz', 0)],
          writes=['dltf'])
        E('sp', lambda e: e.dma_start(dbg["dbg_delta"], dltf[:, :]), reads=['dltf'], dma='dbg')

    p.fence()
    phb.close()
    st_w.close()

    if maxphase < 5:
        if debug:
            p.wait_all_dma('pool', 'dbg')
        p.build(); p.close()
        sbstack.close()
        return nc

    phc = contextlib.ExitStack()
    wr_sb = sbuf(phc, "wr_sb_s", [128, 8, NE], BF16)
    b1_sb = sbuf(phc, "b1_sb_s", [NE, 32 * 128], BF16)
    gam_b = sbuf(phc, "gam_b_s", [128, D], F32)
    bet_b = sbuf(phc, "bet_b_s", [128, D], F32)
    bof_sb = sbuf(phc, "bof_sb_s", [1, D], BF16)
    gb_tmp = sbuf(phc, "gb_tmp_s", [1, D], BF16)
    xq_in = [sbuf(phc, f"xq_in{i}", [128, D], F32) for i in range(2)]
    Sel8 = sbuf(phc, "Sel8_s", [128, 8, 128], BF16)
    ish8 = sbuf(phc, "ish8_s", [128, 8], F32)
    idxv8 = sbuf(phc, "idxv8_s", [128, 8], F32)
    attn_f = sbuf(phc, "attn_f_s", [128, D], F32)
    y1_t = sbuf(phc, "y1_t_s", [128, D], F32)
    t1_t = sbuf(phc, "t1_t_s", [128, D], F32)
    t1b8 = sbuf(phc, "t1b8_s", [128, 8, D], BF16)
    t1T_tmp = sbuf(phc, "t1T_tmp_s", [128, D], BF16)
    sts = [sbuf(phc, f"sts{i}", [128, 4], F32) for i in range(4)]
    oh4 = sbuf(phc, "oh4_s", [128, NE], F32)
    ohbf = sbuf(phc, "ohbf_s", [128, NE], BF16)
    noty = sbuf(phc, "noty_s", [128, 1], F32)
    eq4 = sbuf(phc, "eq4_s", [128, NE], F32)
    m4 = sbuf(phc, "m4_s", [128, 1], F32)
    diag8 = sbuf(phc, "diag8_s", [128, 8, 512], BF16)
    t1eT = sbuf(phc, "t1eT_s", [128, 8, NE, 1024], BF16)
    onehT2 = [sbuf(phc, f"onehT{i}_s", [NE, 1024], BF16) for i in range(2)]
    inter_st = [sbuf(phc, f"inter_st{i}_x", [128, 512], BF16) for i in range(2)]
    intc = [sbuf(phc, f"intc{i}_x", [128, 512], BF16) for i in range(2)]
    w1sl = [sbuf(phc, f"w1sl{i}", [128, 8, NE, 128], BF16) for i in range(2)]
    wofc = [sbuf(phc, f"wofc{i}", [128, D], BF16) for i in range(2)]
    t2_t = sbuf(phc, "t2_t_s", [128, D], F32)
    t3_t = sbuf(phc, "t3_t_s", [128, D], F32)
    ps_ab = [psum(phc, f"ps_ab{i}", [128, 1024], F32) for i in range(2)]
    ps_z = [psum(phc, f"ps_z{i}", [128, 512], F32) for i in range(2)]
    ps_tT = [psum(phc, f"ps_tT{i}", [128, 512], F32) for i in range(2)]

    E = mkE('gb')
    E('pool', lambda e: e.dma_start(wr_sb.ap(), wr_d.rearrange("(kc p) f -> p kc f", p=128)),
      writes=['wr_sb'], dma='c_wr')
    E('pool', lambda e: e.dma_start(b1_sb.ap(), b1r_d), writes=['b1_sb'], dma='c_b1')
    E('pool', lambda e: e.dma_start(bof_sb[:, :], bof_d), writes=['bof_sb'], dma='c_bof')
    E('pool', lambda e: e.dma_start(gb_tmp[:, :], gam_d), writes=['gb_tmp'], dma='c_gb')
    for hf in range(2):
        E('pe', lambda e, hf=hf: e.matmul(ps_z[0][:, :], ones_row[0:1, 0:128],
                                          gb_tmp[:, hf*512:(hf+1)*512],
                                          start=True, stop=True),
          reads=['gb_tmp', 'ones_row'], writes=[('ps_z', 0)])
        E('dve', lambda e, hf=hf: e.tensor_copy(gam_b[:, hf*512:(hf+1)*512],
                                                ps_z[0][:, :]),
          reads=[('ps_z', 0)], writes=['gam_b'])
    E('pool', lambda e: e.dma_start(gb_tmp[:, :], bet_d), writes=['gb_tmp'], dma='c_gb')
    for hf in range(2):
        E('pe', lambda e, hf=hf: e.matmul(ps_z[0][:, :], ones_row[0:1, 0:128],
                                          gb_tmp[:, hf*512:(hf+1)*512],
                                          start=True, stop=True),
          reads=['gb_tmp', 'ones_row'], writes=[('ps_z', 0)])
        E('dve', lambda e, hf=hf: e.tensor_copy(bet_b[:, hf*512:(hf+1)*512],
                                                ps_z[0][:, :]),
          reads=[('ps_z', 0)], writes=['bet_b'])

    for ch in range(8):
        E('dve', lambda e, ch=ch: e.tensor_copy(idxv8[:, ch:ch+1], idxv[ch][:, :]),
          reads=[('idxv', ch, 0), ('idxv', ch, 64), ('iz', ch)], writes=['idxv8'])
    dkeys = [('delta', ch, bs, hf) for ch in range(8) for bs in (0, 64) for hf in range(2)] + \
            [('dz', c) for c in range(8)]
    ikeys = [('idxv', ch, bs) for ch in range(8) for bs in (0, 64)] + \
            [('iz', c) for c in range(8)]

    def emit_A1_tile(sg, tl):
        sgt0 = sg * 8
        t = sgt0 + tl
        E = mkE('as')
        xi = xq_in[tl % 2]
        xik = ('xq_in', tl % 2)
        E('sp', lambda e, t=t, xi=xi: e.dma_start(xi.ap(), xq_d[t*128:(t+1)*128, :]),
          writes=[xik], dma=f'xq{tl % 2}')
        pab = ps_ab[tl % 2]
        pa, pb = pab[:, 0:512], pab[:, 512:1024]
        pak, pbk = ('ps_ab', tl % 2, 0), ('ps_ab', tl % 2, 1)
        E('dve', lambda e, t=t: e.tensor_scalar(
            ish8[:, :], idxv8[:, :], qoffc[:, :], float(t * 128),
            ALU.subtract, ALU.subtract),
          reads=['idxv8', 'qoffc'], writes=['ish8'])
        E('dve', lambda e: e.tensor_tensor(
            Sel8.ap(),
            ish8.ap().unsqueeze(2).to_broadcast([128, 8, 128]),
            iotacol.ap().unsqueeze(1).to_broadcast([128, 8, 128]),
            ALU.is_equal),
          reads=['ish8', 'iotacol'], writes=['Sel8'])
        for ch in range(8):
            E('pe', lambda e, ch=ch, pa=pa: e.matmul(
                pa, Sel8[:, ch, :], delta[ch][:, 0:512],
                start=(ch == 0), stop=False),
              reads=['Sel8'] + dkeys, writes=[pak])
            E('pe', lambda e, ch=ch, pb=pb: e.matmul(
                pb, Sel8[:, ch, :], delta[ch][:, 512:1024],
                start=(ch == 0), stop=False),
              reads=['Sel8'] + dkeys, writes=[pbk])
        E('pe', lambda e, pa=pa: e.matmul(pa, ones_row[0:1, 0:128],
                                          base_bf[:, 0:512], start=False, stop=True),
          reads=['ones_row', 'base_bf'], writes=[pak])
        E('pe', lambda e, pb=pb: e.matmul(pb, ones_row[0:1, 0:128],
                                          base_bf[:, 512:1024], start=False, stop=True),
          reads=['ones_row', 'base_bf'], writes=[pbk])
        E('dve', lambda e, pa=pa: e.tensor_copy(attn_f[:, 0:512], pa),
          reads=[pak], writes=['attn_f'])
        E('dve', lambda e, pb=pb: e.tensor_copy(attn_f[:, 512:1024], pb),
          reads=[pbk], writes=['attn_f'])
        layernorm(attn_f[:, :], ['attn_f'], y1_t[:, :], 'y1', 128, epsA, 'epsA',
                  sts[0], 'sA')
        E('dve', lambda e, xi=xi: e.tensor_tensor(y1_t[:, :], y1_t[:, :], xi.ap(),
                                                  ALU.add),
          reads=['y1', xik], writes=['y1'])
        E('sp', lambda e, t=t: e.dma_start(y1s_d[t*128:(t+1)*128, :], y1_t[:, :]),
          reads=['y1'], writes=[('y1s', t)], dma='y1w')
        layernorm(y1_t[:, :], ['y1'], t1_t[:, :], 't1', 128, epsA, 'epsA',
                  sts[1], 'sB')
        E('sp', lambda e, t=t: e.dma_start(t1s_d[t*128:(t+1)*128, :], t1_t[:, :]),
          reads=['t1'], writes=[('t1s', t)], dma='t1w')
        E('dve', lambda e, tl=tl: e.tensor_copy(t1b8[:, tl, :], t1_t[:, :]),
          reads=['t1'], writes=[('t1b8', tl)])
        if debug and t == 0:
            E('sp', lambda e: e.dma_start(dbg["dbg_y1"], y1_t[:, :]), reads=['y1'], dma='dbg')
            E('sp', lambda e: e.dma_start(dbg["dbg_t1"], t1_t[:, :]), reads=['t1'], dma='dbg')

    def emit_A2_tile(sg, tl):
        E = mkE('as2')
        for cb in range(2):
            bank = ps_tT[cb]
            bkk = ('ps_tT', cb)
            for j in range(4):
                c = cb * 4 + j
                E('pe', lambda e, c=c, j=j, bank=bank, tl=tl: e.matmul(
                    bank[:, j*128:(j+1)*128], t1b8[:, tl, c*128:(c+1)*128], id_bf[:, :],
                    start=True, stop=True),
                  reads=[('t1b8', tl), 'id_bf'], writes=[bkk])
            E('dve', lambda e, cb=cb, bank=bank: e.tensor_copy(
                t1T_tmp[:, cb*512:(cb+1)*512], bank[:, :]),
              reads=[bkk], writes=[('t1T', cb)])
        def lg_mm(e):
            last = None
            for c in range(8):
                last = e.matmul(ps_tT[0][:, 0:NE], t1T_tmp[:, c*128:(c+1)*128],
                                wr_sb[:, c, :], start=(c == 0), stop=(c == 7))
            return last
        E('pe', lg_mm, reads=[('t1T', 0), ('t1T', 1), 'wr_sb'], writes=[('ps_tT', 0)])
        E('dve', lambda e: e.reduce_max(m4[:, :], ps_tT[0][:, 0:NE], axis=AX.X),
          reads=[('ps_tT', 0)], writes=['m4'])
        E('dve', lambda e: e.tensor_scalar(eq4[:, :], ps_tT[0][:, 0:NE], m4[:, :],
                                           None, ALU.is_ge),
          reads=[('ps_tT', 0), 'm4'], writes=['eq4'])
        E('dve', lambda e: e.tensor_copy(oh4[:, 0:1], eq4[:, 0:1]),
          reads=['eq4'], writes=['oh4'])
        E('dve', lambda e: e.tensor_scalar(noty[:, :], eq4[:, 0:1], -1.0, 1.0,
                                           ALU.mult, ALU.add),
          reads=['eq4'], writes=['noty'])
        for ee in range(1, NE):
            E('dve', lambda e, ee=ee: e.tensor_tensor(
                oh4[:, ee:ee+1], eq4[:, ee:ee+1], noty[:, :], ALU.mult),
              reads=['eq4', 'noty'], writes=['oh4'])
            if ee < NE - 1:
                E('dve', lambda e, ee=ee: e.tensor_scalar(
                    junk1[:, :], oh4[:, ee:ee+1], -1.0, 1.0, ALU.mult, ALU.add),
                  reads=['oh4'], writes=['junk1'])
                E('dve', lambda e: e.tensor_tensor(noty[:, :], noty[:, :],
                                                   junk1[:, :], ALU.mult),
                  reads=['noty', 'junk1'], writes=['noty'])
        E('dve', lambda e: e.tensor_copy(ohbf[:, :], oh4[:, :]),
          reads=['oh4'], writes=['ohbf'])

    def emit_A3_tile(sg, tl):
        sgp = sg % 2
        E = mkE('as3')
        E('pe', lambda e: e.matmul(ps_tT[1][0:NE, 0:128], ohbf[:, :], id_bf[:, :],
                                   start=True, stop=True),
          reads=['ohbf', 'id_bf'], writes=[('ps_tT', 1)])
        E('dve', lambda e, tl=tl, sgp=sgp: e.tensor_copy(
            onehT2[sgp][:, tl*128:(tl+1)*128], ps_tT[1][0:NE, 0:128]),
          reads=[('ps_tT', 1)], writes=[('onehT', sgp, tl)])
        for ee in range(NE):
            E('dve', lambda e, ee=ee, tl=tl: e.tensor_scalar(
                diag8[:, tl, ee*128:(ee+1)*128], id_bf[:, :], oh4[:, ee:ee+1], None,
                ALU.mult),
              reads=['id_bf', 'oh4'], writes=[('diag8', tl, ee)])

    def emit_B_tile(sg, tl):
        E = mkE('as3')
        for c in range(8):
            bank = ps_z[c % 2]
            bkk = ('ps_z', c % 2)
            E('pe', lambda e, c=c, tl=tl, bank=bank: e.matmul(
                bank[:, :], t1b8[:, tl, c*128:(c+1)*128], diag8[:, tl, :],
                start=True, stop=True),
              reads=[('t1b8', tl)] + [('diag8', tl, ee) for ee in range(NE)],
              writes=[bkk])
            if c % 4 < 2:
                E('dve', lambda e, c=c, tl=tl, bank=bank: e.tensor_copy(
                    t1eT[:, c, :, tl*128:(tl+1)*128],
                    bank.ap().rearrange("p (e f) -> p e f", e=NE)),
                  reads=[bkk], writes=[('t1eT', c, tl)])
            else:
                E('act', lambda e, c=c, tl=tl, bank=bank: e.activation(
                    t1eT[:, c, :, tl*128:(tl+1)*128],
                    bank.ap().rearrange("p (e f) -> p e f", e=NE), AF.Copy),
                  reads=[bkk], writes=[('t1eT', c, tl)])

    t1e_keys = [('t1eT', c, tl) for c in range(8) for tl in range(8)]

    def emit_w1_prefetch():
        E = mkE('hp')
        for ifc in range(2):
            E('pool', lambda e, ifc=ifc, wsl=w1sl[ifc]: e.dma_start(
                wsl.ap(), w1r_d[ifc].rearrange("kc e p f -> p kc e f")),
              writes=[('w1sl', ifc)], dma=f'w1_{ifc}')

    def emit_h(sg, hook=None):
        sgp = sg % 2
        E = mkE('hp')
        oh_keys = [('onehT', sgp, tl) for tl in range(8)]
        for ifc in range(32):
            wsl = w1sl[ifc % 2]
            wk = ('w1sl', ifc % 2)
            if ifc >= 2:
                E('pool', lambda e, ifc=ifc, wsl=wsl: e.dma_start(
                    wsl.ap(), w1r_d[ifc].rearrange("kc e p f -> p kc e f")),
                  writes=[wk], dma=f'w1_{ifc % 2}')
            if ifc == 16:
                for pfc in range(2):
                    E('pool', lambda e, pfc=pfc, wc=wofc[pfc]: e.dma_start(
                        wc.ap(), wof_d[pfc*128:(pfc+1)*128, :]),
                      writes=[('wofc', pfc)], dma=f'wo{pfc}')
            for hg in range(2):
                bank = ps_z[hg]
                bkk = ('ps_z', hg)
                def hmm(e, ifc=ifc, hg=hg, wsl=wsl, bank=bank, sgp=sgp):
                    first = True
                    for ee in range(NE):
                        for k in range(8):
                            e.matmul(bank[:, :], wsl[:, k, ee, :],
                                     t1eT[:, k, ee, hg*512:(hg+1)*512],
                                     start=first, stop=False)
                            first = False
                    return e.matmul(bank[:, :], b1_sb[:, ifc*128:(ifc+1)*128],
                                    onehT2[sgp][:, hg*512:(hg+1)*512],
                                    start=False, stop=True)
                E('pe', hmm, reads=[wk, 'b1_sb'] + t1e_keys + oh_keys, writes=[bkk])
                ist = inter_st[(ifc * 2 + hg) % 2]
                istk = ('ist', (ifc * 2 + hg) % 2)
                E('act', lambda e, bank=bank, ist=ist: e.activation(
                    ist[:, :], bank[:, :], AF.Gelu),
                  reads=[bkk], writes=[istk])
                E('sp', lambda e, ifc=ifc, hg=hg, ist=ist: e.dma_start(
                    intT_d[ifc, :, hg*512:(hg+1)*512], ist[:, :]),
                  reads=[istk], writes=[('intd', ifc, hg)], dma='intw')
            if hook is not None:
                hook(ifc)

    def emit_wo(sg):
        sgt0 = sg * 8
        if sg < 3:
            emit_w1_prefetch()
        E = mkE('wo')
        wo_banks = [ps_ab[0][:, 0:512], ps_ab[0][:, 512:1024],
                    ps_ab[1][:, 0:512], ps_ab[1][:, 512:1024],
                    ps_z[0][:, :], ps_z[1][:, :], ps_tT[0][:, :], ps_tT[1][:, :]]
        wo_keys = [('ps_ab', 0, 0), ('ps_ab', 0, 1), ('ps_ab', 1, 0), ('ps_ab', 1, 1),
                   ('ps_z', 0), ('ps_z', 1), ('ps_tT', 0), ('ps_tT', 1)]
        for hb in range(2):
            for ifc in range(32):
                wc = wofc[ifc % 2]
                wck = ('wofc', ifc % 2)
                if hb == 1 or ifc >= 2:
                    E('pool', lambda e, ifc=ifc, wc=wc: e.dma_start(
                        wc.ap(), wof_d[ifc*128:(ifc+1)*128, :]),
                      writes=[wck], dma=f'wo{ifc % 2}')
                ic = intc[ifc % 2]
                ick = ('intc', ifc % 2)
                E('pool', lambda e, ifc=ifc, hb=hb, ic=ic: e.dma_start(
                    ic.ap(), intT_d[ifc, :, hb*512:(hb+1)*512]),
                  reads=[('intd', ifc, hb)], writes=[ick], dma=f'ir{ifc % 2}')
                for t4 in range(4):
                    tl = hb * 4 + t4
                    t = sgt0 + tl
                    pa, pb = wo_banks[2*t4], wo_banks[2*t4+1]
                    pak, pbk = wo_keys[2*t4], wo_keys[2*t4+1]
                    E('pe', lambda e, ifc=ifc, t4=t4, wc=wc, ic=ic, pa=pa: e.matmul(
                        pa, ic[:, t4*128:(t4+1)*128], wc[:, 0:512],
                        start=(ifc == 0), stop=False),
                      reads=[wck, ick], writes=[pak])
                    E('pe', lambda e, ifc=ifc, t4=t4, wc=wc, ic=ic, pb=pb: e.matmul(
                        pb, ic[:, t4*128:(t4+1)*128], wc[:, 512:1024],
                        start=(ifc == 0), stop=False),
                      reads=[wck, ick], writes=[pbk])
            for t4 in range(4):
                pa, pb = wo_banks[2*t4], wo_banks[2*t4+1]
                pak, pbk = wo_keys[2*t4], wo_keys[2*t4+1]
                E('pe', lambda e, pa=pa: e.matmul(pa, ones_row[0:1, 0:128],
                                                  bof_sb[:, 0:512], start=False, stop=True),
                  reads=['ones_row', 'bof_sb'], writes=[pak])
                E('pe', lambda e, pb=pb: e.matmul(pb, ones_row[0:1, 0:128],
                                                  bof_sb[:, 512:1024], start=False, stop=True),
                  reads=['ones_row', 'bof_sb'], writes=[pbk])
            order = [0, 1, 2, 3] if hb == 0 else [2, 3, 0, 1]
            tbufs = [(xq_in[0], ('xq_in', 0)), (xq_in[1], ('xq_in', 1))]
            ybufs = [(attn_f, 'attn_f'), (y1_t, 'y1')]

            def rl_t1(i):
                t = sgt0 + hb * 4 + order[i]
                buf, key = tbufs[i % 2]
                E('sp', lambda e, t=t, buf=buf: e.dma_start(
                    buf.ap(), t1s_d[t*128:(t+1)*128, :]),
                  reads=[('t1s', t)], writes=[key], dma=f'rl{i % 2}')

            def rl_y1(i):
                t = sgt0 + hb * 4 + order[i]
                buf, key = ybufs[i % 2]
                E('sp', lambda e, t=t, buf=buf: e.dma_start(
                    buf.ap(), y1s_d[t*128:(t+1)*128, :]),
                  reads=[('y1s', t)], writes=[key], dma=f'rl{2 + i % 2}')

            rl_t1(0); rl_y1(0); rl_t1(1); rl_y1(1)
            for i, t4 in enumerate(order):
                t = sgt0 + hb * 4 + t4
                pa, pb = wo_banks[2*t4], wo_banks[2*t4+1]
                pak, pbk = wo_keys[2*t4], wo_keys[2*t4+1]
                tb, tbk = tbufs[i % 2]
                yb, ybk = ybufs[i % 2]
                E('dve', lambda e, pa=pa, tb=tb: e.tensor_tensor(
                    t2_t[:, 0:512], pa, tb[:, 0:512], ALU.add),
                  reads=[pak, tbk], writes=['T2'])
                E('dve', lambda e, pb=pb, tb=tb: e.tensor_tensor(
                    t2_t[:, 512:1024], pb, tb[:, 512:1024], ALU.add),
                  reads=[pbk, tbk], writes=['T2'])
                if i + 2 < 4:
                    rl_t1(i + 2)
                layernorm(t2_t[:, :], ['T2'], t3_t[:, :], 'T3', 128, epsB, 'epsB',
                          sts[2], 'sC')
                E('dve', lambda e: e.tensor_tensor(t3_t[:, :], t3_t[:, :], gam_b[:, :],
                                                   ALU.mult),
                  reads=['T3', 'gam_b'], writes=['T3'])
                E('dve', lambda e: e.tensor_tensor(t3_t[:, :], t3_t[:, :], bet_b[:, :],
                                                   ALU.add),
                  reads=['T3', 'bet_b'], writes=['T3'])
                layernorm(t3_t[:, :], ['T3'], t2_t[:, :], 'T2', 128, epsA, 'epsA',
                          sts[3], 'sD')
                E('dve', lambda e, yb=yb: e.tensor_tensor(t3_t[:, :], t2_t[:, :],
                                                          yb[:, :], ALU.add),
                  reads=['T2', ybk], writes=['T3'])
                if i + 2 < 4:
                    rl_y1(i + 2)
                E('sp', lambda e, t=t: e.dma_start(out_d[t*128:(t+1)*128, :], t3_t[:, :]),
                  reads=['T3'], dma='out')

    emit_w1_prefetch()
    for tl in range(8):
        emit_A1_tile(0, tl)
        emit_A2_tile(0, tl)
        emit_A3_tile(0, tl)
    for tl in range(8):
        emit_B_tile(0, tl)
    for sg in range(4):
        def hook(ifc, sg=sg):
            if sg == 3:
                return
            if ifc >= 6 and (ifc - 6) % 3 == 0 and (ifc - 6) // 3 < 8:
                emit_A1_tile(sg + 1, (ifc - 6) // 3)
            if ifc >= 8 and (ifc - 8) % 3 == 0 and (ifc - 8) // 3 < 8:
                emit_A2_tile(sg + 1, (ifc - 8) // 3)
            if ifc >= 9 and (ifc - 9) % 3 == 0 and (ifc - 9) // 3 < 8:
                emit_A3_tile(sg + 1, (ifc - 9) // 3)
        emit_h(sg, hook)
        if sg < 3:
            for tl in range(8):
                emit_B_tile(sg + 1, tl)
        emit_wo(sg)

    E = _E
    if debug:
        p.wait_all_dma('pool', 'dbg')
    p.wait_all_dma('pool', 'out')
    p.build()
    p.close()
    phc.close()
    sbstack.close()
    return nc


_CACHE = {}


def _host_inputs(inputs):
    x = np.asarray(inputs['L0_attn_x'], np.float32)
    wqT = np.ascontiguousarray(np.asarray(inputs['L0_attn_Wq'], np.float32).T)
    wkT = np.ascontiguousarray(np.asarray(inputs['L0_attn_Wk'], np.float32).T)
    wvT = np.ascontiguousarray(np.asarray(inputs['L0_attn_Wv'], np.float32).T)
    woT = np.ascontiguousarray(np.asarray(inputs['L0_attn_Wo'], np.float32).T)
    bq = np.ascontiguousarray(np.asarray(inputs['L0_attn_bq'], np.float32)[None, :])
    bk = np.ascontiguousarray(np.asarray(inputs['L0_attn_bk'], np.float32)[None, :])
    bv = np.ascontiguousarray(np.asarray(inputs['L0_attn_bv'], np.float32)[None, :])
    bo = np.ascontiguousarray(np.asarray(inputs['L0_attn_bo'], np.float32)[None, :])
    idxk = np.ascontiguousarray(np.asarray(inputs['L0_attn_idx_k'], np.int32))
    Wr = np.ascontiguousarray(np.asarray(inputs['L0_ffn_W_router'], np.float32))
    W1 = np.asarray(inputs['L0_ffn_W1'], np.float32)
    w1r = np.ascontiguousarray(W1.reshape(NE, 8, 128, 32, 128).transpose(3, 1, 0, 2, 4))
    b1r = np.ascontiguousarray(np.asarray(inputs['L0_ffn_b1'], np.float32)
                               .reshape(NE, 32 * 128))
    Wof = np.ascontiguousarray(np.asarray(inputs['L0_ffn_Wo'], np.float32))
    bof = np.ascontiguousarray(np.asarray(inputs['L0_ffn_bo'], np.float32)[None, :])
    gam = np.ascontiguousarray(np.asarray(inputs['L0_ffn_ln_gamma'], np.float32)[None, :])
    bet = np.ascontiguousarray(np.asarray(inputs['L0_ffn_ln_beta'], np.float32)[None, :])
    ident = np.eye(128, dtype=np.float32)
    iotacol = np.tile(np.arange(128, dtype=np.float32)[None, :], (128, 1))
    iotaM = np.empty((128, H * 128), np.float32)
    for h in range(H):
        iotaM[:, h*128:(h+1)*128] = (np.arange(128)[None, :] * 128
                                     + np.arange(128)[:, None])
    xbs = [np.ascontiguousarray(x[0]), np.ascontiguousarray(x[1])]
    in_maps = []
    for c in range(8):
        b, q = c // 4, c % 4
        in_maps.append({
            "x": xbs[b],
            "xq": np.ascontiguousarray(xbs[b][q*QTOK:(q+1)*QTOK]),
            "wqT": wqT, "wkT": wkT, "wvT": wvT, "woT": woT,
            "bq": bq, "bk": bk, "bv": bv, "bo": bo,
            "idxk": idxk, "wr": Wr, "w1r": w1r, "b1r": b1r,
            "wof": Wof, "bof": bof, "gam": gam, "bet": bet,
            "ident": ident, "iotacol": iotacol, "iotaM": iotaM,
            "qoffc": np.full((128, 1), q * QTOK, np.float32),
        })
    return in_maps


def kernel(**inputs):
    if 'nc' not in _CACHE:
        _CACHE['nc'] = build_program(debug=False)
    nc = _CACHE['nc']
    in_maps = _host_inputs(inputs)
    res = run_bass_kernel_spmd(nc, in_maps, core_ids=list(range(8)))
    out = np.empty((2, L, D), np.float32)
    for c in range(8):
        b, q = c // 4, c % 4
        out[b, q*QTOK:(q+1)*QTOK] = res.results[c]["y"]
    return out
